# version 06
# speedup vs baseline: 1.2197x; 1.0464x over previous
import numpy as np
from contextlib import ExitStack

import concourse.bass as bass
import concourse.mybir as mybir
from concourse.bass_utils import run_bass_kernel_spmd

F32 = mybir.dt.float32
BF16 = mybir.dt.bfloat16
AF = mybir.ActivationFunctionType
ALU = mybir.AluOpType
AX = mybir.AxisListType.X

T = 2048
D = 1024
NTILE = T // 128
NCH = D // 128
RMS_EPS0 = 1.1920929e-07
DEBUG = None
DBGSEQ = [0]

ENG_METHOD = {"pe": "tensor", "act": "scalar", "dve": "vector",
              "pool": "gpsimd", "sync": "sync"}
SEM_TOTALS = []


def host_constants():
    c = {}
    c["c_id32"] = np.eye(128, dtype=np.float32)
    c["c_idbf"] = np.eye(128, dtype=np.float32)
    c["c_ones_r"] = np.ones((1, 128), dtype=np.float32)
    c["c_ones_c"] = np.ones((128, 1), dtype=np.float32)
    k = np.arange(128)[:, None]
    q = np.arange(128)[None, :]
    c["c_striu"] = (k < q).astype(np.float32)
    c["c_triu"] = (k <= q).astype(np.float32)
    c["c_dmall"] = (k <= (np.arange(896)[None, :] - 384)).astype(np.float32)
    d_rope = 32
    t_idx = np.arange(T, dtype=np.float32)
    freqs = (1.0 / (100000.0 ** (np.arange(0, d_rope, 2, dtype=np.float32) / d_rope))).astype(np.float32)
    ang = (t_idx[:, None] * freqs[None, :]).astype(np.float32)
    cos = np.cos(ang).astype(np.float32)
    sin = np.sin(ang).astype(np.float32)
    C = np.ones((128, T), dtype=np.float32)
    S = np.zeros((128, T), dtype=np.float32)
    for bb in (32, 96):
        for dd in range(d_rope):
            C[bb + dd, :] = cos[:, dd // 2]
            S[bb + dd, :] = sin[:, dd // 2]
    c["c_ropeC"] = C
    c["c_ropeS"] = S
    Sp = np.zeros((128, 128), dtype=np.float32)
    for bb in (32, 96):
        for p in range(d_rope // 2):
            r0 = bb + 2 * p
            Sp[r0 + 1, r0] = -1.0
            Sp[r0, r0 + 1] = 1.0
    c["c_sperm"] = Sp
    Sp32 = np.zeros((32, 32), dtype=np.float32)
    for p in range(d_rope // 2):
        Sp32[2 * p + 1, 2 * p] = -1.0
        Sp32[2 * p, 2 * p + 1] = 1.0
    c["c_sperm32"] = Sp32
    hi = np.zeros((6, 768), dtype=np.float32)
    for h in range(6):
        hi[h, h * 128:(h + 1) * 128] = 1.0
    c["c_hind"] = hi
    c["c_ones6"] = np.ones((6, 128), dtype=np.float32)
    sel2 = np.zeros((128, 2), dtype=np.float32)
    sel2[0:64, 0] = 1.0
    sel2[64:128, 1] = 1.0
    c["c_sel2"] = sel2
    c["c_exp2"] = sel2.T.copy()
    c["c_qsc"] = np.full((1, 128), 128.0 ** -0.5, dtype=np.float32)
    return c

CONST_DTYPES = {"c_idbf": BF16, "c_sperm": BF16, "c_sperm32": BF16,
                "c_ropeC": BF16, "c_ropeS": BF16, "c_dmall": BF16}
CONST_SHAPES = {"c_id32": [128, 128], "c_idbf": [128, 128],
                "c_ones_r": [1, 128], "c_ones_c": [128, 1],
                "c_striu": [128, 128], "c_triu": [128, 128],
                "c_dmall": [128, 896],
                "c_ropeC": [128, T], "c_ropeS": [128, T],
                "c_sperm": [128, 128], "c_sperm32": [32, 32],
                "c_hind": [6, 768], "c_ones6": [6, 128],
                "c_sel2": [128, 2], "c_exp2": [2, 128], "c_qsc": [1, 128]}
BASE_CONSTS = ["c_id32", "c_idbf", "c_ones_r", "c_ones_c", "c_striu",
               "c_triu", "c_dmall", "c_sperm",
               "c_sperm32", "c_hind", "c_ones6", "c_sel2", "c_exp2", "c_qsc"]


def dm_mask(P, j):
    return P.c["c_dmall"][:, 384 - 128 * j:896 - 128 * j]

WSHAPES = {
    "x0": ([2, T, D], F32),
    "L0_attn_W_DQ": ([512, 64], F32), "L0_attn_W_UQ": ([64, 512], F32),
    "L0_attn_W_DKV": ([1024, 64], F32), "L0_attn_W_UKV": ([128, 1024], F32),
    "L0_attn_W_QR": ([64, 64], F32), "L0_attn_W_KR": ([64, 64], F32),
    "L0_attn_W_O": ([64, 64], F32),
    "L0_attn_q_norm_w": ([64], F32), "L0_attn_k_norm_w": ([64], F32),
    "L0_ffn_gate_w": ([16, 1024], F32), "L0_ffn_expert_bias": ([16], F32),
    "L0_ffn_w_gate": ([16, 1024, 1024], F32), "L0_ffn_w_up": ([16, 1024, 1024], F32),
    "L0_ffn_w_down": ([16, 1024, 1024], F32),
    "L0_ffn_shared_gate_w": ([1024, 1024], F32),
    "L0_ffn_shared_up_w": ([1024, 1024], F32),
    "L0_ffn_shared_down_w": ([1024, 1024], F32),
    "L1_attn_Wq": ([1024, 768], F32), "L1_attn_Wk": ([1024, 768], F32),
    "L1_attn_Wv": ([1024, 1536], F32), "L1_attn_Wb": ([1024, 6], F32),
    "L1_attn_Wa": ([1024, 6], F32), "L1_attn_A_log": ([6], F32),
    "L1_attn_dt_bias": ([6], F32),
    "L1_attn_conv_q": ([768, 4], F32), "L1_attn_conv_k": ([768, 4], F32),
    "L1_attn_conv_v": ([1536, 4], F32),
    "L1_attn_Wg": ([1024, 1536], F32), "L1_attn_norm_w": ([256], F32),
    "L1_attn_Wo": ([1536, 1024], F32),
    "L1_ffn_w_ln1": ([1024], F32), "L1_ffn_Wq": ([1024, 1024], F32),
    "L1_ffn_Wkva": ([1024, 160], F32), "L1_ffn_Wkvb": ([128, 512], F32),
    "L1_ffn_Wo": ([1024, 1024], F32), "L1_ffn_w_ln2": ([1024], F32),
    "L1_ffn_Wgate": ([1024, 8], F32), "L1_ffn_expert_bias": ([8], F32),
    "L1_ffn_Wg": ([8, 1024, 512], F32), "L1_ffn_Wu": ([8, 1024, 512], F32),
    "L1_ffn_Wd": ([8, 512, 1024], F32),
    "L1_ffn_sWg": ([1, 1024, 512], F32), "L1_ffn_sWu": ([1, 1024, 512], F32),
    "L1_ffn_sWd": ([1, 512, 1024], F32),
}
for _n, _s in CONST_SHAPES.items():
    WSHAPES[_n] = (_s, CONST_DTYPES.get(_n, F32))



class Sem:
    def __init__(self, ctx, nc, name):
        self.h = ctx.enter_context(nc.semaphore(name))
        self.v = 0


class Prog:
    def __init__(self, nc, ctx):
        self.nc = nc
        self.ctx = ctx
        self.w = {}
        self.c = {}
        self.sb = {}
        self.uid = 0
        self.s = {n: Sem(ctx, nc, "S_" + n) for n in
                  ["sd", "sg", "sq", "pe", "act", "dve", "pool",
                   "p0", "p1", "p2", "p3"]}

    def name(self, base):
        self.uid += 1
        return f"{base}_{self.uid}"

    def sbuf(self, ctx, base, shape, dtype):
        return ctx.enter_context(self.nc.sbuf_tensor(self.name(base), shape, dtype))

    def psum(self, ctx, base, shape, dtype=F32):
        return ctx.enter_context(self.nc.psum_tensor(self.name(base), shape, dtype))

    def check(self):
        SEM_TOTALS.append({n: sm.v for n, sm in self.s.items()})
        for n, sm in self.s.items():
            assert sm.v < 60000, f"sem {n} total {sm.v}"


def chain(P, steps):
    nc = P.nc
    per_eng = {}
    prev = getattr(P, "last", None)
    prev_ename = None
    for step in steps:
        if len(step) == 2:
            eng, fn = step
            ndma = 1
        else:
            eng, fn, ndma = step
        if eng == "sync_dma":
            ename, sem, inc = "sync", P.s["sd"], 16 * ndma
        elif eng == "pool_dma":
            ename, sem, inc = "pool", P.s["sg"], 16 * ndma
        elif eng == "act_dma":
            ename, sem, inc = "act", P.s["sq"], 16 * ndma
        elif eng == "pool":
            ename, sem, inc = "pool", P.s["pool"], 1
        else:
            ename, sem, inc = eng, P.s[eng], 1
        waits = []
        dma_like = inc >= 16
        if prev is not None:
            waits.append(prev)
        sem.v += inc
        per_eng.setdefault(ename, []).append((waits, fn, sem, sem.v, inc, ndma))
        prev = (sem, sem.v)
        prev_ename = "_dma_" + ename if dma_like else ename
    with nc.Block() as blk:
        for ename, items in per_eng.items():
            def body(e, items=items):
                for waits, fn, sem, val, inc, ndma in items:
                    for (wsem, wval) in waits:
                        e.wait_ge(wsem.h, wval)
                    P.cur_dma = (sem.h, 16)
                    inst = fn(e)
                    inst.then_inc(sem.h, 16 if inc >= 16 else inc)
            getattr(blk, ENG_METHOD[ename])(body)
    P.last = prev


def chain_multi(P, streams):
    nc = P.nc
    entry = getattr(P, "last", None)
    per_eng = {}
    prevs = [entry for _ in streams]
    idx = [0] * len(streams)
    remaining = sum(len(s) for s in streams)
    while remaining:
        for i, s in enumerate(streams):
            if idx[i] >= len(s):
                continue
            step = s[idx[i]]
            idx[i] += 1
            remaining -= 1
            if len(step) == 2:
                eng, fn = step
                ndma = 1
            else:
                eng, fn, ndma = step
            if eng == "sync_dma":
                ename, sem, inc = "sync", P.s["sd"], 16 * ndma
            elif eng == "pool_dma":
                ename, sem, inc = "pool", P.s["sg"], 16 * ndma
            elif eng == "act_dma":
                ename, sem, inc = "act", P.s["sq"], 16 * ndma
            elif eng == "pool":
                ename, sem, inc = "pool", P.s["pool"], 1
            else:
                ename, sem, inc = eng, P.s[eng], 1
            waits = [prevs[i]] if prevs[i] is not None else []
            sem.v += inc
            per_eng.setdefault(ename, []).append((waits, fn, sem, sem.v, inc))
            prevs[i] = (sem, sem.v)
    with nc.Block() as blk:
        for ename, items in per_eng.items():
            def body(e, items=items):
                for waits, fn, sem, val, inc in items:
                    for (wsem, wval) in waits:
                        e.wait_ge(wsem.h, wval)
                    inst = fn(e)
                    inst.then_inc(sem.h, 16 if inc >= 16 else inc)
            getattr(blk, ENG_METHOD[ename])(body)
    jsem = P.s["pool"]
    jsem.v += 1
    jv = jsem.v
    tails = [p for p in prevs if p is not None]
    with nc.Block() as blk:
        @blk.gpsimd
        def _(e):
            for (wsem, wval) in tails:
                e.wait_ge(wsem.h, wval)
            e.memset(P.sb["joinT"][:], 0.0).then_inc(jsem.h, 1)
    P.last = (jsem, jv)


def barrier(P):
    nc = P.nc
    last = getattr(P, "last", None)
    with nc.Block() as blk:
        @blk.sync
        def _(e):
            if last is not None:
                e.wait_ge(last[0].h, last[1])
    P.last = None


def full_barrier(P):
    nc = P.nc
    sems = [(s.h, s.v) for s in P.s.values() if s.v > 0]
    with nc.Block() as blk:
        for ename in ("tensor", "scalar", "vector", "gpsimd", "sync"):
            def body(e, sems=sems):
                for h, v in sems:
                    e.wait_ge(h, v)
            getattr(blk, ename)(body)
    P.last = None


def fresh_sems(P):
    P.s = {n: Sem(P.ctx, P.nc, P.name("S2_" + n)) for n in
           ["sd", "sg", "sq", "pe", "act", "dve", "pool",
            "p0", "p1", "p2", "p3"]}



def load_consts(P, names):
    nc = P.nc
    for n in names:
        P.c[n] = P.sbuf(P.ctx, n, CONST_SHAPES[n], CONST_DTYPES.get(n, F32))
    with nc.Block() as blk:
        @blk.sync
        def _(e):
            for n in names:
                if CONST_DTYPES.get(n, F32) == F32:
                    e.dma_start(P.c[n][:], P.w[n]).then_inc(P.s["sd"].h, 16)
                    P.s["sd"].v += 16
        @blk.gpsimd
        def _(e):
            for n in names:
                if CONST_DTYPES.get(n, F32) == BF16:
                    e.dma_start(P.c[n][:], P.w[n]).then_inc(P.s["sg"].h, 16)
                    P.s["sg"].v += 16
    with nc.Block() as blk:
        @blk.vector
        def _(e):
            e.wait_ge(P.s["sd"].h, P.s["sd"].v)
            e.wait_ge(P.s["sg"].h, P.s["sg"].v)


def stage_l0_small(P):
    nc = P.nc
    for nm, shape, dt in [("weff_q", [128, 64], BF16), ("weff_kv", [128, 128], BF16),
                          ("wqrT", [128, 64], BF16), ("wkrT", [128, 64], BF16),
                          ("woT", [64, 64], BF16), ("qnw", [64, 1], F32),
                          ("knw", [64, 1], F32), ("qnw2", [128, 1], F32),
                          ("knw2", [128, 1], F32), ("weff_qbd", [128, 128], BF16),
                          ("weff_kbd", [128, 128], BF16)]:
        P.sb[nm] = P.sbuf(P.ctx, nm, shape, dt)
    with ExitStack() as ctx:
        wdq = P.sbuf(ctx, "wdq", [128, 4, 64], F32)
        wuq = P.sbuf(ctx, "wuq", [64, 512], F32)
        wuqT = P.sbuf(ctx, "wuqT", [128, 4, 64], F32)
        wdkv = P.sbuf(ctx, "wdkv", [128, 8, 64], F32)
        wukv = P.sbuf(ctx, "wukv", [128, 1024], F32)
        wukvT = P.sbuf(ctx, "wukvT", [128, 8, 128], F32)
        sm64 = P.sbuf(ctx, "sm64", [64, 64], F32)
        pt = P.psum(ctx, "pt", [128, 128], F32)
        pm = P.psum(ctx, "pm", [64, 128], F32)

        steps = []

        def load6(e):
            sh = P.s["sd"].h
            e.dma_start(wdq[:], P.w["L0_attn_W_DQ"].rearrange("(c p) d -> p c d", p=128)).then_inc(sh, 16)
            e.dma_start(wuq[:], P.w["L0_attn_W_UQ"]).then_inc(sh, 16)
            e.dma_start(wdkv[:], P.w["L0_attn_W_DKV"].rearrange("(c p) d -> p c d", p=128)).then_inc(sh, 16)
            e.dma_start(wukv[:], P.w["L0_attn_W_UKV"]).then_inc(sh, 16)
            e.dma_start(P.sb["qnw"][:], P.w["L0_attn_q_norm_w"].rearrange("(p o) -> p o", o=1)).then_inc(sh, 16)
            e.dma_start(P.sb["knw"][:], P.w["L0_attn_k_norm_w"].rearrange("(p o) -> p o", o=1)).then_inc(sh, 16)
            e.dma_start(P.sb["qnw2"][0:64, :], P.w["L0_attn_q_norm_w"].rearrange("(p o) -> p o", o=1)).then_inc(sh, 16)
            e.dma_start(P.sb["qnw2"][64:128, :], P.w["L0_attn_q_norm_w"].rearrange("(p o) -> p o", o=1)).then_inc(sh, 16)
            e.dma_start(P.sb["knw2"][0:64, :], P.w["L0_attn_k_norm_w"].rearrange("(p o) -> p o", o=1)).then_inc(sh, 16)
            return e.dma_start(P.sb["knw2"][64:128, :], P.w["L0_attn_k_norm_w"].rearrange("(p o) -> p o", o=1))
        steps.append(("sync_dma", load6, 10))

        for j in range(4):
            steps.append(("pe", lambda e, j=j: e.transpose(
                pt[0:128, 0:64], wuq[0:64, j * 128:(j + 1) * 128], P.c["c_id32"][0:64, 0:64])))
            steps.append(("act", lambda e, j=j: e.copy(wuqT[:, j, :], pt[0:128, 0:64])))
        for j in range(8):
            steps.append(("pe", lambda e, j=j: e.transpose(
                pt[:], wukv[:, j * 128:(j + 1) * 128], P.c["c_id32"][:])))
            steps.append(("act", lambda e, j=j: e.copy(wukvT[:, j, :], pt[:])))

        def mm_weffq(e):
            for j in range(4):
                i = e.matmul(pm[0:64, 0:64], wdq[:, j, :], wuqT[:, j, :],
                             start=(j == 0), stop=(j == 3))
            return i
        steps.append(("pe", mm_weffq))
        steps.append(("act", lambda e: (e.copy(P.sb["weff_q"][0:64, :], pm[0:64, 0:64]),
                                        e.copy(P.sb["weff_q"][64:128, :], pm[0:64, 0:64]))[-1]))

        def mm_weffkv(e):
            for j in range(8):
                i = e.matmul(pm[0:64, :], wdkv[:, j, :], wukvT[:, j, :],
                             start=(j == 0), stop=(j == 7))
            return i
        steps.append(("pe", mm_weffkv))
        steps.append(("act", lambda e: (e.copy(P.sb["weff_kv"][0:64, :], pm[0:64, :]),
                                        e.copy(P.sb["weff_kv"][64:128, :], pm[0:64, :]))[-1]))

        for wname, dst, dual in [("L0_attn_W_QR", "wqrT", True),
                                 ("L0_attn_W_KR", "wkrT", True),
                                 ("L0_attn_W_O", "woT", False)]:
            steps.append(("sync_dma", lambda e, wname=wname: e.dma_start(sm64[:], P.w[wname])))
            steps.append(("pe", lambda e: e.transpose(pt[0:64, 0:64], sm64[:], P.c["c_id32"][0:64, 0:64])))
            if dual:
                steps.append(("act", lambda e, dst=dst: (
                    e.copy(P.sb[dst][0:64, :], pt[0:64, 0:64]),
                    e.copy(P.sb[dst][64:128, :], pt[0:64, 0:64]))[-1]))
            else:
                steps.append(("act", lambda e, dst=dst: e.copy(P.sb[dst][:], pt[0:64, 0:64])))
        steps.append(("pool", lambda e: (
            e.memset(P.sb["weff_qbd"][:], 0.0),
            e.memset(P.sb["weff_kbd"][:], 0.0))[-1]))
        steps.append(("act", lambda e: (
            e.copy(P.sb["weff_qbd"][0:64, 0:64], P.sb["weff_q"][0:64, :]),
            e.copy(P.sb["weff_qbd"][64:128, 64:128], P.sb["weff_q"][64:128, :]),
            e.copy(P.sb["weff_kbd"][0:64, 0:64], P.sb["weff_kv"][0:64, 0:64]),
            e.copy(P.sb["weff_kbd"][64:128, 64:128], P.sb["weff_kv"][64:128, 0:64]))[-1]))
        chain(P, steps)


def stage_l0_big(P):
    nc = P.nc
    P.w["scr_w0"] = nc.dram_tensor("scr_w0", [51, 1024, 1024], BF16, kind="Internal").ap()
    scr_tmp = nc.dram_tensor("scr_wtmp", [2, 1024, 1024], BF16, kind="Internal").ap()
    srcs = ([("L0_ffn_w_gate", e, e) for e in range(16)] +
            [("L0_ffn_w_up", e, 16 + e) for e in range(16)] +
            [("L0_ffn_w_down", e, 32 + e) for e in range(16)] +
            [("L0_ffn_shared_gate_w", None, 48),
             ("L0_ffn_shared_up_w", None, 49),
             ("L0_ffn_shared_down_w", None, 50)])
    with ExitStack() as ctx:
        tile = [P.sbuf(ctx, "stg", [128, 8, 1024], BF16) for _ in range(2)]
        sd, sg = P.s["sd"], P.s["sg"]
        sq = P.s["sq"]
        n = len(srcs)
        cast_m = [16 * (i + 1) + sg.v for i in range(n)]
        tr_m = [sd.v + 16 * 8 * (i + 1) for i in range(n)]
        st_m = [sq.v + 16 * (i + 1) for i in range(n)]
        with nc.Block() as blk:
            @blk.gpsimd
            def _(e):
                for i, (name, eidx, m) in enumerate(srcs):
                    src = P.w[name] if eidx is None else P.w[name][eidx]
                    if i >= 2:
                        e.wait_ge(sd.h, tr_m[i - 2])
                    e.dma_start(scr_tmp[i % 2], src).then_inc(sg.h, 16)
            @blk.sync
            def _(e):
                for i, (name, eidx, m) in enumerate(srcs):
                    e.wait_ge(sg.h, cast_m[i])
                    if i >= 2:
                        e.wait_ge(sq.h, st_m[i - 2])
                    for j in range(8):
                        e.dma_start(tile[i % 2][:, j, :],
                                    scr_tmp[i % 2][:, j * 128:(j + 1) * 128],
                                    transpose=True).then_inc(sd.h, 16)
            @blk.scalar
            def _(e):
                for i, (name, eidx, m) in enumerate(srcs):
                    e.wait_ge(sd.h, tr_m[i])
                    e.dma_start(P.w["scr_w0"][m].rearrange("(c p) i -> p c i", p=128),
                                tile[i % 2][:]).then_inc(sq.h, 16)
            @blk.vector
            def _(e):
                e.wait_ge(sq.h, st_m[-1])
        sg.v = cast_m[-1]
        sd.v = tr_m[-1]
        sq.v = st_m[-1]

    P.sb["gwT"] = P.sbuf(P.ctx, "gwT", [128, 8, 16], BF16)
    P.sb["l0bias"] = P.sbuf(P.ctx, "l0bias", [128, 16], F32)
    with ExitStack() as ctx:
        gw = P.sbuf(ctx, "gw", [16, 1024], F32)
        brow = P.sbuf(ctx, "brow", [1, 16], F32)
        pt = P.psum(ctx, "ptg", [128, 16], F32)
        def load2(e):
            e.dma_start(gw[:], P.w["L0_ffn_gate_w"]).then_inc(P.s["sd"].h, 16)
            return e.dma_start(brow[:], P.w["L0_ffn_expert_bias"].rearrange("(o e) -> o e", o=1))
        steps = [("sync_dma", load2, 2)]
        for j in range(8):
            steps.append(("pe", lambda e, j=j: e.transpose(
                pt[:], gw[:, j * 128:(j + 1) * 128], P.c["c_id32"][0:16, 0:16])))
            steps.append(("act", lambda e, j=j: e.copy(P.sb["gwT"][:, j, :], pt[:, 0:16])))
        steps.append(("pe", lambda e: e.matmul(pt[:], P.c["c_ones_r"][:], brow[:], start=True, stop=True)))
        steps.append(("act", lambda e: e.copy(P.sb["l0bias"][:], pt[:, 0:16])))
        chain(P, steps)



def emit_ln(P, x_ap, xn_fm, xn32_scr=None):
    nc = P.nc
    with ExitStack() as ctx:
        xt = [P.sbuf(ctx, "lnx", [128, D], F32) for _ in range(2)]
        xnb = P.sbuf(ctx, "lnbf", [128, D], BF16)
        xn32 = P.sbuf(ctx, "ln32", [128, D], F32)
        xn32b = P.sbuf(ctx, "ln32b", [128, D], F32)
        sq = P.sbuf(ctx, "lnsq", [128, D], F32)
        st = P.sbuf(ctx, "lnst", [128, 8], F32)
        ptr = [P.psum(ctx, "lnp", [128, 128], BF16) for _ in range(2)]
        for i in range(NTILE):
            b = i % 2
            steps = []
            steps.append(("sync_dma", lambda e, i=i, b=b: e.dma_start(
                xt[b][:], x_ap[i * 128:(i + 1) * 128, :])))
            steps.append(("act", lambda e, b=b: e.activation(
                sq[:], xt[b][:], AF.Square, accum_out=st[:, 1:2])))
            steps.append(("dve", lambda e, b=b: e.reduce_sum(st[:, 0:1], xt[b][:], axis=AX)))
            steps.append(("dve", lambda e: e.tensor_scalar_mul(st[:, 2:3], st[:, 0:1], 1.0 / D)))
            steps.append(("dve", lambda e: e.tensor_mul(st[:, 4:5], st[:, 2:3], st[:, 2:3])))
            steps.append(("dve", lambda e: e.scalar_tensor_tensor(
                st[:, 3:4], st[:, 1:2], 1.0 / D, st[:, 4:5], ALU.mult, ALU.subtract)))
            steps.append(("dve", lambda e: e.tensor_scalar_add(st[:, 3:4], st[:, 3:4], 1e-5)))
            steps.append(("act", lambda e: e.activation(st[:, 5:6], st[:, 3:4], AF.Ln)))
            steps.append(("act", lambda e: e.activation(st[:, 5:6], st[:, 5:6], AF.Exp, scale=-0.5)))
            steps.append(("dve", lambda e, b=b: e.tensor_scalar_sub(xn32[:], xt[b][:], st[:, 2:3])))
            def norm(e, scr=(xn32_scr is not None)):
                i1 = e.tensor_scalar_mul(xnb[:], xn32[:], st[:, 5:6])
                if scr:
                    i1 = e.tensor_scalar_mul(xn32b[:], xn32[:], st[:, 5:6])
                return i1
            steps.append(("dve", norm))
            if xn32_scr is not None:
                steps.append(("act_dma", lambda e, i=i: e.dma_start(
                    xn32_scr[i * 128:(i + 1) * 128, :], xn32b[:])))
            for j in range(NCH):
                steps.append(("pe", lambda e, j=j: e.transpose(
                    ptr[j % 2][:], xnb[:, j * 128:(j + 1) * 128], P.c["c_idbf"][:])))
                steps.append(("dve", lambda e, i=i, j=j: e.tensor_copy(
                    xn_fm[:, j, i * 128:(i + 1) * 128], ptr[j % 2][:])))
            chain(P, steps)



def emit_l0_attn(P, xn_fm, s_buf):
    nc = P.nc
    with ExitStack() as ctx:
        q_t = P.sbuf(ctx, "qt", [128, T], BF16)
        k_t = P.sbuf(ctx, "kt", [128, T], BF16)
        v_tm = P.sbuf(ctx, "vtm", [128, NTILE, 65], BF16)
        vbf = P.sbuf(ctx, "vbf", [64, 512], BF16)
        qc_s = P.sbuf(ctx, "qcs", [64, 512], F32)
        sqs = P.sbuf(ctx, "sqs", [128, 512], F32)
        sqs2 = P.sbuf(ctx, "sqs2", [64, 512], F32)
        rrow = P.sbuf(ctx, "rrow", [1, 512], F32)
        srow = P.sbuf(ctx, "srow", [1, 16], F32)
        bnd = P.sbuf(ctx, "bnd", [128, 1], F32)
        ctxs = P.sbuf(ctx, "ctxs", [64, 512], BF16)
        invr = P.sbuf(ctx, "invr", [1, 512], F32)
        invc = P.sbuf(ctx, "invc", [128, 4], F32)
        p_t = [P.sbuf(ctx, "ptile", [128, 512], BF16) for _ in range(2)]

        pA = P.psum(ctx, "pA", [128, 512], F32)
        pB = P.psum(ctx, "pB", [128, 512], F32)
        pC = P.psum(ctx, "pC", [128, 512], F32)
        pD = P.psum(ctx, "pD", [128, 512], F32)
        pEbf = P.psum(ctx, "pEbf", [128, 128], BF16)
        pF = P.psum(ctx, "pF", [128, 72], F32)
        pkv = pA
        pq = pB[0:64, :]
        pbc = pC[0:64, :]
        prr = pC[0:1, :]
        pst = [pA, pB]
        pctx = pD[0:65, :]
        ptv = pEbf
        psh = pF[:, 0:64]
        pcl = pF[:, 64:72]

        pe, act, dve = P.s["pe"], P.s["act"], P.s["dve"]

        with nc.Block() as blk:
            @blk.gpsimd
            def _(e):
                e.memset(v_tm[:], 0.0)
                e.memset(v_tm[:, :, 64:65], 1.0)
        P.last = None

        for h in range(16):
            base = (h % 2) * 64
            chk = h // 2
            weff_q = P.sb["weff_q"][base:base + 64, :]
            weff_kv = P.sb["weff_kv"][base:base + 64, :]
            wkrT = P.sb["wkrT"][base:base + 64, :]

            for n in range(4):
                sl = slice(n * 512, (n + 1) * 512)
                xh_n = xn_fm[base:base + 64, chk, sl]
                steps = []
                steps.append(("pe", lambda e, xh_n=xh_n, weff_q=weff_q, weff_kv=weff_kv: (
                    e.matmul(pq[:], weff_q, xh_n, start=True, stop=True),
                    e.matmul(pkv[:], weff_kv, xh_n, start=True, stop=True))[-1]))
                steps.append(("act", lambda e: (
                    e.activation(sqs[0:64, :], pq[:], AF.Square),
                    e.activation(sqs2[:], pkv[0:64, :], AF.Square),
                    e.copy(vbf[:], pkv[64:128, :]))[-1]))
                steps.append(("pe", lambda e: e.matmul(prr[:], P.c["c_ones_c"][0:64, :], sqs[0:64, :], start=True, stop=True)))
                steps.append(("dve", lambda e: e.tensor_scalar_mul(rrow[:], prr[:], 1.0 / 64.0)))
                steps.append(("dve", lambda e: e.tensor_scalar_add(rrow[:], rrow[:], RMS_EPS0)))
                steps.append(("act", lambda e: e.activation(rrow[:], rrow[:], AF.Ln)))
                steps.append(("act", lambda e: e.activation(rrow[:], rrow[:], AF.Exp, scale=-0.5)))
                steps.append(("pe", lambda e: e.matmul(pbc[:], P.c["c_ones_r"][0:1, 0:64], rrow[:], start=True, stop=True)))
                steps.append(("dve", lambda e: e.tensor_scalar_mul(qc_s[:], pq[:], P.sb["qnw"][:, 0:1])))
                steps.append(("dve", lambda e, sl=sl: e.tensor_mul(q_t[0:64, sl], qc_s[:], pbc[:])))
                steps.append(("pe", lambda e: e.matmul(prr[:], P.c["c_ones_c"][0:64, :], sqs2[:], start=True, stop=True)))
                steps.append(("dve", lambda e: e.tensor_scalar_mul(rrow[:], prr[:], 1.0 / 64.0)))
                steps.append(("dve", lambda e: e.tensor_scalar_add(rrow[:], rrow[:], RMS_EPS0)))
                steps.append(("act", lambda e: e.activation(rrow[:], rrow[:], AF.Ln)))
                steps.append(("act", lambda e: e.activation(rrow[:], rrow[:], AF.Exp, scale=-0.5)))
                steps.append(("pe", lambda e: e.matmul(pbc[:], P.c["c_ones_r"][0:1, 0:64], rrow[:], start=True, stop=True)))
                steps.append(("dve", lambda e: e.tensor_scalar_mul(qc_s[:], pkv[0:64, :], P.sb["knw"][:, 0:1])))
                steps.append(("dve", lambda e, sl=sl: e.tensor_mul(k_t[0:64, sl], qc_s[:], pbc[:])))
                for jj in range(4):
                    steps.append(("pe", lambda e, jj=jj: e.transpose(
                        ptv[:, 0:64], vbf[:, jj * 128:(jj + 1) * 128], P.c["c_idbf"][0:64, 0:64])))
                    steps.append(("act", lambda e, n=n, jj=jj: e.copy(
                        v_tm[:, n * 4 + jj, 0:64], ptv[:, 0:64])))
                steps.append(("pe", lambda e, sl=sl, xh_n=xh_n, wkrT=wkrT: (
                    e.matmul(pq[:], P.sb["wqrT"][:], q_t[0:64, sl], start=True, stop=True),
                    e.matmul(pbc[:], wkrT, xh_n, start=True, stop=True))[-1]))
                steps.append(("act", lambda e, sl=sl: (
                    e.copy(q_t[64:128, sl], pq[:]),
                    e.copy(k_t[64:128, sl], pbc[:]))[-1]))
                chain(P, steps)

            steps = []
            for (tile, off) in ((q_t, 0), (k_t, 4)):
                for n in range(4):
                    steps.append(("act", lambda e, tile=tile, n=n: e.activation(
                        sqs[:], tile[:, n * 512:(n + 1) * 512], AF.Square)))
                    steps.append(("pe", lambda e: e.matmul(
                        prr[:], P.c["c_ones_c"][:], sqs[:], start=True, stop=True)))
                    steps.append(("dve", lambda e, off=off, n=n: e.reduce_max(
                        srow[:, off + n:off + n + 1], prr[:], axis=AX)))
            steps.append(("dve", lambda e: (
                e.reduce_max(srow[:, 8:9], srow[:, 0:4], axis=AX),
                e.reduce_max(srow[:, 9:10], srow[:, 4:8], axis=AX))[-1]))
            steps.append(("dve", lambda e: e.tensor_mul(srow[:, 8:9], srow[:, 8:9], srow[:, 9:10])))
            steps.append(("act", lambda e: e.activation(srow[:, 8:9], srow[:, 8:9], AF.Ln)))
            steps.append(("act", lambda e: e.activation(srow[:, 8:9], srow[:, 8:9], AF.Exp, scale=0.5)))
            steps.append(("act", lambda e: e.mul(srow[:, 8:9], srow[:, 8:9], -0.125)))
            steps.append(("pe", lambda e: e.matmul(pcl[:, 0:1], P.c["c_ones_r"][:], srow[:, 8:9], start=True, stop=True)))
            steps.append(("dve", lambda e: e.tensor_copy(bnd[:], pcl[:, 0:1])))
            if DEBUG is not None:
                steps.append(("sync_dma", lambda e, h=h: e.dma_start(
                    DEBUG["srow"][DBGSEQ[0], h], srow[:])))
                if h == 0:
                    steps.append(("sync_dma", lambda e: e.dma_start(
                        DEBUG["q"][DBGSEQ[0]], q_t[:])))
                    steps.append(("sync_dma", lambda e: e.dma_start(
                        DEBUG["k"][DBGSEQ[0]], k_t[:])))
            chain(P, steps)

            for qc in range(4):
                nkt = (qc + 1) * 4
                qsl = slice(qc * 512, (qc + 1) * 512)
                pe0 = pe.v
                act0 = act.v
                dve0 = dve.v
                ST_m = {}
                PV_m = {}
                m = pe0
                for kt in range(nkt):
                    m += 1
                    ST_m[kt] = m
                    if kt >= 1:
                        m += 1
                        PV_m[kt - 1] = m
                m += 1
                PV_m[nkt - 1] = m
                exp_m = {kt: act0 + kt + 1 for kt in range(nkt)}
                msk_m = {kt: dve0 + (kt - (nkt - 4)) + 1 for kt in range(nkt - 4, nkt)}

                def p_ready(kt):
                    if kt >= nkt - 4:
                        return (dve, msk_m[kt])
                    return (act, exp_m[kt])

                start_wait = P.last
                with nc.Block() as blk:
                    @blk.tensor
                    def _(e):
                        if start_wait is not None:
                            e.wait_ge(start_wait[0].h, start_wait[1])
                        for kt in range(nkt):
                            bb = kt % 2
                            if kt >= 2:
                                e.wait_ge(act.h, exp_m[kt - 2])
                            e.matmul(pst[bb][:], k_t[:, kt * 128:(kt + 1) * 128],
                                     q_t[:, qsl], start=True, stop=True).then_inc(pe.h, 1)
                            if kt >= 1:
                                ss, vv = p_ready(kt - 1)
                                e.wait_ge(ss.h, vv)
                                e.matmul(pctx[:], v_tm[:, kt - 1, :], p_t[(kt - 1) % 2][:],
                                         start=(kt - 1 == 0), stop=False).then_inc(pe.h, 1)
                        ss, vv = p_ready(nkt - 1)
                        e.wait_ge(ss.h, vv)
                        e.matmul(pctx[:], v_tm[:, nkt - 1, :], p_t[(nkt - 1) % 2][:],
                                 start=(nkt - 1 == 0), stop=True).then_inc(pe.h, 1)
                    @blk.scalar
                    def _(e):
                        for kt in range(nkt):
                            bb = kt % 2
                            e.wait_ge(pe.h, ST_m[kt])
                            if kt >= 2:
                                e.wait_ge(pe.h, PV_m[kt - 2])
                            e.activation(p_t[bb][:], pst[bb][:], AF.Exp, scale=0.125,
                                         bias=bnd[:, 0:1]).then_inc(act.h, 1)
                    @blk.vector
                    def _(e):
                        for kt in range(nkt - 4, nkt):
                            e.wait_ge(act.h, exp_m[kt])
                            j = kt - (nkt - 4)
                            e.tensor_mul(p_t[kt % 2][:], p_t[kt % 2][:],
                                         dm_mask(P, j)).then_inc(dve.h, 1)
                pe.v = m
                act.v = act0 + nkt
                dve.v = dve0 + 4
                P.last = (pe, m)

                steps = []
                steps.append(("dve", lambda e: e.reciprocal(invr[:], pctx[64:65, :])))
                for j in range(4):
                    steps.append(("pe", lambda e, j=j: e.transpose(
                        pcl[:, j:j + 1], invr[0:1, j * 128:(j + 1) * 128], P.c["c_id32"][0:1, 0:1])))
                    steps.append(("dve", lambda e, j=j: e.tensor_copy(invc[:, j:j + 1], pcl[:, j:j + 1])))
                steps.append(("act", lambda e: e.copy(ctxs[:], pctx[0:64, :])))
                for j in range(4):
                    steps.append(("pe", lambda e, j=j: e.matmul(
                        psh[:], ctxs[:, j * 128:(j + 1) * 128], P.sb["woT"][:],
                        start=True, stop=True)))
                    steps.append(("act", lambda e, qc=qc, j=j, h=h: e.activation(
                        s_buf[:, qc * 4 + j, h * 64:(h + 1) * 64], psh[:],
                        AF.Copy, scale=invc[:, j:j + 1])))
                chain(P, steps)


def _attn_pipe(P, kh, qh, vtm_h, bndcol, pst, pctx, p_t, qc):
    nc = P.nc
    pe, act, dve = P.s["pe"], P.s["act"], P.s["dve"]
    nkt = (qc + 1) * 4
    qsl = slice(qc * 512, (qc + 1) * 512)
    pe0, act0, dve0 = pe.v, act.v, dve.v
    ST_m = {}
    PV_m = {}
    m = pe0
    for kt in range(nkt):
        m += 1
        ST_m[kt] = m
        if kt >= 1:
            m += 1
            PV_m[kt - 1] = m
    m += 1
    PV_m[nkt - 1] = m
    exp_m = {kt: act0 + kt + 1 for kt in range(nkt)}
    msk_m = {kt: dve0 + (kt - (nkt - 4)) + 1 for kt in range(nkt - 4, nkt)}

    def p_ready(kt):
        if kt >= nkt - 4:
            return (dve, msk_m[kt])
        return (act, exp_m[kt])

    start_wait = P.last
    with nc.Block() as blk:
        @blk.tensor
        def _(e):
            if start_wait is not None:
                e.wait_ge(start_wait[0].h, start_wait[1])
            for kt in range(nkt):
                bb = kt % 2
                if kt >= 2:
                    e.wait_ge(act.h, exp_m[kt - 2])
                e.matmul(pst[bb][:], kh[:, kt * 128:(kt + 1) * 128],
                         qh[:, qsl], start=True, stop=True).then_inc(pe.h, 1)
                if kt >= 1:
                    ss, vv = p_ready(kt - 1)
                    e.wait_ge(ss.h, vv)
                    e.matmul(pctx[:], vtm_h[:, kt - 1, :], p_t[(kt - 1) % 2][:],
                             start=(kt - 1 == 0), stop=False).then_inc(pe.h, 1)
            ss, vv = p_ready(nkt - 1)
            e.wait_ge(ss.h, vv)
            e.matmul(pctx[:], vtm_h[:, nkt - 1, :], p_t[(nkt - 1) % 2][:],
                     start=(nkt - 1 == 0), stop=True).then_inc(pe.h, 1)
        @blk.scalar
        def _(e):
            for kt in range(nkt):
                bb = kt % 2
                e.wait_ge(pe.h, ST_m[kt])
                if kt >= 2:
                    e.wait_ge(pe.h, PV_m[kt - 2])
                e.activation(p_t[bb][:], pst[bb][:], AF.Exp, scale=0.125,
                             bias=bndcol).then_inc(act.h, 1)
        @blk.vector
        def _(e):
            for kt in range(nkt - 4, nkt):
                e.wait_ge(act.h, exp_m[kt])
                j = kt - (nkt - 4)
                e.tensor_mul(p_t[kt % 2][:], p_t[kt % 2][:],
                             dm_mask(P, j)).then_inc(dve.h, 1)
    pe.v = m
    act.v = act0 + nkt
    dve.v = dve0 + 4
    P.last = (pe, m)


def emit_l0_attn2(P, xn_fm, s_buf):
    nc = P.nc
    with ExitStack() as ctx:
        Qc2 = P.sbuf(ctx, "Qc2", [128, T], F32)
        Kc2 = P.sbuf(ctx, "Kc2", [128, T], F32)
        SQA = P.sbuf(ctx, "SQA", [128, T], F32)
        qhE = P.sbuf(ctx, "qhE", [128, T], BF16)
        qhO = P.sbuf(ctx, "qhO", [128, T], BF16)
        khE = P.sbuf(ctx, "khE", [128, T], BF16)
        khO = P.sbuf(ctx, "khO", [128, T], BF16)
        vtmE = P.sbuf(ctx, "vtmE", [128, NTILE, 65], BF16)
        vtmO = P.sbuf(ctx, "vtmO", [128, NTILE, 65], BF16)
        rr2q = P.sbuf(ctx, "rr2q", [2, T], F32)
        rr2k = P.sbuf(ctx, "rr2k", [2, T], F32)
        qmax = P.sbuf(ctx, "qmax", [1, 16], F32)
        mx = P.sbuf(ctx, "mx", [1, 8], F32)
        bnd2 = P.sbuf(ctx, "bnd2", [128, 2], F32)
        invrE = P.sbuf(ctx, "invrE", [1, 512], F32)
        invrO = P.sbuf(ctx, "invrO", [1, 512], F32)
        invc = P.sbuf(ctx, "invc", [128, 8], F32)
        ctxsE = P.sbuf(ctx, "ctxsE", [64, 512], BF16)
        ctxsO = P.sbuf(ctx, "ctxsO", [64, 512], BF16)
        p_t = [P.sbuf(ctx, "ptile", [128, 512], BF16) for _ in range(2)]
        pA = P.psum(ctx, "pA", [128, 512], F32)
        pB = P.psum(ctx, "pB", [128, 512], F32)
        pC = P.psum(ctx, "pC", [128, 512], F32)
        pD = P.psum(ctx, "pD", [128, 512], F32)
        pF = P.psum(ctx, "pF", [128, 72], F32)

        chain(P, [("pool", lambda e: (
            e.memset(vtmE[:], 0.0), e.memset(vtmE[:, :, 64:65], 1.0),
            e.memset(vtmO[:], 0.0), e.memset(vtmO[:, :, 64:65], 1.0))[-1])])

        for p in range(8):
            he, ho = 2 * p, 2 * p + 1
            steps = []
            for n in range(4):
                sl = slice(n * 512, (n + 1) * 512)
                pq = pA if n % 2 == 0 else pB
                pk = pC if n % 2 == 0 else pD
                steps.append(("pe", lambda e, sl=sl, pq=pq, pk=pk, p=p: (
                    e.matmul(pq[:], P.sb["weff_qbd"][:], xn_fm[:, p, sl], start=True, stop=True),
                    e.matmul(pk[:], P.sb["weff_kbd"][:], xn_fm[:, p, sl], start=True, stop=True))[-1]))
                steps.append(("act", lambda e, sl=sl, pq=pq, pk=pk: (
                    e.copy(Qc2[:, sl], pq[:]),
                    e.copy(Kc2[:, sl], pk[:]))[-1]))
            for half in range(2):
                t0 = half * 8
                def mmv(e, t0=t0, p=p):
                    for j in range(8):
                        tsl = slice((t0 + j) * 128, (t0 + j + 1) * 128)
                        e.matmul(pA[:, j * 64:(j + 1) * 64], xn_fm[0:64, p, tsl],
                                 P.sb["weff_kv"][0:64, 64:128], start=True, stop=True)
                        i1 = e.matmul(pB[:, j * 64:(j + 1) * 64], xn_fm[64:128, p, tsl],
                                      P.sb["weff_kv"][64:128, 64:128], start=True, stop=True)
                    return i1
                steps.append(("pe", mmv))
                steps.append(("act", lambda e, t0=t0: (
                    e.copy(vtmE[:, t0:t0 + 8, 0:64], pA[:].rearrange("p (a b) -> p a b", a=8)),
                    e.copy(vtmO[:, t0:t0 + 8, 0:64], pB[:].rearrange("p (a b) -> p a b", a=8)))[-1]))
            steps.append(("act", lambda e: e.activation(SQA[:], Qc2[:], AF.Square)))
            for n in range(4):
                sl = slice(n * 512, (n + 1) * 512)
                pX = pC if n % 2 == 0 else pD
                steps.append(("pe", lambda e, sl=sl, pX=pX: e.matmul(
                    pX[0:2, :], P.c["c_sel2"][:], SQA[:, sl], start=True, stop=True)))
                steps.append(("dve", lambda e, sl=sl, pX=pX: e.tensor_scalar_mul(
                    rr2q[:, sl], pX[0:2, :], 1.0 / 64.0)))
            steps.append(("act", lambda e: e.activation(SQA[:], Kc2[:], AF.Square)))
            for n in range(4):
                sl = slice(n * 512, (n + 1) * 512)
                pX = pC if n % 2 == 0 else pD
                steps.append(("pe", lambda e, sl=sl, pX=pX: e.matmul(
                    pX[0:2, :], P.c["c_sel2"][:], SQA[:, sl], start=True, stop=True)))
                steps.append(("dve", lambda e, sl=sl, pX=pX: e.tensor_scalar_mul(
                    rr2k[:, sl], pX[0:2, :], 1.0 / 64.0)))
            steps.append(("dve", lambda e: (
                e.tensor_scalar_add(rr2q[:], rr2q[:], RMS_EPS0),
                e.tensor_scalar_add(rr2k[:], rr2k[:], RMS_EPS0))[-1]))
            steps.append(("act", lambda e: (
                e.activation(rr2q[:], rr2q[:], AF.Ln),
                e.activation(rr2k[:], rr2k[:], AF.Ln))[-1]))
            steps.append(("act", lambda e: (
                e.activation(rr2q[:], rr2q[:], AF.Exp, scale=-0.5),
                e.activation(rr2k[:], rr2k[:], AF.Exp, scale=-0.5))[-1]))
            steps.append(("dve", lambda e: (
                e.tensor_scalar_mul(Qc2[:], Qc2[:], P.sb["qnw2"][:, 0:1]),
                e.tensor_scalar_mul(Kc2[:], Kc2[:], P.sb["knw2"][:, 0:1]))[-1]))
            for n in range(4):
                sl = slice(n * 512, (n + 1) * 512)
                steps.append(("pe", lambda e, sl=sl: (
                    e.matmul(pA[:], P.c["c_exp2"][:], rr2q[:, sl], start=True, stop=True),
                    e.matmul(pB[:], P.c["c_exp2"][:], rr2k[:, sl], start=True, stop=True))[-1]))
                steps.append(("dve", lambda e, sl=sl: (
                    e.tensor_mul(qhE[0:64, sl], Qc2[0:64, sl], pA[0:64, :]),
                    e.tensor_mul(qhO[64:128, sl], Qc2[64:128, sl], pA[64:128, :]),
                    e.tensor_mul(khE[0:64, sl], Kc2[0:64, sl], pB[0:64, :]),
                    e.tensor_mul(khO[64:128, sl], Kc2[64:128, sl], pB[64:128, :]))[-1]))
            for n in range(4):
                sl = slice(n * 512, (n + 1) * 512)
                steps.append(("pe", lambda e, sl=sl, p=p: (
                    e.matmul(pA[0:64, :], P.sb["wqrT"][0:64, :], qhE[0:64, sl], start=True, stop=True),
                    e.matmul(pB[0:64, :], P.sb["wqrT"][64:128, :], qhO[64:128, sl], start=True, stop=True),
                    e.matmul(pC[0:64, :], P.sb["wkrT"][0:64, :], xn_fm[0:64, p, sl], start=True, stop=True),
                    e.matmul(pD[0:64, :], P.sb["wkrT"][64:128, :], xn_fm[64:128, p, sl], start=True, stop=True))[-1]))
                steps.append(("act", lambda e, sl=sl: (
                    e.copy(qhE[64:128, sl], pA[0:64, :]),
                    e.copy(qhO[0:64, sl], pB[0:64, :]),
                    e.copy(khE[64:128, sl], pC[0:64, :]),
                    e.copy(khO[0:64, sl], pD[0:64, :]))[-1]))
            steps.append(("act", lambda e: (
                e.activation(SQA[:], qhE[:], AF.Square),
                e.activation(Qc2[:], qhO[:], AF.Square))[-1]))
            for n in range(4):
                sl = slice(n * 512, (n + 1) * 512)
                pX, pY = (pA, pB) if n % 2 == 0 else (pC, pD)
                steps.append(("pe", lambda e, sl=sl, pX=pX, pY=pY: (
                    e.matmul(pX[0:1, :], P.c["c_ones_c"][:], SQA[:, sl], start=True, stop=True),
                    e.matmul(pY[0:1, :], P.c["c_ones_c"][:], Qc2[:, sl], start=True, stop=True))[-1]))
                steps.append(("dve", lambda e, n=n, pX=pX, pY=pY: (
                    e.reduce_max(qmax[:, n:n + 1], pX[0:1, :], axis=AX),
                    e.reduce_max(qmax[:, 4 + n:5 + n], pY[0:1, :], axis=AX))[-1]))
            steps.append(("act", lambda e: (
                e.activation(SQA[:], khE[:], AF.Square),
                e.activation(Qc2[:], khO[:], AF.Square))[-1]))
            for n in range(4):
                sl = slice(n * 512, (n + 1) * 512)
                pX, pY = (pA, pB) if n % 2 == 0 else (pC, pD)
                steps.append(("pe", lambda e, sl=sl, pX=pX, pY=pY: (
                    e.matmul(pX[0:1, :], P.c["c_ones_c"][:], SQA[:, sl], start=True, stop=True),
                    e.matmul(pY[0:1, :], P.c["c_ones_c"][:], Qc2[:, sl], start=True, stop=True))[-1]))
                steps.append(("dve", lambda e, n=n, pX=pX, pY=pY: (
                    e.reduce_max(qmax[:, 8 + n:9 + n], pX[0:1, :], axis=AX),
                    e.reduce_max(qmax[:, 12 + n:13 + n], pY[0:1, :], axis=AX))[-1]))
            steps.append(("dve", lambda e: (
                e.reduce_max(mx[:, 0:1], qmax[:, 0:4], axis=AX),
                e.reduce_max(mx[:, 1:2], qmax[:, 4:8], axis=AX),
                e.reduce_max(mx[:, 2:3], qmax[:, 8:12], axis=AX),
                e.reduce_max(mx[:, 3:4], qmax[:, 12:16], axis=AX))[-1]))
            steps.append(("dve", lambda e: (
                e.tensor_mul(mx[:, 4:5], mx[:, 0:1], mx[:, 2:3]),
                e.tensor_mul(mx[:, 5:6], mx[:, 1:2], mx[:, 3:4]))[-1]))
            steps.append(("act", lambda e: e.activation(mx[:, 4:6], mx[:, 4:6], AF.Ln)))
            steps.append(("act", lambda e: e.activation(mx[:, 4:6], mx[:, 4:6], AF.Exp, scale=0.5)))
            steps.append(("act", lambda e: e.mul(mx[:, 4:6], mx[:, 4:6], -0.125)))
            steps.append(("pe", lambda e: e.matmul(pF[:, 0:2], P.c["c_ones_r"][:], mx[:, 4:6], start=True, stop=True)))
            steps.append(("dve", lambda e: e.tensor_copy(bnd2[:], pF[:, 0:2])))
            chain(P, steps)

            for qc in range(4):
                _attn_pipe(P, khE, qhE, vtmE, bnd2[:, 0:1], [pA, pB], pC[0:65, :], p_t, qc)
                _attn_pipe(P, khO, qhO, vtmO, bnd2[:, 1:2], [pA, pB], pD[0:65, :], p_t, qc)
                steps = []
                steps.append(("dve", lambda e: (
                    e.reciprocal(invrE[:], pC[64:65, :]),
                    e.reciprocal(invrO[:], pD[64:65, :]))[-1]))
                def trs(e):
                    for j in range(4):
                        e.transpose(pF[:, j:j + 1], invrE[0:1, j * 128:(j + 1) * 128],
                                    P.c["c_id32"][0:1, 0:1])
                    for j in range(4):
                        i1 = e.transpose(pF[:, 4 + j:5 + j], invrO[0:1, j * 128:(j + 1) * 128],
                                         P.c["c_id32"][0:1, 0:1])
                    return i1
                steps.append(("pe", trs))
                steps.append(("dve", lambda e: e.tensor_copy(invc[:], pF[:, 0:8])))
                steps.append(("dve", lambda e: (
                    e.tensor_copy(ctxsE[:], pC[0:64, :]),
                    e.tensor_copy(ctxsO[:], pD[0:64, :]))[-1]))
                def outmm(e):
                    for j in range(4):
                        e.matmul(pA[:, j * 64:(j + 1) * 64], ctxsE[:, j * 128:(j + 1) * 128],
                                 P.sb["woT"][:], start=True, stop=True)
                    for j in range(4):
                        i1 = e.matmul(pB[:, j * 64:(j + 1) * 64], ctxsO[:, j * 128:(j + 1) * 128],
                                      P.sb["woT"][:], start=True, stop=True)
                    return i1
                steps.append(("pe", outmm))
                def sout(e, qc=qc, he=he, ho=ho):
                    for j in range(4):
                        e.tensor_scalar_mul(s_buf[:, qc * 4 + j, he * 64:(he + 1) * 64],
                                            pA[:, j * 64:(j + 1) * 64], invc[:, j:j + 1])
                    for j in range(4):
                        i1 = e.tensor_scalar_mul(s_buf[:, qc * 4 + j, ho * 64:(ho + 1) * 64],
                                                 pB[:, j * 64:(j + 1) * 64], invc[:, 4 + j:5 + j])
                    return i1
                steps.append(("dve", sout))
                chain(P, steps)



def emit_l0_gating(P, xn_fm, combine):
    nc = P.nc
    with ExitStack() as ctx:
        sc = P.sbuf(ctx, "gsc", [128, 16], F32)
        sel = P.sbuf(ctx, "gsel", [128, 16], F32)
        top8 = P.sbuf(ctx, "gtop", [128, 8], F32)
        den = P.sbuf(ctx, "gden", [128, 1], F32)
        pg = P.psum(ctx, "pg", [128, 16], F32)
        for i in range(NTILE):
            steps = []
            def mms(e, i=i):
                for c in range(NCH):
                    i1 = e.matmul(pg[:], xn_fm[:, c, i * 128:(i + 1) * 128],
                                  P.sb["gwT"][:, c, :], start=(c == 0), stop=(c == 7))
                return i1
            steps.append(("pe", mms))
            steps.append(("act", lambda e: e.activation(sc[:], pg[:], AF.Sigmoid)))
            steps.append(("dve", lambda e: e.tensor_add(sel[:], sc[:], P.sb["l0bias"][:])))
            steps.append(("dve", lambda e: e.max(out=top8[:], in_=sel[:])))
            steps.append(("dve", lambda e: e.tensor_single_scalar(sel[:], sel[:], top8[:, 7:8], ALU.is_ge)))
            steps.append(("dve", lambda e: e.tensor_mul(sel[:], sel[:], sc[:])))
            steps.append(("dve", lambda e: e.reduce_sum(den[:], sel[:], axis=AX)))
            steps.append(("dve", lambda e: e.reciprocal(den[:], den[:])))
            steps.append(("dve", lambda e: e.tensor_scalar_mul(sel[:], sel[:], den[:, 0:1])))
            steps.append(("dve", lambda e, i=i: e.tensor_scalar_mul(combine[:, i, :], sel[:], 2.826)))
            chain(P, steps)


def emit_l0_ffn(P, xn_fm, y_buf, combine):
    nc = P.nc
    with ExitStack() as ctx:
        h_full = P.sbuf(ctx, "hfull", [128, NCH, T], BF16)
        wg_b = [P.sbuf(ctx, "wgb", [128, 8, 128], BF16) for _ in range(2)]
        wu_b = [P.sbuf(ctx, "wub", [128, 8, 128], BF16) for _ in range(2)]
        wd_b = [P.sbuf(ctx, "wdb", [128, 8, 1024], BF16) for _ in range(2)]
        ts = [P.sbuf(ctx, "tsilu", [128, 512], BF16) for _ in range(2)]
        pg = [P.psum(ctx, "pgm", [128, 512], F32) for _ in range(2)]
        pu = [P.psum(ctx, "pum", [128, 512], F32) for _ in range(2)]
        pd = [P.psum(ctx, "pdm", [128, 512], F32) for _ in range(2)]
        sd, pe, act, dve = P.s["sd"], P.s["pe"], P.s["act"], P.s["dve"]
        sg = P.s["sg"]

        experts = [("sh", 48, 49, 50, None)] + [
            ("e%d" % e, e, 16 + e, 32 + e, e) for e in range(16)]

        scr = P.w["scr_w0"]

        def wg_src(m, c):
            return scr[m].rearrange("(c p) i -> p c i", p=128)[:, :, c * 128:(c + 1) * 128]

        def wd_src(m):
            return scr[m].rearrange("(c p) dd -> p c dd", p=128)

        for ei, (nm, mg, mu, md, eidx) in enumerate(experts):
            with nc.Block() as blk:
                @blk.sync
                def _(e):
                    e.dma_start(wd_b[ei % 2][:], wd_src(md)).then_inc(sd.h, 16)
                    sd.v += 16
                    e.wait_ge(sd.h, sd.v)
            for c in range(NCH):
                with nc.Block() as blk:
                    @blk.sync
                    def _(e):
                        e.dma_start(wg_b[c % 2][:], wg_src(mg, c)).then_inc(sd.h, 16)
                        e.dma_start(wu_b[c % 2][:], wg_src(mu, c)).then_inc(sd.h, 16)
                        sd.v += 32
                        e.wait_ge(sd.h, sd.v)
                pe0, act0, dve0 = pe.v, act.v, dve.v
                g_m = {}
                u_m = {}
                m = pe0
                for n in range(4):
                    m += 1; g_m[n] = m
                    m += 1; u_m[n] = m
                silu_m = {n: act0 + n + 1 for n in range(4)}
                h_m = {n: dve0 + n + 1 for n in range(4)}
                with nc.Block() as blk:
                    @blk.tensor
                    def _(e):
                        for n in range(4):
                            bb = n % 2
                            if n >= 2:
                                e.wait_ge(act.h, silu_m[n - 2])
                            for kc in range(NCH):
                                i1 = e.matmul(pg[bb][:], wg_b[c % 2][:, kc, :],
                                              xn_fm[:, kc, n * 512:(n + 1) * 512],
                                              start=(kc == 0), stop=(kc == 7))
                            i1.then_inc(pe.h, 1)
                            if n >= 2:
                                e.wait_ge(dve.h, h_m[n - 2])
                            for kc in range(NCH):
                                i1 = e.matmul(pu[bb][:], wu_b[c % 2][:, kc, :],
                                              xn_fm[:, kc, n * 512:(n + 1) * 512],
                                              start=(kc == 0), stop=(kc == 7))
                            i1.then_inc(pe.h, 1)
                    @blk.scalar
                    def _(e):
                        for n in range(4):
                            bb = n % 2
                            e.wait_ge(pe.h, g_m[n])
                            if n >= 2:
                                e.wait_ge(dve.h, h_m[n - 2])
                            e.activation(ts[bb][:], pg[bb][:], AF.Silu).then_inc(act.h, 1)
                    @blk.vector
                    def _(e):
                        for n in range(4):
                            bb = n % 2
                            e.wait_ge(act.h, silu_m[n])
                            e.wait_ge(pe.h, u_m[n])
                            e.tensor_mul(h_full[:, c, n * 512:(n + 1) * 512],
                                         ts[bb][:], pu[bb][:]).then_inc(dve.h, 1)
                pe.v = m
                act.v = act0 + 4
                dve.v = dve0 + 4
            pe0, dve0 = pe.v, dve.v
            dn_m = {}
            m = pe0
            items = [(t, j2) for t in range(NTILE) for j2 in range(2)]
            for idx, (t, j2) in enumerate(items):
                m += 1
                dn_m[(t, j2)] = m
            ev_m = {it: dve0 + i + 1 for i, it in enumerate(items)}
            with nc.Block() as blk:
                @blk.tensor
                def _(e):
                    for idx, (t, j2) in enumerate(items):
                        bb = idx % 2
                        if idx >= 2:
                            e.wait_ge(dve.h, ev_m[items[idx - 2]])
                        for kc in range(NCH):
                            i1 = e.matmul(pd[bb][:], h_full[:, kc, t * 128:(t + 1) * 128],
                                          wd_b[ei % 2][:, kc, j2 * 512:(j2 + 1) * 512],
                                          start=(kc == 0), stop=(kc == 7))
                        i1.then_inc(pe.h, 1)
                @blk.vector
                def _(e):
                    for idx, (t, j2) in enumerate(items):
                        bb = idx % 2
                        e.wait_ge(pe.h, dn_m[(t, j2)])
                        dst = y_buf[:, t, j2 * 512:(j2 + 1) * 512]
                        if eidx is None:
                            e.tensor_copy(dst, pd[bb][:]).then_inc(dve.h, 1)
                        else:
                            e.scalar_tensor_tensor(dst, pd[bb][:],
                                                   combine[:, t, eidx:eidx + 1], dst,
                                                   ALU.mult, ALU.add).then_inc(dve.h, 1)
            pe.v = m
            dve.v = dve0 + len(items)





def stage_l1_small(P):
    nc = P.nc
    P.sb["convq"] = P.sbuf(P.ctx, "convq", [128, 6, 4], F32)
    P.sb["convk"] = P.sbuf(P.ctx, "convk", [128, 6, 4], F32)
    P.sb["convv"] = P.sbuf(P.ctx, "convv", [128, 12, 4], F32)
    P.sb["dtb"] = P.sbuf(P.ctx, "dtb", [128, 6], F32)
    P.sb["negA"] = P.sbuf(P.ctx, "negA", [128, 6], F32)
    P.sb["nwb"] = P.sbuf(P.ctx, "nwb", [128, 256], F32)
    P.sb["l1bias"] = P.sbuf(P.ctx, "l1bias", [128, 8], F32)
    P.sb["ln1b"] = P.sbuf(P.ctx, "ln1b", [128, 1024], F32)
    P.sb["ln2b"] = P.sbuf(P.ctx, "ln2b", [128, 1024], F32)
    with ExitStack() as ctx:
        row = P.sbuf(ctx, "strow", [1, 1024], F32)
        row2 = P.sbuf(ctx, "strow2", [1, 6], F32)
        pb = P.psum(ctx, "pstb", [128, 512], F32)
        steps = []
        steps.append(("sync_dma", lambda e: (
            e.dma_start(P.sb["convq"][:], P.w["L1_attn_conv_q"].rearrange("(c p) j -> p c j", p=128)).then_inc(P.s["sd"].h, 16),
            e.dma_start(P.sb["convk"][:], P.w["L1_attn_conv_k"].rearrange("(c p) j -> p c j", p=128)).then_inc(P.s["sd"].h, 16),
            e.dma_start(P.sb["convv"][:], P.w["L1_attn_conv_v"].rearrange("(c p) j -> p c j", p=128)))[-1], 3))
        steps.append(("sync_dma", lambda e: e.dma_start(row2[:], P.w["L1_attn_dt_bias"].rearrange("(o e) -> o e", o=1))))
        steps.append(("pe", lambda e: e.matmul(pb[:, 0:6], P.c["c_ones_r"][:], row2[:], start=True, stop=True)))
        steps.append(("act", lambda e: e.copy(P.sb["dtb"][:], pb[:, 0:6])))
        steps.append(("sync_dma", lambda e: e.dma_start(row2[:], P.w["L1_attn_A_log"].rearrange("(o e) -> o e", o=1))))
        steps.append(("act", lambda e: e.activation(row2[:], row2[:], AF.Exp)))
        steps.append(("act", lambda e: e.mul(row2[:], row2[:], -1.0)))
        steps.append(("pe", lambda e: e.matmul(pb[:, 0:6], P.c["c_ones_r"][:], row2[:], start=True, stop=True)))
        steps.append(("act", lambda e: e.copy(P.sb["negA"][:], pb[:, 0:6])))
        steps.append(("sync_dma", lambda e: e.dma_start(row[:, 0:256], P.w["L1_attn_norm_w"].rearrange("(o e) -> o e", o=1))))
        steps.append(("pe", lambda e: e.matmul(pb[:, 0:256], P.c["c_ones_r"][:], row[:, 0:256], start=True, stop=True)))
        steps.append(("act", lambda e: e.copy(P.sb["nwb"][:], pb[:, 0:256])))
        steps.append(("sync_dma", lambda e: e.dma_start(row[:, 0:8], P.w["L1_ffn_expert_bias"].rearrange("(o e) -> o e", o=1))))
        steps.append(("pe", lambda e: e.matmul(pb[:, 0:8], P.c["c_ones_r"][:], row[:, 0:8], start=True, stop=True)))
        steps.append(("act", lambda e: e.copy(P.sb["l1bias"][:], pb[:, 0:8])))
        for wname, dst in (("L1_ffn_w_ln1", "ln1b"), ("L1_ffn_w_ln2", "ln2b")):
            steps.append(("sync_dma", lambda e, wname=wname: e.dma_start(row[:], P.w[wname].rearrange("(o e) -> o e", o=1))))
            for j in range(2):
                steps.append(("pe", lambda e, j=j: e.matmul(pb[:], P.c["c_ones_r"][:], row[:, j * 512:(j + 1) * 512], start=True, stop=True)))
                steps.append(("act", lambda e, dst=dst, j=j: e.copy(P.sb[dst][:, j * 512:(j + 1) * 512], pb[:])))
        chain(P, steps)


def emit_l1_attn_passA(P, xn_fm, scr_q, scr_k, scr_v, scr_g, beta_all, g_all):
    nc = P.nc
    GS = 512
    NG = T // GS
    with ExitStack() as ctx:
        wb = P.sbuf(ctx, "wbs", [128, 8, 6], BF16)
        wa = P.sbuf(ctx, "was", [128, 8, 6], BF16)
        t6 = P.sbuf(ctx, "t6", [128, 6], F32)
        t6b = P.sbuf(ctx, "t6b", [128, 6], F32)
        pba = P.psum(ctx, "pba", [128, 6], F32)
        pbb = P.psum(ctx, "pbb", [128, 6], F32)
        steps = [("pool_dma", lambda e: (
            e.dma_start(wb[:], P.w["L1_attn_Wb"].rearrange("(c p) h -> p c h", p=128)).then_inc(P.s["sg"].h, 16),
            e.dma_start(wa[:], P.w["L1_attn_Wa"].rearrange("(c p) h -> p c h", p=128)))[-1], 2)]
        for i in range(NTILE):
            def mmb(e, i=i):
                for c in range(NCH):
                    i1 = e.matmul(pba[:], xn_fm[:, c, i * 128:(i + 1) * 128], wb[:, c, :],
                                  start=(c == 0), stop=(c == 7))
                return i1
            steps.append(("pe", mmb))
            def mma(e, i=i):
                for c in range(NCH):
                    i1 = e.matmul(pbb[:], xn_fm[:, c, i * 128:(i + 1) * 128], wa[:, c, :],
                                  start=(c == 0), stop=(c == 7))
                return i1
            steps.append(("pe", mma))
            steps.append(("act", lambda e, i=i: e.activation(beta_all[:, i, :], pba[:], AF.Sigmoid)))
            steps.append(("dve", lambda e: e.tensor_add(t6[:], pbb[:], P.sb["dtb"][:])))
            steps.append(("act", lambda e: e.activation(t6b[:], t6[:], AF.Abs)))
            steps.append(("act", lambda e: e.activation(t6b[:], t6b[:], AF.Exp, scale=-1.0)))
            steps.append(("act", lambda e: e.activation(t6b[:], t6b[:], AF.Ln, bias=1.0)))
            steps.append(("dve", lambda e: e.tensor_scalar_max(t6[:], t6[:], 0.0)))
            steps.append(("dve", lambda e: e.tensor_add(t6[:], t6[:], t6b[:])))
            steps.append(("dve", lambda e, i=i: e.tensor_mul(g_all[:, i, :], t6[:], P.sb["negA"][:])))
        chain(P, steps)
    with ExitStack() as ctx:
        wg = P.sbuf(ctx, "wgs", [128, 8, 1536], BF16)
        gb = P.sbuf(ctx, "gbs", [128, 512], BF16)
        pgg = P.psum(ctx, "pgg", [128, 512], F32)
        steps = [("pool_dma", lambda e: e.dma_start(wg[:], P.w["L1_attn_Wg"].rearrange("(c p) h -> p c h", p=128)))]
        for i in range(NTILE):
            for n3 in range(3):
                def mmg(e, i=i, n3=n3):
                    for c in range(NCH):
                        i1 = e.matmul(pgg[:], xn_fm[:, c, i * 128:(i + 1) * 128],
                                      wg[:, c, n3 * 512:(n3 + 1) * 512], start=(c == 0), stop=(c == 7))
                    return i1
                steps.append(("pe", mmg))
                steps.append(("act", lambda e: e.activation(gb[:], pgg[:], AF.Silu)))
                steps.append(("sync_dma", lambda e, i=i, n3=n3: e.dma_start(
                    scr_g[i * 128:(i + 1) * 128, n3 * 512:(n3 + 1) * 512], gb[:])))
        chain(P, steps)
    for which, wname, cname, nmrow, scr, do_norm, qscale in (
            ("q", "L1_attn_Wq", "convq", 6, scr_q, True, 128 ** -0.5),
            ("k", "L1_attn_Wk", "convk", 6, scr_k, True, 1.0),
            ("v", "L1_attn_Wv", "convv", 12, scr_v, False, 1.0)):
        with ExitStack() as ctx:
            wqs = P.sbuf(ctx, "wqs", [128, 8, nmrow * 128], BF16)
            xb = P.sbuf(ctx, "xb", [128, nmrow, GS + 3], F32)
            cacc = P.sbuf(ctx, "cacc", [128, GS], F32)
            cout = P.sbuf(ctx, "cout", [128, nmrow, GS], BF16)
            sqv = P.sbuf(ctx, "sqv", [128, GS], F32)
            rr = P.sbuf(ctx, "rr", [1, GS], F32)
            pqv = P.psum(ctx, "pqv", [128, 512], F32)
            pnb = P.psum(ctx, "pnb", [128, 512], F32)
            pnr = P.psum(ctx, "pnr", [1, 512], F32)
            ptt = P.psum(ctx, "ptt", [128, 128], BF16)
            steps = [("pool_dma", lambda e, wname=wname: e.dma_start(
                wqs[:], P.w[wname].rearrange("(c p) h -> p c h", p=128)))]
            steps.append(("dve", lambda e: e.memset(xb[:, :, 0:3], 0.0)))
            for gdx in range(NG):
                if gdx > 0:
                    steps.append(("dve", lambda e: e.tensor_copy(xb[:, :, 0:3], xb[:, :, GS:GS + 3])))
                for m in range(nmrow):
                    def mmx(e, gdx=gdx, m=m):
                        for c in range(NCH):
                            i1 = e.matmul(pqv[:], wqs[:, c, m * 128:(m + 1) * 128],
                                          xn_fm[:, c, gdx * GS:(gdx + 1) * GS],
                                          start=(c == 0), stop=(c == 7))
                        return i1
                    steps.append(("pe", mmx))
                    steps.append(("act", lambda e, m=m: e.copy(xb[:, m, 3:3 + GS], pqv[:])))
                for m in range(nmrow):
                    steps.append(("dve", lambda e, m=m, cname=cname: e.tensor_scalar_mul(
                        cacc[:], xb[:, m, 0:GS], P.sb[cname][:, m, 0:1])))
                    for j in range(1, 4):
                        steps.append(("dve", lambda e, m=m, j=j, cname=cname: e.scalar_tensor_tensor(
                            cacc[:], xb[:, m, j:j + GS], P.sb[cname][:, m, j:j + 1], cacc[:],
                            ALU.mult, ALU.add)))
                    steps.append(("act", lambda e, m=m: e.activation(cout[:, m, :], cacc[:], AF.Silu)))
                    if do_norm:
                        steps.append(("act", lambda e, m=m: e.activation(sqv[:], cout[:, m, :], AF.Square)))
                        steps.append(("pe", lambda e: e.matmul(pnr[:], P.c["c_ones_c"][:], sqv[:], start=True, stop=True)))
                        steps.append(("dve", lambda e: e.tensor_scalar_add(rr[:], pnr[:], 1e-12)))
                        steps.append(("act", lambda e: e.activation(rr[:], rr[:], AF.Ln)))
                        steps.append(("act", lambda e: e.activation(rr[:], rr[:], AF.Exp, scale=-0.5)))
                        if qscale != 1.0:
                            steps.append(("act", lambda e, qscale=qscale: e.mul(rr[:], rr[:], qscale)))
                        steps.append(("pe", lambda e: e.matmul(pnb[:], P.c["c_ones_r"][:], rr[:], start=True, stop=True)))
                        steps.append(("dve", lambda e, m=m: e.tensor_mul(cout[:, m, :], cout[:, m, :], pnb[:])))
                if which in ("q", "k"):
                    steps.append(("sync_dma", lambda e, gdx=gdx, scr=scr: e.dma_start(
                        scr[:, :, gdx * GS:(gdx + 1) * GS], cout[:])))
                else:
                    for hh in range(6):
                        for tt4 in range(4):
                            for half in range(2):
                                steps.append(("pe", lambda e, hh=hh, tt4=tt4, half=half: e.transpose(
                                    ptt[:], cout[:, 2 * hh + half, tt4 * 128:(tt4 + 1) * 128],
                                    P.c["c_idbf"][:])))
                                steps.append(("act", lambda e, hh=hh, tt4=tt4, half=half: e.copy(
                                    sqv.bitcast(BF16)[:, half * 128:(half + 1) * 128] if False else cacc.bitcast(BF16)[:, half * 128:(half + 1) * 128], ptt[:])))
                            steps.append(("sync_dma", lambda e, hh=hh, gdx=gdx, tt4=tt4, scr=scr: e.dma_start(
                                scr[hh, gdx * GS + tt4 * 128: gdx * GS + (tt4 + 1) * 128, :],
                                cacc.bitcast(BF16)[:, 0:256])))
            chain(P, steps)


def emit_l1_attn_passA2(P, xn_fm, buf_a, scr_q, scr_k, scr_v, scr_g, beta_all, g_all):
    nc = P.nc
    GS = 512
    NG = T // GS
    flat32 = buf_a[:].rearrange("p t d -> p (t d)")
    flat16 = buf_a.bitcast(BF16)[:].rearrange("p t d -> p (t d)")

    def carve(base_words, nmrow):
        xb = flat32[:, base_words:base_words + nmrow * 515].rearrange(
            "p (a b) -> p a b", a=nmrow)
        cacc = flat32[:, base_words + nmrow * 515 + 14:
                      base_words + nmrow * 515 + 14 + nmrow * 512].rearrange(
            "p (a b) -> p a b", a=nmrow)
        cw = base_words + nmrow * 515 + 14 + nmrow * 512
        cout = flat16[:, 2 * cw:2 * cw + nmrow * 512].rearrange(
            "p (a b) -> p a b", a=nmrow)
        return xb, cacc, cout

    with ExitStack() as ctx:
        wb = P.sbuf(ctx, "wbs", [128, 8, 6], BF16)
        wa = P.sbuf(ctx, "was", [128, 8, 6], BF16)
        t96 = P.sbuf(ctx, "t96", [128, 96], F32)
        t96b = P.sbuf(ctx, "t96b", [128, 96], F32)
        wqs_q = P.sbuf(ctx, "wqsq", [128, 8, 768], BF16)
        wqs_k = P.sbuf(ctx, "wqsk", [128, 8, 768], BF16)
        rr_q = P.sbuf(ctx, "rrq", [1, 3072], F32)
        rr_k = P.sbuf(ctx, "rrk", [1, 3072], F32)
        pX = [P.psum(ctx, "pXq", [128, 512], F32) for _ in range(2)]
        pY = [P.psum(ctx, "pYk", [128, 512], F32) for _ in range(2)]
        pba = P.psum(ctx, "pba", [128, 96], F32)
        pbb = P.psum(ctx, "pbb", [128, 96], F32)

        dtb_bc = P.sb["dtb"][:].unsqueeze(1).to_broadcast([128, NTILE, 6])
        negA_bc = P.sb["negA"][:].unsqueeze(1).to_broadcast([128, NTILE, 6])

        sbg = [("pool_dma", lambda e: (
            e.dma_start(wb[:], P.w["L1_attn_Wb"].rearrange("(c p) h -> p c h", p=128)).then_inc(P.s["sg"].h, 16),
            e.dma_start(wa[:], P.w["L1_attn_Wa"].rearrange("(c p) h -> p c h", p=128)))[-1], 2)]

        def mmb(e):
            for i in range(NTILE):
                for c in range(NCH):
                    i1 = e.matmul(pba[:, i * 6:(i + 1) * 6],
                                  xn_fm[:, c, i * 128:(i + 1) * 128], wb[:, c, :],
                                  start=(c == 0), stop=(c == 7))
            return i1
        sbg.append(("pe", mmb))
        sbg.append(("act", lambda e: e.activation(
            beta_all[:].rearrange("p t h -> p (t h)"), pba[:], AF.Sigmoid)))

        def mma(e):
            for i in range(NTILE):
                for c in range(NCH):
                    i1 = e.matmul(pbb[:, i * 6:(i + 1) * 6],
                                  xn_fm[:, c, i * 128:(i + 1) * 128], wa[:, c, :],
                                  start=(c == 0), stop=(c == 7))
            return i1
        sbg.append(("pe", mma))
        sbg.append(("dve", lambda e: e.tensor_add(
            t96[:].rearrange("p (t h) -> p t h", t=NTILE),
            pbb[:].rearrange("p (t h) -> p t h", t=NTILE), dtb_bc)))
        sbg.append(("act", lambda e: e.activation(t96b[:], t96[:], AF.Abs)))
        sbg.append(("act", lambda e: e.activation(t96b[:], t96b[:], AF.Exp, scale=-1.0)))
        sbg.append(("act", lambda e: e.activation(t96b[:], t96b[:], AF.Ln, bias=1.0)))
        sbg.append(("dve", lambda e: e.tensor_scalar_max(t96[:], t96[:], 0.0)))
        sbg.append(("dve", lambda e: e.tensor_add(t96[:], t96[:], t96b[:])))
        sbg.append(("dve", lambda e: e.tensor_mul(
            g_all[:], t96[:].rearrange("p (t h) -> p t h", t=NTILE), negA_bc)))

        def qk_stream(wname, cname, wqs, rr, xb, cacc, cout, scr, qscale):
            st = [("pool_dma", lambda e: e.dma_start(
                wqs[:], P.w[wname].rearrange("(c p) h -> p c h", p=128)))]
            st.append(("dve", lambda e: e.memset(xb[:, :, 0:3], 0.0)))
            pp = pX if scr is scr_q else pY
            for gdx in range(NG):
                if gdx > 0:
                    st.append(("dve", lambda e: e.tensor_copy(xb[:, :, 0:3], xb[:, :, GS:GS + 3])))
                for m in range(6):
                    def mmx(e, gdx=gdx, m=m):
                        for c in range(NCH):
                            i1 = e.matmul(pp[m % 2][:], wqs[:, c, m * 128:(m + 1) * 128],
                                          xn_fm[:, c, gdx * GS:(gdx + 1) * GS],
                                          start=(c == 0), stop=(c == 7))
                        return i1
                    st.append(("pe", mmx))
                    st.append(("act", lambda e, m=m: e.copy(xb[:, m, 3:3 + GS], pp[m % 2][:])))
                for j in range(4):
                    def cstep(e, j=j):
                        for m in range(6):
                            if j == 0:
                                i1 = e.tensor_scalar_mul(cacc[:, m, :], xb[:, m, 0:GS],
                                                         P.sb[cname][:, m, 0:1])
                            else:
                                i1 = e.scalar_tensor_tensor(cacc[:, m, :], xb[:, m, j:j + GS],
                                                            P.sb[cname][:, m, j:j + 1],
                                                            cacc[:, m, :], ALU.mult, ALU.add)
                        return i1
                    st.append(("dve", cstep))
                st.append(("act", lambda e: e.activation(
                    cout[:].rearrange("p a b -> p (a b)"), cacc[:].rearrange("p a b -> p (a b)"), AF.Silu)))
                st.append(("act", lambda e: e.activation(
                    cacc[:].rearrange("p a b -> p (a b)"), cout[:].rearrange("p a b -> p (a b)"), AF.Square)))
                for m in range(6):
                    st.append(("pe", lambda e, m=m: e.matmul(
                        pp[m % 2][0:1, :], P.c["c_ones_c"][:], cacc[:, m, :], start=True, stop=True)))
                    st.append(("dve", lambda e, m=m: e.tensor_scalar_add(
                        rr[:, m * 512:(m + 1) * 512], pp[m % 2][0:1, :], 1e-12)))
                st.append(("act", lambda e: e.activation(rr[:], rr[:], AF.Ln)))
                st.append(("act", lambda e: e.activation(rr[:], rr[:], AF.Exp, scale=-0.5)))
                lhsb = P.c["c_qsc"] if qscale != 1.0 else P.c["c_ones_r"]
                for m in range(6):
                    st.append(("pe", lambda e, m=m, lhsb=lhsb: e.matmul(
                        pp[m % 2][:], lhsb[:], rr[:, m * 512:(m + 1) * 512], start=True, stop=True)))
                    st.append(("dve", lambda e, m=m: e.tensor_mul(
                        cout[:, m, :], cout[:, m, :], pp[m % 2][:])))
                st.append(("sync_dma", lambda e, gdx=gdx: e.dma_start(
                    scr[:, :, gdx * GS:(gdx + 1) * GS], cout[:])))
            return st

        xb_q, cacc_q, cout_q = carve(0, 6)
        xb_k, cacc_k, cout_k = carve(8192, 6)
        sq = qk_stream("L1_attn_Wq", "convq", wqs_q, rr_q, xb_q, cacc_q, cout_q, scr_q, 128 ** -0.5)
        sk = qk_stream("L1_attn_Wk", "convk", wqs_k, rr_k, xb_k, cacc_k, cout_k, scr_k, 1.0)
        chain_multi(P, [sbg, sq, sk])

    with ExitStack() as ctx:
        wqs_v = P.sbuf(ctx, "wqsv", [128, 8, 1536], BF16)
        wg = P.sbuf(ctx, "wgs", [128, 8, 1536], BF16)
        gb = [P.sbuf(ctx, "gbs", [128, 512], BF16) for _ in range(2)]
        vstage = P.sbuf(ctx, "vstage", [128, 1536], BF16)
        pX = [P.psum(ctx, "pXv", [128, 512], F32) for _ in range(2)]
        pY = [P.psum(ctx, "pYg", [128, 512], F32) for _ in range(2)]
        pVT = P.psum(ctx, "pVT", [128, 512], BF16)
        xb_v, cacc_v, cout_v = carve(0, 12)

        sv = [("pool_dma", lambda e: e.dma_start(
            wqs_v[:], P.w["L1_attn_Wv"].rearrange("(c p) h -> p c h", p=128)))]
        sv.append(("dve", lambda e: e.memset(xb_v[:, :, 0:3], 0.0)))
        for gdx in range(NG):
            if gdx > 0:
                sv.append(("dve", lambda e: e.tensor_copy(xb_v[:, :, 0:3], xb_v[:, :, GS:GS + 3])))
            for m in range(12):
                def mmx(e, gdx=gdx, m=m):
                    for c in range(NCH):
                        i1 = e.matmul(pX[m % 2][:], wqs_v[:, c, m * 128:(m + 1) * 128],
                                      xn_fm[:, c, gdx * GS:(gdx + 1) * GS],
                                      start=(c == 0), stop=(c == 7))
                    return i1
                sv.append(("pe", mmx))
                sv.append(("act", lambda e, m=m: e.copy(xb_v[:, m, 3:3 + GS], pX[m % 2][:])))
            for j in range(4):
                def cstep(e, j=j):
                    for m in range(12):
                        if j == 0:
                            i1 = e.tensor_scalar_mul(cacc_v[:, m, :], xb_v[:, m, 0:GS],
                                                     P.sb["convv"][:, m, 0:1])
                        else:
                            i1 = e.scalar_tensor_tensor(cacc_v[:, m, :], xb_v[:, m, j:j + GS],
                                                        P.sb["convv"][:, m, j:j + 1],
                                                        cacc_v[:, m, :], ALU.mult, ALU.add)
                    return i1
                sv.append(("dve", cstep))
            sv.append(("act", lambda e: e.activation(
                cout_v[:].rearrange("p a b -> p (a b)"), cacc_v[:].rearrange("p a b -> p (a b)"), AF.Silu)))
            for tt in range(4):
                tsl = slice(tt * 128, (tt + 1) * 128)
                for blk3 in range(3):
                    def trp(e, blk3=blk3, tsl=tsl):
                        for u in range(4):
                            r = blk3 * 4 + u
                            i1 = e.transpose(pVT[:, u * 128:(u + 1) * 128],
                                             cout_v[:, r, tsl], P.c["c_idbf"][:])
                        return i1
                    sv.append(("pe", trp))
                    sv.append(("act", lambda e, blk3=blk3: e.copy(
                        vstage[:, blk3 * 512:(blk3 + 1) * 512], pVT[:])))
                sv.append(("sync_dma", lambda e, gdx=gdx, tt=tt: e.dma_start(
                    scr_v[:, gdx * 512 + tt * 128:gdx * 512 + (tt + 1) * 128, :].transpose([1, 0, 2]),
                    vstage[:].rearrange("p (h d) -> p h d", h=6))))

        sg_ = [("pool_dma", lambda e: e.dma_start(
            wg[:], P.w["L1_attn_Wg"].rearrange("(c p) h -> p c h", p=128)))]
        for i in range(NTILE):
            for n3 in range(3):
                idx = i * 3 + n3
                def mmg(e, i=i, n3=n3, idx=idx):
                    for c in range(NCH):
                        i1 = e.matmul(pY[idx % 2][:], xn_fm[:, c, i * 128:(i + 1) * 128],
                                      wg[:, c, n3 * 512:(n3 + 1) * 512], start=(c == 0), stop=(c == 7))
                    return i1
                sg_.append(("pe", mmg))
                sg_.append(("act", lambda e, idx=idx: e.activation(gb[idx % 2][:], pY[idx % 2][:], AF.Silu)))
                sg_.append(("act_dma", lambda e, i=i, n3=n3, idx=idx: e.dma_start(
                    scr_g[i * 128:(i + 1) * 128, n3 * 512:(n3 + 1) * 512], gb[idx % 2][:])))
        chain_multi(P, [sv, sg_])


def emit_l1_attn_passB(P, s_buf, scr_q, scr_k, scr_v, scr_g, beta_all, g_all):
    nc = P.nc
    NH = 6
    W = NH * 128
    with ExitStack() as ctx:
        wo = P.sbuf(ctx, "wo1", [128, 12, 1024], BF16)
        kc_s = P.sbuf(ctx, "kcs", [128, NH, 128], BF16)
        qc_sb = P.sbuf(ctx, "qcsb", [128, NH, 128], BF16)
        vtm = P.sbuf(ctx, "vtm1", [128, NH, 256], BF16)
        gate_c = P.sbuf(ctx, "gatec", [128, 1536], BF16)
        Gtm = P.sbuf(ctx, "Gtm", [128, 6], F32)
        GhT = P.sbuf(ctx, "GhT", [6, 128], F32)
        bhT = P.sbuf(ctx, "bhT", [6, 128], F32)
        big = P.sbuf(ctx, "bigscr", [128, 2048], F32)
        tmp6 = big[0:6, 0:768]
        PkM = big[:, 0:768]
        bcol = big[:, 768:1536]
        Ed = P.sbuf(ctx, "Ed", [128, 768], F32)
        expEd = P.sbuf(ctx, "expEd", [128, 768], F32)
        expGb = P.sbuf(ctx, "expGb", [128, 768], BF16)
        BBs = P.sbuf(ctx, "BBs", [128, NH, 128], BF16)
        bbs = P.sbuf(ctx, "bbs", [128, NH, 128], BF16)
        B2f = P.sbuf(ctx, "B2f", [128, NH, 128], BF16)
        b2f = P.sbuf(ctx, "b2f", [128, NH, 128], BF16)
        Xp = P.sbuf(ctx, "Xp", [128, NH, 128], BF16)
        xp = P.sbuf(ctx, "xp", [128, NH, 128], BF16)
        kbar = P.sbuf(ctx, "kbar", [128, NH, 128], BF16)
        qbar = P.sbuf(ctx, "qbar", [128, NH, 128], BF16)
        W2t = P.sbuf(ctx, "W2t", [128, NH, 128], BF16)
        RHS0 = P.sbuf(ctx, "RHS0", [128, NH, 256], BF16)
        Us = P.sbuf(ctx, "Us", [128, NH, 256], BF16)
        ktm = P.sbuf(ctx, "ktm", [128, 128], BF16)
        k2tm = P.sbuf(ctx, "k2tm", [128, 128], BF16)
        o_s = big[:, 0:1536]
        o_bf = P.sbuf(ctx, "o_bf", [128, 1536], BF16)
        o_fm = P.sbuf(ctx, "o_fm", [128, 12, 128], BF16)
        oss = P.sbuf(ctx, "oss", [128, 8], F32)
        sqsc = P.sbuf(ctx, "sqsc", [128, 256], F32)
        cC = P.sbuf(ctx, "cC", [128, 6], F32)
        r16 = P.sbuf(ctx, "r16", [1, 6], F32)
        S_st = [P.sbuf(ctx, "Sst", [128, 256], F32) for _ in range(NH)]
        S_bf = P.sbuf(ctx, "Sbf", [128, NH, 256], BF16)
        pP1 = P.psum(ctx, "pP1", [128, 512], F32)
        pP2 = P.psum(ctx, "pP2", [128, 256], F32)
        pO = [P.psum(ctx, "pO", [128, 512], F32) for _ in range(3)]
        pU = P.psum(ctx, "pU", [128, 256], F32)
        pT = P.psum(ctx, "pT", [128, 128], BF16)
        pS = P.psum(ctx, "pSx", [128, 512], F32)

        def mm6(steps, dst_sb, lhs_fn, rhs_fn, cast=True):
            def mms(e):
                for h in range(NH):
                    tgt = pP1[:, h * 128:(h + 1) * 128] if h < 4 else pP2[:, (h - 4) * 128:(h - 3) * 128]
                    i1 = e.matmul(tgt, lhs_fn(h), rhs_fn(h), start=True, stop=True)
                return i1
            steps.append(("pe", mms))
            steps.append(("act", lambda e: e.copy(dst_sb[:, 0:512], pP1[:])))
            steps.append(("act", lambda e: e.copy(dst_sb[:, 512:768], pP2[:])))

        with nc.Block() as blk:
            @blk.gpsimd
            def _(e):
                e.dma_start(wo[:], P.w["L1_attn_Wo"].rearrange("(c p) h -> p c h", p=128)).then_inc(P.s["sg"].h, 16)
                P.s["sg"].v += 16
                e.wait_ge(P.s["sg"].h, P.s["sg"].v)
            @blk.vector
            def _(e):
                for h in range(NH):
                    e.memset(S_st[h][:], 0.0)
                e.memset(S_bf[:], 0.0)
        P.last = None

        striu_bc = P.c["c_striu"][:].unsqueeze(1).to_broadcast([128, NH, 128])
        triu_bc = P.c["c_triu"][:].unsqueeze(1).to_broadcast([128, NH, 128])
        idbf_bc = P.c["c_idbf"][:].unsqueeze(1).to_broadcast([128, NH, 128])
        nwb_bc = P.sb["nwb"][:].unsqueeze(1).to_broadcast([128, NH, 256])

        for ci in range(NTILE):
            csl = slice(ci * 128, (ci + 1) * 128)
            steps = []
            steps.append(("sync_dma", lambda e, ci=ci: (
                e.dma_start(kc_s[:], scr_k[:, :, ci * 128:(ci + 1) * 128]).then_inc(P.s["sd"].h, 16),
                e.dma_start(qc_sb[:], scr_q[:, :, ci * 128:(ci + 1) * 128]).then_inc(P.s["sd"].h, 16),
                e.dma_start(vtm[:], scr_v[:, ci * 128:(ci + 1) * 128, :].transpose([1, 0, 2])).then_inc(P.s["sd"].h, 16),
                e.dma_start(gate_c[:], scr_g[ci * 128:(ci + 1) * 128, :]))[-1], 4))
            steps.append(("pe", lambda e, ci=ci: e.matmul(pP1[:, 0:6], P.c["c_triu"][:], g_all[:, ci, :], start=True, stop=True)))
            steps.append(("dve", lambda e: e.tensor_copy(Gtm[:], pP1[:, 0:6])))
            steps.append(("pe", lambda e: e.transpose(pP1[0:6, 0:128], Gtm[:], P.c["c_id32"][:])))
            steps.append(("dve", lambda e: e.tensor_copy(GhT[:], pP1[0:6, 0:128])))
            steps.append(("pe", lambda e, ci=ci: e.transpose(pP1[0:6, 0:128], beta_all[:, ci, :], P.c["c_id32"][:])))
            steps.append(("dve", lambda e: e.tensor_copy(bhT[:], pP1[0:6, 0:128])))
            steps.append(("dve", lambda e: e.tensor_mul(tmp6.rearrange("p (a t) -> p a t", a=6), GhT[:].unsqueeze(1).to_broadcast([6, 6, 128]), P.c["c_hind"].rearrange("p (a t) -> p a t", a=6))))
            steps.append(("pe", lambda e: e.matmul(pP1[:], P.c["c_ones6"][:], tmp6[:, 0:512], start=True, stop=True)))
            steps.append(("pe", lambda e: e.matmul(pP2[:], P.c["c_ones6"][:], tmp6[:, 512:768], start=True, stop=True)))
            steps.append(("act", lambda e: e.activation(expGb[:, 0:512], pP1[:], AF.Exp)))
            steps.append(("act", lambda e: e.activation(expGb[:, 512:768], pP2[:], AF.Exp)))
            steps.append(("dve", lambda e: e.tensor_copy(Ed[:, 0:512], pP1[:])))
            steps.append(("dve", lambda e: e.tensor_copy(Ed[:, 512:768], pP2[:])))
            steps.append(("pe", lambda e: e.matmul(pP1[:], GhT[:], P.c["c_hind"][:, 0:512], start=True, stop=True)))
            steps.append(("pe", lambda e: e.matmul(pP2[:], GhT[:], P.c["c_hind"][:, 512:768], start=True, stop=True)))
            steps.append(("dve", lambda e: e.tensor_sub(Ed[:, 0:512], Ed[:, 0:512], pP1[:])))
            steps.append(("dve", lambda e: e.tensor_sub(Ed[:, 512:768], Ed[:, 512:768], pP2[:])))
            steps.append(("dve", lambda e: e.tensor_mul(
                Ed.rearrange("p (h t) -> p h t", h=NH)[:], Ed.rearrange("p (h t) -> p h t", h=NH)[:], striu_bc)))
            steps.append(("act", lambda e: e.activation(expEd[:], Ed[:], AF.Exp)))
            steps.append(("pe", lambda e: e.matmul(pP1[:], bhT[:], P.c["c_hind"][:, 0:512], start=True, stop=True)))
            steps.append(("pe", lambda e: e.matmul(pP2[:], bhT[:], P.c["c_hind"][:, 512:768], start=True, stop=True)))
            steps.append(("dve", lambda e: e.tensor_copy(bcol[:, 0:512], pP1[:])))
            steps.append(("dve", lambda e: e.tensor_copy(bcol[:, 512:768], pP2[:])))
            steps.append(("pe", lambda e: e.transpose(pP1[0:1, 0:6], GhT[:, 127:128], P.c["c_id32"][0:6, 0:6])))
            steps.append(("dve", lambda e: e.tensor_copy(r16[:], pP1[0:1, 0:6])))
            steps.append(("act", lambda e: e.activation(r16[:], r16[:], AF.Exp)))
            steps.append(("pe", lambda e: e.matmul(pP1[:, 0:6], P.c["c_ones_r"][:], r16[:], start=True, stop=True)))
            steps.append(("dve", lambda e: e.tensor_copy(cC[:], pP1[:, 0:6])))
            mm6(steps, PkM, lambda h: kc_s[:, h, :], lambda h: kc_s[:, h, :])
            steps.append(("dve", lambda e: e.tensor_mul(
                PkM.rearrange("p (h t) -> p h t", h=NH)[:], PkM.rearrange("p (h t) -> p h t", h=NH)[:], striu_bc)))
            steps.append(("dve", lambda e: e.tensor_mul(Ed[:], expEd[:], PkM[:])))
            steps.append(("dve", lambda e: e.tensor_mul(BBs.rearrange("p h t -> p (h t)"), Ed[:], bcol[:])))
            for h in range(NH):
                steps.append(("pe", lambda e, h=h: e.transpose(pT[:], BBs[:, h, :], P.c["c_idbf"][:])))
                steps.append(("act", lambda e, h=h: e.copy(bbs[:, h, :], pT[:])))
            if DEBUG is not None and ci == 0 and getattr(P, "dbg_seq", 1) == 0:
                steps.append(("sync_dma", lambda e: (
                    e.dma_start(DEBUG["bb"], BBs[:]).then_inc(P.s["sd"].h, 16),
                    e.dma_start(DEBUG["xp2"], bbs[:]))[-1], 2))
                steps.append(("sync_dma", lambda e: (
                    e.dma_start(DEBUG["pkm"], PkM),
                    )[-1], 1))
                steps.append(("sync_dma", lambda e: (
                    e.dma_start(DEBUG["bcol"], bcol),
                    )[-1], 1))
            steps.append(("dve", lambda e: e.tensor_sub(Xp[:], idbf_bc, BBs[:])))
            steps.append(("dve", lambda e: e.tensor_sub(xp[:], idbf_bc, bbs[:])))
            curB, curb = BBs, bbs
            for lvl in range(6):
                nB, nb = (B2f, b2f) if curB is BBs else (BBs, bbs)
                mm6(steps, nB[:].rearrange("p h t -> p (h t)"), lambda h, curb=curb: curb[:, h, :], lambda h, curB=curB: curB[:, h, :])
                if lvl < 5:
                    mm6(steps, nb[:].rearrange("p h t -> p (h t)"), lambda h, curB=curB: curB[:, h, :], lambda h, curb=curb: curb[:, h, :])
                def mmP(e, nB=nB):
                    for h in range(NH):
                        tgt = pP1[:, h * 128:(h + 1) * 128] if h < 4 else pP2[:, (h - 4) * 128:(h - 3) * 128]
                        i1 = e.matmul(tgt, xp[:, h, :], nB[:, h, :], start=True, stop=True)
                    return i1
                steps.append(("pe", mmP))
                steps.append(("dve", lambda e: e.tensor_add(Xp.rearrange("p h t -> p (h t)")[:, 0:512], Xp.rearrange("p h t -> p (h t)")[:, 0:512], pP1[:])))
                steps.append(("dve", lambda e: e.tensor_add(Xp.rearrange("p h t -> p (h t)")[:, 512:768], Xp.rearrange("p h t -> p (h t)")[:, 512:768], pP2[:])))
                if lvl < 5:
                    def mmp2(e, nB=nB):
                        for h in range(NH):
                            tgt = pP1[:, h * 128:(h + 1) * 128] if h < 4 else pP2[:, (h - 4) * 128:(h - 3) * 128]
                            i1 = e.matmul(tgt, nB[:, h, :], xp[:, h, :], start=True, stop=True)
                        return i1
                    steps.append(("pe", mmp2))
                    steps.append(("dve", lambda e: e.tensor_add(xp.rearrange("p h t -> p (h t)")[:, 0:512], xp.rearrange("p h t -> p (h t)")[:, 0:512], pP1[:])))
                    steps.append(("dve", lambda e: e.tensor_add(xp.rearrange("p h t -> p (h t)")[:, 512:768], xp.rearrange("p h t -> p (h t)")[:, 512:768], pP2[:])))
                curB, curb = nB, nb
            steps.append(("dve", lambda e: e.tensor_mul(kbar.rearrange("p h t -> p (h t)"), kc_s.rearrange("p h t -> p (h t)"), expGb[:])))
            steps.append(("dve", lambda e: e.tensor_mul(qbar.rearrange("p h t -> p (h t)"), qc_sb.rearrange("p h t -> p (h t)"), expGb[:])))
            for h in range(NH):
                steps.append(("pe", lambda e, h=h: e.matmul(pU[:], kbar[:, h, :], S_bf[:, h, :], start=True, stop=True)))
                steps.append(("dve", lambda e, h=h: e.tensor_sub(RHS0[:, h, :], vtm[:, h, :], pU[:])))
                steps.append(("pe", lambda e, h=h: e.matmul(pU[:], Xp[:, h, :], RHS0[:, h, :], start=True, stop=True)))
                steps.append(("dve", lambda e, h=h, ci=ci: e.tensor_scalar_mul(Us[:, h, :], pU[:], beta_all[:, ci, h:h + 1])))
            mm6(steps, PkM, lambda h: kc_s[:, h, :], lambda h: qc_sb[:, h, :])
            steps.append(("dve", lambda e: e.tensor_mul(
                PkM.rearrange("p (h t) -> p h t", h=NH)[:], PkM.rearrange("p (h t) -> p h t", h=NH)[:], triu_bc)))
            steps.append(("dve", lambda e: e.tensor_mul(W2t.rearrange("p h t -> p (h t)"), PkM[:], expEd[:])))
            def mmO(e):
                for h in range(NH):
                    tgt = pO[h // 2][:, (h % 2) * 256:(h % 2) * 256 + 256]
                    e.matmul(tgt, qbar[:, h, :], S_bf[:, h, :], start=True, stop=False)
                    i1 = e.matmul(tgt, W2t[:, h, :], Us[:, h, :], start=False, stop=True)
                return i1
            steps.append(("pe", mmO))
            steps.append(("dve", lambda e: e.memset(oss[:], 0.0)))
            for h in range(NH):
                steps.append(("act", lambda e, h=h: e.activation(
                    sqsc[:], pO[h // 2][:, (h % 2) * 256:(h % 2) * 256 + 256], AF.Square,
                    accum_out=oss[:, h:h + 1])))
            steps.append(("dve", lambda e: e.tensor_scalar_mul(oss[:, 0:6], oss[:, 0:6], 1.0 / 256.0)))
            steps.append(("dve", lambda e: e.tensor_scalar_add(oss[:, 0:6], oss[:, 0:6], 1e-5)))
            steps.append(("act", lambda e: e.activation(oss[:, 0:6], oss[:, 0:6], AF.Ln)))
            steps.append(("act", lambda e: e.activation(oss[:, 0:6], oss[:, 0:6], AF.Exp, scale=-0.5)))
            for h in range(NH):
                steps.append(("dve", lambda e, h=h: e.tensor_scalar_mul(
                    o_s[:, h * 256:(h + 1) * 256], pO[h // 2][:, (h % 2) * 256:(h % 2) * 256 + 256],
                    oss[:, h:h + 1])))
            steps.append(("dve", lambda e: e.tensor_mul(
                o_s.rearrange("p (h d) -> p h d", h=NH)[:], o_s.rearrange("p (h d) -> p h d", h=NH)[:], nwb_bc)))
            steps.append(("dve", lambda e: e.tensor_mul(o_s[:], o_s[:], gate_c[:])))
            steps.append(("dve", lambda e: e.tensor_copy(o_bf[:], o_s[:])))
            for kc2 in range(12):
                steps.append(("pe", lambda e, kc2=kc2: e.transpose(
                    pT[:], o_bf[:, kc2 * 128:(kc2 + 1) * 128], P.c["c_idbf"][:])))
                steps.append(("act", lambda e, kc2=kc2: e.copy(o_fm[:, kc2, :], pT[:])))
            for n2 in range(2):
                def mmS(e, n2=n2):
                    for kc2 in range(12):
                        i1 = e.matmul(pS[:], o_fm[:, kc2, :], wo[:, kc2, n2 * 512:(n2 + 1) * 512],
                                      start=(kc2 == 0), stop=(kc2 == 11))
                    return i1
                steps.append(("pe", mmS))
                steps.append(("act", lambda e, ci=ci, n2=n2: e.copy(s_buf[:, ci, n2 * 512:(n2 + 1) * 512], pS[:])))
            if DEBUG is not None and ci == 0 and getattr(P, "dbg_seq", 1) == 0:
                steps.append(("sync_dma", lambda e: (
                    e.dma_start(DEBUG["ed"], expEd[:]).then_inc(P.s["sd"].h, 16),
                    e.dma_start(DEBUG["xp"], Xp[:]).then_inc(P.s["sd"].h, 16),
                    e.dma_start(DEBUG["us"], Us[:]).then_inc(P.s["sd"].h, 16),
                    e.dma_start(DEBUG["os"], o_s))[-1], 4))
            for h in range(NH):
                steps.append(("pe", lambda e, h=h: e.transpose(pT[:], kc_s[:, h, :], P.c["c_idbf"][:])))
                steps.append(("act", lambda e: e.copy(ktm[:], pT[:])))
                steps.append(("dve", lambda e, h=h: e.tensor_scalar_mul(
                    k2tm[:], ktm[:], expEd[:, h * 128 + 127:h * 128 + 128])))
                steps.append(("pe", lambda e, h=h: e.matmul(pU[:], k2tm[:], Us[:, h, :], start=True, stop=True)))
                steps.append(("dve", lambda e, h=h: e.scalar_tensor_tensor(
                    S_st[h][:], S_st[h][:], cC[:, h:h + 1], pU[:], ALU.mult, ALU.add)))
                steps.append(("dve", lambda e, h=h: e.tensor_copy(S_bf[:, h, :], S_st[h][:])))
            chain(P, steps)



def emit_combine(P, x_ap, s_scr, y_buf, out_ap):
    nc = P.nc
    with ExitStack() as ctx:
        xt = [P.sbuf(ctx, "cbx", [128, D], F32) for _ in range(2)]
        st_ = [P.sbuf(ctx, "cbs", [128, D], F32) for _ in range(2)]
        gy = P.sbuf(ctx, "cbg", [128, D], F32)
        for i in range(NTILE):
            b = i % 2
            steps = []

            def load2(e, i=i, b=b):
                e.dma_start(xt[b][:], x_ap[i * 128:(i + 1) * 128, :]).then_inc(P.s["sd"].h, 16)
                return e.dma_start(st_[b][:], s_scr[:, i * 1024:(i + 1) * 1024])
            steps.append(("sync_dma", load2, 2))
            steps.append(("act", lambda e, i=i: e.activation(gy[:], y_buf[:, i, :], AF.Silu)))
            steps.append(("dve", lambda e, b=b: e.tensor_mul(gy[:], gy[:], st_[b][:])))
            steps.append(("dve", lambda e, b=b: e.tensor_add(gy[:], gy[:], xt[b][:])))
            steps.append(("act_dma", lambda e, i=i: e.dma_start(
                out_ap[i * 128:(i + 1) * 128, :], gy[:])))
            chain(P, steps)
    with nc.Block() as blk:
        @blk.vector
        def _(e):
            e.wait_ge(P.s["sq"].h, P.s["sq"].v)


def park_s(P, s_buf, s_scr):
    with P.nc.Block() as blk:
        @blk.sync
        def _(e):
            e.dma_start(s_scr, s_buf.rearrange("p t d -> p (t d)")).then_inc(P.s["sd"].h, 16)
            P.s["sd"].v += 16
            e.wait_ge(P.s["sd"].h, P.s["sd"].v)





def _emit_rms(P, R, xr_fm, wb_name):
    nc = P.nc
    with ExitStack() as ctx:
        sq = P.sbuf(ctx, "rsq", [128, D], F32)
        st = P.sbuf(ctx, "rst", [128, 4], F32)
        tb = P.sbuf(ctx, "rtb", [128, D], F32)
        xnb = P.sbuf(ctx, "rbf", [128, D], BF16)
        ptr = [P.psum(ctx, "rp", [128, 128], BF16) for _ in range(2)]
        for i in range(NTILE):
            steps = []
            steps.append(("act", lambda e, i=i: e.activation(sq[:], R[:, i, :], AF.Square, accum_out=st[:, 0:1])))
            steps.append(("dve", lambda e: e.tensor_scalar_mul(st[:, 1:2], st[:, 0:1], 1.0 / D)))
            steps.append(("dve", lambda e: e.tensor_scalar_add(st[:, 1:2], st[:, 1:2], 1e-5)))
            steps.append(("act", lambda e: e.activation(st[:, 2:3], st[:, 1:2], AF.Ln)))
            steps.append(("act", lambda e: e.activation(st[:, 2:3], st[:, 2:3], AF.Exp, scale=-0.5)))
            steps.append(("dve", lambda e, i=i: e.tensor_scalar_mul(tb[:], R[:, i, :], st[:, 2:3])))
            steps.append(("dve", lambda e, wb_name=wb_name: e.tensor_mul(xnb[:], tb[:], P.sb[wb_name][:])))
            for j in range(NCH):
                steps.append(("pe", lambda e, j=j: e.transpose(
                    ptr[j % 2][:], xnb[:, j * 128:(j + 1) * 128], P.c["c_idbf"][:])))
                steps.append(("dve", lambda e, i=i, j=j: e.tensor_copy(
                    xr_fm[:, j, i * 128:(i + 1) * 128], ptr[j % 2][:])))
            chain(P, steps)


def emit_l1_ffn_attn(P, xr_fm, R):
    nc = P.nc
    scr_q1 = P.w["scr_q1"]
    with ExitStack() as octx:
        kv_fm = P.sbuf(octx, "kvfm", [128, 4, T], BF16)
        krope = P.sbuf(octx, "krope", [32, T], BF16)
        ctx_full = P.sbuf(octx, "ctxf", [128, NCH, T], BF16)
        with P.nc.Block() as blk:
            @blk.gpsimd
            def _(e):
                e.memset(ctx_full[:], 0.0)
        P.last = None
        with ExitStack() as ctx:
            q_buf = P.sbuf(ctx, "qb1", [128, 4, T], BF16)
            wq = P.sbuf(ctx, "wq1", [128, 8, 512], BF16)
            ropeC = P.sbuf(ctx, "ropeC", [128, T], BF16)
            ropeS = P.sbuf(ctx, "ropeS", [128, T], BF16)
            t1 = P.sbuf(ctx, "rt1", [128, 512], F32)
            t2 = P.sbuf(ctx, "rt2", [128, 512], F32)
            pq = [P.psum(ctx, "pq1", [128, 512], F32) for _ in range(2)]
            psw = P.psum(ctx, "psw", [128, 512], F32)
            steps = [("sync_dma", lambda e: (
                e.dma_start(ropeC[:], P.w["c_ropeC"]).then_inc(P.s["sd"].h, 16),
                e.dma_start(ropeS[:], P.w["c_ropeS"]))[-1], 2)]
            for half in range(2):
                steps.append(("pool_dma", lambda e, half=half: e.dma_start(
                    wq[:], P.w["L1_ffn_Wq"].rearrange("(c p) n -> p c n", p=128)[:, :, half * 512:(half + 1) * 512])))
                for m in range(4):
                    for n in range(4):
                        def mmq(e, m=m, n=n):
                            for c in range(NCH):
                                i1 = e.matmul(pq[n % 2][:], wq[:, c, m * 128:(m + 1) * 128],
                                              xr_fm[:, c, n * 512:(n + 1) * 512], start=(c == 0), stop=(c == 7))
                            return i1
                        steps.append(("pe", mmq))
                        steps.append(("act", lambda e, m=m, n=n: e.copy(q_buf[:, m, n * 512:(n + 1) * 512], pq[n % 2][:])))
                for m in range(4):
                    for n in range(4):
                        sl = slice(n * 512, (n + 1) * 512)
                        steps.append(("pe", lambda e, m=m, sl=sl: e.matmul(
                            psw[:], P.c["c_sperm"][:], q_buf[:, m, sl], start=True, stop=True)))
                        steps.append(("dve", lambda e, m=m, sl=sl: e.tensor_mul(t1[:], q_buf[:, m, sl], ropeC[:, sl])))
                        steps.append(("dve", lambda e, sl=sl: e.tensor_mul(t2[:], psw[:], ropeS[:, sl])))
                        steps.append(("dve", lambda e, m=m, sl=sl: e.tensor_add(q_buf[:, m, sl], t1[:], t2[:])))
                steps.append(("sync_dma", lambda e, half=half: e.dma_start(
                    scr_q1[:, half * 4:(half + 1) * 4, :], q_buf[:])))
            chain(P, steps)
        with ExitStack() as ctx:
            wkva = P.sbuf(ctx, "wkva", [128, 8, 160], BF16)
            wkvb = P.sbuf(ctx, "wkvb", [128, 512], BF16)
            kvlat = P.sbuf(ctx, "kvlat", [128, T], BF16)
            krf = P.sbuf(ctx, "krf", [32, T], BF16)
            ropeC = P.sbuf(ctx, "ropeC2", [32, T], BF16)
            ropeS = P.sbuf(ctx, "ropeS2", [32, T], BF16)
            t1 = P.sbuf(ctx, "rt3", [32, 512], F32)
            t2 = P.sbuf(ctx, "rt4", [32, 512], F32)
            pq = [P.psum(ctx, "pq2", [128, 512], F32) for _ in range(2)]
            psw = P.psum(ctx, "psw2", [32, 512], F32)
            steps = [("pool_dma", lambda e: (
                e.dma_start(wkva[:], P.w["L1_ffn_Wkva"].rearrange("(c p) n -> p c n", p=128)).then_inc(P.s["sg"].h, 16),
                e.dma_start(wkvb[:], P.w["L1_ffn_Wkvb"]))[-1], 2)]
            steps.append(("sync_dma", lambda e: (
                e.dma_start(ropeC[:], P.w["c_ropeC"][32:64, :]).then_inc(P.s["sd"].h, 16),
                e.dma_start(ropeS[:], P.w["c_ropeS"][32:64, :]))[-1], 2))
            for n in range(4):
                sl = slice(n * 512, (n + 1) * 512)
                def mma(e, n=n, sl=sl):
                    for c in range(NCH):
                        i1 = e.matmul(pq[0][:], wkva[:, c, 0:128],
                                      xr_fm[:, c, sl], start=(c == 0), stop=(c == 7))
                    return i1
                steps.append(("pe", mma))
                steps.append(("act", lambda e, sl=sl: e.copy(kvlat[:, sl], pq[0][:])))
                def mmb(e, n=n, sl=sl):
                    for c in range(NCH):
                        i1 = e.matmul(pq[1][0:32, :], wkva[:, c, 128:160],
                                      xr_fm[:, c, sl], start=(c == 0), stop=(c == 7))
                    return i1
                steps.append(("pe", mmb))
                steps.append(("act", lambda e, sl=sl: e.copy(krf[:, sl], pq[1][0:32, :])))
                steps.append(("pe", lambda e, sl=sl: e.matmul(psw[:], P.c["c_sperm32"][:], krf[:, sl], start=True, stop=True)))
                steps.append(("dve", lambda e, sl=sl: e.tensor_mul(t1[:], krf[:, sl], ropeC[:, sl])))
                steps.append(("dve", lambda e, sl=sl: e.tensor_mul(t2[:], psw[:], ropeS[:, sl])))
                steps.append(("dve", lambda e, sl=sl: e.tensor_add(krope[:, sl], t1[:], t2[:])))
                for m4 in range(4):
                    steps.append(("pe", lambda e, m4=m4, sl=sl: e.matmul(
                        pq[0][:], wkvb[:, m4 * 128:(m4 + 1) * 128], kvlat[:, sl], start=True, stop=True)))
                    steps.append(("act", lambda e, m4=m4, sl=sl: e.copy(kv_fm[:, m4, sl], pq[0][:])))
            chain(P, steps)
        with ExitStack() as ctx:
            qp = P.sbuf(ctx, "qp3", [128, T], BF16)
            kpair = P.sbuf(ctx, "kp3", [128, T], BF16)
            vsrcO = P.sbuf(ctx, "vsrcO3", [32, T], BF16)
            tmp64 = P.sbuf(ctx, "tmp64_3", [64, T], BF16)
            v_tmE = P.sbuf(ctx, "vtmE3", [128, NTILE, 33], BF16)
            v_tmO = P.sbuf(ctx, "vtmO3", [128, NTILE, 33], BF16)
            sqh = P.sbuf(ctx, "sqh3", [128, 1024], F32)
            qmax2 = P.sbuf(ctx, "qmax2", [2, 8], F32)
            m2 = P.sbuf(ctx, "m2s", [2, 2], F32)
            row2 = P.sbuf(ctx, "row2s", [1, 2], F32)
            bnd2 = P.sbuf(ctx, "bnd2s", [128, 2], F32)
            invrE = P.sbuf(ctx, "invrE3", [1, 512], F32)
            invrO = P.sbuf(ctx, "invrO3", [1, 512], F32)
            ctx32E = P.sbuf(ctx, "ctx32E", [32, 512], BF16)
            ctx32O = P.sbuf(ctx, "ctx32O", [32, 512], BF16)
            p_tE = [P.sbuf(ctx, "ptE3", [128, 512], BF16) for _ in range(2)]
            p_tO = [P.sbuf(ctx, "ptO3", [128, 512], BF16) for _ in range(2)]
            pstE = [P.psum(ctx, "pstE3", [128, 512], F32) for _ in range(2)]
            pstO = [P.psum(ctx, "pstO3", [128, 512], F32) for _ in range(2)]
            pctxE = P.psum(ctx, "pctxE3", [33, 512], F32)
            pctxO = P.psum(ctx, "pctxO3", [33, 512], F32)
            pVT = P.psum(ctx, "pVT3", [128, 512], BF16)
            pSml = P.psum(ctx, "pSml3", [128, 8], F32)
            pe, act, dve = P.s["pe"], P.s["act"], P.s["dve"]
            chain(P, [("pool", lambda e: (
                e.memset(v_tmE[:], 0.0), e.memset(v_tmE[:, :, 32:33], 1.0),
                e.memset(v_tmO[:], 0.0), e.memset(v_tmO[:, :, 32:33], 1.0))[-1])])
            for j in range(8):
                he, ho = 2 * j, 2 * j + 1
                kb = (j % 2) * 64
                steps = []
                steps.append(("sync_dma", lambda e, j=j: e.dma_start(qp[:], scr_q1[:, j, :])))
                if j % 2 == 0:
                    def asm(e, j=j):
                        e.copy(kpair[0:32, :], kv_fm[0:32, j // 2, :])
                        e.copy(kpair[64:96, :], kv_fm[32:64, j // 2, :])
                        e.copy(kpair[32:64, :], krope[:])
                        e.copy(kpair[96:128, :], krope[:])
                        return e.copy(vsrcO[0:32, :], kv_fm[32:64, j // 2, :])
                else:
                    def asm(e, j=j):
                        e.copy(tmp64[0:64, :], kv_fm[64:128, j // 2, :])
                        e.copy(kpair[0:32, :], tmp64[0:32, :])
                        e.copy(kpair[64:96, :], tmp64[32:64, :])
                        e.copy(kpair[32:64, :], krope[:])
                        e.copy(kpair[96:128, :], krope[:])
                        return e.copy(vsrcO[0:32, :], tmp64[32:64, :])
                steps.append(("act", asm))
                for (src, vdst) in ((kpair, v_tmE), (vsrcO, v_tmO)):
                    def trv(e, src=src):
                        for u in range(16):
                            i1 = e.transpose(pVT[:, u * 32:(u + 1) * 32],
                                             src[0:32, u * 128:(u + 1) * 128],
                                             P.c["c_idbf"][0:32, 0:32])
                        return i1
                    steps.append(("pe", trv))
                    def vcp(e, vdst=vdst):
                        for u in range(16):
                            i1 = e.copy(vdst[:, u, 0:32], pVT[:, u * 32:(u + 1) * 32])
                        return i1
                    steps.append(("act", vcp))
                for ti, tile in ((0, qp), (1, kpair)):
                    for half in range(2):
                        hsl = slice(half * 1024, (half + 1) * 1024)
                        steps.append(("act", lambda e, tile=tile, hsl=hsl: e.activation(
                            sqh[:], tile[:, hsl], AF.Square)))
                        for n in range(2):
                            pX = pstE[n % 2]
                            steps.append(("pe", lambda e, n=n, pX=pX: e.matmul(
                                pX[0:2, :], P.c["c_sel2"][:], sqh[:, n * 512:(n + 1) * 512],
                                start=True, stop=True)))
                            steps.append(("dve", lambda e, ti=ti, half=half, n=n, pX=pX: e.reduce_max(
                                qmax2[:, ti * 4 + half * 2 + n:ti * 4 + half * 2 + n + 1],
                                pX[0:2, :], axis=AX)))
                steps.append(("dve", lambda e: (
                    e.reduce_max(m2[:, 0:1], qmax2[:, 0:4], axis=AX),
                    e.reduce_max(m2[:, 1:2], qmax2[:, 4:8], axis=AX))[-1]))
                steps.append(("dve", lambda e: e.tensor_mul(m2[:, 0:1], m2[:, 0:1], m2[:, 1:2])))
                steps.append(("act", lambda e: e.activation(m2[:, 0:1], m2[:, 0:1], AF.Ln)))
                steps.append(("act", lambda e: e.activation(m2[:, 0:1], m2[:, 0:1], AF.Exp, scale=0.5)))
                steps.append(("act", lambda e: e.mul(m2[:, 0:1], m2[:, 0:1], -0.125)))
                steps.append(("pe", lambda e: e.transpose(pSml[0:1, 0:2], m2[:, 0:1], P.c["c_id32"][0:2, 0:2])))
                steps.append(("dve", lambda e: e.tensor_copy(row2[:], pSml[0:1, 0:2])))
                steps.append(("pe", lambda e: e.matmul(pSml[:, 0:2], P.c["c_ones_r"][:], row2[:], start=True, stop=True)))
                steps.append(("dve", lambda e: e.tensor_copy(bnd2[:], pSml[:, 0:2])))
                chain(P, steps)
                for qc in range(4):
                    nkt = (qc + 1) * 4
                    qsl = slice(qc * 512, (qc + 1) * 512)
                    pe0, act0, dve0 = pe.v, act.v, dve.v
                    ST_m = {}
                    PV_m = {}
                    m = pe0
                    for kt in range(nkt):
                        m += 1
                        ST_m[kt] = m
                        if kt >= 1:
                            m += 1
                            PV_m[kt - 1] = m
                    m += 1
                    PV_m[nkt - 1] = m
                    exp_m = {kt: act0 + kt + 1 for kt in range(nkt)}
                    msk_m = {kt: dve0 + (kt - (nkt - 4)) + 1 for kt in range(nkt - 4, nkt)}

                    def p_ready(kt, nkt=nkt, msk_m=msk_m, exp_m=exp_m):
                        if kt >= nkt - 4:
                            return (dve, msk_m[kt])
                        return (act, exp_m[kt])

                    start_wait = P.last
                    with nc.Block() as blk:
                        @blk.tensor
                        def _(e):
                            if start_wait is not None:
                                e.wait_ge(start_wait[0].h, start_wait[1])
                            for kt in range(nkt):
                                bb = kt % 2
                                if kt >= 2:
                                    e.wait_ge(act.h, exp_m[kt - 2])
                                e.matmul(pstE[bb][:], kpair[0:64, kt * 128:(kt + 1) * 128],
                                         qp[0:64, qsl], start=True, stop=True)
                                e.matmul(pstO[bb][:], kpair[64:128, kt * 128:(kt + 1) * 128],
                                         qp[64:128, qsl], start=True, stop=True).then_inc(pe.h, 1)
                                if kt >= 1:
                                    ss, vv = p_ready(kt - 1)
                                    e.wait_ge(ss.h, vv)
                                    e.matmul(pctxE[:], v_tmE[:, kt - 1, :], p_tE[(kt - 1) % 2][:],
                                             start=(kt - 1 == 0), stop=False)
                                    e.matmul(pctxO[:], v_tmO[:, kt - 1, :], p_tO[(kt - 1) % 2][:],
                                             start=(kt - 1 == 0), stop=False).then_inc(pe.h, 1)
                            ss, vv = p_ready(nkt - 1)
                            e.wait_ge(ss.h, vv)
                            e.matmul(pctxE[:], v_tmE[:, nkt - 1, :], p_tE[(nkt - 1) % 2][:],
                                     start=(nkt - 1 == 0), stop=True)
                            e.matmul(pctxO[:], v_tmO[:, nkt - 1, :], p_tO[(nkt - 1) % 2][:],
                                     start=(nkt - 1 == 0), stop=True).then_inc(pe.h, 1)
                        @blk.scalar
                        def _(e):
                            for kt in range(nkt):
                                bb = kt % 2
                                e.wait_ge(pe.h, ST_m[kt])
                                if kt >= 2:
                                    e.wait_ge(pe.h, PV_m[kt - 2])
                                e.activation(p_tE[bb][:], pstE[bb][:], AF.Exp, scale=0.125,
                                             bias=bnd2[:, 0:1])
                                e.activation(p_tO[bb][:], pstO[bb][:], AF.Exp, scale=0.125,
                                             bias=bnd2[:, 1:2]).then_inc(act.h, 1)
                        @blk.vector
                        def _(e):
                            for kt in range(nkt - 4, nkt):
                                e.wait_ge(act.h, exp_m[kt])
                                jj = kt - (nkt - 4)
                                e.tensor_mul(p_tE[kt % 2][:], p_tE[kt % 2][:], dm_mask(P, jj))
                                e.tensor_mul(p_tO[kt % 2][:], p_tO[kt % 2][:],
                                             dm_mask(P, jj)).then_inc(dve.h, 1)
                    pe.v = m
                    act.v = act0 + nkt
                    dve.v = dve0 + 4
                    P.last = (pe, m)
                    steps = []
                    steps.append(("dve", lambda e: (
                        e.reciprocal(invrE[:], pctxE[32:33, :]),
                        e.reciprocal(invrO[:], pctxO[32:33, :]))[-1]))
                    steps.append(("act", lambda e: (
                        e.copy(ctx32E[:], pctxE[0:32, :]),
                        e.copy(ctx32O[:], pctxO[0:32, :]))[-1]))
                    steps.append(("pe", lambda e: (
                        e.matmul(pstE[0][0:32, :], P.c["c_ones_r"][0:1, 0:32], invrE[:], start=True, stop=True),
                        e.matmul(pstO[0][0:32, :], P.c["c_ones_r"][0:1, 0:32], invrO[:], start=True, stop=True))[-1]))
                    steps.append(("dve", lambda e, qsl=qsl, j=j: (
                        e.tensor_mul(ctx_full[0:32, j, qsl], ctx32E[:], pstE[0][0:32, :]),
                        e.tensor_mul(ctx_full[64:96, j, qsl], ctx32O[:], pstO[0][0:32, :]))[-1]))
                    chain(P, steps)
        with ExitStack() as ctx:
            wo = P.sbuf(ctx, "woeff", [128, 8, 1024], BF16)
            pS = P.psum(ctx, "pS4", [128, 512], F32)
            steps = [("pool_dma", lambda e: e.dma_start(
                wo[:], P.w["L1_ffn_Wo"].rearrange("(c p) n -> p c n", p=128)))]
            for t in range(NTILE):
                for n2 in range(2):
                    def mmS(e, t=t, n2=n2):
                        for kc in range(NCH):
                            i1 = e.matmul(pS[:], ctx_full[:, kc, t * 128:(t + 1) * 128],
                                          wo[:, kc, n2 * 512:(n2 + 1) * 512], start=(kc == 0), stop=(kc == 7))
                        return i1
                    steps.append(("pe", mmS))
                    steps.append(("dve", lambda e, t=t, n2=n2: e.tensor_add(
                        R[:, t, n2 * 512:(n2 + 1) * 512], R[:, t, n2 * 512:(n2 + 1) * 512], pS[:])))
            chain(P, steps)


def emit_l1_moe(P, xr_fm, R):
    nc = P.nc
    with ExitStack() as ctx:
        combine = P.sbuf(ctx, "comb1", [128, NTILE, 8], F32)
        with ExitStack() as c2:
            wgate = P.sbuf(c2, "wgate", [128, 8, 8], BF16)
            zrow = P.sbuf(c2, "zrow", [128, 8], F32)
            m1 = P.sbuf(c2, "m1", [128, 2], F32)
            top8 = P.sbuf(c2, "top81", [128, 8], F32)
            pgl = P.psum(c2, "pgl", [128, 8], F32)
            steps = [("pool_dma", lambda e: e.dma_start(wgate[:], P.w["L1_ffn_Wgate"].rearrange("(c p) n -> p c n", p=128)))]
            for i in range(NTILE):
                def mmg(e, i=i):
                    for c in range(NCH):
                        i1 = e.matmul(pgl[:], xr_fm[:, c, i * 128:(i + 1) * 128],
                                      wgate[:, c, :], start=(c == 0), stop=(c == 7))
                    return i1
                steps.append(("pe", mmg))
                steps.append(("dve", lambda e: e.tensor_add(zrow[:], pgl[:], P.sb["l1bias"][:])))
                steps.append(("dve", lambda e: e.reduce_max(m1[:, 0:1], zrow[:], axis=AX)))
                steps.append(("dve", lambda e: e.tensor_scalar_mul(m1[:, 0:1], m1[:, 0:1], -1.0)))
                steps.append(("act", lambda e: e.activation(zrow[:], zrow[:], AF.Exp, bias=m1[:, 0:1])))
                steps.append(("dve", lambda e: e.max(out=top8[:], in_=zrow[:])))
                steps.append(("dve", lambda e: e.tensor_single_scalar(top8[:], zrow[:], top8[:, 1:2], ALU.is_ge)))
                steps.append(("dve", lambda e: e.tensor_mul(zrow[:], zrow[:], top8[:])))
                steps.append(("dve", lambda e: e.reduce_sum(m1[:, 1:2], zrow[:], axis=AX)))
                steps.append(("dve", lambda e: e.reciprocal(m1[:, 1:2], m1[:, 1:2])))
                steps.append(("dve", lambda e, i=i: e.tensor_scalar_mul(combine[:, i, :], zrow[:], m1[:, 1:2])))
            chain(P, steps)
        h_full = P.sbuf(ctx, "hf1", [128, 4, T], BF16)
        wg_b = [P.sbuf(ctx, "wgb1", [128, 8, 512], BF16) for _ in range(2)]
        wu_b = [P.sbuf(ctx, "wub1", [128, 8, 512], BF16) for _ in range(2)]
        wd_b = [P.sbuf(ctx, "wdb1", [128, 4, 1024], BF16) for _ in range(2)]
        ts = [P.sbuf(ctx, "ts1", [128, 512], BF16) for _ in range(2)]
        pg = [P.psum(ctx, "pg1", [128, 512], F32) for _ in range(2)]
        pu = [P.psum(ctx, "pu1", [128, 512], F32) for _ in range(2)]
        pd = [P.psum(ctx, "pd1", [128, 512], F32) for _ in range(2)]
        sd, sg, pe, act, dve = P.s["sd"], P.s["sg"], P.s["pe"], P.s["act"], P.s["dve"]
        experts = [("sh", "L1_ffn_sWg", "L1_ffn_sWu", "L1_ffn_sWd", 0, None)] + [
            ("e%d" % e, "L1_ffn_Wg", "L1_ffn_Wu", "L1_ffn_Wd", e, e) for e in range(8)]
        for ei, (nm, ng, nu, nd, srcidx, eidx) in enumerate(experts):
            with nc.Block() as blk:
                @blk.gpsimd
                def _(e):
                    e.dma_start(wg_b[ei % 2][:], P.w[ng][srcidx].rearrange("(c p) i -> p c i", p=128)).then_inc(sg.h, 16)
                    e.dma_start(wu_b[ei % 2][:], P.w[nu][srcidx].rearrange("(c p) i -> p c i", p=128)).then_inc(sg.h, 16)
                    e.dma_start(wd_b[ei % 2][:], P.w[nd][srcidx].rearrange("(c p) i -> p c i", p=128)).then_inc(sg.h, 16)
                    sg.v += 48
                    e.wait_ge(sg.h, sg.v)
            pe0, act0, dve0 = pe.v, act.v, dve.v
            items = [(mm, n) for mm in range(4) for n in range(4)]
            g_m = {}
            u_m = {}
            mct = pe0
            for it in items:
                mct += 1
                g_m[it] = mct
                mct += 1
                u_m[it] = mct
            silu_m = {it: act0 + i + 1 for i, it in enumerate(items)}
            h_m = {it: dve0 + i + 1 for i, it in enumerate(items)}
            with nc.Block() as blk:
                @blk.tensor
                def _(e):
                    for idx, (mm, n) in enumerate(items):
                        bb = idx % 2
                        if idx >= 2:
                            e.wait_ge(act.h, silu_m[items[idx - 2]])
                        for kc in range(NCH):
                            i1 = e.matmul(pg[bb][:], wg_b[ei % 2][:, kc, mm * 128:(mm + 1) * 128],
                                          xr_fm[:, kc, n * 512:(n + 1) * 512], start=(kc == 0), stop=(kc == 7))
                        i1.then_inc(pe.h, 1)
                        if idx >= 2:
                            e.wait_ge(dve.h, h_m[items[idx - 2]])
                        for kc in range(NCH):
                            i1 = e.matmul(pu[bb][:], wu_b[ei % 2][:, kc, mm * 128:(mm + 1) * 128],
                                          xr_fm[:, kc, n * 512:(n + 1) * 512], start=(kc == 0), stop=(kc == 7))
                        i1.then_inc(pe.h, 1)
                @blk.scalar
                def _(e):
                    for idx, it in enumerate(items):
                        bb = idx % 2
                        e.wait_ge(pe.h, g_m[it])
                        if idx >= 2:
                            e.wait_ge(dve.h, h_m[items[idx - 2]])
                        e.activation(ts[bb][:], pg[bb][:], AF.Silu).then_inc(act.h, 1)
                @blk.vector
                def _(e):
                    for idx, (mm, n) in enumerate(items):
                        bb = idx % 2
                        e.wait_ge(act.h, silu_m[(mm, n)])
                        e.wait_ge(pe.h, u_m[(mm, n)])
                        e.tensor_mul(h_full[:, mm, n * 512:(n + 1) * 512],
                                     ts[bb][:], pu[bb][:]).then_inc(dve.h, 1)
            pe.v = mct
            act.v = act0 + 16
            dve.v = dve0 + 16
            pe0, dve0 = pe.v, dve.v
            items2 = [(t, j2) for t in range(NTILE) for j2 in range(2)]
            dn_m = {}
            mct = pe0
            for it in items2:
                mct += 1
                dn_m[it] = mct
            ev_m = {it: dve0 + i + 1 for i, it in enumerate(items2)}
            with nc.Block() as blk:
                @blk.tensor
                def _(e):
                    for idx, (t, j2) in enumerate(items2):
                        bb = idx % 2
                        if idx >= 2:
                            e.wait_ge(dve.h, ev_m[items2[idx - 2]])
                        for kc in range(4):
                            i1 = e.matmul(pd[bb][:], h_full[:, kc, t * 128:(t + 1) * 128],
                                          wd_b[ei % 2][:, kc, j2 * 512:(j2 + 1) * 512],
                                          start=(kc == 0), stop=(kc == 3))
                        i1.then_inc(pe.h, 1)
                @blk.vector
                def _(e):
                    for idx, (t, j2) in enumerate(items2):
                        bb = idx % 2
                        e.wait_ge(pe.h, dn_m[(t, j2)])
                        dst = R[:, t, j2 * 512:(j2 + 1) * 512]
                        if eidx is None:
                            e.tensor_add(dst, dst, pd[bb][:]).then_inc(dve.h, 1)
                        else:
                            e.scalar_tensor_tensor(dst, pd[bb][:], combine[:, t, eidx:eidx + 1],
                                                   dst, ALU.mult, ALU.add).then_inc(dve.h, 1)
            pe.v = mct
            dve.v = dve0 + len(items2)


def build_layer1():
    nc = bass.Bass("TRN2", target_bir_lowering=False)
    ctx = ExitStack()
    P = Prog(nc, ctx)
    names = (["c_ropeC", "c_ropeS"] + [n for n in WSHAPES if n.startswith("L1_")] + BASE_CONSTS)
    for n in names:
        shape, dt = WSHAPES[n]
        P.w[n] = nc.dram_tensor(n, list(shape), dt, kind="ExternalInput").ap()
    P.w["x1in"] = nc.dram_tensor("x1in", [2, T, D], F32, kind="ExternalInput").ap()
    out = nc.dram_tensor("out", [2, T, D], F32, kind="ExternalOutput").ap()
    scr_s = nc.dram_tensor("scr_s", [128, NTILE * D], F32, kind="Internal").ap()
    scr_xn = nc.dram_tensor("scr_xn", [T, D], F32, kind="Internal").ap()
    scr_q = nc.dram_tensor("scr_q", [128, 6, T], BF16, kind="Internal").ap()
    scr_k = nc.dram_tensor("scr_k", [128, 6, T], BF16, kind="Internal").ap()
    scr_v = nc.dram_tensor("scr_v", [6, T, 256], BF16, kind="Internal").ap()
    scr_g = nc.dram_tensor("scr_g", [T, 1536], BF16, kind="Internal").ap()
    P_scr_q1 = nc.dram_tensor("scr_q1", [128, NCH, T], BF16, kind="Internal").ap()

    P.w["scr_q1"] = P_scr_q1
    import os
    global DEBUG
    if os.environ.get("KDEBUG"):
        DEBUG = {"s": nc.dram_tensor("dbg_s", [2, 128, NTILE * D], F32, kind="ExternalOutput").ap(),
                 "y": nc.dram_tensor("dbg_y", [2, 128, NTILE, D], F32, kind="ExternalOutput").ap(),
                 "q": nc.dram_tensor("dbg_q", [128, 6, T], BF16, kind="ExternalOutput").ap(),
                 "k": nc.dram_tensor("dbg_k", [128, 6, T], BF16, kind="ExternalOutput").ap(),
                 "v": nc.dram_tensor("dbg_v", [6, T, 256], BF16, kind="ExternalOutput").ap(),
                 "bg": nc.dram_tensor("dbg_bg", [2, 128, NTILE, 6], F32, kind="ExternalOutput").ap(),
                 "ed": nc.dram_tensor("dbg_ed", [128, 768], F32, kind="ExternalOutput").ap(),
                 "xp": nc.dram_tensor("dbg_xp", [128, 6, 128], BF16, kind="ExternalOutput").ap(),
                 "us": nc.dram_tensor("dbg_us", [128, 6, 256], BF16, kind="ExternalOutput").ap(),
                 "os": nc.dram_tensor("dbg_os", [128, 1536], F32, kind="ExternalOutput").ap(),
                 "bb": nc.dram_tensor("dbg_bb", [128, 6, 128], BF16, kind="ExternalOutput").ap(),
                 "xp2": nc.dram_tensor("dbg_xp2", [128, 6, 128], BF16, kind="ExternalOutput").ap(),
                 "pkm": nc.dram_tensor("dbg_pkm", [128, 768], F32, kind="ExternalOutput").ap(),
                 "bcol": nc.dram_tensor("dbg_bcol", [128, 768], F32, kind="ExternalOutput").ap()}
    else:
        DEBUG = None
    with ctx:
        buf_a = P.sbuf(ctx, "BUFA", [128, NTILE, D], F32)
        xn_fm = P.sbuf(ctx, "XN", [128, NCH, T], BF16)
        load_consts(P, BASE_CONSTS)
        stage_l1_small(P)
        beta_all = P.sbuf(ctx, "betaA", [128, NTILE, 6], F32)
        g_all = P.sbuf(ctx, "gA", [128, NTILE, 6], F32)
        for seq in range(2):
            P.dbg_seq = seq
            emit_ln(P, P.w["x1in"][seq], xn_fm, xn32_scr=scr_xn)
            barrier(P)
            emit_l1_attn_passA(P, xn_fm, scr_q, scr_k, scr_v, scr_g, beta_all, g_all)
            barrier(P)
            if DEBUG is not None and seq == 0:
                with nc.Block() as blk:
                    @blk.sync
                    def _(e):
                        e.dma_start(DEBUG["q"], scr_q).then_inc(P.s["sd"].h, 16)
                        e.dma_start(DEBUG["k"], scr_k).then_inc(P.s["sd"].h, 16)
                        e.dma_start(DEBUG["v"], scr_v).then_inc(P.s["sd"].h, 16)
                        e.dma_start(DEBUG["bg"][0], beta_all[:]).then_inc(P.s["sd"].h, 16)
                        e.dma_start(DEBUG["bg"][1], g_all[:]).then_inc(P.s["sd"].h, 16)
                        P.s["sd"].v += 80
                        e.wait_ge(P.s["sd"].h, P.s["sd"].v)
            emit_l1_attn_passB(P, buf_a, scr_q, scr_k, scr_v, scr_g, beta_all, g_all)
            barrier(P)
            park_s(P, buf_a, scr_s)
            with nc.Block() as blk:
                @blk.sync
                def _(e):
                    e.dma_start(buf_a[:],
                                scr_xn.rearrange("(t p) d -> p t d", p=128)).then_inc(P.s["sd"].h, 16)
                    P.s["sd"].v += 16
                    e.wait_ge(P.s["sd"].h, P.s["sd"].v)
            _emit_rms(P, buf_a, xn_fm, "ln1b")
            barrier(P)
            emit_l1_ffn_attn(P, xn_fm, buf_a)
            barrier(P)
            _emit_rms(P, buf_a, xn_fm, "ln2b")
            barrier(P)
            emit_l1_moe(P, xn_fm, buf_a)
            barrier(P)
            if DEBUG is not None:
                with nc.Block() as blk:
                    @blk.sync
                    def _(e):
                        e.dma_start(DEBUG["s"][seq], scr_s).then_inc(P.s["sd"].h, 16)
                        e.dma_start(DEBUG["y"][seq], buf_a[:]).then_inc(P.s["sd"].h, 16)
                        P.s["sd"].v += 32
                        e.wait_ge(P.s["sd"].h, P.s["sd"].v)
            emit_combine(P, P.w["x1in"][seq], scr_s, buf_a, out[seq])
            barrier(P)
        P.check()
    return nc



def build_layer0():
    nc = bass.Bass("TRN2", target_bir_lowering=False)
    ctx = ExitStack()
    P = Prog(nc, ctx)
    names = ["x0"] + [n for n in WSHAPES if n.startswith("L0_")] + BASE_CONSTS
    for n in names:
        shape, dt = WSHAPES[n]
        P.w[n] = nc.dram_tensor(n, list(shape), dt, kind="ExternalInput").ap()
    x1_out = nc.dram_tensor("x1", [2, T, D], F32, kind="ExternalOutput").ap()
    global DEBUG
    import os
    if os.environ.get("KDEBUG"):
        DEBUG = {"srow": nc.dram_tensor("dbg_srow", [2, 16, 1, 16], F32, kind="ExternalOutput").ap(),
                 "q": nc.dram_tensor("dbg_q", [2, 128, T], BF16, kind="ExternalOutput").ap(),
                 "k": nc.dram_tensor("dbg_k", [2, 128, T], BF16, kind="ExternalOutput").ap()}
    scr_s = nc.dram_tensor("scr_s", [128, NTILE * D], F32, kind="Internal").ap()

    with ctx:
        buf_a = P.sbuf(ctx, "BUFA", [128, NTILE, D], F32)
        xn_fm = P.sbuf(ctx, "XN", [128, NCH, T], BF16)
        load_consts(P, BASE_CONSTS)
        stage_l0_small(P)
        stage_l0_big(P)
        combine = P.sbuf(ctx, "comb", [128, NTILE, 16], F32)
        for seq in range(2):
            DBGSEQ[0] = seq
            emit_ln(P, P.w["x0"][seq], xn_fm)
            barrier(P)
            emit_l0_attn(P, xn_fm, buf_a)
            barrier(P)
            park_s(P, buf_a, scr_s)
            emit_l0_gating(P, xn_fm, combine)
            barrier(P)
            emit_l0_ffn(P, xn_fm, buf_a, combine)
            barrier(P)
            emit_combine(P, P.w["x0"][seq], scr_s, buf_a, x1_out[seq])
            barrier(P)
        P.check()
    return nc



def build_full():
    global DEBUG
    DEBUG = None
    nc = bass.Bass("TRN2", target_bir_lowering=False)
    ctx = ExitStack()
    P = Prog(nc, ctx)
    names = (["x0", "c_ropeC", "c_ropeS"] + BASE_CONSTS +
             [n for n in WSHAPES if n.startswith("L0_") or n.startswith("L1_")])
    for n in names:
        shape, dt = WSHAPES[n]
        P.w[n] = nc.dram_tensor(n, list(shape), dt, kind="ExternalInput").ap()
    out = nc.dram_tensor("out", [2, T, D], F32, kind="ExternalOutput").ap()
    x1scr = nc.dram_tensor("x1scr", [2, T, D], F32, kind="Internal").ap()
    scr_s = nc.dram_tensor("scr_s", [128, NTILE * D], F32, kind="Internal").ap()
    scr_xn = nc.dram_tensor("scr_xn", [T, D], F32, kind="Internal").ap()
    scr_q = nc.dram_tensor("scr_q", [128, 6, T], BF16, kind="Internal").ap()
    scr_k = nc.dram_tensor("scr_k", [128, 6, T], BF16, kind="Internal").ap()
    scr_v = nc.dram_tensor("scr_v", [6, T, 256], BF16, kind="Internal").ap()
    scr_g = nc.dram_tensor("scr_g", [T, 1536], BF16, kind="Internal").ap()
    P.w["scr_q1"] = nc.dram_tensor("scr_q1", [128, NCH, T], BF16, kind="Internal").ap()
    with ctx:
        buf_a = P.sbuf(ctx, "BUFA", [128, NTILE, D], F32)
        xn_fm = P.sbuf(ctx, "XN", [128, NCH, T], BF16)
        P.sb["joinT"] = P.sbuf(ctx, "joinT", [1, 8], F32)
        load_consts(P, BASE_CONSTS)
        stage_l0_small(P)
        stage_l0_big(P)
        stage_l1_small(P)
        combine0 = P.sbuf(ctx, "comb", [128, NTILE, 16], F32)
        beta_all = P.sbuf(ctx, "betaA", [128, NTILE, 6], F32)
        g_all = P.sbuf(ctx, "gA", [128, NTILE, 6], F32)
        for seq in range(2):
            emit_ln(P, P.w["x0"][seq], xn_fm)
            emit_l0_attn2(P, xn_fm, buf_a)
            barrier(P)
            park_s(P, buf_a, scr_s)
            emit_l0_gating(P, xn_fm, combine0)
            barrier(P)
            emit_l0_ffn(P, xn_fm, buf_a, combine0)
            barrier(P)
            emit_combine(P, P.w["x0"][seq], scr_s, buf_a, x1scr[seq])
            barrier(P)
        P.check()
        full_barrier(P)
        fresh_sems(P)
        for seq in range(2):
            emit_ln(P, x1scr[seq], xn_fm, xn32_scr=scr_xn)
            emit_l1_attn_passA2(P, xn_fm, buf_a, scr_q, scr_k, scr_v, scr_g,
                                beta_all, g_all)
            barrier(P)
            emit_l1_attn_passB(P, buf_a, scr_q, scr_k, scr_v, scr_g, beta_all, g_all)
            barrier(P)
            park_s(P, buf_a, scr_s)
            with nc.Block() as blk:
                @blk.sync
                def _(e):
                    e.dma_start(buf_a[:],
                                scr_xn.rearrange("(t p) d -> p t d", p=128)).then_inc(P.s["sd"].h, 16)
                    P.s["sd"].v += 16
                    e.wait_ge(P.s["sd"].h, P.s["sd"].v)
            _emit_rms(P, buf_a, xn_fm, "ln1b")
            barrier(P)
            emit_l1_ffn_attn(P, xn_fm, buf_a)
            barrier(P)
            _emit_rms(P, buf_a, xn_fm, "ln2b")
            barrier(P)
            emit_l1_moe(P, xn_fm, buf_a)
            barrier(P)
            emit_combine(P, x1scr[seq], scr_s, buf_a, out[seq])
            barrier(P)
        P.check()
    return nc



_HOST_C = None
_PROGS = {}

def _consts():
    global _HOST_C
    if _HOST_C is None:
        import ml_dtypes
        _HOST_C = host_constants()
        for n, dt in CONST_DTYPES.items():
            if dt == BF16:
                _HOST_C[n] = _HOST_C[n].astype(ml_dtypes.bfloat16)
    return _HOST_C


def kernel(**inputs):
    consts = _consts()
    if "full" not in _PROGS:
        _PROGS["full"] = build_full()
    x = np.ascontiguousarray(np.asarray(inputs["L0_attn_x"], dtype=np.float32))
    maps = []
    for i in range(8):
        m = {"x0": x[2 * i:2 * i + 2]}
        for n in inputs:
            if n.startswith("L0_") or n.startswith("L1_"):
                m[n] = inputs[n]
        for n in BASE_CONSTS + ["c_ropeC", "c_ropeS"]:
            m[n] = consts[n]
        maps.append(m)
    res = run_bass_kernel_spmd(_PROGS["full"], maps, core_ids=list(range(8)))
    out = np.concatenate([np.asarray(res.results[i]["out"], dtype=np.float32)
                          for i in range(8)], axis=0)
    return out


# version 13
# speedup vs baseline: 1.2867x; 1.0550x over previous
import numpy as np
from contextlib import ExitStack

import concourse.bass as bass
import concourse.mybir as mybir
from concourse.bass_utils import run_bass_kernel_spmd

F32 = mybir.dt.float32
BF16 = mybir.dt.bfloat16
AF = mybir.ActivationFunctionType
ALU = mybir.AluOpType
AX = mybir.AxisListType.X

T = 2048
D = 1024
NTILE = T // 128
NCH = D // 128
RMS_EPS0 = 1.1920929e-07
DEBUG = None
DBGSEQ = [0]

ENG_METHOD = {"pe": "tensor", "act": "scalar", "dve": "vector",
              "pool": "gpsimd", "sync": "sync"}
SEM_TOTALS = []


def host_constants():
    c = {}
    c["c_id32"] = np.eye(128, dtype=np.float32)
    c["c_idbf"] = np.eye(128, dtype=np.float32)
    c["c_ones_r"] = np.ones((1, 128), dtype=np.float32)
    c["c_ones_c"] = np.ones((128, 1), dtype=np.float32)
    k = np.arange(128)[:, None]
    q = np.arange(128)[None, :]
    c["c_striu"] = (k < q).astype(np.float32)
    c["c_triu"] = (k <= q).astype(np.float32)
    c["c_dmall"] = (k <= (np.arange(896)[None, :] - 384)).astype(np.float32)
    d_rope = 32
    t_idx = np.arange(T, dtype=np.float32)
    freqs = (1.0 / (100000.0 ** (np.arange(0, d_rope, 2, dtype=np.float32) / d_rope))).astype(np.float32)
    ang = (t_idx[:, None] * freqs[None, :]).astype(np.float32)
    cos = np.cos(ang).astype(np.float32)
    sin = np.sin(ang).astype(np.float32)
    C = np.ones((128, T), dtype=np.float32)
    S = np.zeros((128, T), dtype=np.float32)
    for bb in (32, 96):
        for dd in range(d_rope):
            C[bb + dd, :] = cos[:, dd // 2]
            S[bb + dd, :] = sin[:, dd // 2]
    c["c_ropeC"] = C
    c["c_ropeS"] = S
    Sp = np.zeros((128, 128), dtype=np.float32)
    for bb in (32, 96):
        for p in range(d_rope // 2):
            r0 = bb + 2 * p
            Sp[r0 + 1, r0] = -1.0
            Sp[r0, r0 + 1] = 1.0
    c["c_sperm"] = Sp
    Sp32 = np.zeros((32, 32), dtype=np.float32)
    for p in range(d_rope // 2):
        Sp32[2 * p + 1, 2 * p] = -1.0
        Sp32[2 * p, 2 * p + 1] = 1.0
    c["c_sperm32"] = Sp32
    hi = np.zeros((6, 768), dtype=np.float32)
    for h in range(6):
        hi[h, h * 128:(h + 1) * 128] = 1.0
    c["c_hind"] = hi
    c["c_ones6"] = np.ones((6, 128), dtype=np.float32)
    sel2 = np.zeros((128, 2), dtype=np.float32)
    sel2[0:64, 0] = 1.0
    sel2[64:128, 1] = 1.0
    c["c_sel2"] = sel2
    c["c_exp2"] = sel2.T.copy()
    c["c_qsc"] = np.full((1, 128), 128.0 ** -0.5, dtype=np.float32)
    return c

CONST_DTYPES = {"c_idbf": BF16, "c_sperm": BF16, "c_sperm32": BF16,
                "c_ropeC": BF16, "c_ropeS": BF16, "c_dmall": BF16}
CONST_SHAPES = {"c_id32": [128, 128], "c_idbf": [128, 128],
                "c_ones_r": [1, 128], "c_ones_c": [128, 1],
                "c_striu": [128, 128], "c_triu": [128, 128],
                "c_dmall": [128, 896],
                "c_ropeC": [128, T], "c_ropeS": [128, T],
                "c_sperm": [128, 128], "c_sperm32": [32, 32],
                "c_hind": [6, 768], "c_ones6": [6, 128],
                "c_sel2": [128, 2], "c_exp2": [2, 128], "c_qsc": [1, 128]}
BASE_CONSTS = ["c_id32", "c_idbf", "c_ones_r", "c_ones_c", "c_striu",
               "c_triu", "c_dmall", "c_sperm",
               "c_sperm32", "c_hind", "c_ones6", "c_sel2", "c_exp2", "c_qsc"]


def dm_mask(P, j):
    return P.c["c_dmall"][:, 384 - 128 * j:896 - 128 * j]

WSHAPES = {
    "x0": ([2, T, D], F32),
    "L0_attn_W_DQ": ([512, 64], F32), "L0_attn_W_UQ": ([64, 512], F32),
    "L0_attn_W_DKV": ([1024, 64], F32), "L0_attn_W_UKV": ([128, 1024], F32),
    "L0_attn_W_QR": ([64, 64], F32), "L0_attn_W_KR": ([64, 64], F32),
    "L0_attn_W_O": ([64, 64], F32),
    "L0_attn_q_norm_w": ([64], F32), "L0_attn_k_norm_w": ([64], F32),
    "L0_ffn_gate_w": ([16, 1024], F32), "L0_ffn_expert_bias": ([16], F32),
    "L0_w0T": ([51, 1024, 1024], F32),
    "L1_attn_Wq": ([1024, 768], F32), "L1_attn_Wk": ([1024, 768], F32),
    "L1_attn_Wv": ([1024, 1536], F32), "L1_attn_Wb": ([1024, 6], F32),
    "L1_attn_Wa": ([1024, 6], F32), "L1_attn_A_log": ([6], F32),
    "L1_attn_dt_bias": ([6], F32),
    "L1_attn_conv_q": ([768, 4], F32), "L1_attn_conv_k": ([768, 4], F32),
    "L1_attn_conv_v": ([1536, 4], F32),
    "L1_attn_Wg": ([1024, 1536], F32), "L1_attn_norm_w": ([256], F32),
    "L1_attn_Wo": ([1536, 1024], F32),
    "L1_ffn_w_ln1": ([1024], F32), "L1_ffn_Wq": ([1024, 1024], F32),
    "L1_ffn_Wkva": ([1024, 160], F32), "L1_ffn_Wkvb": ([128, 512], F32),
    "L1_ffn_Wo": ([1024, 1024], F32), "L1_ffn_w_ln2": ([1024], F32),
    "L1_ffn_Wgate": ([1024, 8], F32), "L1_ffn_expert_bias": ([8], F32),
    "L1_ffn_Wg": ([8, 1024, 512], F32), "L1_ffn_Wu": ([8, 1024, 512], F32),
    "L1_ffn_Wd": ([8, 512, 1024], F32),
    "L1_ffn_sWg": ([1, 1024, 512], F32), "L1_ffn_sWu": ([1, 1024, 512], F32),
    "L1_ffn_sWd": ([1, 512, 1024], F32),
}
for _n, _s in CONST_SHAPES.items():
    WSHAPES[_n] = (_s, CONST_DTYPES.get(_n, F32))



class Sem:
    def __init__(self, ctx, nc, name):
        self.h = ctx.enter_context(nc.semaphore(name))
        self.v = 0


class Prog:
    def __init__(self, nc, ctx):
        self.nc = nc
        self.ctx = ctx
        self.w = {}
        self.c = {}
        self.sb = {}
        self.uid = 0
        self.s = {n: Sem(ctx, nc, "S_" + n) for n in
                  ["sd", "sg", "sq", "pe", "act", "dve", "pool",
                   "p0", "p1", "p2", "p3"]}

    def name(self, base):
        self.uid += 1
        return f"{base}_{self.uid}"

    def sbuf(self, ctx, base, shape, dtype):
        return ctx.enter_context(self.nc.sbuf_tensor(self.name(base), shape, dtype))

    def psum(self, ctx, base, shape, dtype=F32):
        return ctx.enter_context(self.nc.psum_tensor(self.name(base), shape, dtype))

    def check(self):
        SEM_TOTALS.append({n: sm.v for n, sm in self.s.items()})
        for n, sm in self.s.items():
            assert sm.v < 60000, f"sem {n} total {sm.v}"


def chain(P, steps):
    nc = P.nc
    per_eng = {}
    prev = getattr(P, "last", None)
    prev_ename = None
    for step in steps:
        if len(step) == 2:
            eng, fn = step
            ndma = 1
        else:
            eng, fn, ndma = step
        if eng == "sync_dma":
            ename, sem, inc = "sync", P.s["sd"], 16 * ndma
        elif eng == "pool_dma":
            ename, sem, inc = "pool", P.s["sg"], 16 * ndma
        elif eng == "act_dma":
            ename, sem, inc = "act", P.s["sq"], 16 * ndma
        elif eng == "pool":
            ename, sem, inc = "pool", P.s["pool"], 1
        else:
            ename, sem, inc = eng, P.s[eng], 1
        waits = []
        dma_like = inc >= 16
        if prev is not None:
            waits.append(prev)
        sem.v += inc
        per_eng.setdefault(ename, []).append((waits, fn, sem, sem.v, inc, ndma))
        prev = (sem, sem.v)
        prev_ename = "_dma_" + ename if dma_like else ename
    with nc.Block() as blk:
        for ename, items in per_eng.items():
            def body(e, items=items):
                for waits, fn, sem, val, inc, ndma in items:
                    for (wsem, wval) in waits:
                        e.wait_ge(wsem.h, wval)
                    P.cur_dma = (sem.h, 16)
                    inst = fn(e)
                    inst.then_inc(sem.h, 16 if inc >= 16 else inc)
            getattr(blk, ENG_METHOD[ename])(body)
    P.last = prev


def chain_multi(P, streams):
    nc = P.nc
    entry = getattr(P, "last", None)
    per_eng = {}
    prevs = [entry for _ in streams]
    idx = [0] * len(streams)
    remaining = sum(len(s) for s in streams)
    while remaining:
        for i, s in enumerate(streams):
            if idx[i] >= len(s):
                continue
            step = s[idx[i]]
            idx[i] += 1
            remaining -= 1
            if len(step) == 2:
                eng, fn = step
                ndma = 1
            else:
                eng, fn, ndma = step
            if eng == "sync_dma":
                ename, sem, inc = "sync", P.s["sd"], 16 * ndma
            elif eng == "pool_dma":
                ename, sem, inc = "pool", P.s["sg"], 16 * ndma
            elif eng == "act_dma":
                ename, sem, inc = "act", P.s["sq"], 16 * ndma
            elif eng == "pool":
                ename, sem, inc = "pool", P.s["pool"], 1
            else:
                ename, sem, inc = eng, P.s[eng], 1
            waits = [prevs[i]] if prevs[i] is not None else []
            sem.v += inc
            per_eng.setdefault(ename, []).append((waits, fn, sem, sem.v, inc))
            prevs[i] = (sem, sem.v)
    with nc.Block() as blk:
        for ename, items in per_eng.items():
            def body(e, items=items):
                for waits, fn, sem, val, inc in items:
                    for (wsem, wval) in waits:
                        e.wait_ge(wsem.h, wval)
                    inst = fn(e)
                    inst.then_inc(sem.h, 16 if inc >= 16 else inc)
            getattr(blk, ENG_METHOD[ename])(body)
    jsem = P.s["pool"]
    jsem.v += 1
    jv = jsem.v
    tails = [p for p in prevs if p is not None]
    with nc.Block() as blk:
        @blk.gpsimd
        def _(e):
            for (wsem, wval) in tails:
                e.wait_ge(wsem.h, wval)
            e.memset(P.sb["joinT"][:], 0.0).then_inc(jsem.h, 1)
    P.last = (jsem, jv)


def barrier(P):
    nc = P.nc
    last = getattr(P, "last", None)
    with nc.Block() as blk:
        @blk.sync
        def _(e):
            if last is not None:
                e.wait_ge(last[0].h, last[1])
    P.last = None


def full_barrier(P):
    nc = P.nc
    sems = [(s.h, s.v) for s in P.s.values() if s.v > 0]
    with nc.Block() as blk:
        for ename in ("tensor", "scalar", "vector", "gpsimd", "sync"):
            def body(e, sems=sems):
                for h, v in sems:
                    e.wait_ge(h, v)
            getattr(blk, ename)(body)
    P.last = None


def fresh_sems(P):
    P.s = {n: Sem(P.ctx, P.nc, P.name("S2_" + n)) for n in
           ["sd", "sg", "sq", "pe", "act", "dve", "pool",
            "p0", "p1", "p2", "p3"]}



def load_consts(P, names):
    nc = P.nc
    for n in names:
        P.c[n] = P.sbuf(P.ctx, n, CONST_SHAPES[n], CONST_DTYPES.get(n, F32))
    with nc.Block() as blk:
        @blk.sync
        def _(e):
            for n in names:
                if CONST_DTYPES.get(n, F32) == F32:
                    e.dma_start(P.c[n][:], P.w[n]).then_inc(P.s["sd"].h, 16)
                    P.s["sd"].v += 16
        @blk.gpsimd
        def _(e):
            for n in names:
                if CONST_DTYPES.get(n, F32) == BF16:
                    e.dma_start(P.c[n][:], P.w[n]).then_inc(P.s["sg"].h, 16)
                    P.s["sg"].v += 16
    with nc.Block() as blk:
        @blk.vector
        def _(e):
            e.wait_ge(P.s["sd"].h, P.s["sd"].v)
            e.wait_ge(P.s["sg"].h, P.s["sg"].v)


def stage_l0_small(P):
    nc = P.nc
    for nm, shape, dt in [("weff_q", [128, 64], BF16), ("weff_kv", [128, 128], BF16),
                          ("wqrT", [128, 64], BF16), ("wkrT", [128, 64], BF16),
                          ("woT", [64, 64], BF16), ("qnw", [64, 1], F32),
                          ("knw", [64, 1], F32), ("qnw2", [128, 1], F32),
                          ("knw2", [128, 1], F32), ("weff_qbd", [128, 128], BF16),
                          ("weff_kbd", [128, 128], BF16)]:
        P.sb[nm] = P.sbuf(P.ctx, nm, shape, dt)
    with ExitStack() as ctx:
        wdq = P.sbuf(ctx, "wdq", [128, 4, 64], F32)
        wuq = P.sbuf(ctx, "wuq", [64, 512], F32)
        wuqT = P.sbuf(ctx, "wuqT", [128, 4, 64], F32)
        wdkv = P.sbuf(ctx, "wdkv", [128, 8, 64], F32)
        wukv = P.sbuf(ctx, "wukv", [128, 1024], F32)
        wukvT = P.sbuf(ctx, "wukvT", [128, 8, 128], F32)
        sm64 = P.sbuf(ctx, "sm64", [64, 64], F32)
        pt = P.psum(ctx, "pt", [128, 128], F32)
        pm = P.psum(ctx, "pm", [64, 128], F32)

        steps = []

        def load6(e):
            sh = P.s["sd"].h
            e.dma_start(wdq[:], P.w["L0_attn_W_DQ"].rearrange("(c p) d -> p c d", p=128)).then_inc(sh, 16)
            e.dma_start(wuq[:], P.w["L0_attn_W_UQ"]).then_inc(sh, 16)
            e.dma_start(wdkv[:], P.w["L0_attn_W_DKV"].rearrange("(c p) d -> p c d", p=128)).then_inc(sh, 16)
            e.dma_start(wukv[:], P.w["L0_attn_W_UKV"]).then_inc(sh, 16)
            e.dma_start(P.sb["qnw"][:], P.w["L0_attn_q_norm_w"].rearrange("(p o) -> p o", o=1)).then_inc(sh, 16)
            e.dma_start(P.sb["knw"][:], P.w["L0_attn_k_norm_w"].rearrange("(p o) -> p o", o=1)).then_inc(sh, 16)
            e.dma_start(P.sb["qnw2"][0:64, :], P.w["L0_attn_q_norm_w"].rearrange("(p o) -> p o", o=1)).then_inc(sh, 16)
            e.dma_start(P.sb["qnw2"][64:128, :], P.w["L0_attn_q_norm_w"].rearrange("(p o) -> p o", o=1)).then_inc(sh, 16)
            e.dma_start(P.sb["knw2"][0:64, :], P.w["L0_attn_k_norm_w"].rearrange("(p o) -> p o", o=1)).then_inc(sh, 16)
            return e.dma_start(P.sb["knw2"][64:128, :], P.w["L0_attn_k_norm_w"].rearrange("(p o) -> p o", o=1))
        steps.append(("sync_dma", load6, 10))

        for j in range(4):
            steps.append(("pe", lambda e, j=j: e.transpose(
                pt[0:128, 0:64], wuq[0:64, j * 128:(j + 1) * 128], P.c["c_id32"][0:64, 0:64])))
            steps.append(("act", lambda e, j=j: e.copy(wuqT[:, j, :], pt[0:128, 0:64])))
        for j in range(8):
            steps.append(("pe", lambda e, j=j: e.transpose(
                pt[:], wukv[:, j * 128:(j + 1) * 128], P.c["c_id32"][:])))
            steps.append(("act", lambda e, j=j: e.copy(wukvT[:, j, :], pt[:])))

        def mm_weffq(e):
            for j in range(4):
                i = e.matmul(pm[0:64, 0:64], wdq[:, j, :], wuqT[:, j, :],
                             start=(j == 0), stop=(j == 3))
            return i
        steps.append(("pe", mm_weffq))
        steps.append(("act", lambda e: (e.copy(P.sb["weff_q"][0:64, :], pm[0:64, 0:64]),
                                        e.copy(P.sb["weff_q"][64:128, :], pm[0:64, 0:64]))[-1]))

        def mm_weffkv(e):
            for j in range(8):
                i = e.matmul(pm[0:64, :], wdkv[:, j, :], wukvT[:, j, :],
                             start=(j == 0), stop=(j == 7))
            return i
        steps.append(("pe", mm_weffkv))
        steps.append(("act", lambda e: (e.copy(P.sb["weff_kv"][0:64, :], pm[0:64, :]),
                                        e.copy(P.sb["weff_kv"][64:128, :], pm[0:64, :]))[-1]))

        for wname, dst, dual in [("L0_attn_W_QR", "wqrT", True),
                                 ("L0_attn_W_KR", "wkrT", True),
                                 ("L0_attn_W_O", "woT", False)]:
            steps.append(("sync_dma", lambda e, wname=wname: e.dma_start(sm64[:], P.w[wname])))
            steps.append(("pe", lambda e: e.transpose(pt[0:64, 0:64], sm64[:], P.c["c_id32"][0:64, 0:64])))
            if dual:
                steps.append(("act", lambda e, dst=dst: (
                    e.copy(P.sb[dst][0:64, :], pt[0:64, 0:64]),
                    e.copy(P.sb[dst][64:128, :], pt[0:64, 0:64]))[-1]))
            else:
                steps.append(("act", lambda e, dst=dst: e.copy(P.sb[dst][:], pt[0:64, 0:64])))
        steps.append(("pool", lambda e: (
            e.memset(P.sb["weff_qbd"][:], 0.0),
            e.memset(P.sb["weff_kbd"][:], 0.0))[-1]))
        steps.append(("act", lambda e: (
            e.copy(P.sb["weff_qbd"][0:64, 0:64], P.sb["weff_q"][0:64, :]),
            e.copy(P.sb["weff_qbd"][64:128, 64:128], P.sb["weff_q"][64:128, :]),
            e.copy(P.sb["weff_kbd"][0:64, 0:64], P.sb["weff_kv"][0:64, 0:64]),
            e.copy(P.sb["weff_kbd"][64:128, 64:128], P.sb["weff_kv"][64:128, 0:64]))[-1]))
        chain(P, steps)


def stage_l0_cast(P):
    nc = P.nc
    P.w["scr_w0"] = nc.dram_tensor("scr_w0", [51, 1024, 1024], BF16, kind="Internal").ap()
    P.s["ws"] = Sem(P.ctx, nc, "ws")
    ws = P.s["ws"]
    order = [48, 49, 50]
    for e in range(16):
        order += [e, 16 + e, 32 + e]
    with nc.Block() as blk:
        @blk.gpsimd
        def _(e):
            for m in order:
                e.dma_start(P.w["scr_w0"][m], P.w["L0_w0T"][m]).then_inc(ws.h, 16)
    ws.v = 16 * len(order)


def stage_l0_big(P):
    nc = P.nc

    P.sb["gwT"] = P.sbuf(P.ctx, "gwT", [128, 8, 16], BF16)
    P.sb["l0bias"] = P.sbuf(P.ctx, "l0bias", [128, 16], F32)
    with ExitStack() as ctx:
        gw = P.sbuf(ctx, "gw", [16, 1024], F32)
        brow = P.sbuf(ctx, "brow", [1, 16], F32)
        pt = P.psum(ctx, "ptg", [128, 16], F32)
        def load2(e):
            e.dma_start(gw[:], P.w["L0_ffn_gate_w"]).then_inc(P.s["sd"].h, 16)
            return e.dma_start(brow[:], P.w["L0_ffn_expert_bias"].rearrange("(o e) -> o e", o=1))
        steps = [("sync_dma", load2, 2)]
        for j in range(8):
            steps.append(("pe", lambda e, j=j: e.transpose(
                pt[:], gw[:, j * 128:(j + 1) * 128], P.c["c_id32"][0:16, 0:16])))
            steps.append(("act", lambda e, j=j: e.copy(P.sb["gwT"][:, j, :], pt[:, 0:16])))
        steps.append(("pe", lambda e: e.matmul(pt[:], P.c["c_ones_r"][:], brow[:], start=True, stop=True)))
        steps.append(("act", lambda e: e.copy(P.sb["l0bias"][:], pt[:, 0:16])))
        chain(P, steps)



def emit_ln(P, x_ap, xn_fm, xn32_scr=None):
    nc = P.nc
    with ExitStack() as ctx:
        xt = [P.sbuf(ctx, "lnx", [128, D], F32) for _ in range(2)]
        xnb = P.sbuf(ctx, "lnbf", [128, D], BF16)
        xn32 = P.sbuf(ctx, "ln32", [128, D], F32)
        xn32b = P.sbuf(ctx, "ln32b", [128, D], F32)
        sq = P.sbuf(ctx, "lnsq", [128, D], F32)
        st = P.sbuf(ctx, "lnst", [128, 8], F32)
        ptr = [P.psum(ctx, "lnp", [128, 128], BF16) for _ in range(2)]
        for i in range(NTILE):
            b = i % 2
            steps = []
            steps.append(("sync_dma", lambda e, i=i, b=b: e.dma_start(
                xt[b][:], x_ap[i * 128:(i + 1) * 128, :])))
            steps.append(("act", lambda e, b=b: e.activation(
                sq[:], xt[b][:], AF.Square, accum_out=st[:, 1:2])))
            steps.append(("dve", lambda e, b=b: e.reduce_sum(st[:, 0:1], xt[b][:], axis=AX)))
            steps.append(("dve", lambda e: e.tensor_scalar_mul(st[:, 2:3], st[:, 0:1], 1.0 / D)))
            steps.append(("dve", lambda e: e.tensor_mul(st[:, 4:5], st[:, 2:3], st[:, 2:3])))
            steps.append(("dve", lambda e: e.scalar_tensor_tensor(
                st[:, 3:4], st[:, 1:2], 1.0 / D, st[:, 4:5], ALU.mult, ALU.subtract)))
            steps.append(("dve", lambda e: e.tensor_scalar_add(st[:, 3:4], st[:, 3:4], 1e-5)))
            steps.append(("act", lambda e: e.activation(st[:, 5:6], st[:, 3:4], AF.Ln)))
            steps.append(("act", lambda e: e.activation(st[:, 5:6], st[:, 5:6], AF.Exp, scale=-0.5)))
            steps.append(("dve", lambda e, b=b: e.tensor_scalar_sub(xn32[:], xt[b][:], st[:, 2:3])))
            def norm(e, scr=(xn32_scr is not None)):
                i1 = e.tensor_scalar_mul(xnb[:], xn32[:], st[:, 5:6])
                if scr:
                    i1 = e.tensor_scalar_mul(xn32b[:], xn32[:], st[:, 5:6])
                return i1
            steps.append(("dve", norm))
            if xn32_scr is not None:
                steps.append(("act_dma", lambda e, i=i: e.dma_start(
                    xn32_scr[i * 128:(i + 1) * 128, :], xn32b[:])))
            for j in range(NCH):
                steps.append(("pe", lambda e, j=j: e.transpose(
                    ptr[j % 2][:], xnb[:, j * 128:(j + 1) * 128], P.c["c_idbf"][:])))
                steps.append(("dve", lambda e, i=i, j=j: e.tensor_copy(
                    xn_fm[:, j, i * 128:(i + 1) * 128], ptr[j % 2][:])))
            chain(P, steps)



def emit_l0_attn(P, xn_fm, s_buf):
    nc = P.nc
    with ExitStack() as ctx:
        q_t = P.sbuf(ctx, "qt", [128, T], BF16)
        k_t = P.sbuf(ctx, "kt", [128, T], BF16)
        v_tm = P.sbuf(ctx, "vtm", [128, NTILE, 65], BF16)
        vbf = P.sbuf(ctx, "vbf", [64, 512], BF16)
        qc_s = P.sbuf(ctx, "qcs", [64, 512], F32)
        sqs = P.sbuf(ctx, "sqs", [128, 512], F32)
        sqs2 = P.sbuf(ctx, "sqs2", [64, 512], F32)
        rrow = P.sbuf(ctx, "rrow", [1, 512], F32)
        srow = P.sbuf(ctx, "srow", [1, 16], F32)
        bnd = P.sbuf(ctx, "bnd", [128, 1], F32)
        ctxs = P.sbuf(ctx, "ctxs", [64, 512], BF16)
        invr = P.sbuf(ctx, "invr", [1, 512], F32)
        invc = P.sbuf(ctx, "invc", [128, 4], F32)
        p_t = [P.sbuf(ctx, "ptile", [128, 512], BF16) for _ in range(2)]

        pA = P.psum(ctx, "pA", [128, 512], F32)
        pB = P.psum(ctx, "pB", [128, 512], F32)
        pC = P.psum(ctx, "pC", [128, 512], F32)
        pD = P.psum(ctx, "pD", [128, 512], F32)
        pEbf = P.psum(ctx, "pEbf", [128, 128], BF16)
        pF = P.psum(ctx, "pF", [128, 72], F32)
        pkv = pA
        pq = pB[0:64, :]
        pbc = pC[0:64, :]
        prr = pC[0:1, :]
        pst = [pA, pB]
        pctx = pD[0:65, :]
        ptv = pEbf
        psh = pF[:, 0:64]
        pcl = pF[:, 64:72]

        pe, act, dve = P.s["pe"], P.s["act"], P.s["dve"]

        with nc.Block() as blk:
            @blk.gpsimd
            def _(e):
                e.memset(v_tm[:], 0.0)
                e.memset(v_tm[:, :, 64:65], 1.0)
        P.last = None

        for h in range(16):
            base = (h % 2) * 64
            chk = h // 2
            weff_q = P.sb["weff_q"][base:base + 64, :]
            weff_kv = P.sb["weff_kv"][base:base + 64, :]
            wkrT = P.sb["wkrT"][base:base + 64, :]

            for n in range(4):
                sl = slice(n * 512, (n + 1) * 512)
                xh_n = xn_fm[base:base + 64, chk, sl]
                steps = []
                steps.append(("pe", lambda e, xh_n=xh_n, weff_q=weff_q, weff_kv=weff_kv: (
                    e.matmul(pq[:], weff_q, xh_n, start=True, stop=True),
                    e.matmul(pkv[:], weff_kv, xh_n, start=True, stop=True))[-1]))
                steps.append(("act", lambda e: (
                    e.activation(sqs[0:64, :], pq[:], AF.Square),
                    e.activation(sqs2[:], pkv[0:64, :], AF.Square),
                    e.copy(vbf[:], pkv[64:128, :]))[-1]))
                steps.append(("pe", lambda e: e.matmul(prr[:], P.c["c_ones_c"][0:64, :], sqs[0:64, :], start=True, stop=True)))
                steps.append(("dve", lambda e: e.tensor_scalar_mul(rrow[:], prr[:], 1.0 / 64.0)))
                steps.append(("dve", lambda e: e.tensor_scalar_add(rrow[:], rrow[:], RMS_EPS0)))
                steps.append(("act", lambda e: e.activation(rrow[:], rrow[:], AF.Ln)))
                steps.append(("act", lambda e: e.activation(rrow[:], rrow[:], AF.Exp, scale=-0.5)))
                steps.append(("pe", lambda e: e.matmul(pbc[:], P.c["c_ones_r"][0:1, 0:64], rrow[:], start=True, stop=True)))
                steps.append(("dve", lambda e: e.tensor_scalar_mul(qc_s[:], pq[:], P.sb["qnw"][:, 0:1])))
                steps.append(("dve", lambda e, sl=sl: e.tensor_mul(q_t[0:64, sl], qc_s[:], pbc[:])))
                steps.append(("pe", lambda e: e.matmul(prr[:], P.c["c_ones_c"][0:64, :], sqs2[:], start=True, stop=True)))
                steps.append(("dve", lambda e: e.tensor_scalar_mul(rrow[:], prr[:], 1.0 / 64.0)))
                steps.append(("dve", lambda e: e.tensor_scalar_add(rrow[:], rrow[:], RMS_EPS0)))
                steps.append(("act", lambda e: e.activation(rrow[:], rrow[:], AF.Ln)))
                steps.append(("act", lambda e: e.activation(rrow[:], rrow[:], AF.Exp, scale=-0.5)))
                steps.append(("pe", lambda e: e.matmul(pbc[:], P.c["c_ones_r"][0:1, 0:64], rrow[:], start=True, stop=True)))
                steps.append(("dve", lambda e: e.tensor_scalar_mul(qc_s[:], pkv[0:64, :], P.sb["knw"][:, 0:1])))
                steps.append(("dve", lambda e, sl=sl: e.tensor_mul(k_t[0:64, sl], qc_s[:], pbc[:])))
                for jj in range(4):
                    steps.append(("pe", lambda e, jj=jj: e.transpose(
                        ptv[:, 0:64], vbf[:, jj * 128:(jj + 1) * 128], P.c["c_idbf"][0:64, 0:64])))
                    steps.append(("act", lambda e, n=n, jj=jj: e.copy(
                        v_tm[:, n * 4 + jj, 0:64], ptv[:, 0:64])))
                steps.append(("pe", lambda e, sl=sl, xh_n=xh_n, wkrT=wkrT: (
                    e.matmul(pq[:], P.sb["wqrT"][:], q_t[0:64, sl], start=True, stop=True),
                    e.matmul(pbc[:], wkrT, xh_n, start=True, stop=True))[-1]))
                steps.append(("act", lambda e, sl=sl: (
                    e.copy(q_t[64:128, sl], pq[:]),
                    e.copy(k_t[64:128, sl], pbc[:]))[-1]))
                chain(P, steps)

            steps = []
            for (tile, off) in ((q_t, 0), (k_t, 4)):
                for n in range(4):
                    steps.append(("act", lambda e, tile=tile, n=n: e.activation(
                        sqs[:], tile[:, n * 512:(n + 1) * 512], AF.Square)))
                    steps.append(("pe", lambda e: e.matmul(
                        prr[:], P.c["c_ones_c"][:], sqs[:], start=True, stop=True)))
                    steps.append(("dve", lambda e, off=off, n=n: e.reduce_max(
                        srow[:, off + n:off + n + 1], prr[:], axis=AX)))
            steps.append(("dve", lambda e: (
                e.reduce_max(srow[:, 8:9], srow[:, 0:4], axis=AX),
                e.reduce_max(srow[:, 9:10], srow[:, 4:8], axis=AX))[-1]))
            steps.append(("dve", lambda e: e.tensor_mul(srow[:, 8:9], srow[:, 8:9], srow[:, 9:10])))
            steps.append(("act", lambda e: e.activation(srow[:, 8:9], srow[:, 8:9], AF.Ln)))
            steps.append(("act", lambda e: e.activation(srow[:, 8:9], srow[:, 8:9], AF.Exp, scale=0.5)))
            steps.append(("act", lambda e: e.mul(srow[:, 8:9], srow[:, 8:9], -0.125)))
            steps.append(("pe", lambda e: e.matmul(pcl[:, 0:1], P.c["c_ones_r"][:], srow[:, 8:9], start=True, stop=True)))
            steps.append(("dve", lambda e: e.tensor_copy(bnd[:], pcl[:, 0:1])))
            if DEBUG is not None:
                steps.append(("sync_dma", lambda e, h=h: e.dma_start(
                    DEBUG["srow"][DBGSEQ[0], h], srow[:])))
                if h == 0:
                    steps.append(("sync_dma", lambda e: e.dma_start(
                        DEBUG["q"][DBGSEQ[0]], q_t[:])))
                    steps.append(("sync_dma", lambda e: e.dma_start(
                        DEBUG["k"][DBGSEQ[0]], k_t[:])))
            chain(P, steps)

            for qc in range(4):
                nkt = (qc + 1) * 4
                qsl = slice(qc * 512, (qc + 1) * 512)
                pe0 = pe.v
                act0 = act.v
                dve0 = dve.v
                ST_m = {}
                PV_m = {}
                m = pe0
                for kt in range(nkt):
                    m += 1
                    ST_m[kt] = m
                    if kt >= 1:
                        m += 1
                        PV_m[kt - 1] = m
                m += 1
                PV_m[nkt - 1] = m
                exp_m = {kt: act0 + kt + 1 for kt in range(nkt)}
                msk_m = {kt: dve0 + (kt - (nkt - 4)) + 1 for kt in range(nkt - 4, nkt)}

                def p_ready(kt):
                    if kt >= nkt - 4:
                        return (dve, msk_m[kt])
                    return (act, exp_m[kt])

                start_wait = P.last
                with nc.Block() as blk:
                    @blk.tensor
                    def _(e):
                        if start_wait is not None:
                            e.wait_ge(start_wait[0].h, start_wait[1])
                        for kt in range(nkt):
                            bb = kt % 2
                            if kt >= 2:
                                e.wait_ge(act.h, exp_m[kt - 2])
                            e.matmul(pst[bb][:], k_t[:, kt * 128:(kt + 1) * 128],
                                     q_t[:, qsl], start=True, stop=True).then_inc(pe.h, 1)
                            if kt >= 1:
                                ss, vv = p_ready(kt - 1)
                                e.wait_ge(ss.h, vv)
                                e.matmul(pctx[:], v_tm[:, kt - 1, :], p_t[(kt - 1) % 2][:],
                                         start=(kt - 1 == 0), stop=False).then_inc(pe.h, 1)
                        ss, vv = p_ready(nkt - 1)
                        e.wait_ge(ss.h, vv)
                        e.matmul(pctx[:], v_tm[:, nkt - 1, :], p_t[(nkt - 1) % 2][:],
                                 start=(nkt - 1 == 0), stop=True).then_inc(pe.h, 1)
                    @blk.scalar
                    def _(e):
                        for kt in range(nkt):
                            bb = kt % 2
                            e.wait_ge(pe.h, ST_m[kt])
                            if kt >= 2:
                                e.wait_ge(pe.h, PV_m[kt - 2])
                            e.activation(p_t[bb][:], pst[bb][:], AF.Exp, scale=0.125,
                                         bias=bnd[:, 0:1]).then_inc(act.h, 1)
                    @blk.vector
                    def _(e):
                        for kt in range(nkt - 4, nkt):
                            e.wait_ge(act.h, exp_m[kt])
                            j = kt - (nkt - 4)
                            e.tensor_mul(p_t[kt % 2][:], p_t[kt % 2][:],
                                         dm_mask(P, j)).then_inc(dve.h, 1)
                pe.v = m
                act.v = act0 + nkt
                dve.v = dve0 + 4
                P.last = (pe, m)

                steps = []
                steps.append(("dve", lambda e: e.reciprocal(invr[:], pctx[64:65, :])))
                for j in range(4):
                    steps.append(("pe", lambda e, j=j: e.transpose(
                        pcl[:, j:j + 1], invr[0:1, j * 128:(j + 1) * 128], P.c["c_id32"][0:1, 0:1])))
                    steps.append(("dve", lambda e, j=j: e.tensor_copy(invc[:, j:j + 1], pcl[:, j:j + 1])))
                steps.append(("act", lambda e: e.copy(ctxs[:], pctx[0:64, :])))
                for j in range(4):
                    steps.append(("pe", lambda e, j=j: e.matmul(
                        psh[:], ctxs[:, j * 128:(j + 1) * 128], P.sb["woT"][:],
                        start=True, stop=True)))
                    steps.append(("act", lambda e, qc=qc, j=j, h=h: e.activation(
                        s_buf[:, qc * 4 + j, h * 64:(h + 1) * 64], psh[:],
                        AF.Copy, scale=invc[:, j:j + 1])))
                chain(P, steps)


def _attn_pipe(P, kh, qh, vtm_h, bndcol, pst, pctx, p_t, qc):
    nc = P.nc
    pe, act, dve = P.s["pe"], P.s["act"], P.s["dve"]
    nkt = (qc + 1) * 4
    qsl = slice(qc * 512, (qc + 1) * 512)
    pe0, act0, dve0 = pe.v, act.v, dve.v
    ST_m = {}
    PV_m = {}
    m = pe0
    for kt in range(nkt):
        m += 1
        ST_m[kt] = m
        if kt >= 1:
            m += 1
            PV_m[kt - 1] = m
    m += 1
    PV_m[nkt - 1] = m
    exp_m = {kt: act0 + kt + 1 for kt in range(nkt)}
    msk_m = {kt: dve0 + (kt - (nkt - 4)) + 1 for kt in range(nkt - 4, nkt)}

    def p_ready(kt):
        if kt >= nkt - 4:
            return (dve, msk_m[kt])
        return (act, exp_m[kt])

    start_wait = P.last
    with nc.Block() as blk:
        @blk.tensor
        def _(e):
            if start_wait is not None:
                e.wait_ge(start_wait[0].h, start_wait[1])
            for kt in range(nkt):
                bb = kt % 2
                if kt >= 2:
                    e.wait_ge(act.h, exp_m[kt - 2])
                e.matmul(pst[bb][:], kh[:, kt * 128:(kt + 1) * 128],
                         qh[:, qsl], start=True, stop=True).then_inc(pe.h, 1)
                if kt >= 1:
                    ss, vv = p_ready(kt - 1)
                    e.wait_ge(ss.h, vv)
                    e.matmul(pctx[:], vtm_h[:, kt - 1, :], p_t[(kt - 1) % 2][:],
                             start=(kt - 1 == 0), stop=False).then_inc(pe.h, 1)
            ss, vv = p_ready(nkt - 1)
            e.wait_ge(ss.h, vv)
            e.matmul(pctx[:], vtm_h[:, nkt - 1, :], p_t[(nkt - 1) % 2][:],
                     start=(nkt - 1 == 0), stop=True).then_inc(pe.h, 1)
        @blk.scalar
        def _(e):
            for kt in range(nkt):
                bb = kt % 2
                e.wait_ge(pe.h, ST_m[kt])
                if kt >= 2:
                    e.wait_ge(pe.h, PV_m[kt - 2])
                e.activation(p_t[bb][:], pst[bb][:], AF.Exp, scale=0.125,
                             bias=bndcol).then_inc(act.h, 1)
        @blk.vector
        def _(e):
            for kt in range(nkt - 4, nkt):
                e.wait_ge(act.h, exp_m[kt])
                j = kt - (nkt - 4)
                e.tensor_mul(p_t[kt % 2][:], p_t[kt % 2][:],
                             dm_mask(P, j)).then_inc(dve.h, 1)
    pe.v = m
    act.v = act0 + nkt
    dve.v = dve0 + 4
    P.last = (pe, m)


def emit_l0_attn2(P, xn_fm, s_buf):
    nc = P.nc
    with ExitStack() as ctx:
        Qc2 = P.sbuf(ctx, "Qc2", [128, T], F32)
        Kc2 = P.sbuf(ctx, "Kc2", [128, T], F32)
        SQA = P.sbuf(ctx, "SQA", [128, T], F32)
        qhE = P.sbuf(ctx, "qhE", [128, T], BF16)
        qhO = P.sbuf(ctx, "qhO", [128, T], BF16)
        khE = P.sbuf(ctx, "khE", [128, T], BF16)
        khO = P.sbuf(ctx, "khO", [128, T], BF16)
        vtmE = P.sbuf(ctx, "vtmE", [128, NTILE, 65], BF16)
        vtmO = P.sbuf(ctx, "vtmO", [128, NTILE, 65], BF16)
        rr2q = P.sbuf(ctx, "rr2q", [2, T], F32)
        rr2k = P.sbuf(ctx, "rr2k", [2, T], F32)
        qmax = P.sbuf(ctx, "qmax", [1, 16], F32)
        mx = P.sbuf(ctx, "mx", [1, 8], F32)
        bnd2 = P.sbuf(ctx, "bnd2", [128, 2], F32)
        invrE = P.sbuf(ctx, "invrE", [1, 512], F32)
        invrO = P.sbuf(ctx, "invrO", [1, 512], F32)
        invc = P.sbuf(ctx, "invc", [128, 8], F32)
        ctxsE = P.sbuf(ctx, "ctxsE", [64, 512], BF16)
        ctxsO = P.sbuf(ctx, "ctxsO", [64, 512], BF16)
        p_t = [P.sbuf(ctx, "ptile", [128, 512], BF16) for _ in range(2)]
        pA = P.psum(ctx, "pA", [128, 512], F32)
        pB = P.psum(ctx, "pB", [128, 512], F32)
        pC = P.psum(ctx, "pC", [128, 512], F32)
        pD = P.psum(ctx, "pD", [128, 512], F32)
        pF = P.psum(ctx, "pF", [128, 72], F32)

        chain(P, [("dve", lambda e: (
            e.memset(vtmE[:], 0.0), e.memset(vtmE[:, :, 64:65], 1.0),
            e.memset(vtmO[:], 0.0), e.memset(vtmO[:, :, 64:65], 1.0))[-1])])

        for p in range(8):
            he, ho = 2 * p, 2 * p + 1
            steps = []
            for n in range(4):
                sl = slice(n * 512, (n + 1) * 512)
                pq = pA if n % 2 == 0 else pB
                pk = pC if n % 2 == 0 else pD
                steps.append(("pe", lambda e, sl=sl, pq=pq, pk=pk, p=p: (
                    e.matmul(pq[:], P.sb["weff_qbd"][:], xn_fm[:, p, sl], start=True, stop=True),
                    e.matmul(pk[:], P.sb["weff_kbd"][:], xn_fm[:, p, sl], start=True, stop=True))[-1]))
                steps.append(("act", lambda e, sl=sl, pq=pq, pk=pk: (
                    e.copy(Qc2[:, sl], pq[:]),
                    e.copy(Kc2[:, sl], pk[:]))[-1]))
            for half in range(2):
                t0 = half * 8
                def mmv(e, t0=t0, p=p):
                    for j in range(8):
                        tsl = slice((t0 + j) * 128, (t0 + j + 1) * 128)
                        e.matmul(pA[:, j * 64:(j + 1) * 64], xn_fm[0:64, p, tsl],
                                 P.sb["weff_kv"][0:64, 64:128], start=True, stop=True)
                        i1 = e.matmul(pB[:, j * 64:(j + 1) * 64], xn_fm[64:128, p, tsl],
                                      P.sb["weff_kv"][64:128, 64:128], start=True, stop=True)
                    return i1
                steps.append(("pe", mmv))
                steps.append(("act", lambda e, t0=t0: (
                    e.copy(vtmE[:, t0:t0 + 8, 0:64], pA[:].rearrange("p (a b) -> p a b", a=8)),
                    e.copy(vtmO[:, t0:t0 + 8, 0:64], pB[:].rearrange("p (a b) -> p a b", a=8)))[-1]))
            steps.append(("act", lambda e: e.activation(SQA[:], Qc2[:], AF.Square)))
            for n in range(4):
                sl = slice(n * 512, (n + 1) * 512)
                pX = pC if n % 2 == 0 else pD
                steps.append(("pe", lambda e, sl=sl, pX=pX: e.matmul(
                    pX[0:2, :], P.c["c_sel2"][:], SQA[:, sl], start=True, stop=True)))
                steps.append(("dve", lambda e, sl=sl, pX=pX: e.tensor_scalar_mul(
                    rr2q[:, sl], pX[0:2, :], 1.0 / 64.0)))
            steps.append(("act", lambda e: e.activation(SQA[:], Kc2[:], AF.Square)))
            for n in range(4):
                sl = slice(n * 512, (n + 1) * 512)
                pX = pC if n % 2 == 0 else pD
                steps.append(("pe", lambda e, sl=sl, pX=pX: e.matmul(
                    pX[0:2, :], P.c["c_sel2"][:], SQA[:, sl], start=True, stop=True)))
                steps.append(("dve", lambda e, sl=sl, pX=pX: e.tensor_scalar_mul(
                    rr2k[:, sl], pX[0:2, :], 1.0 / 64.0)))
            steps.append(("dve", lambda e: (
                e.tensor_scalar_add(rr2q[:], rr2q[:], RMS_EPS0),
                e.tensor_scalar_add(rr2k[:], rr2k[:], RMS_EPS0))[-1]))
            steps.append(("act", lambda e: (
                e.activation(rr2q[:], rr2q[:], AF.Ln),
                e.activation(rr2k[:], rr2k[:], AF.Ln))[-1]))
            steps.append(("act", lambda e: (
                e.activation(rr2q[:], rr2q[:], AF.Exp, scale=-0.5),
                e.activation(rr2k[:], rr2k[:], AF.Exp, scale=-0.5))[-1]))
            steps.append(("dve", lambda e: (
                e.tensor_scalar_mul(Qc2[:], Qc2[:], P.sb["qnw2"][:, 0:1]),
                e.tensor_scalar_mul(Kc2[:], Kc2[:], P.sb["knw2"][:, 0:1]))[-1]))
            for n in range(4):
                sl = slice(n * 512, (n + 1) * 512)
                steps.append(("pe", lambda e, sl=sl: (
                    e.matmul(pA[:], P.c["c_exp2"][:], rr2q[:, sl], start=True, stop=True),
                    e.matmul(pB[:], P.c["c_exp2"][:], rr2k[:, sl], start=True, stop=True))[-1]))
                steps.append(("dve", lambda e, sl=sl: (
                    e.tensor_mul(qhE[0:64, sl], Qc2[0:64, sl], pA[0:64, :]),
                    e.tensor_mul(qhO[64:128, sl], Qc2[64:128, sl], pA[64:128, :]),
                    e.tensor_mul(khE[0:64, sl], Kc2[0:64, sl], pB[0:64, :]),
                    e.tensor_mul(khO[64:128, sl], Kc2[64:128, sl], pB[64:128, :]))[-1]))
            for n in range(4):
                sl = slice(n * 512, (n + 1) * 512)
                steps.append(("pe", lambda e, sl=sl, p=p: (
                    e.matmul(pA[0:64, :], P.sb["wqrT"][0:64, :], qhE[0:64, sl], start=True, stop=True),
                    e.matmul(pB[0:64, :], P.sb["wqrT"][64:128, :], qhO[64:128, sl], start=True, stop=True),
                    e.matmul(pC[0:64, :], P.sb["wkrT"][0:64, :], xn_fm[0:64, p, sl], start=True, stop=True),
                    e.matmul(pD[0:64, :], P.sb["wkrT"][64:128, :], xn_fm[64:128, p, sl], start=True, stop=True))[-1]))
                steps.append(("act", lambda e, sl=sl: (
                    e.copy(qhE[64:128, sl], pA[0:64, :]),
                    e.copy(qhO[0:64, sl], pB[0:64, :]),
                    e.copy(khE[64:128, sl], pC[0:64, :]),
                    e.copy(khO[0:64, sl], pD[0:64, :]))[-1]))
            steps.append(("act", lambda e: (
                e.activation(SQA[:], qhE[:], AF.Square),
                e.activation(Qc2[:], qhO[:], AF.Square))[-1]))
            for n in range(4):
                sl = slice(n * 512, (n + 1) * 512)
                pX, pY = (pA, pB) if n % 2 == 0 else (pC, pD)
                steps.append(("pe", lambda e, sl=sl, pX=pX, pY=pY: (
                    e.matmul(pX[0:1, :], P.c["c_ones_c"][:], SQA[:, sl], start=True, stop=True),
                    e.matmul(pY[0:1, :], P.c["c_ones_c"][:], Qc2[:, sl], start=True, stop=True))[-1]))
                steps.append(("dve", lambda e, n=n, pX=pX, pY=pY: (
                    e.reduce_max(qmax[:, n:n + 1], pX[0:1, :], axis=AX),
                    e.reduce_max(qmax[:, 4 + n:5 + n], pY[0:1, :], axis=AX))[-1]))
            steps.append(("act", lambda e: (
                e.activation(SQA[:], khE[:], AF.Square),
                e.activation(Qc2[:], khO[:], AF.Square))[-1]))
            for n in range(4):
                sl = slice(n * 512, (n + 1) * 512)
                pX, pY = (pA, pB) if n % 2 == 0 else (pC, pD)
                steps.append(("pe", lambda e, sl=sl, pX=pX, pY=pY: (
                    e.matmul(pX[0:1, :], P.c["c_ones_c"][:], SQA[:, sl], start=True, stop=True),
                    e.matmul(pY[0:1, :], P.c["c_ones_c"][:], Qc2[:, sl], start=True, stop=True))[-1]))
                steps.append(("dve", lambda e, n=n, pX=pX, pY=pY: (
                    e.reduce_max(qmax[:, 8 + n:9 + n], pX[0:1, :], axis=AX),
                    e.reduce_max(qmax[:, 12 + n:13 + n], pY[0:1, :], axis=AX))[-1]))
            steps.append(("dve", lambda e: (
                e.reduce_max(mx[:, 0:1], qmax[:, 0:4], axis=AX),
                e.reduce_max(mx[:, 1:2], qmax[:, 4:8], axis=AX),
                e.reduce_max(mx[:, 2:3], qmax[:, 8:12], axis=AX),
                e.reduce_max(mx[:, 3:4], qmax[:, 12:16], axis=AX))[-1]))
            steps.append(("dve", lambda e: (
                e.tensor_mul(mx[:, 4:5], mx[:, 0:1], mx[:, 2:3]),
                e.tensor_mul(mx[:, 5:6], mx[:, 1:2], mx[:, 3:4]))[-1]))
            steps.append(("act", lambda e: e.activation(mx[:, 4:6], mx[:, 4:6], AF.Ln)))
            steps.append(("act", lambda e: e.activation(mx[:, 4:6], mx[:, 4:6], AF.Exp, scale=0.5)))
            steps.append(("act", lambda e: e.mul(mx[:, 4:6], mx[:, 4:6], -0.125)))
            steps.append(("pe", lambda e: e.matmul(pF[:, 0:2], P.c["c_ones_r"][:], mx[:, 4:6], start=True, stop=True)))
            steps.append(("dve", lambda e: e.tensor_copy(bnd2[:], pF[:, 0:2])))
            chain(P, steps)

            for qc in range(4):
                _attn_pipe(P, khE, qhE, vtmE, bnd2[:, 0:1], [pA, pB], pC[0:65, :], p_t, qc)
                _attn_pipe(P, khO, qhO, vtmO, bnd2[:, 1:2], [pA, pB], pD[0:65, :], p_t, qc)
                steps = []
                steps.append(("dve", lambda e: (
                    e.reciprocal(invrE[:], pC[64:65, :]),
                    e.reciprocal(invrO[:], pD[64:65, :]))[-1]))
                def trs(e):
                    for j in range(4):
                        e.transpose(pF[:, j:j + 1], invrE[0:1, j * 128:(j + 1) * 128],
                                    P.c["c_id32"][0:1, 0:1])
                    for j in range(4):
                        i1 = e.transpose(pF[:, 4 + j:5 + j], invrO[0:1, j * 128:(j + 1) * 128],
                                         P.c["c_id32"][0:1, 0:1])
                    return i1
                steps.append(("pe", trs))
                steps.append(("dve", lambda e: e.tensor_copy(invc[:], pF[:, 0:8])))
                steps.append(("dve", lambda e: (
                    e.tensor_copy(ctxsE[:], pC[0:64, :]),
                    e.tensor_copy(ctxsO[:], pD[0:64, :]))[-1]))
                def outmm(e):
                    for j in range(4):
                        e.matmul(pA[:, j * 64:(j + 1) * 64], ctxsE[:, j * 128:(j + 1) * 128],
                                 P.sb["woT"][:], start=True, stop=True)
                    for j in range(4):
                        i1 = e.matmul(pB[:, j * 64:(j + 1) * 64], ctxsO[:, j * 128:(j + 1) * 128],
                                      P.sb["woT"][:], start=True, stop=True)
                    return i1
                steps.append(("pe", outmm))
                def sout(e, qc=qc, he=he, ho=ho):
                    for j in range(4):
                        e.tensor_scalar_mul(s_buf[:, qc * 4 + j, he * 64:(he + 1) * 64],
                                            pA[:, j * 64:(j + 1) * 64], invc[:, j:j + 1])
                    for j in range(4):
                        i1 = e.tensor_scalar_mul(s_buf[:, qc * 4 + j, ho * 64:(ho + 1) * 64],
                                                 pB[:, j * 64:(j + 1) * 64], invc[:, 4 + j:5 + j])
                    return i1
                steps.append(("dve", sout))
                chain(P, steps)



def emit_l0_gating(P, xn_fm, combine):
    nc = P.nc
    with ExitStack() as ctx:
        sc = P.sbuf(ctx, "gsc", [128, 16], F32)
        sel = P.sbuf(ctx, "gsel", [128, 16], F32)
        top8 = P.sbuf(ctx, "gtop", [128, 8], F32)
        den = P.sbuf(ctx, "gden", [128, 1], F32)
        pg = P.psum(ctx, "pg", [128, 16], F32)
        for i in range(NTILE):
            steps = []
            def mms(e, i=i):
                for c in range(NCH):
                    i1 = e.matmul(pg[:], xn_fm[:, c, i * 128:(i + 1) * 128],
                                  P.sb["gwT"][:, c, :], start=(c == 0), stop=(c == 7))
                return i1
            steps.append(("pe", mms))
            steps.append(("act", lambda e: e.activation(sc[:], pg[:], AF.Sigmoid)))
            steps.append(("dve", lambda e: e.tensor_add(sel[:], sc[:], P.sb["l0bias"][:])))
            steps.append(("dve", lambda e: e.max(out=top8[:], in_=sel[:])))
            steps.append(("dve", lambda e: e.tensor_single_scalar(sel[:], sel[:], top8[:, 7:8], ALU.is_ge)))
            steps.append(("dve", lambda e: e.tensor_mul(sel[:], sel[:], sc[:])))
            steps.append(("dve", lambda e: e.reduce_sum(den[:], sel[:], axis=AX)))
            steps.append(("dve", lambda e: e.reciprocal(den[:], den[:])))
            steps.append(("dve", lambda e: e.tensor_scalar_mul(sel[:], sel[:], den[:, 0:1])))
            steps.append(("dve", lambda e, i=i: e.tensor_scalar_mul(combine[:, i, :], sel[:], 2.826)))
            chain(P, steps)


def emit_l0_ffn(P, xn_fm, y_buf, combine):
    nc = P.nc
    with ExitStack() as ctx:
        h_full = P.sbuf(ctx, "hfull", [128, NCH, T], BF16)
        wg_b = [P.sbuf(ctx, "wgb", [128, 8, 128], BF16) for _ in range(2)]
        wu_b = [P.sbuf(ctx, "wub", [128, 8, 128], BF16) for _ in range(2)]
        wd_b = [P.sbuf(ctx, "wdb", [128, 8, 1024], BF16) for _ in range(2)]
        ts = [P.sbuf(ctx, "tsilu", [128, 512], BF16) for _ in range(2)]
        pg = [P.psum(ctx, "pgm", [128, 512], F32) for _ in range(2)]
        pu = [P.psum(ctx, "pum", [128, 512], F32) for _ in range(2)]
        pd = [P.psum(ctx, "pdm", [128, 512], F32) for _ in range(2)]
        sd, pe, act, dve = P.s["sd"], P.s["pe"], P.s["act"], P.s["dve"]
        sg = P.s["sg"]

        experts = [("sh", 48, 49, 50, None)] + [
            ("e%d" % e, e, 16 + e, 32 + e, e) for e in range(16)]

        scr = P.w["scr_w0"]

        def wg_src(m, c):
            return scr[m].rearrange("(c p) i -> p c i", p=128)[:, :, c * 128:(c + 1) * 128]

        def wd_src(m):
            return scr[m].rearrange("(c p) dd -> p c dd", p=128)

        for ei, (nm, mg, mu, md, eidx) in enumerate(experts):
            with nc.Block() as blk:
                @blk.sync
                def _(e, ei=ei):
                    if "ws" in P.s:
                        e.wait_ge(P.s["ws"].h, 16 * 3 * (ei + 1))
                    e.dma_start(wd_b[ei % 2][:], wd_src(md)).then_inc(sd.h, 16)
                    sd.v += 16
                    e.wait_ge(sd.h, sd.v)
            for c in range(NCH):
                with nc.Block() as blk:
                    @blk.sync
                    def _(e):
                        e.dma_start(wg_b[c % 2][:], wg_src(mg, c)).then_inc(sd.h, 16)
                        e.dma_start(wu_b[c % 2][:], wg_src(mu, c)).then_inc(sd.h, 16)
                        sd.v += 32
                        e.wait_ge(sd.h, sd.v)
                pe0, act0, dve0 = pe.v, act.v, dve.v
                g_m = {}
                u_m = {}
                m = pe0
                for n in range(4):
                    m += 1; g_m[n] = m
                    m += 1; u_m[n] = m
                silu_m = {n: act0 + n + 1 for n in range(4)}
                h_m = {n: dve0 + n + 1 for n in range(4)}
                with nc.Block() as blk:
                    @blk.tensor
                    def _(e):
                        for n in range(4):
                            bb = n % 2
                            if n >= 2:
                                e.wait_ge(act.h, silu_m[n - 2])
                            for kc in range(NCH):
                                i1 = e.matmul(pg[bb][:], wg_b[c % 2][:, kc, :],
                                              xn_fm[:, kc, n * 512:(n + 1) * 512],
                                              start=(kc == 0), stop=(kc == 7))
                            i1.then_inc(pe.h, 1)
                            if n >= 2:
                                e.wait_ge(dve.h, h_m[n - 2])
                            for kc in range(NCH):
                                i1 = e.matmul(pu[bb][:], wu_b[c % 2][:, kc, :],
                                              xn_fm[:, kc, n * 512:(n + 1) * 512],
                                              start=(kc == 0), stop=(kc == 7))
                            i1.then_inc(pe.h, 1)
                    @blk.scalar
                    def _(e):
                        for n in range(4):
                            bb = n % 2
                            e.wait_ge(pe.h, g_m[n])
                            if n >= 2:
                                e.wait_ge(dve.h, h_m[n - 2])
                            e.activation(ts[bb][:], pg[bb][:], AF.Silu).then_inc(act.h, 1)
                    @blk.vector
                    def _(e):
                        for n in range(4):
                            bb = n % 2
                            e.wait_ge(act.h, silu_m[n])
                            e.wait_ge(pe.h, u_m[n])
                            e.tensor_mul(h_full[:, c, n * 512:(n + 1) * 512],
                                         ts[bb][:], pu[bb][:]).then_inc(dve.h, 1)
                pe.v = m
                act.v = act0 + 4
                dve.v = dve0 + 4
            pe0, dve0 = pe.v, dve.v
            dn_m = {}
            m = pe0
            items = [(t, j2) for t in range(NTILE) for j2 in range(2)]
            for idx, (t, j2) in enumerate(items):
                m += 1
                dn_m[(t, j2)] = m
            ev_m = {it: dve0 + i + 1 for i, it in enumerate(items)}
            with nc.Block() as blk:
                @blk.tensor
                def _(e):
                    for idx, (t, j2) in enumerate(items):
                        bb = idx % 2
                        if idx >= 2:
                            e.wait_ge(dve.h, ev_m[items[idx - 2]])
                        for kc in range(NCH):
                            i1 = e.matmul(pd[bb][:], h_full[:, kc, t * 128:(t + 1) * 128],
                                          wd_b[ei % 2][:, kc, j2 * 512:(j2 + 1) * 512],
                                          start=(kc == 0), stop=(kc == 7))
                        i1.then_inc(pe.h, 1)
                @blk.vector
                def _(e):
                    for idx, (t, j2) in enumerate(items):
                        bb = idx % 2
                        e.wait_ge(pe.h, dn_m[(t, j2)])
                        dst = y_buf[:, t, j2 * 512:(j2 + 1) * 512]
                        if eidx is None:
                            e.tensor_copy(dst, pd[bb][:]).then_inc(dve.h, 1)
                        else:
                            e.scalar_tensor_tensor(dst, pd[bb][:],
                                                   combine[:, t, eidx:eidx + 1], dst,
                                                   ALU.mult, ALU.add).then_inc(dve.h, 1)
            pe.v = m
            dve.v = dve0 + len(items)





def stage_l1_small(P):
    nc = P.nc
    P.sb["convq"] = P.sbuf(P.ctx, "convq", [128, 6, 4], F32)
    P.sb["convk"] = P.sbuf(P.ctx, "convk", [128, 6, 4], F32)
    P.sb["convv"] = P.sbuf(P.ctx, "convv", [128, 12, 4], F32)
    P.sb["dtb"] = P.sbuf(P.ctx, "dtb", [128, 6], F32)
    P.sb["negA"] = P.sbuf(P.ctx, "negA", [128, 6], F32)
    P.sb["nwb"] = P.sbuf(P.ctx, "nwb", [128, 256], F32)
    P.sb["l1bias"] = P.sbuf(P.ctx, "l1bias", [128, 8], F32)
    P.sb["ln1b"] = P.sbuf(P.ctx, "ln1b", [128, 1024], F32)
    P.sb["ln2b"] = P.sbuf(P.ctx, "ln2b", [128, 1024], F32)
    with ExitStack() as ctx:
        row = P.sbuf(ctx, "strow", [1, 1024], F32)
        row2 = P.sbuf(ctx, "strow2", [1, 6], F32)
        pb = P.psum(ctx, "pstb", [128, 512], F32)
        steps = []
        steps.append(("sync_dma", lambda e: (
            e.dma_start(P.sb["convq"][:], P.w["L1_attn_conv_q"].rearrange("(c p) j -> p c j", p=128)).then_inc(P.s["sd"].h, 16),
            e.dma_start(P.sb["convk"][:], P.w["L1_attn_conv_k"].rearrange("(c p) j -> p c j", p=128)).then_inc(P.s["sd"].h, 16),
            e.dma_start(P.sb["convv"][:], P.w["L1_attn_conv_v"].rearrange("(c p) j -> p c j", p=128)))[-1], 3))
        steps.append(("sync_dma", lambda e: e.dma_start(row2[:], P.w["L1_attn_dt_bias"].rearrange("(o e) -> o e", o=1))))
        steps.append(("pe", lambda e: e.matmul(pb[:, 0:6], P.c["c_ones_r"][:], row2[:], start=True, stop=True)))
        steps.append(("act", lambda e: e.copy(P.sb["dtb"][:], pb[:, 0:6])))
        steps.append(("sync_dma", lambda e: e.dma_start(row2[:], P.w["L1_attn_A_log"].rearrange("(o e) -> o e", o=1))))
        steps.append(("act", lambda e: e.activation(row2[:], row2[:], AF.Exp)))
        steps.append(("act", lambda e: e.mul(row2[:], row2[:], -1.0)))
        steps.append(("pe", lambda e: e.matmul(pb[:, 0:6], P.c["c_ones_r"][:], row2[:], start=True, stop=True)))
        steps.append(("act", lambda e: e.copy(P.sb["negA"][:], pb[:, 0:6])))
        steps.append(("sync_dma", lambda e: e.dma_start(row[:, 0:256], P.w["L1_attn_norm_w"].rearrange("(o e) -> o e", o=1))))
        steps.append(("pe", lambda e: e.matmul(pb[:, 0:256], P.c["c_ones_r"][:], row[:, 0:256], start=True, stop=True)))
        steps.append(("act", lambda e: e.copy(P.sb["nwb"][:], pb[:, 0:256])))
        steps.append(("sync_dma", lambda e: e.dma_start(row[:, 0:8], P.w["L1_ffn_expert_bias"].rearrange("(o e) -> o e", o=1))))
        steps.append(("pe", lambda e: e.matmul(pb[:, 0:8], P.c["c_ones_r"][:], row[:, 0:8], start=True, stop=True)))
        steps.append(("act", lambda e: e.copy(P.sb["l1bias"][:], pb[:, 0:8])))
        for wname, dst in (("L1_ffn_w_ln1", "ln1b"), ("L1_ffn_w_ln2", "ln2b")):
            steps.append(("sync_dma", lambda e, wname=wname: e.dma_start(row[:], P.w[wname].rearrange("(o e) -> o e", o=1))))
            for j in range(2):
                steps.append(("pe", lambda e, j=j: e.matmul(pb[:], P.c["c_ones_r"][:], row[:, j * 512:(j + 1) * 512], start=True, stop=True)))
                steps.append(("act", lambda e, dst=dst, j=j: e.copy(P.sb[dst][:, j * 512:(j + 1) * 512], pb[:])))
        chain(P, steps)


def emit_l1_attn_passA(P, xn_fm, scr_q, scr_k, scr_v, scr_g, beta_all, g_all):
    nc = P.nc
    GS = 512
    NG = T // GS
    with ExitStack() as ctx:
        wb = P.sbuf(ctx, "wbs", [128, 8, 6], BF16)
        wa = P.sbuf(ctx, "was", [128, 8, 6], BF16)
        t6 = P.sbuf(ctx, "t6", [128, 6], F32)
        t6b = P.sbuf(ctx, "t6b", [128, 6], F32)
        pba = P.psum(ctx, "pba", [128, 6], F32)
        pbb = P.psum(ctx, "pbb", [128, 6], F32)
        steps = [("pool_dma", lambda e: (
            e.dma_start(wb[:], P.w["L1_attn_Wb"].rearrange("(c p) h -> p c h", p=128)).then_inc(P.s["sg"].h, 16),
            e.dma_start(wa[:], P.w["L1_attn_Wa"].rearrange("(c p) h -> p c h", p=128)))[-1], 2)]
        for i in range(NTILE):
            def mmb(e, i=i):
                for c in range(NCH):
                    i1 = e.matmul(pba[:], xn_fm[:, c, i * 128:(i + 1) * 128], wb[:, c, :],
                                  start=(c == 0), stop=(c == 7))
                return i1
            steps.append(("pe", mmb))
            def mma(e, i=i):
                for c in range(NCH):
                    i1 = e.matmul(pbb[:], xn_fm[:, c, i * 128:(i + 1) * 128], wa[:, c, :],
                                  start=(c == 0), stop=(c == 7))
                return i1
            steps.append(("pe", mma))
            steps.append(("act", lambda e, i=i: e.activation(beta_all[:, i, :], pba[:], AF.Sigmoid)))
            steps.append(("dve", lambda e: e.tensor_add(t6[:], pbb[:], P.sb["dtb"][:])))
            steps.append(("act", lambda e: e.activation(t6b[:], t6[:], AF.Abs)))
            steps.append(("act", lambda e: e.activation(t6b[:], t6b[:], AF.Exp, scale=-1.0)))
            steps.append(("act", lambda e: e.activation(t6b[:], t6b[:], AF.Ln, bias=1.0)))
            steps.append(("dve", lambda e: e.tensor_scalar_max(t6[:], t6[:], 0.0)))
            steps.append(("dve", lambda e: e.tensor_add(t6[:], t6[:], t6b[:])))
            steps.append(("dve", lambda e, i=i: e.tensor_mul(g_all[:, i, :], t6[:], P.sb["negA"][:])))
        chain(P, steps)
    with ExitStack() as ctx:
        wg = P.sbuf(ctx, "wgs", [128, 8, 1536], BF16)
        gb = P.sbuf(ctx, "gbs", [128, 512], BF16)
        pgg = P.psum(ctx, "pgg", [128, 512], F32)
        steps = [("pool_dma", lambda e: e.dma_start(wg[:], P.w["L1_attn_Wg"].rearrange("(c p) h -> p c h", p=128)))]
        for i in range(NTILE):
            for n3 in range(3):
                def mmg(e, i=i, n3=n3):
                    for c in range(NCH):
                        i1 = e.matmul(pgg[:], xn_fm[:, c, i * 128:(i + 1) * 128],
                                      wg[:, c, n3 * 512:(n3 + 1) * 512], start=(c == 0), stop=(c == 7))
                    return i1
                steps.append(("pe", mmg))
                steps.append(("act", lambda e: e.activation(gb[:], pgg[:], AF.Silu)))
                steps.append(("sync_dma", lambda e, i=i, n3=n3: e.dma_start(
                    scr_g[i * 128:(i + 1) * 128, n3 * 512:(n3 + 1) * 512], gb[:])))
        chain(P, steps)
    for which, wname, cname, nmrow, scr, do_norm, qscale in (
            ("q", "L1_attn_Wq", "convq", 6, scr_q, True, 128 ** -0.5),
            ("k", "L1_attn_Wk", "convk", 6, scr_k, True, 1.0),
            ("v", "L1_attn_Wv", "convv", 12, scr_v, False, 1.0)):
        with ExitStack() as ctx:
            wqs = P.sbuf(ctx, "wqs", [128, 8, nmrow * 128], BF16)
            xb = P.sbuf(ctx, "xb", [128, nmrow, GS + 3], F32)
            cacc = P.sbuf(ctx, "cacc", [128, GS], F32)
            cout = P.sbuf(ctx, "cout", [128, nmrow, GS], BF16)
            sqv = P.sbuf(ctx, "sqv", [128, GS], F32)
            rr = P.sbuf(ctx, "rr", [1, GS], F32)
            pqv = P.psum(ctx, "pqv", [128, 512], F32)
            pnb = P.psum(ctx, "pnb", [128, 512], F32)
            pnr = P.psum(ctx, "pnr", [1, 512], F32)
            ptt = P.psum(ctx, "ptt", [128, 128], BF16)
            steps = [("pool_dma", lambda e, wname=wname: e.dma_start(
                wqs[:], P.w[wname].rearrange("(c p) h -> p c h", p=128)))]
            steps.append(("dve", lambda e: e.memset(xb[:, :, 0:3], 0.0)))
            for gdx in range(NG):
                if gdx > 0:
                    steps.append(("dve", lambda e: e.tensor_copy(xb[:, :, 0:3], xb[:, :, GS:GS + 3])))
                for m in range(nmrow):
                    def mmx(e, gdx=gdx, m=m):
                        for c in range(NCH):
                            i1 = e.matmul(pqv[:], wqs[:, c, m * 128:(m + 1) * 128],
                                          xn_fm[:, c, gdx * GS:(gdx + 1) * GS],
                                          start=(c == 0), stop=(c == 7))
                        return i1
                    steps.append(("pe", mmx))
                    steps.append(("act", lambda e, m=m: e.copy(xb[:, m, 3:3 + GS], pqv[:])))
                for m in range(nmrow):
                    steps.append(("dve", lambda e, m=m, cname=cname: e.tensor_scalar_mul(
                        cacc[:], xb[:, m, 0:GS], P.sb[cname][:, m, 0:1])))
                    for j in range(1, 4):
                        steps.append(("dve", lambda e, m=m, j=j, cname=cname: e.scalar_tensor_tensor(
                            cacc[:], xb[:, m, j:j + GS], P.sb[cname][:, m, j:j + 1], cacc[:],
                            ALU.mult, ALU.add)))
                    steps.append(("act", lambda e, m=m: e.activation(cout[:, m, :], cacc[:], AF.Silu)))
                    if do_norm:
                        steps.append(("act", lambda e, m=m: e.activation(sqv[:], cout[:, m, :], AF.Square)))
                        steps.append(("pe", lambda e: e.matmul(pnr[:], P.c["c_ones_c"][:], sqv[:], start=True, stop=True)))
                        steps.append(("dve", lambda e: e.tensor_scalar_add(rr[:], pnr[:], 1e-12)))
                        steps.append(("act", lambda e: e.activation(rr[:], rr[:], AF.Ln)))
                        steps.append(("act", lambda e: e.activation(rr[:], rr[:], AF.Exp, scale=-0.5)))
                        if qscale != 1.0:
                            steps.append(("act", lambda e, qscale=qscale: e.mul(rr[:], rr[:], qscale)))
                        steps.append(("pe", lambda e: e.matmul(pnb[:], P.c["c_ones_r"][:], rr[:], start=True, stop=True)))
                        steps.append(("dve", lambda e, m=m: e.tensor_mul(cout[:, m, :], cout[:, m, :], pnb[:])))
                if which in ("q", "k"):
                    steps.append(("sync_dma", lambda e, gdx=gdx, scr=scr: e.dma_start(
                        scr[:, :, gdx * GS:(gdx + 1) * GS], cout[:])))
                else:
                    for hh in range(6):
                        for tt4 in range(4):
                            for half in range(2):
                                steps.append(("pe", lambda e, hh=hh, tt4=tt4, half=half: e.transpose(
                                    ptt[:], cout[:, 2 * hh + half, tt4 * 128:(tt4 + 1) * 128],
                                    P.c["c_idbf"][:])))
                                steps.append(("act", lambda e, hh=hh, tt4=tt4, half=half: e.copy(
                                    sqv.bitcast(BF16)[:, half * 128:(half + 1) * 128] if False else cacc.bitcast(BF16)[:, half * 128:(half + 1) * 128], ptt[:])))
                            steps.append(("sync_dma", lambda e, hh=hh, gdx=gdx, tt4=tt4, scr=scr: e.dma_start(
                                scr[hh, gdx * GS + tt4 * 128: gdx * GS + (tt4 + 1) * 128, :],
                                cacc.bitcast(BF16)[:, 0:256])))
            chain(P, steps)


def emit_l1_attn_passA2(P, xn_fm, buf_a, scr_q, scr_k, scr_v, scr_g, beta_all, g_all):
    nc = P.nc
    GS = 512
    NG = T // GS
    flat32 = buf_a[:].rearrange("p t d -> p (t d)")
    flat16 = buf_a.bitcast(BF16)[:].rearrange("p t d -> p (t d)")

    def carve(base_words, nmrow):
        xb = flat32[:, base_words:base_words + nmrow * 515].rearrange(
            "p (a b) -> p a b", a=nmrow)
        cacc = flat32[:, base_words + nmrow * 515 + 14:
                      base_words + nmrow * 515 + 14 + nmrow * 512].rearrange(
            "p (a b) -> p a b", a=nmrow)
        cw = base_words + nmrow * 515 + 14 + nmrow * 512
        cout = flat16[:, 2 * cw:2 * cw + nmrow * 512].rearrange(
            "p (a b) -> p a b", a=nmrow)
        return xb, cacc, cout

    with ExitStack() as ctx:
        wb = P.sbuf(ctx, "wbs", [128, 8, 6], BF16)
        wa = P.sbuf(ctx, "was", [128, 8, 6], BF16)
        t96 = P.sbuf(ctx, "t96", [128, 96], F32)
        t96b = P.sbuf(ctx, "t96b", [128, 96], F32)
        wqs_q = P.sbuf(ctx, "wqsq", [128, 8, 768], BF16)
        wqs_k = P.sbuf(ctx, "wqsk", [128, 8, 768], BF16)
        rr_q = P.sbuf(ctx, "rrq", [1, 3072], F32)
        rr_k = P.sbuf(ctx, "rrk", [1, 3072], F32)
        pX = [P.psum(ctx, "pXq", [128, 512], F32) for _ in range(2)]
        pY = [P.psum(ctx, "pYk", [128, 512], F32) for _ in range(2)]
        pba = P.psum(ctx, "pba", [128, 96], F32)
        pbb = P.psum(ctx, "pbb", [128, 96], F32)

        dtb_bc = P.sb["dtb"][:].unsqueeze(1).to_broadcast([128, NTILE, 6])
        negA_bc = P.sb["negA"][:].unsqueeze(1).to_broadcast([128, NTILE, 6])

        sbg = [("pool_dma", lambda e: (
            e.dma_start(wb[:], P.w["L1_attn_Wb"].rearrange("(c p) h -> p c h", p=128)).then_inc(P.s["sg"].h, 16),
            e.dma_start(wa[:], P.w["L1_attn_Wa"].rearrange("(c p) h -> p c h", p=128)))[-1], 2)]

        def mmb(e):
            for i in range(NTILE):
                for c in range(NCH):
                    i1 = e.matmul(pba[:, i * 6:(i + 1) * 6],
                                  xn_fm[:, c, i * 128:(i + 1) * 128], wb[:, c, :],
                                  start=(c == 0), stop=(c == 7))
            return i1
        sbg.append(("pe", mmb))
        sbg.append(("act", lambda e: e.activation(
            beta_all[:].rearrange("p t h -> p (t h)"), pba[:], AF.Sigmoid)))

        def mma(e):
            for i in range(NTILE):
                for c in range(NCH):
                    i1 = e.matmul(pbb[:, i * 6:(i + 1) * 6],
                                  xn_fm[:, c, i * 128:(i + 1) * 128], wa[:, c, :],
                                  start=(c == 0), stop=(c == 7))
            return i1
        sbg.append(("pe", mma))
        sbg.append(("dve", lambda e: e.tensor_add(
            t96[:].rearrange("p (t h) -> p t h", t=NTILE),
            pbb[:].rearrange("p (t h) -> p t h", t=NTILE), dtb_bc)))
        sbg.append(("act", lambda e: e.activation(t96b[:], t96[:], AF.Abs)))
        sbg.append(("act", lambda e: e.activation(t96b[:], t96b[:], AF.Exp, scale=-1.0)))
        sbg.append(("act", lambda e: e.activation(t96b[:], t96b[:], AF.Ln, bias=1.0)))
        sbg.append(("dve", lambda e: e.tensor_scalar_max(t96[:], t96[:], 0.0)))
        sbg.append(("dve", lambda e: e.tensor_add(t96[:], t96[:], t96b[:])))
        sbg.append(("dve", lambda e: e.tensor_mul(
            g_all[:], t96[:].rearrange("p (t h) -> p t h", t=NTILE), negA_bc)))

        def qk_stream(wname, cname, wqs, rr, xb, cacc, cout, scr, qscale):
            st = [("pool_dma", lambda e: e.dma_start(
                wqs[:], P.w[wname].rearrange("(c p) h -> p c h", p=128)))]
            st.append(("dve", lambda e: e.memset(xb[:, :, 0:3], 0.0)))
            pp = pX if scr is scr_q else pY
            for gdx in range(NG):
                if gdx > 0:
                    st.append(("dve", lambda e: e.tensor_copy(xb[:, :, 0:3], xb[:, :, GS:GS + 3])))
                for m in range(6):
                    def mmx(e, gdx=gdx, m=m):
                        for c in range(NCH):
                            i1 = e.matmul(pp[m % 2][:], wqs[:, c, m * 128:(m + 1) * 128],
                                          xn_fm[:, c, gdx * GS:(gdx + 1) * GS],
                                          start=(c == 0), stop=(c == 7))
                        return i1
                    st.append(("pe", mmx))
                    st.append(("act", lambda e, m=m: e.copy(xb[:, m, 3:3 + GS], pp[m % 2][:])))
                for j in range(4):
                    def cstep(e, j=j):
                        for m in range(6):
                            if j == 0:
                                i1 = e.tensor_scalar_mul(cacc[:, m, :], xb[:, m, 0:GS],
                                                         P.sb[cname][:, m, 0:1])
                            else:
                                i1 = e.scalar_tensor_tensor(cacc[:, m, :], xb[:, m, j:j + GS],
                                                            P.sb[cname][:, m, j:j + 1],
                                                            cacc[:, m, :], ALU.mult, ALU.add)
                        return i1
                    st.append(("dve", cstep))
                st.append(("act", lambda e: e.activation(
                    cout[:].rearrange("p a b -> p (a b)"), cacc[:].rearrange("p a b -> p (a b)"), AF.Silu)))
                st.append(("act", lambda e: e.activation(
                    cacc[:].rearrange("p a b -> p (a b)"), cout[:].rearrange("p a b -> p (a b)"), AF.Square)))
                for m in range(6):
                    st.append(("pe", lambda e, m=m: e.matmul(
                        pp[m % 2][0:1, :], P.c["c_ones_c"][:], cacc[:, m, :], start=True, stop=True)))
                    st.append(("dve", lambda e, m=m: e.tensor_scalar_add(
                        rr[:, m * 512:(m + 1) * 512], pp[m % 2][0:1, :], 1e-12)))
                st.append(("act", lambda e: e.activation(rr[:], rr[:], AF.Ln)))
                st.append(("act", lambda e: e.activation(rr[:], rr[:], AF.Exp, scale=-0.5)))
                lhsb = P.c["c_qsc"] if qscale != 1.0 else P.c["c_ones_r"]
                for m in range(6):
                    st.append(("pe", lambda e, m=m, lhsb=lhsb: e.matmul(
                        pp[m % 2][:], lhsb[:], rr[:, m * 512:(m + 1) * 512], start=True, stop=True)))
                    st.append(("dve", lambda e, m=m: e.tensor_mul(
                        cout[:, m, :], cout[:, m, :], pp[m % 2][:])))
                st.append(("sync_dma", lambda e, gdx=gdx: e.dma_start(
                    scr[:, :, gdx * GS:(gdx + 1) * GS], cout[:])))
            return st

        xb_q, cacc_q, cout_q = carve(0, 6)
        xb_k, cacc_k, cout_k = carve(8192, 6)
        sq = qk_stream("L1_attn_Wq", "convq", wqs_q, rr_q, xb_q, cacc_q, cout_q, scr_q, 128 ** -0.5)
        sk = qk_stream("L1_attn_Wk", "convk", wqs_k, rr_k, xb_k, cacc_k, cout_k, scr_k, 1.0)
        chain_multi(P, [sbg, sq, sk])

    with ExitStack() as ctx:
        wqs_v = P.sbuf(ctx, "wqsv", [128, 8, 1536], BF16)
        wg = P.sbuf(ctx, "wgs", [128, 8, 1536], BF16)
        gb = [P.sbuf(ctx, "gbs", [128, 512], BF16) for _ in range(2)]
        vstage = P.sbuf(ctx, "vstage", [128, 1536], BF16)
        pX = [P.psum(ctx, "pXv", [128, 512], F32) for _ in range(2)]
        pY = [P.psum(ctx, "pYg", [128, 512], F32) for _ in range(2)]
        pVT = P.psum(ctx, "pVT", [128, 512], BF16)
        xb_v, cacc_v, cout_v = carve(0, 12)

        sv = [("pool_dma", lambda e: e.dma_start(
            wqs_v[:], P.w["L1_attn_Wv"].rearrange("(c p) h -> p c h", p=128)))]
        sv.append(("dve", lambda e: e.memset(xb_v[:, :, 0:3], 0.0)))
        for gdx in range(NG):
            if gdx > 0:
                sv.append(("dve", lambda e: e.tensor_copy(xb_v[:, :, 0:3], xb_v[:, :, GS:GS + 3])))
            for m in range(12):
                def mmx(e, gdx=gdx, m=m):
                    for c in range(NCH):
                        i1 = e.matmul(pX[m % 2][:], wqs_v[:, c, m * 128:(m + 1) * 128],
                                      xn_fm[:, c, gdx * GS:(gdx + 1) * GS],
                                      start=(c == 0), stop=(c == 7))
                    return i1
                sv.append(("pe", mmx))
                sv.append(("act", lambda e, m=m: e.copy(xb_v[:, m, 3:3 + GS], pX[m % 2][:])))
            for j in range(4):
                def cstep(e, j=j):
                    for m in range(12):
                        if j == 0:
                            i1 = e.tensor_scalar_mul(cacc_v[:, m, :], xb_v[:, m, 0:GS],
                                                     P.sb["convv"][:, m, 0:1])
                        else:
                            i1 = e.scalar_tensor_tensor(cacc_v[:, m, :], xb_v[:, m, j:j + GS],
                                                        P.sb["convv"][:, m, j:j + 1],
                                                        cacc_v[:, m, :], ALU.mult, ALU.add)
                    return i1
                sv.append(("dve", cstep))
            sv.append(("act", lambda e: e.activation(
                cout_v[:].rearrange("p a b -> p (a b)"), cacc_v[:].rearrange("p a b -> p (a b)"), AF.Silu)))
            for tt in range(4):
                tsl = slice(tt * 128, (tt + 1) * 128)
                for blk3 in range(3):
                    def trp(e, blk3=blk3, tsl=tsl):
                        for u in range(4):
                            r = blk3 * 4 + u
                            i1 = e.transpose(pVT[:, u * 128:(u + 1) * 128],
                                             cout_v[:, r, tsl], P.c["c_idbf"][:])
                        return i1
                    sv.append(("pe", trp))
                    sv.append(("act", lambda e, blk3=blk3: e.copy(
                        vstage[:, blk3 * 512:(blk3 + 1) * 512], pVT[:])))
                sv.append(("sync_dma", lambda e, gdx=gdx, tt=tt: e.dma_start(
                    scr_v[:, gdx * 512 + tt * 128:gdx * 512 + (tt + 1) * 128, :].transpose([1, 0, 2]),
                    vstage[:].rearrange("p (h d) -> p h d", h=6))))

        sg_ = [("pool_dma", lambda e: e.dma_start(
            wg[:], P.w["L1_attn_Wg"].rearrange("(c p) h -> p c h", p=128)))]
        for i in range(NTILE):
            for n3 in range(3):
                idx = i * 3 + n3
                def mmg(e, i=i, n3=n3, idx=idx):
                    for c in range(NCH):
                        i1 = e.matmul(pY[idx % 2][:], xn_fm[:, c, i * 128:(i + 1) * 128],
                                      wg[:, c, n3 * 512:(n3 + 1) * 512], start=(c == 0), stop=(c == 7))
                    return i1
                sg_.append(("pe", mmg))
                sg_.append(("act", lambda e, idx=idx: e.activation(gb[idx % 2][:], pY[idx % 2][:], AF.Silu)))
                sg_.append(("act_dma", lambda e, i=i, n3=n3, idx=idx: e.dma_start(
                    scr_g[i * 128:(i + 1) * 128, n3 * 512:(n3 + 1) * 512], gb[idx % 2][:])))
        chain_multi(P, [sv, sg_])


def emit_l1_attn_passB(P, s_buf, scr_q, scr_k, scr_v, scr_g, beta_all, g_all):
    nc = P.nc
    NH = 6
    W = NH * 128
    with ExitStack() as ctx:
        wo = P.sbuf(ctx, "wo1", [128, 12, 1024], BF16)
        kc_s = P.sbuf(ctx, "kcs", [128, NH, 128], BF16)
        qc_sb = P.sbuf(ctx, "qcsb", [128, NH, 128], BF16)
        vtm = P.sbuf(ctx, "vtm1", [128, NH, 256], BF16)
        gate_c = P.sbuf(ctx, "gatec", [128, 1536], BF16)
        Gtm = P.sbuf(ctx, "Gtm", [128, 6], F32)
        GhT = P.sbuf(ctx, "GhT", [6, 128], F32)
        bhT = P.sbuf(ctx, "bhT", [6, 128], F32)
        big = P.sbuf(ctx, "bigscr", [128, 2048], F32)
        tmp6 = big[0:6, 0:768]
        PkM = big[:, 0:768]
        bcol = big[:, 768:1536]
        Ed = P.sbuf(ctx, "Ed", [128, 768], F32)
        expEd = P.sbuf(ctx, "expEd", [128, 768], F32)
        expGb = P.sbuf(ctx, "expGb", [128, 768], BF16)
        BBs = P.sbuf(ctx, "BBs", [128, NH, 128], BF16)
        bbs = P.sbuf(ctx, "bbs", [128, NH, 128], BF16)
        B2f = P.sbuf(ctx, "B2f", [128, NH, 128], BF16)
        b2f = P.sbuf(ctx, "b2f", [128, NH, 128], BF16)
        Xp = P.sbuf(ctx, "Xp", [128, NH, 128], BF16)
        xp = P.sbuf(ctx, "xp", [128, NH, 128], BF16)
        kbar = P.sbuf(ctx, "kbar", [128, NH, 128], BF16)
        qbar = P.sbuf(ctx, "qbar", [128, NH, 128], BF16)
        W2t = P.sbuf(ctx, "W2t", [128, NH, 128], BF16)
        RHS0 = P.sbuf(ctx, "RHS0", [128, NH, 256], BF16)
        Us = P.sbuf(ctx, "Us", [128, NH, 256], BF16)
        ktm = P.sbuf(ctx, "ktm", [128, 128], BF16)
        k2tm = P.sbuf(ctx, "k2tm", [128, 128], BF16)
        o_s = big[:, 0:1536]
        o_bf = P.sbuf(ctx, "o_bf", [128, 1536], BF16)
        o_fm = P.sbuf(ctx, "o_fm", [128, 12, 128], BF16)
        oss = P.sbuf(ctx, "oss", [128, 8], F32)
        sqsc = P.sbuf(ctx, "sqsc", [128, 256], F32)
        cC = P.sbuf(ctx, "cC", [128, 6], F32)
        r16 = P.sbuf(ctx, "r16", [1, 6], F32)
        S_st = [P.sbuf(ctx, "Sst", [128, 256], F32) for _ in range(NH)]
        S_bf = P.sbuf(ctx, "Sbf", [128, NH, 256], BF16)
        pP1 = P.psum(ctx, "pP1", [128, 512], F32)
        pP2 = P.psum(ctx, "pP2", [128, 256], F32)
        pO = [P.psum(ctx, "pO", [128, 512], F32) for _ in range(3)]
        pU = P.psum(ctx, "pU", [128, 256], F32)
        pT = P.psum(ctx, "pT", [128, 128], BF16)
        pS = P.psum(ctx, "pSx", [128, 512], F32)

        def mm6(steps, dst_sb, lhs_fn, rhs_fn, cast=True):
            def mms(e):
                for h in range(NH):
                    tgt = pP1[:, h * 128:(h + 1) * 128] if h < 4 else pP2[:, (h - 4) * 128:(h - 3) * 128]
                    i1 = e.matmul(tgt, lhs_fn(h), rhs_fn(h), start=True, stop=True)
                return i1
            steps.append(("pe", mms))
            steps.append(("act", lambda e: e.copy(dst_sb[:, 0:512], pP1[:])))
            steps.append(("act", lambda e: e.copy(dst_sb[:, 512:768], pP2[:])))

        with nc.Block() as blk:
            @blk.gpsimd
            def _(e):
                e.dma_start(wo[:], P.w["L1_attn_Wo"].rearrange("(c p) h -> p c h", p=128)).then_inc(P.s["sg"].h, 16)
                P.s["sg"].v += 16
                e.wait_ge(P.s["sg"].h, P.s["sg"].v)
            @blk.vector
            def _(e):
                for h in range(NH):
                    e.memset(S_st[h][:], 0.0)
                e.memset(S_bf[:], 0.0)
        P.last = None

        striu_bc = P.c["c_striu"][:].unsqueeze(1).to_broadcast([128, NH, 128])
        triu_bc = P.c["c_triu"][:].unsqueeze(1).to_broadcast([128, NH, 128])
        idbf_bc = P.c["c_idbf"][:].unsqueeze(1).to_broadcast([128, NH, 128])
        nwb_bc = P.sb["nwb"][:].unsqueeze(1).to_broadcast([128, NH, 256])

        for ci in range(NTILE):
            csl = slice(ci * 128, (ci + 1) * 128)
            steps = []
            steps.append(("sync_dma", lambda e, ci=ci: (
                e.dma_start(kc_s[:], scr_k[:, :, ci * 128:(ci + 1) * 128]).then_inc(P.s["sd"].h, 16),
                e.dma_start(qc_sb[:], scr_q[:, :, ci * 128:(ci + 1) * 128]).then_inc(P.s["sd"].h, 16),
                e.dma_start(vtm[:], scr_v[:, ci * 128:(ci + 1) * 128, :].transpose([1, 0, 2])).then_inc(P.s["sd"].h, 16),
                e.dma_start(gate_c[:], scr_g[ci * 128:(ci + 1) * 128, :]))[-1], 4))
            steps.append(("pe", lambda e, ci=ci: e.matmul(pP1[:, 0:6], P.c["c_triu"][:], g_all[:, ci, :], start=True, stop=True)))
            steps.append(("dve", lambda e: e.tensor_copy(Gtm[:], pP1[:, 0:6])))
            steps.append(("pe", lambda e: e.transpose(pP1[0:6, 0:128], Gtm[:], P.c["c_id32"][:])))
            steps.append(("dve", lambda e: e.tensor_copy(GhT[:], pP1[0:6, 0:128])))
            steps.append(("pe", lambda e, ci=ci: e.transpose(pP1[0:6, 0:128], beta_all[:, ci, :], P.c["c_id32"][:])))
            steps.append(("dve", lambda e: e.tensor_copy(bhT[:], pP1[0:6, 0:128])))
            steps.append(("dve", lambda e: e.tensor_mul(tmp6.rearrange("p (a t) -> p a t", a=6), GhT[:].unsqueeze(1).to_broadcast([6, 6, 128]), P.c["c_hind"].rearrange("p (a t) -> p a t", a=6))))
            steps.append(("pe", lambda e: e.matmul(pP1[:], P.c["c_ones6"][:], tmp6[:, 0:512], start=True, stop=True)))
            steps.append(("pe", lambda e: e.matmul(pP2[:], P.c["c_ones6"][:], tmp6[:, 512:768], start=True, stop=True)))
            steps.append(("act", lambda e: e.activation(expGb[:, 0:512], pP1[:], AF.Exp)))
            steps.append(("act", lambda e: e.activation(expGb[:, 512:768], pP2[:], AF.Exp)))
            steps.append(("dve", lambda e: e.tensor_copy(Ed[:, 0:512], pP1[:])))
            steps.append(("dve", lambda e: e.tensor_copy(Ed[:, 512:768], pP2[:])))
            steps.append(("pe", lambda e: e.matmul(pP1[:], GhT[:], P.c["c_hind"][:, 0:512], start=True, stop=True)))
            steps.append(("pe", lambda e: e.matmul(pP2[:], GhT[:], P.c["c_hind"][:, 512:768], start=True, stop=True)))
            steps.append(("dve", lambda e: e.tensor_sub(Ed[:, 0:512], Ed[:, 0:512], pP1[:])))
            steps.append(("dve", lambda e: e.tensor_sub(Ed[:, 512:768], Ed[:, 512:768], pP2[:])))
            steps.append(("dve", lambda e: e.tensor_mul(
                Ed.rearrange("p (h t) -> p h t", h=NH)[:], Ed.rearrange("p (h t) -> p h t", h=NH)[:], striu_bc)))
            steps.append(("act", lambda e: e.activation(expEd[:], Ed[:], AF.Exp)))
            steps.append(("pe", lambda e: e.matmul(pP1[:], bhT[:], P.c["c_hind"][:, 0:512], start=True, stop=True)))
            steps.append(("pe", lambda e: e.matmul(pP2[:], bhT[:], P.c["c_hind"][:, 512:768], start=True, stop=True)))
            steps.append(("dve", lambda e: e.tensor_copy(bcol[:, 0:512], pP1[:])))
            steps.append(("dve", lambda e: e.tensor_copy(bcol[:, 512:768], pP2[:])))
            steps.append(("pe", lambda e: e.transpose(pP1[0:1, 0:6], GhT[:, 127:128], P.c["c_id32"][0:6, 0:6])))
            steps.append(("dve", lambda e: e.tensor_copy(r16[:], pP1[0:1, 0:6])))
            steps.append(("act", lambda e: e.activation(r16[:], r16[:], AF.Exp)))
            steps.append(("pe", lambda e: e.matmul(pP1[:, 0:6], P.c["c_ones_r"][:], r16[:], start=True, stop=True)))
            steps.append(("dve", lambda e: e.tensor_copy(cC[:], pP1[:, 0:6])))
            mm6(steps, PkM, lambda h: kc_s[:, h, :], lambda h: kc_s[:, h, :])
            steps.append(("dve", lambda e: e.tensor_mul(
                PkM.rearrange("p (h t) -> p h t", h=NH)[:], PkM.rearrange("p (h t) -> p h t", h=NH)[:], striu_bc)))
            steps.append(("dve", lambda e: e.tensor_mul(Ed[:], expEd[:], PkM[:])))
            steps.append(("dve", lambda e: e.tensor_mul(BBs.rearrange("p h t -> p (h t)"), Ed[:], bcol[:])))
            for h in range(NH):
                steps.append(("pe", lambda e, h=h: e.transpose(pT[:], BBs[:, h, :], P.c["c_idbf"][:])))
                steps.append(("act", lambda e, h=h: e.copy(bbs[:, h, :], pT[:])))
            if DEBUG is not None and ci == 0 and getattr(P, "dbg_seq", 1) == 0:
                steps.append(("sync_dma", lambda e: (
                    e.dma_start(DEBUG["bb"], BBs[:]).then_inc(P.s["sd"].h, 16),
                    e.dma_start(DEBUG["xp2"], bbs[:]))[-1], 2))
                steps.append(("sync_dma", lambda e: (
                    e.dma_start(DEBUG["pkm"], PkM),
                    )[-1], 1))
                steps.append(("sync_dma", lambda e: (
                    e.dma_start(DEBUG["bcol"], bcol),
                    )[-1], 1))
            steps.append(("dve", lambda e: e.tensor_sub(Xp[:], idbf_bc, BBs[:])))
            steps.append(("dve", lambda e: e.tensor_sub(xp[:], idbf_bc, bbs[:])))
            curB, curb = BBs, bbs
            for lvl in range(6):
                nB, nb = (B2f, b2f) if curB is BBs else (BBs, bbs)
                mm6(steps, nB[:].rearrange("p h t -> p (h t)"), lambda h, curb=curb: curb[:, h, :], lambda h, curB=curB: curB[:, h, :])
                if lvl < 5:
                    mm6(steps, nb[:].rearrange("p h t -> p (h t)"), lambda h, curB=curB: curB[:, h, :], lambda h, curb=curb: curb[:, h, :])
                def mmP(e, nB=nB):
                    for h in range(NH):
                        tgt = pP1[:, h * 128:(h + 1) * 128] if h < 4 else pP2[:, (h - 4) * 128:(h - 3) * 128]
                        i1 = e.matmul(tgt, xp[:, h, :], nB[:, h, :], start=True, stop=True)
                    return i1
                steps.append(("pe", mmP))
                steps.append(("dve", lambda e: e.tensor_add(Xp.rearrange("p h t -> p (h t)")[:, 0:512], Xp.rearrange("p h t -> p (h t)")[:, 0:512], pP1[:])))
                steps.append(("dve", lambda e: e.tensor_add(Xp.rearrange("p h t -> p (h t)")[:, 512:768], Xp.rearrange("p h t -> p (h t)")[:, 512:768], pP2[:])))
                if lvl < 5:
                    def mmp2(e, nB=nB):
                        for h in range(NH):
                            tgt = pP1[:, h * 128:(h + 1) * 128] if h < 4 else pP2[:, (h - 4) * 128:(h - 3) * 128]
                            i1 = e.matmul(tgt, nB[:, h, :], xp[:, h, :], start=True, stop=True)
                        return i1
                    steps.append(("pe", mmp2))
                    steps.append(("dve", lambda e: e.tensor_add(xp.rearrange("p h t -> p (h t)")[:, 0:512], xp.rearrange("p h t -> p (h t)")[:, 0:512], pP1[:])))
                    steps.append(("dve", lambda e: e.tensor_add(xp.rearrange("p h t -> p (h t)")[:, 512:768], xp.rearrange("p h t -> p (h t)")[:, 512:768], pP2[:])))
                curB, curb = nB, nb
            steps.append(("dve", lambda e: e.tensor_mul(kbar.rearrange("p h t -> p (h t)"), kc_s.rearrange("p h t -> p (h t)"), expGb[:])))
            steps.append(("dve", lambda e: e.tensor_mul(qbar.rearrange("p h t -> p (h t)"), qc_sb.rearrange("p h t -> p (h t)"), expGb[:])))
            for h in range(NH):
                steps.append(("pe", lambda e, h=h: e.matmul(pU[:], kbar[:, h, :], S_bf[:, h, :], start=True, stop=True)))
                steps.append(("dve", lambda e, h=h: e.tensor_sub(RHS0[:, h, :], vtm[:, h, :], pU[:])))
                steps.append(("pe", lambda e, h=h: e.matmul(pU[:], Xp[:, h, :], RHS0[:, h, :], start=True, stop=True)))
                steps.append(("dve", lambda e, h=h, ci=ci: e.tensor_scalar_mul(Us[:, h, :], pU[:], beta_all[:, ci, h:h + 1])))
            mm6(steps, PkM, lambda h: kc_s[:, h, :], lambda h: qc_sb[:, h, :])
            steps.append(("dve", lambda e: e.tensor_mul(
                PkM.rearrange("p (h t) -> p h t", h=NH)[:], PkM.rearrange("p (h t) -> p h t", h=NH)[:], triu_bc)))
            steps.append(("dve", lambda e: e.tensor_mul(W2t.rearrange("p h t -> p (h t)"), PkM[:], expEd[:])))
            def mmO(e):
                for h in range(NH):
                    tgt = pO[h // 2][:, (h % 2) * 256:(h % 2) * 256 + 256]
                    e.matmul(tgt, qbar[:, h, :], S_bf[:, h, :], start=True, stop=False)
                    i1 = e.matmul(tgt, W2t[:, h, :], Us[:, h, :], start=False, stop=True)
                return i1
            steps.append(("pe", mmO))
            steps.append(("dve", lambda e: e.memset(oss[:], 0.0)))
            for h in range(NH):
                steps.append(("act", lambda e, h=h: e.activation(
                    sqsc[:], pO[h // 2][:, (h % 2) * 256:(h % 2) * 256 + 256], AF.Square,
                    accum_out=oss[:, h:h + 1])))
            steps.append(("dve", lambda e: e.tensor_scalar_mul(oss[:, 0:6], oss[:, 0:6], 1.0 / 256.0)))
            steps.append(("dve", lambda e: e.tensor_scalar_add(oss[:, 0:6], oss[:, 0:6], 1e-5)))
            steps.append(("act", lambda e: e.activation(oss[:, 0:6], oss[:, 0:6], AF.Ln)))
            steps.append(("act", lambda e: e.activation(oss[:, 0:6], oss[:, 0:6], AF.Exp, scale=-0.5)))
            for h in range(NH):
                steps.append(("dve", lambda e, h=h: e.tensor_scalar_mul(
                    o_s[:, h * 256:(h + 1) * 256], pO[h // 2][:, (h % 2) * 256:(h % 2) * 256 + 256],
                    oss[:, h:h + 1])))
            steps.append(("dve", lambda e: e.tensor_mul(
                o_s.rearrange("p (h d) -> p h d", h=NH)[:], o_s.rearrange("p (h d) -> p h d", h=NH)[:], nwb_bc)))
            steps.append(("dve", lambda e: e.tensor_mul(o_s[:], o_s[:], gate_c[:])))
            steps.append(("dve", lambda e: e.tensor_copy(o_bf[:], o_s[:])))
            for kc2 in range(12):
                steps.append(("pe", lambda e, kc2=kc2: e.transpose(
                    pT[:], o_bf[:, kc2 * 128:(kc2 + 1) * 128], P.c["c_idbf"][:])))
                steps.append(("act", lambda e, kc2=kc2: e.copy(o_fm[:, kc2, :], pT[:])))
            for n2 in range(2):
                def mmS(e, n2=n2):
                    for kc2 in range(12):
                        i1 = e.matmul(pS[:], o_fm[:, kc2, :], wo[:, kc2, n2 * 512:(n2 + 1) * 512],
                                      start=(kc2 == 0), stop=(kc2 == 11))
                    return i1
                steps.append(("pe", mmS))
                steps.append(("act", lambda e, ci=ci, n2=n2: e.copy(s_buf[:, ci, n2 * 512:(n2 + 1) * 512], pS[:])))
            if DEBUG is not None and ci == 0 and getattr(P, "dbg_seq", 1) == 0:
                steps.append(("sync_dma", lambda e: (
                    e.dma_start(DEBUG["ed"], expEd[:]).then_inc(P.s["sd"].h, 16),
                    e.dma_start(DEBUG["xp"], Xp[:]).then_inc(P.s["sd"].h, 16),
                    e.dma_start(DEBUG["us"], Us[:]).then_inc(P.s["sd"].h, 16),
                    e.dma_start(DEBUG["os"], o_s))[-1], 4))
            for h in range(NH):
                steps.append(("pe", lambda e, h=h: e.transpose(pT[:], kc_s[:, h, :], P.c["c_idbf"][:])))
                steps.append(("act", lambda e: e.copy(ktm[:], pT[:])))
                steps.append(("dve", lambda e, h=h: e.tensor_scalar_mul(
                    k2tm[:], ktm[:], expEd[:, h * 128 + 127:h * 128 + 128])))
                steps.append(("pe", lambda e, h=h: e.matmul(pU[:], k2tm[:], Us[:, h, :], start=True, stop=True)))
                steps.append(("dve", lambda e, h=h: e.scalar_tensor_tensor(
                    S_st[h][:], S_st[h][:], cC[:, h:h + 1], pU[:], ALU.mult, ALU.add)))
                steps.append(("dve", lambda e, h=h: e.tensor_copy(S_bf[:, h, :], S_st[h][:])))
            chain(P, steps)



def emit_combine(P, x_ap, s_scr, y_buf, out_ap):
    nc = P.nc
    with ExitStack() as ctx:
        xt = [P.sbuf(ctx, "cbx", [128, D], F32) for _ in range(2)]
        st_ = [P.sbuf(ctx, "cbs", [128, D], F32) for _ in range(2)]
        gy = P.sbuf(ctx, "cbg", [128, D], F32)
        for i in range(NTILE):
            b = i % 2
            steps = []

            def load2(e, i=i, b=b):
                e.dma_start(xt[b][:], x_ap[i * 128:(i + 1) * 128, :]).then_inc(P.s["sd"].h, 16)
                return e.dma_start(st_[b][:], s_scr[:, i * 1024:(i + 1) * 1024])
            steps.append(("sync_dma", load2, 2))
            steps.append(("act", lambda e, i=i: e.activation(gy[:], y_buf[:, i, :], AF.Silu)))
            steps.append(("dve", lambda e, b=b: e.tensor_mul(gy[:], gy[:], st_[b][:])))
            steps.append(("dve", lambda e, b=b: e.tensor_add(gy[:], gy[:], xt[b][:])))
            steps.append(("act_dma", lambda e, i=i: e.dma_start(
                out_ap[i * 128:(i + 1) * 128, :], gy[:])))
            chain(P, steps)
    with nc.Block() as blk:
        @blk.vector
        def _(e):
            e.wait_ge(P.s["sq"].h, P.s["sq"].v)


def park_s(P, s_buf, s_scr):
    with P.nc.Block() as blk:
        @blk.sync
        def _(e):
            e.dma_start(s_scr, s_buf.rearrange("p t d -> p (t d)")).then_inc(P.s["sd"].h, 16)
            P.s["sd"].v += 16
            e.wait_ge(P.s["sd"].h, P.s["sd"].v)





def _emit_rms(P, R, xr_fm, wb_name):
    nc = P.nc
    with ExitStack() as ctx:
        sq = P.sbuf(ctx, "rsq", [128, D], F32)
        st = P.sbuf(ctx, "rst", [128, 4], F32)
        tb = P.sbuf(ctx, "rtb", [128, D], F32)
        xnb = P.sbuf(ctx, "rbf", [128, D], BF16)
        ptr = [P.psum(ctx, "rp", [128, 128], BF16) for _ in range(2)]
        for i in range(NTILE):
            steps = []
            steps.append(("act", lambda e, i=i: e.activation(sq[:], R[:, i, :], AF.Square, accum_out=st[:, 0:1])))
            steps.append(("dve", lambda e: e.tensor_scalar_mul(st[:, 1:2], st[:, 0:1], 1.0 / D)))
            steps.append(("dve", lambda e: e.tensor_scalar_add(st[:, 1:2], st[:, 1:2], 1e-5)))
            steps.append(("act", lambda e: e.activation(st[:, 2:3], st[:, 1:2], AF.Ln)))
            steps.append(("act", lambda e: e.activation(st[:, 2:3], st[:, 2:3], AF.Exp, scale=-0.5)))
            steps.append(("dve", lambda e, i=i: e.tensor_scalar_mul(tb[:], R[:, i, :], st[:, 2:3])))
            steps.append(("dve", lambda e, wb_name=wb_name: e.tensor_mul(xnb[:], tb[:], P.sb[wb_name][:])))
            for j in range(NCH):
                steps.append(("pe", lambda e, j=j: e.transpose(
                    ptr[j % 2][:], xnb[:, j * 128:(j + 1) * 128], P.c["c_idbf"][:])))
                steps.append(("dve", lambda e, i=i, j=j: e.tensor_copy(
                    xr_fm[:, j, i * 128:(i + 1) * 128], ptr[j % 2][:])))
            chain(P, steps)


def emit_l1_ffn_attn(P, xr_fm, R):
    nc = P.nc
    scr_q1 = P.w["scr_q1"]
    with ExitStack() as octx:
        kv_fm = P.sbuf(octx, "kvfm", [128, 4, T], BF16)
        krope = P.sbuf(octx, "krope", [32, T], BF16)
        ctx_full = P.sbuf(octx, "ctxf", [128, NCH, T], BF16)
        with P.nc.Block() as blk:
            @blk.gpsimd
            def _(e):
                e.memset(ctx_full[:], 0.0)
        P.last = None
        with ExitStack() as ctx:
            q_buf = P.sbuf(ctx, "qb1", [128, 4, T], BF16)
            wq = P.sbuf(ctx, "wq1", [128, 8, 512], BF16)
            ropeC = P.sbuf(ctx, "ropeC", [128, T], BF16)
            ropeS = P.sbuf(ctx, "ropeS", [128, T], BF16)
            t1 = P.sbuf(ctx, "rt1", [128, 512], F32)
            t2 = P.sbuf(ctx, "rt2", [128, 512], F32)
            pq = [P.psum(ctx, "pq1", [128, 512], F32) for _ in range(2)]
            psw = P.psum(ctx, "psw", [128, 512], F32)
            steps = [("sync_dma", lambda e: (
                e.dma_start(ropeC[:], P.w["c_ropeC"]).then_inc(P.s["sd"].h, 16),
                e.dma_start(ropeS[:], P.w["c_ropeS"]))[-1], 2)]
            for half in range(2):
                steps.append(("pool_dma", lambda e, half=half: e.dma_start(
                    wq[:], P.w["L1_ffn_Wq"].rearrange("(c p) n -> p c n", p=128)[:, :, half * 512:(half + 1) * 512])))
                for m in range(4):
                    for n in range(4):
                        def mmq(e, m=m, n=n):
                            for c in range(NCH):
                                i1 = e.matmul(pq[n % 2][:], wq[:, c, m * 128:(m + 1) * 128],
                                              xr_fm[:, c, n * 512:(n + 1) * 512], start=(c == 0), stop=(c == 7))
                            return i1
                        steps.append(("pe", mmq))
                        steps.append(("act", lambda e, m=m, n=n: e.copy(q_buf[:, m, n * 512:(n + 1) * 512], pq[n % 2][:])))
                for m in range(4):
                    for n in range(4):
                        sl = slice(n * 512, (n + 1) * 512)
                        steps.append(("pe", lambda e, m=m, sl=sl: e.matmul(
                            psw[:], P.c["c_sperm"][:], q_buf[:, m, sl], start=True, stop=True)))
                        steps.append(("dve", lambda e, m=m, sl=sl: e.tensor_mul(t1[:], q_buf[:, m, sl], ropeC[:, sl])))
                        steps.append(("dve", lambda e, sl=sl: e.tensor_mul(t2[:], psw[:], ropeS[:, sl])))
                        steps.append(("dve", lambda e, m=m, sl=sl: e.tensor_add(q_buf[:, m, sl], t1[:], t2[:])))
                steps.append(("sync_dma", lambda e, half=half: e.dma_start(
                    scr_q1[:, half * 4:(half + 1) * 4, :], q_buf[:])))
            chain(P, steps)
        with ExitStack() as ctx:
            wkva = P.sbuf(ctx, "wkva", [128, 8, 160], BF16)
            wkvb = P.sbuf(ctx, "wkvb", [128, 512], BF16)
            kvlat = P.sbuf(ctx, "kvlat", [128, T], BF16)
            krf = P.sbuf(ctx, "krf", [32, T], BF16)
            ropeC = P.sbuf(ctx, "ropeC2", [32, T], BF16)
            ropeS = P.sbuf(ctx, "ropeS2", [32, T], BF16)
            t1 = P.sbuf(ctx, "rt3", [32, 512], F32)
            t2 = P.sbuf(ctx, "rt4", [32, 512], F32)
            pq = [P.psum(ctx, "pq2", [128, 512], F32) for _ in range(2)]
            psw = P.psum(ctx, "psw2", [32, 512], F32)
            steps = [("pool_dma", lambda e: (
                e.dma_start(wkva[:], P.w["L1_ffn_Wkva"].rearrange("(c p) n -> p c n", p=128)).then_inc(P.s["sg"].h, 16),
                e.dma_start(wkvb[:], P.w["L1_ffn_Wkvb"]))[-1], 2)]
            steps.append(("sync_dma", lambda e: (
                e.dma_start(ropeC[:], P.w["c_ropeC"][32:64, :]).then_inc(P.s["sd"].h, 16),
                e.dma_start(ropeS[:], P.w["c_ropeS"][32:64, :]))[-1], 2))
            for n in range(4):
                sl = slice(n * 512, (n + 1) * 512)
                def mma(e, n=n, sl=sl):
                    for c in range(NCH):
                        i1 = e.matmul(pq[0][:], wkva[:, c, 0:128],
                                      xr_fm[:, c, sl], start=(c == 0), stop=(c == 7))
                    return i1
                steps.append(("pe", mma))
                steps.append(("act", lambda e, sl=sl: e.copy(kvlat[:, sl], pq[0][:])))
                def mmb(e, n=n, sl=sl):
                    for c in range(NCH):
                        i1 = e.matmul(pq[1][0:32, :], wkva[:, c, 128:160],
                                      xr_fm[:, c, sl], start=(c == 0), stop=(c == 7))
                    return i1
                steps.append(("pe", mmb))
                steps.append(("act", lambda e, sl=sl: e.copy(krf[:, sl], pq[1][0:32, :])))
                steps.append(("pe", lambda e, sl=sl: e.matmul(psw[:], P.c["c_sperm32"][:], krf[:, sl], start=True, stop=True)))
                steps.append(("dve", lambda e, sl=sl: e.tensor_mul(t1[:], krf[:, sl], ropeC[:, sl])))
                steps.append(("dve", lambda e, sl=sl: e.tensor_mul(t2[:], psw[:], ropeS[:, sl])))
                steps.append(("dve", lambda e, sl=sl: e.tensor_add(krope[:, sl], t1[:], t2[:])))
                for m4 in range(4):
                    steps.append(("pe", lambda e, m4=m4, sl=sl: e.matmul(
                        pq[0][:], wkvb[:, m4 * 128:(m4 + 1) * 128], kvlat[:, sl], start=True, stop=True)))
                    steps.append(("act", lambda e, m4=m4, sl=sl: e.copy(kv_fm[:, m4, sl], pq[0][:])))
            chain(P, steps)
        with ExitStack() as ctx:
            qp = P.sbuf(ctx, "qp3", [128, T], BF16)
            kpair = P.sbuf(ctx, "kp3", [128, T], BF16)
            vsrcO = P.sbuf(ctx, "vsrcO3", [32, T], BF16)
            tmp64 = P.sbuf(ctx, "tmp64_3", [64, T], BF16)
            v_tmE = P.sbuf(ctx, "vtmE3", [128, NTILE, 33], BF16)
            v_tmO = P.sbuf(ctx, "vtmO3", [128, NTILE, 33], BF16)
            sqh = P.sbuf(ctx, "sqh3", [128, 1024], F32)
            qmax2 = P.sbuf(ctx, "qmax2", [2, 8], F32)
            m2 = P.sbuf(ctx, "m2s", [2, 2], F32)
            row2 = P.sbuf(ctx, "row2s", [1, 2], F32)
            bnd2 = P.sbuf(ctx, "bnd2s", [128, 2], F32)
            invrE = P.sbuf(ctx, "invrE3", [1, 512], F32)
            invrO = P.sbuf(ctx, "invrO3", [1, 512], F32)
            ctx32E = P.sbuf(ctx, "ctx32E", [32, 512], BF16)
            ctx32O = P.sbuf(ctx, "ctx32O", [32, 512], BF16)
            p_tE = [P.sbuf(ctx, "ptE3", [128, 512], BF16) for _ in range(2)]
            p_tO = [P.sbuf(ctx, "ptO3", [128, 512], BF16) for _ in range(2)]
            pstE = [P.psum(ctx, "pstE3", [128, 512], F32) for _ in range(2)]
            pstO = [P.psum(ctx, "pstO3", [128, 512], F32) for _ in range(2)]
            pctxE = P.psum(ctx, "pctxE3", [33, 512], F32)
            pctxO = P.psum(ctx, "pctxO3", [33, 512], F32)
            pVT = P.psum(ctx, "pVT3", [128, 512], BF16)
            pSml = P.psum(ctx, "pSml3", [128, 8], F32)
            pe, act, dve = P.s["pe"], P.s["act"], P.s["dve"]
            chain(P, [("pool", lambda e: (
                e.memset(v_tmE[:], 0.0), e.memset(v_tmE[:, :, 32:33], 1.0),
                e.memset(v_tmO[:], 0.0), e.memset(v_tmO[:, :, 32:33], 1.0))[-1])])
            for j in range(8):
                he, ho = 2 * j, 2 * j + 1
                kb = (j % 2) * 64
                steps = []
                steps.append(("sync_dma", lambda e, j=j: e.dma_start(qp[:], scr_q1[:, j, :])))
                if j % 2 == 0:
                    def asm(e, j=j):
                        e.copy(kpair[0:32, :], kv_fm[0:32, j // 2, :])
                        e.copy(kpair[64:96, :], kv_fm[32:64, j // 2, :])
                        e.copy(kpair[32:64, :], krope[:])
                        e.copy(kpair[96:128, :], krope[:])
                        return e.copy(vsrcO[0:32, :], kv_fm[32:64, j // 2, :])
                else:
                    def asm(e, j=j):
                        e.copy(tmp64[0:64, :], kv_fm[64:128, j // 2, :])
                        e.copy(kpair[0:32, :], tmp64[0:32, :])
                        e.copy(kpair[64:96, :], tmp64[32:64, :])
                        e.copy(kpair[32:64, :], krope[:])
                        e.copy(kpair[96:128, :], krope[:])
                        return e.copy(vsrcO[0:32, :], tmp64[32:64, :])
                steps.append(("act", asm))
                for (src, vdst) in ((kpair, v_tmE), (vsrcO, v_tmO)):
                    def trv(e, src=src):
                        for u in range(16):
                            i1 = e.transpose(pVT[:, u * 32:(u + 1) * 32],
                                             src[0:32, u * 128:(u + 1) * 128],
                                             P.c["c_idbf"][0:32, 0:32])
                        return i1
                    steps.append(("pe", trv))
                    def vcp(e, vdst=vdst):
                        for u in range(16):
                            i1 = e.copy(vdst[:, u, 0:32], pVT[:, u * 32:(u + 1) * 32])
                        return i1
                    steps.append(("act", vcp))
                for ti, tile in ((0, qp), (1, kpair)):
                    for half in range(2):
                        hsl = slice(half * 1024, (half + 1) * 1024)
                        steps.append(("act", lambda e, tile=tile, hsl=hsl: e.activation(
                            sqh[:], tile[:, hsl], AF.Square)))
                        for n in range(2):
                            pX = pstE[n % 2]
                            steps.append(("pe", lambda e, n=n, pX=pX: e.matmul(
                                pX[0:2, :], P.c["c_sel2"][:], sqh[:, n * 512:(n + 1) * 512],
                                start=True, stop=True)))
                            steps.append(("dve", lambda e, ti=ti, half=half, n=n, pX=pX: e.reduce_max(
                                qmax2[:, ti * 4 + half * 2 + n:ti * 4 + half * 2 + n + 1],
                                pX[0:2, :], axis=AX)))
                steps.append(("dve", lambda e: (
                    e.reduce_max(m2[:, 0:1], qmax2[:, 0:4], axis=AX),
                    e.reduce_max(m2[:, 1:2], qmax2[:, 4:8], axis=AX))[-1]))
                steps.append(("dve", lambda e: e.tensor_mul(m2[:, 0:1], m2[:, 0:1], m2[:, 1:2])))
                steps.append(("act", lambda e: e.activation(m2[:, 0:1], m2[:, 0:1], AF.Ln)))
                steps.append(("act", lambda e: e.activation(m2[:, 0:1], m2[:, 0:1], AF.Exp, scale=0.5)))
                steps.append(("act", lambda e: e.mul(m2[:, 0:1], m2[:, 0:1], -0.125)))
                steps.append(("pe", lambda e: e.transpose(pSml[0:1, 0:2], m2[:, 0:1], P.c["c_id32"][0:2, 0:2])))
                steps.append(("dve", lambda e: e.tensor_copy(row2[:], pSml[0:1, 0:2])))
                steps.append(("pe", lambda e: e.matmul(pSml[:, 0:2], P.c["c_ones_r"][:], row2[:], start=True, stop=True)))
                steps.append(("dve", lambda e: e.tensor_copy(bnd2[:], pSml[:, 0:2])))
                chain(P, steps)
                for qc in range(4):
                    nkt = (qc + 1) * 4
                    qsl = slice(qc * 512, (qc + 1) * 512)
                    pe0, act0, dve0 = pe.v, act.v, dve.v
                    ST_m = {}
                    PV_m = {}
                    m = pe0
                    for kt in range(nkt):
                        m += 1
                        ST_m[kt] = m
                        if kt >= 1:
                            m += 1
                            PV_m[kt - 1] = m
                    m += 1
                    PV_m[nkt - 1] = m
                    exp_m = {kt: act0 + kt + 1 for kt in range(nkt)}
                    msk_m = {kt: dve0 + (kt - (nkt - 4)) + 1 for kt in range(nkt - 4, nkt)}

                    def p_ready(kt, nkt=nkt, msk_m=msk_m, exp_m=exp_m):
                        if kt >= nkt - 4:
                            return (dve, msk_m[kt])
                        return (act, exp_m[kt])

                    start_wait = P.last
                    with nc.Block() as blk:
                        @blk.tensor
                        def _(e):
                            if start_wait is not None:
                                e.wait_ge(start_wait[0].h, start_wait[1])
                            for kt in range(nkt):
                                bb = kt % 2
                                if kt >= 2:
                                    e.wait_ge(act.h, exp_m[kt - 2])
                                e.matmul(pstE[bb][:], kpair[0:64, kt * 128:(kt + 1) * 128],
                                         qp[0:64, qsl], start=True, stop=True)
                                e.matmul(pstO[bb][:], kpair[64:128, kt * 128:(kt + 1) * 128],
                                         qp[64:128, qsl], start=True, stop=True).then_inc(pe.h, 1)
                                if kt >= 1:
                                    ss, vv = p_ready(kt - 1)
                                    e.wait_ge(ss.h, vv)
                                    e.matmul(pctxE[:], v_tmE[:, kt - 1, :], p_tE[(kt - 1) % 2][:],
                                             start=(kt - 1 == 0), stop=False)
                                    e.matmul(pctxO[:], v_tmO[:, kt - 1, :], p_tO[(kt - 1) % 2][:],
                                             start=(kt - 1 == 0), stop=False).then_inc(pe.h, 1)
                            ss, vv = p_ready(nkt - 1)
                            e.wait_ge(ss.h, vv)
                            e.matmul(pctxE[:], v_tmE[:, nkt - 1, :], p_tE[(nkt - 1) % 2][:],
                                     start=(nkt - 1 == 0), stop=True)
                            e.matmul(pctxO[:], v_tmO[:, nkt - 1, :], p_tO[(nkt - 1) % 2][:],
                                     start=(nkt - 1 == 0), stop=True).then_inc(pe.h, 1)
                        @blk.scalar
                        def _(e):
                            for kt in range(nkt):
                                bb = kt % 2
                                e.wait_ge(pe.h, ST_m[kt])
                                if kt >= 2:
                                    e.wait_ge(pe.h, PV_m[kt - 2])
                                e.activation(p_tE[bb][:], pstE[bb][:], AF.Exp, scale=0.125,
                                             bias=bnd2[:, 0:1])
                                e.activation(p_tO[bb][:], pstO[bb][:], AF.Exp, scale=0.125,
                                             bias=bnd2[:, 1:2]).then_inc(act.h, 1)
                        @blk.vector
                        def _(e):
                            for kt in range(nkt - 4, nkt):
                                e.wait_ge(act.h, exp_m[kt])
                                jj = kt - (nkt - 4)
                                e.tensor_mul(p_tE[kt % 2][:], p_tE[kt % 2][:], dm_mask(P, jj))
                                e.tensor_mul(p_tO[kt % 2][:], p_tO[kt % 2][:],
                                             dm_mask(P, jj)).then_inc(dve.h, 1)
                    pe.v = m
                    act.v = act0 + nkt
                    dve.v = dve0 + 4
                    P.last = (pe, m)
                    steps = []
                    steps.append(("dve", lambda e: (
                        e.reciprocal(invrE[:], pctxE[32:33, :]),
                        e.reciprocal(invrO[:], pctxO[32:33, :]))[-1]))
                    steps.append(("act", lambda e: (
                        e.copy(ctx32E[:], pctxE[0:32, :]),
                        e.copy(ctx32O[:], pctxO[0:32, :]))[-1]))
                    steps.append(("pe", lambda e: (
                        e.matmul(pstE[0][0:32, :], P.c["c_ones_r"][0:1, 0:32], invrE[:], start=True, stop=True),
                        e.matmul(pstO[0][0:32, :], P.c["c_ones_r"][0:1, 0:32], invrO[:], start=True, stop=True))[-1]))
                    steps.append(("dve", lambda e, qsl=qsl, j=j: (
                        e.tensor_mul(ctx_full[0:32, j, qsl], ctx32E[:], pstE[0][0:32, :]),
                        e.tensor_mul(ctx_full[64:96, j, qsl], ctx32O[:], pstO[0][0:32, :]))[-1]))
                    chain(P, steps)
        with ExitStack() as ctx:
            wo = P.sbuf(ctx, "woeff", [128, 8, 1024], BF16)
            pS = P.psum(ctx, "pS4", [128, 512], F32)
            steps = [("pool_dma", lambda e: e.dma_start(
                wo[:], P.w["L1_ffn_Wo"].rearrange("(c p) n -> p c n", p=128)))]
            for t in range(NTILE):
                for n2 in range(2):
                    def mmS(e, t=t, n2=n2):
                        for kc in range(NCH):
                            i1 = e.matmul(pS[:], ctx_full[:, kc, t * 128:(t + 1) * 128],
                                          wo[:, kc, n2 * 512:(n2 + 1) * 512], start=(kc == 0), stop=(kc == 7))
                        return i1
                    steps.append(("pe", mmS))
                    steps.append(("dve", lambda e, t=t, n2=n2: e.tensor_add(
                        R[:, t, n2 * 512:(n2 + 1) * 512], R[:, t, n2 * 512:(n2 + 1) * 512], pS[:])))
            chain(P, steps)


def emit_l1_moe(P, xr_fm, R):
    nc = P.nc
    with ExitStack() as ctx:
        combine = P.sbuf(ctx, "comb1", [128, NTILE, 8], F32)
        with ExitStack() as c2:
            wgate = P.sbuf(c2, "wgate", [128, 8, 8], BF16)
            zrow = P.sbuf(c2, "zrow", [128, 8], F32)
            m1 = P.sbuf(c2, "m1", [128, 2], F32)
            top8 = P.sbuf(c2, "top81", [128, 8], F32)
            pgl = P.psum(c2, "pgl", [128, 8], F32)
            steps = [("pool_dma", lambda e: e.dma_start(wgate[:], P.w["L1_ffn_Wgate"].rearrange("(c p) n -> p c n", p=128)))]
            for i in range(NTILE):
                def mmg(e, i=i):
                    for c in range(NCH):
                        i1 = e.matmul(pgl[:], xr_fm[:, c, i * 128:(i + 1) * 128],
                                      wgate[:, c, :], start=(c == 0), stop=(c == 7))
                    return i1
                steps.append(("pe", mmg))
                steps.append(("dve", lambda e: e.tensor_add(zrow[:], pgl[:], P.sb["l1bias"][:])))
                steps.append(("dve", lambda e: e.reduce_max(m1[:, 0:1], zrow[:], axis=AX)))
                steps.append(("dve", lambda e: e.tensor_scalar_mul(m1[:, 0:1], m1[:, 0:1], -1.0)))
                steps.append(("act", lambda e: e.activation(zrow[:], zrow[:], AF.Exp, bias=m1[:, 0:1])))
                steps.append(("dve", lambda e: e.max(out=top8[:], in_=zrow[:])))
                steps.append(("dve", lambda e: e.tensor_single_scalar(top8[:], zrow[:], top8[:, 1:2], ALU.is_ge)))
                steps.append(("dve", lambda e: e.tensor_mul(zrow[:], zrow[:], top8[:])))
                steps.append(("dve", lambda e: e.reduce_sum(m1[:, 1:2], zrow[:], axis=AX)))
                steps.append(("dve", lambda e: e.reciprocal(m1[:, 1:2], m1[:, 1:2])))
                steps.append(("dve", lambda e, i=i: e.tensor_scalar_mul(combine[:, i, :], zrow[:], m1[:, 1:2])))
            chain(P, steps)
        h_full = P.sbuf(ctx, "hf1", [128, 4, T], BF16)
        wg_b = [P.sbuf(ctx, "wgb1", [128, 8, 512], BF16) for _ in range(2)]
        wu_b = [P.sbuf(ctx, "wub1", [128, 8, 512], BF16) for _ in range(2)]
        wd_b = [P.sbuf(ctx, "wdb1", [128, 4, 1024], BF16) for _ in range(2)]
        ts = [P.sbuf(ctx, "ts1", [128, 512], BF16) for _ in range(2)]
        pg = [P.psum(ctx, "pg1", [128, 512], F32) for _ in range(2)]
        pu = [P.psum(ctx, "pu1", [128, 512], F32) for _ in range(2)]
        pd = [P.psum(ctx, "pd1", [128, 512], F32) for _ in range(2)]
        sd, sg, pe, act, dve = P.s["sd"], P.s["sg"], P.s["pe"], P.s["act"], P.s["dve"]
        experts = [("sh", "L1_ffn_sWg", "L1_ffn_sWu", "L1_ffn_sWd", 0, None)] + [
            ("e%d" % e, "L1_ffn_Wg", "L1_ffn_Wu", "L1_ffn_Wd", e, e) for e in range(8)]
        for ei, (nm, ng, nu, nd, srcidx, eidx) in enumerate(experts):
            with nc.Block() as blk:
                @blk.gpsimd
                def _(e):
                    e.dma_start(wg_b[ei % 2][:], P.w[ng][srcidx].rearrange("(c p) i -> p c i", p=128)).then_inc(sg.h, 16)
                    e.dma_start(wu_b[ei % 2][:], P.w[nu][srcidx].rearrange("(c p) i -> p c i", p=128)).then_inc(sg.h, 16)
                    e.dma_start(wd_b[ei % 2][:], P.w[nd][srcidx].rearrange("(c p) i -> p c i", p=128)).then_inc(sg.h, 16)
                    sg.v += 48
                    e.wait_ge(sg.h, sg.v)
            pe0, act0, dve0 = pe.v, act.v, dve.v
            items = [(mm, n) for mm in range(4) for n in range(4)]
            g_m = {}
            u_m = {}
            mct = pe0
            for it in items:
                mct += 1
                g_m[it] = mct
                mct += 1
                u_m[it] = mct
            silu_m = {it: act0 + i + 1 for i, it in enumerate(items)}
            h_m = {it: dve0 + i + 1 for i, it in enumerate(items)}
            with nc.Block() as blk:
                @blk.tensor
                def _(e):
                    for idx, (mm, n) in enumerate(items):
                        bb = idx % 2
                        if idx >= 2:
                            e.wait_ge(act.h, silu_m[items[idx - 2]])
                        for kc in range(NCH):
                            i1 = e.matmul(pg[bb][:], wg_b[ei % 2][:, kc, mm * 128:(mm + 1) * 128],
                                          xr_fm[:, kc, n * 512:(n + 1) * 512], start=(kc == 0), stop=(kc == 7))
                        i1.then_inc(pe.h, 1)
                        if idx >= 2:
                            e.wait_ge(dve.h, h_m[items[idx - 2]])
                        for kc in range(NCH):
                            i1 = e.matmul(pu[bb][:], wu_b[ei % 2][:, kc, mm * 128:(mm + 1) * 128],
                                          xr_fm[:, kc, n * 512:(n + 1) * 512], start=(kc == 0), stop=(kc == 7))
                        i1.then_inc(pe.h, 1)
                @blk.scalar
                def _(e):
                    for idx, it in enumerate(items):
                        bb = idx % 2
                        e.wait_ge(pe.h, g_m[it])
                        if idx >= 2:
                            e.wait_ge(dve.h, h_m[items[idx - 2]])
                        e.activation(ts[bb][:], pg[bb][:], AF.Silu).then_inc(act.h, 1)
                @blk.vector
                def _(e):
                    for idx, (mm, n) in enumerate(items):
                        bb = idx % 2
                        e.wait_ge(act.h, silu_m[(mm, n)])
                        e.wait_ge(pe.h, u_m[(mm, n)])
                        e.tensor_mul(h_full[:, mm, n * 512:(n + 1) * 512],
                                     ts[bb][:], pu[bb][:]).then_inc(dve.h, 1)
            pe.v = mct
            act.v = act0 + 16
            dve.v = dve0 + 16
            pe0, dve0 = pe.v, dve.v
            items2 = [(t, j2) for t in range(NTILE) for j2 in range(2)]
            dn_m = {}
            mct = pe0
            for it in items2:
                mct += 1
                dn_m[it] = mct
            ev_m = {it: dve0 + i + 1 for i, it in enumerate(items2)}
            with nc.Block() as blk:
                @blk.tensor
                def _(e):
                    for idx, (t, j2) in enumerate(items2):
                        bb = idx % 2
                        if idx >= 2:
                            e.wait_ge(dve.h, ev_m[items2[idx - 2]])
                        for kc in range(4):
                            i1 = e.matmul(pd[bb][:], h_full[:, kc, t * 128:(t + 1) * 128],
                                          wd_b[ei % 2][:, kc, j2 * 512:(j2 + 1) * 512],
                                          start=(kc == 0), stop=(kc == 3))
                        i1.then_inc(pe.h, 1)
                @blk.vector
                def _(e):
                    for idx, (t, j2) in enumerate(items2):
                        bb = idx % 2
                        e.wait_ge(pe.h, dn_m[(t, j2)])
                        dst = R[:, t, j2 * 512:(j2 + 1) * 512]
                        if eidx is None:
                            e.tensor_add(dst, dst, pd[bb][:]).then_inc(dve.h, 1)
                        else:
                            e.scalar_tensor_tensor(dst, pd[bb][:], combine[:, t, eidx:eidx + 1],
                                                   dst, ALU.mult, ALU.add).then_inc(dve.h, 1)
            pe.v = mct
            dve.v = dve0 + len(items2)


def build_layer1():
    nc = bass.Bass("TRN2", target_bir_lowering=False)
    ctx = ExitStack()
    P = Prog(nc, ctx)
    names = (["c_ropeC", "c_ropeS"] + [n for n in WSHAPES if n.startswith("L1_")] + BASE_CONSTS)
    for n in names:
        shape, dt = WSHAPES[n]
        P.w[n] = nc.dram_tensor(n, list(shape), dt, kind="ExternalInput").ap()
    P.w["x1in"] = nc.dram_tensor("x1in", [2, T, D], F32, kind="ExternalInput").ap()
    out = nc.dram_tensor("out", [2, T, D], F32, kind="ExternalOutput").ap()
    scr_s = nc.dram_tensor("scr_s", [128, NTILE * D], F32, kind="Internal").ap()
    scr_xn = nc.dram_tensor("scr_xn", [T, D], F32, kind="Internal").ap()
    scr_q = nc.dram_tensor("scr_q", [128, 6, T], BF16, kind="Internal").ap()
    scr_k = nc.dram_tensor("scr_k", [128, 6, T], BF16, kind="Internal").ap()
    scr_v = nc.dram_tensor("scr_v", [6, T, 256], BF16, kind="Internal").ap()
    scr_g = nc.dram_tensor("scr_g", [T, 1536], BF16, kind="Internal").ap()
    P_scr_q1 = nc.dram_tensor("scr_q1", [128, NCH, T], BF16, kind="Internal").ap()

    P.w["scr_q1"] = P_scr_q1
    import os
    global DEBUG
    if os.environ.get("KDEBUG"):
        DEBUG = {"s": nc.dram_tensor("dbg_s", [2, 128, NTILE * D], F32, kind="ExternalOutput").ap(),
                 "y": nc.dram_tensor("dbg_y", [2, 128, NTILE, D], F32, kind="ExternalOutput").ap(),
                 "q": nc.dram_tensor("dbg_q", [128, 6, T], BF16, kind="ExternalOutput").ap(),
                 "k": nc.dram_tensor("dbg_k", [128, 6, T], BF16, kind="ExternalOutput").ap(),
                 "v": nc.dram_tensor("dbg_v", [6, T, 256], BF16, kind="ExternalOutput").ap(),
                 "bg": nc.dram_tensor("dbg_bg", [2, 128, NTILE, 6], F32, kind="ExternalOutput").ap(),
                 "ed": nc.dram_tensor("dbg_ed", [128, 768], F32, kind="ExternalOutput").ap(),
                 "xp": nc.dram_tensor("dbg_xp", [128, 6, 128], BF16, kind="ExternalOutput").ap(),
                 "us": nc.dram_tensor("dbg_us", [128, 6, 256], BF16, kind="ExternalOutput").ap(),
                 "os": nc.dram_tensor("dbg_os", [128, 1536], F32, kind="ExternalOutput").ap(),
                 "bb": nc.dram_tensor("dbg_bb", [128, 6, 128], BF16, kind="ExternalOutput").ap(),
                 "xp2": nc.dram_tensor("dbg_xp2", [128, 6, 128], BF16, kind="ExternalOutput").ap(),
                 "pkm": nc.dram_tensor("dbg_pkm", [128, 768], F32, kind="ExternalOutput").ap(),
                 "bcol": nc.dram_tensor("dbg_bcol", [128, 768], F32, kind="ExternalOutput").ap()}
    else:
        DEBUG = None
    with ctx:
        buf_a = P.sbuf(ctx, "BUFA", [128, NTILE, D], F32)
        xn_fm = P.sbuf(ctx, "XN", [128, NCH, T], BF16)
        load_consts(P, BASE_CONSTS)
        stage_l1_small(P)
        beta_all = P.sbuf(ctx, "betaA", [128, NTILE, 6], F32)
        g_all = P.sbuf(ctx, "gA", [128, NTILE, 6], F32)
        for seq in range(2):
            P.dbg_seq = seq
            emit_ln(P, P.w["x1in"][seq], xn_fm, xn32_scr=scr_xn)
            barrier(P)
            emit_l1_attn_passA(P, xn_fm, scr_q, scr_k, scr_v, scr_g, beta_all, g_all)
            barrier(P)
            if DEBUG is not None and seq == 0:
                with nc.Block() as blk:
                    @blk.sync
                    def _(e):
                        e.dma_start(DEBUG["q"], scr_q).then_inc(P.s["sd"].h, 16)
                        e.dma_start(DEBUG["k"], scr_k).then_inc(P.s["sd"].h, 16)
                        e.dma_start(DEBUG["v"], scr_v).then_inc(P.s["sd"].h, 16)
                        e.dma_start(DEBUG["bg"][0], beta_all[:]).then_inc(P.s["sd"].h, 16)
                        e.dma_start(DEBUG["bg"][1], g_all[:]).then_inc(P.s["sd"].h, 16)
                        P.s["sd"].v += 80
                        e.wait_ge(P.s["sd"].h, P.s["sd"].v)
            emit_l1_attn_passB(P, buf_a, scr_q, scr_k, scr_v, scr_g, beta_all, g_all)
            barrier(P)
            park_s(P, buf_a, scr_s)
            with nc.Block() as blk:
                @blk.sync
                def _(e):
                    e.dma_start(buf_a[:],
                                scr_xn.rearrange("(t p) d -> p t d", p=128)).then_inc(P.s["sd"].h, 16)
                    P.s["sd"].v += 16
                    e.wait_ge(P.s["sd"].h, P.s["sd"].v)
            _emit_rms(P, buf_a, xn_fm, "ln1b")
            barrier(P)
            emit_l1_ffn_attn(P, xn_fm, buf_a)
            barrier(P)
            _emit_rms(P, buf_a, xn_fm, "ln2b")
            barrier(P)
            emit_l1_moe(P, xn_fm, buf_a)
            barrier(P)
            if DEBUG is not None:
                with nc.Block() as blk:
                    @blk.sync
                    def _(e):
                        e.dma_start(DEBUG["s"][seq], scr_s).then_inc(P.s["sd"].h, 16)
                        e.dma_start(DEBUG["y"][seq], buf_a[:]).then_inc(P.s["sd"].h, 16)
                        P.s["sd"].v += 32
                        e.wait_ge(P.s["sd"].h, P.s["sd"].v)
            emit_combine(P, P.w["x1in"][seq], scr_s, buf_a, out[seq])
            barrier(P)
        P.check()
    return nc



def build_layer0():
    nc = bass.Bass("TRN2", target_bir_lowering=False)
    ctx = ExitStack()
    P = Prog(nc, ctx)
    names = ["x0"] + [n for n in WSHAPES if n.startswith("L0_")] + BASE_CONSTS
    for n in names:
        shape, dt = WSHAPES[n]
        P.w[n] = nc.dram_tensor(n, list(shape), dt, kind="ExternalInput").ap()
    x1_out = nc.dram_tensor("x1", [2, T, D], F32, kind="ExternalOutput").ap()
    global DEBUG
    import os
    if os.environ.get("KDEBUG"):
        DEBUG = {"srow": nc.dram_tensor("dbg_srow", [2, 16, 1, 16], F32, kind="ExternalOutput").ap(),
                 "q": nc.dram_tensor("dbg_q", [2, 128, T], BF16, kind="ExternalOutput").ap(),
                 "k": nc.dram_tensor("dbg_k", [2, 128, T], BF16, kind="ExternalOutput").ap()}
    scr_s = nc.dram_tensor("scr_s", [128, NTILE * D], F32, kind="Internal").ap()

    with ctx:
        buf_a = P.sbuf(ctx, "BUFA", [128, NTILE, D], F32)
        xn_fm = P.sbuf(ctx, "XN", [128, NCH, T], BF16)
        load_consts(P, BASE_CONSTS)
        stage_l0_small(P)
        stage_l0_big(P)
        combine = P.sbuf(ctx, "comb", [128, NTILE, 16], F32)
        for seq in range(2):
            DBGSEQ[0] = seq
            emit_ln(P, P.w["x0"][seq], xn_fm)
            barrier(P)
            emit_l0_attn(P, xn_fm, buf_a)
            barrier(P)
            park_s(P, buf_a, scr_s)
            emit_l0_gating(P, xn_fm, combine)
            barrier(P)
            emit_l0_ffn(P, xn_fm, buf_a, combine)
            barrier(P)
            emit_combine(P, P.w["x0"][seq], scr_s, buf_a, x1_out[seq])
            barrier(P)
        P.check()
    return nc



def build_full():
    global DEBUG
    DEBUG = None
    nc = bass.Bass("TRN2", target_bir_lowering=False)
    ctx = ExitStack()
    P = Prog(nc, ctx)
    names = (["x0", "c_ropeC", "c_ropeS"] + BASE_CONSTS +
             [n for n in WSHAPES if n.startswith("L0_") or n.startswith("L1_")])
    for n in names:
        shape, dt = WSHAPES[n]
        P.w[n] = nc.dram_tensor(n, list(shape), dt, kind="ExternalInput").ap()
    out = nc.dram_tensor("out", [2, T, D], F32, kind="ExternalOutput").ap()
    x1scr = nc.dram_tensor("x1scr", [2, T, D], F32, kind="Internal").ap()
    scr_s = nc.dram_tensor("scr_s", [128, NTILE * D], F32, kind="Internal").ap()
    scr_xn = nc.dram_tensor("scr_xn", [T, D], F32, kind="Internal").ap()
    scr_q = nc.dram_tensor("scr_q", [128, 6, T], BF16, kind="Internal").ap()
    scr_k = nc.dram_tensor("scr_k", [128, 6, T], BF16, kind="Internal").ap()
    scr_v = nc.dram_tensor("scr_v", [6, T, 256], BF16, kind="Internal").ap()
    scr_g = nc.dram_tensor("scr_g", [T, 1536], BF16, kind="Internal").ap()
    P.w["scr_q1"] = nc.dram_tensor("scr_q1", [128, NCH, T], BF16, kind="Internal").ap()
    with ctx:
        buf_a = P.sbuf(ctx, "BUFA", [128, NTILE, D], F32)
        xn_fm = P.sbuf(ctx, "XN", [128, NCH, T], BF16)
        P.sb["joinT"] = P.sbuf(ctx, "joinT", [1, 8], F32)
        stage_l0_cast(P)
        load_consts(P, BASE_CONSTS)
        stage_l0_small(P)
        stage_l1_small(P)
        stage_l0_big(P)
        combine0 = P.sbuf(ctx, "comb", [128, NTILE, 16], F32)
        beta_all = P.sbuf(ctx, "betaA", [128, NTILE, 6], F32)
        g_all = P.sbuf(ctx, "gA", [128, NTILE, 6], F32)
        for seq in range(2):
            emit_ln(P, P.w["x0"][seq], xn_fm)
            emit_l0_attn2(P, xn_fm, buf_a)
            barrier(P)
            park_s(P, buf_a, scr_s)
            emit_l0_gating(P, xn_fm, combine0)
            barrier(P)
            emit_l0_ffn(P, xn_fm, buf_a, combine0)
            barrier(P)
            emit_combine(P, P.w["x0"][seq], scr_s, buf_a, x1scr[seq])
            barrier(P)
        P.check()
        full_barrier(P)
        fresh_sems(P)
        for seq in range(2):
            emit_ln(P, x1scr[seq], xn_fm, xn32_scr=scr_xn)
            emit_l1_attn_passA2(P, xn_fm, buf_a, scr_q, scr_k, scr_v, scr_g,
                                beta_all, g_all)
            barrier(P)
            emit_l1_attn_passB(P, buf_a, scr_q, scr_k, scr_v, scr_g, beta_all, g_all)
            barrier(P)
            park_s(P, buf_a, scr_s)
            with nc.Block() as blk:
                @blk.sync
                def _(e):
                    e.dma_start(buf_a[:],
                                scr_xn.rearrange("(t p) d -> p t d", p=128)).then_inc(P.s["sd"].h, 16)
                    P.s["sd"].v += 16
                    e.wait_ge(P.s["sd"].h, P.s["sd"].v)
            _emit_rms(P, buf_a, xn_fm, "ln1b")
            barrier(P)
            emit_l1_ffn_attn(P, xn_fm, buf_a)
            barrier(P)
            _emit_rms(P, buf_a, xn_fm, "ln2b")
            barrier(P)
            emit_l1_moe(P, xn_fm, buf_a)
            barrier(P)
            emit_combine(P, x1scr[seq], scr_s, buf_a, out[seq])
            barrier(P)
        P.check()
    return nc



_HOST_C = None
_PROGS = {}

def _consts():
    global _HOST_C
    if _HOST_C is None:
        import ml_dtypes
        _HOST_C = host_constants()
        for n, dt in CONST_DTYPES.items():
            if dt == BF16:
                _HOST_C[n] = _HOST_C[n].astype(ml_dtypes.bfloat16)
    return _HOST_C


def kernel(**inputs):
    consts = _consts()
    if "full" not in _PROGS:
        _PROGS["full"] = build_full()
    x = np.ascontiguousarray(np.asarray(inputs["L0_attn_x"], dtype=np.float32))
    w0T = np.empty((51, 1024, 1024), dtype=np.float32)
    w0T[0:16] = np.asarray(inputs["L0_ffn_w_gate"], dtype=np.float32).transpose(0, 2, 1)
    w0T[16:32] = np.asarray(inputs["L0_ffn_w_up"], dtype=np.float32).transpose(0, 2, 1)
    w0T[32:48] = np.asarray(inputs["L0_ffn_w_down"], dtype=np.float32).transpose(0, 2, 1)
    w0T[48] = np.asarray(inputs["L0_ffn_shared_gate_w"], dtype=np.float32).T
    w0T[49] = np.asarray(inputs["L0_ffn_shared_up_w"], dtype=np.float32).T
    w0T[50] = np.asarray(inputs["L0_ffn_shared_down_w"], dtype=np.float32).T
    maps = []
    for i in range(8):
        m = {"x0": x[2 * i:2 * i + 2], "L0_w0T": w0T}
        for n in inputs:
            if n in WSHAPES:
                m[n] = inputs[n]
        for n in BASE_CONSTS + ["c_ropeC", "c_ropeS"]:
            m[n] = consts[n]
        maps.append(m)
    res = run_bass_kernel_spmd(_PROGS["full"], maps, core_ids=list(range(8)))
    out = np.concatenate([np.asarray(res.results[i]["out"], dtype=np.float32)
                          for i in range(8)], axis=0)
    return out


# version 14
# speedup vs baseline: 1.3861x; 1.0772x over previous
import numpy as np
from contextlib import ExitStack

import concourse.bass as bass
import concourse.mybir as mybir
from concourse.bass_utils import run_bass_kernel_spmd

F32 = mybir.dt.float32
BF16 = mybir.dt.bfloat16
AF = mybir.ActivationFunctionType
ALU = mybir.AluOpType
AX = mybir.AxisListType.X

T = 2048
D = 1024
NTILE = T // 128
NCH = D // 128
RMS_EPS0 = 1.1920929e-07
DEBUG = None
DBGSEQ = [0]

ENG_METHOD = {"pe": "tensor", "act": "scalar", "dve": "vector",
              "pool": "gpsimd", "sync": "sync"}
SEM_TOTALS = []


def host_constants():
    c = {}
    c["c_id32"] = np.eye(128, dtype=np.float32)
    c["c_idbf"] = np.eye(128, dtype=np.float32)
    c["c_ones_r"] = np.ones((1, 128), dtype=np.float32)
    c["c_ones_c"] = np.ones((128, 1), dtype=np.float32)
    k = np.arange(128)[:, None]
    q = np.arange(128)[None, :]
    c["c_striu"] = (k < q).astype(np.float32)
    c["c_triu"] = (k <= q).astype(np.float32)
    c["c_dmall"] = (k <= (np.arange(896)[None, :] - 384)).astype(np.float32)
    d_rope = 32
    t_idx = np.arange(T, dtype=np.float32)
    freqs = (1.0 / (100000.0 ** (np.arange(0, d_rope, 2, dtype=np.float32) / d_rope))).astype(np.float32)
    ang = (t_idx[:, None] * freqs[None, :]).astype(np.float32)
    cos = np.cos(ang).astype(np.float32)
    sin = np.sin(ang).astype(np.float32)
    C = np.ones((128, T), dtype=np.float32)
    S = np.zeros((128, T), dtype=np.float32)
    for bb in (32, 96):
        for dd in range(d_rope):
            C[bb + dd, :] = cos[:, dd // 2]
            S[bb + dd, :] = sin[:, dd // 2]
    c["c_ropeC"] = C
    c["c_ropeS"] = S
    Sp = np.zeros((128, 128), dtype=np.float32)
    for bb in (32, 96):
        for p in range(d_rope // 2):
            r0 = bb + 2 * p
            Sp[r0 + 1, r0] = -1.0
            Sp[r0, r0 + 1] = 1.0
    c["c_sperm"] = Sp
    Sp32 = np.zeros((32, 32), dtype=np.float32)
    for p in range(d_rope // 2):
        Sp32[2 * p + 1, 2 * p] = -1.0
        Sp32[2 * p, 2 * p + 1] = 1.0
    c["c_sperm32"] = Sp32
    hi = np.zeros((6, 768), dtype=np.float32)
    for h in range(6):
        hi[h, h * 128:(h + 1) * 128] = 1.0
    c["c_hind"] = hi
    c["c_ones6"] = np.ones((6, 128), dtype=np.float32)
    sel2 = np.zeros((128, 2), dtype=np.float32)
    sel2[0:64, 0] = 1.0
    sel2[64:128, 1] = 1.0
    c["c_sel2"] = sel2
    c["c_exp2"] = sel2.T.copy()
    c["c_qsc"] = np.full((1, 128), 128.0 ** -0.5, dtype=np.float32)
    return c

CONST_DTYPES = {"c_idbf": BF16, "c_sperm": BF16, "c_sperm32": BF16,
                "c_ropeC": BF16, "c_ropeS": BF16, "c_dmall": BF16}
CONST_SHAPES = {"c_id32": [128, 128], "c_idbf": [128, 128],
                "c_ones_r": [1, 128], "c_ones_c": [128, 1],
                "c_striu": [128, 128], "c_triu": [128, 128],
                "c_dmall": [128, 896],
                "c_ropeC": [128, T], "c_ropeS": [128, T],
                "c_sperm": [128, 128], "c_sperm32": [32, 32],
                "c_hind": [6, 768], "c_ones6": [6, 128],
                "c_sel2": [128, 2], "c_exp2": [2, 128], "c_qsc": [1, 128]}
BASE_CONSTS = ["c_id32", "c_idbf", "c_ones_r", "c_ones_c", "c_striu",
               "c_triu", "c_dmall", "c_sperm",
               "c_sperm32", "c_hind", "c_ones6", "c_sel2", "c_exp2", "c_qsc"]


def dm_mask(P, j):
    return P.c["c_dmall"][:, 384 - 128 * j:896 - 128 * j]

WSHAPES = {
    "x0": ([2, T, D], F32),
    "L0_attn_W_DQ": ([512, 64], F32), "L0_attn_W_UQ": ([64, 512], F32),
    "L0_attn_W_DKV": ([1024, 64], F32), "L0_attn_W_UKV": ([128, 1024], F32),
    "L0_attn_W_QR": ([64, 64], F32), "L0_attn_W_KR": ([64, 64], F32),
    "L0_attn_W_O": ([64, 64], F32),
    "L0_attn_q_norm_w": ([64], F32), "L0_attn_k_norm_w": ([64], F32),
    "L0_ffn_gate_w": ([16, 1024], F32), "L0_ffn_expert_bias": ([16], F32),
    "L0_w0T": ([51, 1024, 1024], F32),
    "L1_attn_Wq": ([1024, 768], F32), "L1_attn_Wk": ([1024, 768], F32),
    "L1_attn_Wv": ([1024, 1536], F32), "L1_attn_Wb": ([1024, 6], F32),
    "L1_attn_Wa": ([1024, 6], F32), "L1_attn_A_log": ([6], F32),
    "L1_attn_dt_bias": ([6], F32),
    "L1_attn_conv_q": ([768, 4], F32), "L1_attn_conv_k": ([768, 4], F32),
    "L1_attn_conv_v": ([1536, 4], F32),
    "L1_attn_Wg": ([1024, 1536], F32), "L1_attn_norm_w": ([256], F32),
    "L1_attn_Wo": ([1536, 1024], F32),
    "L1_ffn_w_ln1": ([1024], F32), "L1_ffn_Wq": ([1024, 1024], F32),
    "L1_ffn_Wkva": ([1024, 160], F32), "L1_ffn_Wkvb": ([128, 512], F32),
    "L1_ffn_Wo": ([1024, 1024], F32), "L1_ffn_w_ln2": ([1024], F32),
    "L1_ffn_Wgate": ([1024, 8], F32), "L1_ffn_expert_bias": ([8], F32),
    "L1_ffn_Wg": ([8, 1024, 512], F32), "L1_ffn_Wu": ([8, 1024, 512], F32),
    "L1_ffn_Wd": ([8, 512, 1024], F32),
    "L1_ffn_sWg": ([1, 1024, 512], F32), "L1_ffn_sWu": ([1, 1024, 512], F32),
    "L1_ffn_sWd": ([1, 512, 1024], F32),
}
for _n, _s in CONST_SHAPES.items():
    WSHAPES[_n] = (_s, CONST_DTYPES.get(_n, F32))



class Sem:
    def __init__(self, ctx, nc, name):
        self.h = ctx.enter_context(nc.semaphore(name))
        self.v = 0


class Prog:
    def __init__(self, nc, ctx):
        self.nc = nc
        self.ctx = ctx
        self.w = {}
        self.c = {}
        self.sb = {}
        self.uid = 0
        self.s = {n: Sem(ctx, nc, "S_" + n) for n in
                  ["sd", "sg", "sq", "pe", "act", "dve", "pool",
                   "p0", "p1", "p2", "p3"]}

    def name(self, base):
        self.uid += 1
        return f"{base}_{self.uid}"

    def sbuf(self, ctx, base, shape, dtype):
        return ctx.enter_context(self.nc.sbuf_tensor(self.name(base), shape, dtype))

    def psum(self, ctx, base, shape, dtype=F32):
        return ctx.enter_context(self.nc.psum_tensor(self.name(base), shape, dtype))

    def check(self):
        SEM_TOTALS.append({n: sm.v for n, sm in self.s.items()})
        for n, sm in self.s.items():
            assert sm.v < 60000, f"sem {n} total {sm.v}"


def chain(P, steps):
    nc = P.nc
    per_eng = {}
    prev = getattr(P, "last", None)
    prev_ename = None
    for step in steps:
        if len(step) == 2:
            eng, fn = step
            ndma = 1
        else:
            eng, fn, ndma = step
        if eng == "sync_dma":
            ename, sem, inc = "sync", P.s["sd"], 16 * ndma
        elif eng == "pool_dma":
            ename, sem, inc = "pool", P.s["sg"], 16 * ndma
        elif eng == "act_dma":
            ename, sem, inc = "act", P.s["sq"], 16 * ndma
        elif eng == "pool":
            ename, sem, inc = "pool", P.s["pool"], 1
        else:
            ename, sem, inc = eng, P.s[eng], 1
        waits = []
        dma_like = inc >= 16
        if prev is not None:
            waits.append(prev)
        sem.v += inc
        per_eng.setdefault(ename, []).append((waits, fn, sem, sem.v, inc, ndma))
        prev = (sem, sem.v)
        prev_ename = "_dma_" + ename if dma_like else ename
    with nc.Block() as blk:
        for ename, items in per_eng.items():
            def body(e, items=items):
                for waits, fn, sem, val, inc, ndma in items:
                    for (wsem, wval) in waits:
                        e.wait_ge(wsem.h, wval)
                    P.cur_dma = (sem.h, 16)
                    inst = fn(e)
                    inst.then_inc(sem.h, 16 if inc >= 16 else inc)
            getattr(blk, ENG_METHOD[ename])(body)
    P.last = prev


def chain_multi(P, streams):
    nc = P.nc
    entry = getattr(P, "last", None)
    per_eng = {}
    prevs = [entry for _ in streams]
    idx = [0] * len(streams)
    remaining = sum(len(s) for s in streams)
    while remaining:
        for i, s in enumerate(streams):
            if idx[i] >= len(s):
                continue
            step = s[idx[i]]
            idx[i] += 1
            remaining -= 1
            if len(step) == 2:
                eng, fn = step
                ndma = 1
            else:
                eng, fn, ndma = step
            if eng == "sync_dma":
                ename, sem, inc = "sync", P.s["sd"], 16 * ndma
            elif eng == "pool_dma":
                ename, sem, inc = "pool", P.s["sg"], 16 * ndma
            elif eng == "act_dma":
                ename, sem, inc = "act", P.s["sq"], 16 * ndma
            elif eng == "pool":
                ename, sem, inc = "pool", P.s["pool"], 1
            else:
                ename, sem, inc = eng, P.s[eng], 1
            waits = [prevs[i]] if prevs[i] is not None else []
            sem.v += inc
            per_eng.setdefault(ename, []).append((waits, fn, sem, sem.v, inc))
            prevs[i] = (sem, sem.v)
    with nc.Block() as blk:
        for ename, items in per_eng.items():
            def body(e, items=items):
                for waits, fn, sem, val, inc in items:
                    for (wsem, wval) in waits:
                        e.wait_ge(wsem.h, wval)
                    inst = fn(e)
                    inst.then_inc(sem.h, 16 if inc >= 16 else inc)
            getattr(blk, ENG_METHOD[ename])(body)
    jsem = P.s["pool"]
    jsem.v += 1
    jv = jsem.v
    tails = [p for p in prevs if p is not None]
    with nc.Block() as blk:
        @blk.gpsimd
        def _(e):
            for (wsem, wval) in tails:
                e.wait_ge(wsem.h, wval)
            e.memset(P.sb["joinT"][:], 0.0).then_inc(jsem.h, 1)
    P.last = (jsem, jv)


def barrier(P):
    nc = P.nc
    last = getattr(P, "last", None)
    with nc.Block() as blk:
        @blk.sync
        def _(e):
            if last is not None:
                e.wait_ge(last[0].h, last[1])
    P.last = None


def full_barrier(P):
    nc = P.nc
    sems = [(s.h, s.v) for s in P.s.values() if s.v > 0]
    with nc.Block() as blk:
        for ename in ("tensor", "scalar", "vector", "gpsimd", "sync"):
            def body(e, sems=sems):
                for h, v in sems:
                    e.wait_ge(h, v)
            getattr(blk, ename)(body)
    P.last = None


def fresh_sems(P):
    P.s = {n: Sem(P.ctx, P.nc, P.name("S2_" + n)) for n in
           ["sd", "sg", "sq", "pe", "act", "dve", "pool",
            "p0", "p1", "p2", "p3"]}



def load_consts(P, names):
    nc = P.nc
    for n in names:
        P.c[n] = P.sbuf(P.ctx, n, CONST_SHAPES[n], CONST_DTYPES.get(n, F32))
    with nc.Block() as blk:
        @blk.sync
        def _(e):
            for n in names:
                if CONST_DTYPES.get(n, F32) == F32:
                    e.dma_start(P.c[n][:], P.w[n]).then_inc(P.s["sd"].h, 16)
                    P.s["sd"].v += 16
        @blk.gpsimd
        def _(e):
            for n in names:
                if CONST_DTYPES.get(n, F32) == BF16:
                    e.dma_start(P.c[n][:], P.w[n]).then_inc(P.s["sg"].h, 16)
                    P.s["sg"].v += 16
    with nc.Block() as blk:
        @blk.vector
        def _(e):
            e.wait_ge(P.s["sd"].h, P.s["sd"].v)
            e.wait_ge(P.s["sg"].h, P.s["sg"].v)


def stage_l0_small(P):
    nc = P.nc
    for nm, shape, dt in [("weff_q", [128, 64], BF16), ("weff_kv", [128, 128], BF16),
                          ("wqrT", [128, 64], BF16), ("wkrT", [128, 64], BF16),
                          ("woT", [64, 64], BF16), ("qnw", [64, 1], F32),
                          ("knw", [64, 1], F32), ("qnw2", [128, 1], F32),
                          ("knw2", [128, 1], F32), ("weff_qbd", [128, 128], BF16),
                          ("weff_kbd", [128, 128], BF16)]:
        P.sb[nm] = P.sbuf(P.ctx, nm, shape, dt)
    with ExitStack() as ctx:
        wdq = P.sbuf(ctx, "wdq", [128, 4, 64], F32)
        wuq = P.sbuf(ctx, "wuq", [64, 512], F32)
        wuqT = P.sbuf(ctx, "wuqT", [128, 4, 64], F32)
        wdkv = P.sbuf(ctx, "wdkv", [128, 8, 64], F32)
        wukv = P.sbuf(ctx, "wukv", [128, 1024], F32)
        wukvT = P.sbuf(ctx, "wukvT", [128, 8, 128], F32)
        sm64 = P.sbuf(ctx, "sm64", [64, 64], F32)
        pt = P.psum(ctx, "pt", [128, 128], F32)
        pm = P.psum(ctx, "pm", [64, 128], F32)

        steps = []

        def load6(e):
            sh = P.s["sd"].h
            e.dma_start(wdq[:], P.w["L0_attn_W_DQ"].rearrange("(c p) d -> p c d", p=128)).then_inc(sh, 16)
            e.dma_start(wuq[:], P.w["L0_attn_W_UQ"]).then_inc(sh, 16)
            e.dma_start(wdkv[:], P.w["L0_attn_W_DKV"].rearrange("(c p) d -> p c d", p=128)).then_inc(sh, 16)
            e.dma_start(wukv[:], P.w["L0_attn_W_UKV"]).then_inc(sh, 16)
            e.dma_start(P.sb["qnw"][:], P.w["L0_attn_q_norm_w"].rearrange("(p o) -> p o", o=1)).then_inc(sh, 16)
            e.dma_start(P.sb["knw"][:], P.w["L0_attn_k_norm_w"].rearrange("(p o) -> p o", o=1)).then_inc(sh, 16)
            e.dma_start(P.sb["qnw2"][0:64, :], P.w["L0_attn_q_norm_w"].rearrange("(p o) -> p o", o=1)).then_inc(sh, 16)
            e.dma_start(P.sb["qnw2"][64:128, :], P.w["L0_attn_q_norm_w"].rearrange("(p o) -> p o", o=1)).then_inc(sh, 16)
            e.dma_start(P.sb["knw2"][0:64, :], P.w["L0_attn_k_norm_w"].rearrange("(p o) -> p o", o=1)).then_inc(sh, 16)
            return e.dma_start(P.sb["knw2"][64:128, :], P.w["L0_attn_k_norm_w"].rearrange("(p o) -> p o", o=1))
        steps.append(("sync_dma", load6, 10))

        for j in range(4):
            steps.append(("pe", lambda e, j=j: e.transpose(
                pt[0:128, 0:64], wuq[0:64, j * 128:(j + 1) * 128], P.c["c_id32"][0:64, 0:64])))
            steps.append(("act", lambda e, j=j: e.copy(wuqT[:, j, :], pt[0:128, 0:64])))
        for j in range(8):
            steps.append(("pe", lambda e, j=j: e.transpose(
                pt[:], wukv[:, j * 128:(j + 1) * 128], P.c["c_id32"][:])))
            steps.append(("act", lambda e, j=j: e.copy(wukvT[:, j, :], pt[:])))

        def mm_weffq(e):
            for j in range(4):
                i = e.matmul(pm[0:64, 0:64], wdq[:, j, :], wuqT[:, j, :],
                             start=(j == 0), stop=(j == 3))
            return i
        steps.append(("pe", mm_weffq))
        steps.append(("act", lambda e: (e.copy(P.sb["weff_q"][0:64, :], pm[0:64, 0:64]),
                                        e.copy(P.sb["weff_q"][64:128, :], pm[0:64, 0:64]))[-1]))

        def mm_weffkv(e):
            for j in range(8):
                i = e.matmul(pm[0:64, :], wdkv[:, j, :], wukvT[:, j, :],
                             start=(j == 0), stop=(j == 7))
            return i
        steps.append(("pe", mm_weffkv))
        steps.append(("act", lambda e: (e.copy(P.sb["weff_kv"][0:64, :], pm[0:64, :]),
                                        e.copy(P.sb["weff_kv"][64:128, :], pm[0:64, :]))[-1]))

        for wname, dst, dual in [("L0_attn_W_QR", "wqrT", True),
                                 ("L0_attn_W_KR", "wkrT", True),
                                 ("L0_attn_W_O", "woT", False)]:
            steps.append(("sync_dma", lambda e, wname=wname: e.dma_start(sm64[:], P.w[wname])))
            steps.append(("pe", lambda e: e.transpose(pt[0:64, 0:64], sm64[:], P.c["c_id32"][0:64, 0:64])))
            if dual:
                steps.append(("act", lambda e, dst=dst: (
                    e.copy(P.sb[dst][0:64, :], pt[0:64, 0:64]),
                    e.copy(P.sb[dst][64:128, :], pt[0:64, 0:64]))[-1]))
            else:
                steps.append(("act", lambda e, dst=dst: e.copy(P.sb[dst][:], pt[0:64, 0:64])))
        steps.append(("pool", lambda e: (
            e.memset(P.sb["weff_qbd"][:], 0.0),
            e.memset(P.sb["weff_kbd"][:], 0.0))[-1]))
        steps.append(("act", lambda e: (
            e.copy(P.sb["weff_qbd"][0:64, 0:64], P.sb["weff_q"][0:64, :]),
            e.copy(P.sb["weff_qbd"][64:128, 64:128], P.sb["weff_q"][64:128, :]),
            e.copy(P.sb["weff_kbd"][0:64, 0:64], P.sb["weff_kv"][0:64, 0:64]),
            e.copy(P.sb["weff_kbd"][64:128, 64:128], P.sb["weff_kv"][64:128, 0:64]))[-1]))
        chain(P, steps)


def stage_l0_cast(P):
    nc = P.nc
    P.w["scr_w0"] = nc.dram_tensor("scr_w0", [51, 1024, 1024], BF16, kind="Internal").ap()
    P.s["ws"] = Sem(P.ctx, nc, "ws")
    ws = P.s["ws"]
    order = [48, 49, 50]
    for e in range(16):
        order += [e, 16 + e, 32 + e]
    with nc.Block() as blk:
        @blk.gpsimd
        def _(e):
            for m in order:
                e.dma_start(P.w["scr_w0"][m], P.w["L0_w0T"][m]).then_inc(ws.h, 16)
    ws.v = 16 * len(order)


def stage_l0_big(P):
    nc = P.nc

    P.sb["gwT"] = P.sbuf(P.ctx, "gwT", [128, 8, 16], BF16)
    P.sb["l0bias"] = P.sbuf(P.ctx, "l0bias", [128, 16], F32)
    with ExitStack() as ctx:
        gw = P.sbuf(ctx, "gw", [16, 1024], F32)
        brow = P.sbuf(ctx, "brow", [1, 16], F32)
        pt = P.psum(ctx, "ptg", [128, 16], F32)
        def load2(e):
            e.dma_start(gw[:], P.w["L0_ffn_gate_w"]).then_inc(P.s["sd"].h, 16)
            return e.dma_start(brow[:], P.w["L0_ffn_expert_bias"].rearrange("(o e) -> o e", o=1))
        steps = [("sync_dma", load2, 2)]
        for j in range(8):
            steps.append(("pe", lambda e, j=j: e.transpose(
                pt[:], gw[:, j * 128:(j + 1) * 128], P.c["c_id32"][0:16, 0:16])))
            steps.append(("act", lambda e, j=j: e.copy(P.sb["gwT"][:, j, :], pt[:, 0:16])))
        steps.append(("pe", lambda e: e.matmul(pt[:], P.c["c_ones_r"][:], brow[:], start=True, stop=True)))
        steps.append(("act", lambda e: e.copy(P.sb["l0bias"][:], pt[:, 0:16])))
        chain(P, steps)



def emit_ln(P, x_ap, xn_fm, xn32_scr=None):
    nc = P.nc
    with ExitStack() as ctx:
        xt = [P.sbuf(ctx, "lnx", [128, D], F32) for _ in range(2)]
        xnb = P.sbuf(ctx, "lnbf", [128, D], BF16)
        xn32 = P.sbuf(ctx, "ln32", [128, D], F32)
        xn32b = P.sbuf(ctx, "ln32b", [128, D], F32)
        sq = P.sbuf(ctx, "lnsq", [128, D], F32)
        st = P.sbuf(ctx, "lnst", [128, 8], F32)
        ptr = [P.psum(ctx, "lnp", [128, 128], BF16) for _ in range(2)]
        for i in range(NTILE):
            b = i % 2
            steps = []
            steps.append(("sync_dma", lambda e, i=i, b=b: e.dma_start(
                xt[b][:], x_ap[i * 128:(i + 1) * 128, :])))
            steps.append(("act", lambda e, b=b: e.activation(
                sq[:], xt[b][:], AF.Square, accum_out=st[:, 1:2])))
            steps.append(("dve", lambda e, b=b: e.reduce_sum(st[:, 0:1], xt[b][:], axis=AX)))
            steps.append(("dve", lambda e: e.tensor_scalar_mul(st[:, 2:3], st[:, 0:1], 1.0 / D)))
            steps.append(("dve", lambda e: e.tensor_mul(st[:, 4:5], st[:, 2:3], st[:, 2:3])))
            steps.append(("dve", lambda e: e.scalar_tensor_tensor(
                st[:, 3:4], st[:, 1:2], 1.0 / D, st[:, 4:5], ALU.mult, ALU.subtract)))
            steps.append(("dve", lambda e: e.tensor_scalar_add(st[:, 3:4], st[:, 3:4], 1e-5)))
            steps.append(("act", lambda e: e.activation(st[:, 5:6], st[:, 3:4], AF.Ln)))
            steps.append(("act", lambda e: e.activation(st[:, 5:6], st[:, 5:6], AF.Exp, scale=-0.5)))
            steps.append(("dve", lambda e, b=b: e.tensor_scalar_sub(xn32[:], xt[b][:], st[:, 2:3])))
            def norm(e, scr=(xn32_scr is not None)):
                i1 = e.tensor_scalar_mul(xnb[:], xn32[:], st[:, 5:6])
                if scr:
                    i1 = e.tensor_scalar_mul(xn32b[:], xn32[:], st[:, 5:6])
                return i1
            steps.append(("dve", norm))
            if xn32_scr is not None:
                steps.append(("act_dma", lambda e, i=i: e.dma_start(
                    xn32_scr[i * 128:(i + 1) * 128, :], xn32b[:])))
            for j in range(NCH):
                steps.append(("pe", lambda e, j=j: e.transpose(
                    ptr[j % 2][:], xnb[:, j * 128:(j + 1) * 128], P.c["c_idbf"][:])))
                steps.append(("dve", lambda e, i=i, j=j: e.tensor_copy(
                    xn_fm[:, j, i * 128:(i + 1) * 128], ptr[j % 2][:])))
            chain(P, steps)



def emit_l0_attn(P, xn_fm, s_buf):
    nc = P.nc
    with ExitStack() as ctx:
        q_t = P.sbuf(ctx, "qt", [128, T], BF16)
        k_t = P.sbuf(ctx, "kt", [128, T], BF16)
        v_tm = P.sbuf(ctx, "vtm", [128, NTILE, 65], BF16)
        vbf = P.sbuf(ctx, "vbf", [64, 512], BF16)
        qc_s = P.sbuf(ctx, "qcs", [64, 512], F32)
        sqs = P.sbuf(ctx, "sqs", [128, 512], F32)
        sqs2 = P.sbuf(ctx, "sqs2", [64, 512], F32)
        rrow = P.sbuf(ctx, "rrow", [1, 512], F32)
        srow = P.sbuf(ctx, "srow", [1, 16], F32)
        bnd = P.sbuf(ctx, "bnd", [128, 1], F32)
        ctxs = P.sbuf(ctx, "ctxs", [64, 512], BF16)
        invr = P.sbuf(ctx, "invr", [1, 512], F32)
        invc = P.sbuf(ctx, "invc", [128, 4], F32)
        p_t = [P.sbuf(ctx, "ptile", [128, 512], BF16) for _ in range(2)]

        pA = P.psum(ctx, "pA", [128, 512], F32)
        pB = P.psum(ctx, "pB", [128, 512], F32)
        pC = P.psum(ctx, "pC", [128, 512], F32)
        pD = P.psum(ctx, "pD", [128, 512], F32)
        pEbf = P.psum(ctx, "pEbf", [128, 128], BF16)
        pF = P.psum(ctx, "pF", [128, 72], F32)
        pkv = pA
        pq = pB[0:64, :]
        pbc = pC[0:64, :]
        prr = pC[0:1, :]
        pst = [pA, pB]
        pctx = pD[0:65, :]
        ptv = pEbf
        psh = pF[:, 0:64]
        pcl = pF[:, 64:72]

        pe, act, dve = P.s["pe"], P.s["act"], P.s["dve"]

        with nc.Block() as blk:
            @blk.gpsimd
            def _(e):
                e.memset(v_tm[:], 0.0)
                e.memset(v_tm[:, :, 64:65], 1.0)
        P.last = None

        for h in range(16):
            base = (h % 2) * 64
            chk = h // 2
            weff_q = P.sb["weff_q"][base:base + 64, :]
            weff_kv = P.sb["weff_kv"][base:base + 64, :]
            wkrT = P.sb["wkrT"][base:base + 64, :]

            for n in range(4):
                sl = slice(n * 512, (n + 1) * 512)
                xh_n = xn_fm[base:base + 64, chk, sl]
                steps = []
                steps.append(("pe", lambda e, xh_n=xh_n, weff_q=weff_q, weff_kv=weff_kv: (
                    e.matmul(pq[:], weff_q, xh_n, start=True, stop=True),
                    e.matmul(pkv[:], weff_kv, xh_n, start=True, stop=True))[-1]))
                steps.append(("act", lambda e: (
                    e.activation(sqs[0:64, :], pq[:], AF.Square),
                    e.activation(sqs2[:], pkv[0:64, :], AF.Square),
                    e.copy(vbf[:], pkv[64:128, :]))[-1]))
                steps.append(("pe", lambda e: e.matmul(prr[:], P.c["c_ones_c"][0:64, :], sqs[0:64, :], start=True, stop=True)))
                steps.append(("dve", lambda e: e.tensor_scalar_mul(rrow[:], prr[:], 1.0 / 64.0)))
                steps.append(("dve", lambda e: e.tensor_scalar_add(rrow[:], rrow[:], RMS_EPS0)))
                steps.append(("act", lambda e: e.activation(rrow[:], rrow[:], AF.Ln)))
                steps.append(("act", lambda e: e.activation(rrow[:], rrow[:], AF.Exp, scale=-0.5)))
                steps.append(("pe", lambda e: e.matmul(pbc[:], P.c["c_ones_r"][0:1, 0:64], rrow[:], start=True, stop=True)))
                steps.append(("dve", lambda e: e.tensor_scalar_mul(qc_s[:], pq[:], P.sb["qnw"][:, 0:1])))
                steps.append(("dve", lambda e, sl=sl: e.tensor_mul(q_t[0:64, sl], qc_s[:], pbc[:])))
                steps.append(("pe", lambda e: e.matmul(prr[:], P.c["c_ones_c"][0:64, :], sqs2[:], start=True, stop=True)))
                steps.append(("dve", lambda e: e.tensor_scalar_mul(rrow[:], prr[:], 1.0 / 64.0)))
                steps.append(("dve", lambda e: e.tensor_scalar_add(rrow[:], rrow[:], RMS_EPS0)))
                steps.append(("act", lambda e: e.activation(rrow[:], rrow[:], AF.Ln)))
                steps.append(("act", lambda e: e.activation(rrow[:], rrow[:], AF.Exp, scale=-0.5)))
                steps.append(("pe", lambda e: e.matmul(pbc[:], P.c["c_ones_r"][0:1, 0:64], rrow[:], start=True, stop=True)))
                steps.append(("dve", lambda e: e.tensor_scalar_mul(qc_s[:], pkv[0:64, :], P.sb["knw"][:, 0:1])))
                steps.append(("dve", lambda e, sl=sl: e.tensor_mul(k_t[0:64, sl], qc_s[:], pbc[:])))
                for jj in range(4):
                    steps.append(("pe", lambda e, jj=jj: e.transpose(
                        ptv[:, 0:64], vbf[:, jj * 128:(jj + 1) * 128], P.c["c_idbf"][0:64, 0:64])))
                    steps.append(("act", lambda e, n=n, jj=jj: e.copy(
                        v_tm[:, n * 4 + jj, 0:64], ptv[:, 0:64])))
                steps.append(("pe", lambda e, sl=sl, xh_n=xh_n, wkrT=wkrT: (
                    e.matmul(pq[:], P.sb["wqrT"][:], q_t[0:64, sl], start=True, stop=True),
                    e.matmul(pbc[:], wkrT, xh_n, start=True, stop=True))[-1]))
                steps.append(("act", lambda e, sl=sl: (
                    e.copy(q_t[64:128, sl], pq[:]),
                    e.copy(k_t[64:128, sl], pbc[:]))[-1]))
                chain(P, steps)

            steps = []
            for (tile, off) in ((q_t, 0), (k_t, 4)):
                for n in range(4):
                    steps.append(("act", lambda e, tile=tile, n=n: e.activation(
                        sqs[:], tile[:, n * 512:(n + 1) * 512], AF.Square)))
                    steps.append(("pe", lambda e: e.matmul(
                        prr[:], P.c["c_ones_c"][:], sqs[:], start=True, stop=True)))
                    steps.append(("dve", lambda e, off=off, n=n: e.reduce_max(
                        srow[:, off + n:off + n + 1], prr[:], axis=AX)))
            steps.append(("dve", lambda e: (
                e.reduce_max(srow[:, 8:9], srow[:, 0:4], axis=AX),
                e.reduce_max(srow[:, 9:10], srow[:, 4:8], axis=AX))[-1]))
            steps.append(("dve", lambda e: e.tensor_mul(srow[:, 8:9], srow[:, 8:9], srow[:, 9:10])))
            steps.append(("act", lambda e: e.activation(srow[:, 8:9], srow[:, 8:9], AF.Ln)))
            steps.append(("act", lambda e: e.activation(srow[:, 8:9], srow[:, 8:9], AF.Exp, scale=0.5)))
            steps.append(("act", lambda e: e.mul(srow[:, 8:9], srow[:, 8:9], -0.125)))
            steps.append(("pe", lambda e: e.matmul(pcl[:, 0:1], P.c["c_ones_r"][:], srow[:, 8:9], start=True, stop=True)))
            steps.append(("dve", lambda e: e.tensor_copy(bnd[:], pcl[:, 0:1])))
            if DEBUG is not None:
                steps.append(("sync_dma", lambda e, h=h: e.dma_start(
                    DEBUG["srow"][DBGSEQ[0], h], srow[:])))
                if h == 0:
                    steps.append(("sync_dma", lambda e: e.dma_start(
                        DEBUG["q"][DBGSEQ[0]], q_t[:])))
                    steps.append(("sync_dma", lambda e: e.dma_start(
                        DEBUG["k"][DBGSEQ[0]], k_t[:])))
            chain(P, steps)

            for qc in range(4):
                nkt = (qc + 1) * 4
                qsl = slice(qc * 512, (qc + 1) * 512)
                pe0 = pe.v
                act0 = act.v
                dve0 = dve.v
                ST_m = {}
                PV_m = {}
                m = pe0
                for kt in range(nkt):
                    m += 1
                    ST_m[kt] = m
                    if kt >= 1:
                        m += 1
                        PV_m[kt - 1] = m
                m += 1
                PV_m[nkt - 1] = m
                exp_m = {kt: act0 + kt + 1 for kt in range(nkt)}
                msk_m = {kt: dve0 + (kt - (nkt - 4)) + 1 for kt in range(nkt - 4, nkt)}

                def p_ready(kt):
                    if kt >= nkt - 4:
                        return (dve, msk_m[kt])
                    return (act, exp_m[kt])

                start_wait = P.last
                with nc.Block() as blk:
                    @blk.tensor
                    def _(e):
                        if start_wait is not None:
                            e.wait_ge(start_wait[0].h, start_wait[1])
                        for kt in range(nkt):
                            bb = kt % 2
                            if kt >= 2:
                                e.wait_ge(act.h, exp_m[kt - 2])
                            e.matmul(pst[bb][:], k_t[:, kt * 128:(kt + 1) * 128],
                                     q_t[:, qsl], start=True, stop=True).then_inc(pe.h, 1)
                            if kt >= 1:
                                ss, vv = p_ready(kt - 1)
                                e.wait_ge(ss.h, vv)
                                e.matmul(pctx[:], v_tm[:, kt - 1, :], p_t[(kt - 1) % 2][:],
                                         start=(kt - 1 == 0), stop=False).then_inc(pe.h, 1)
                        ss, vv = p_ready(nkt - 1)
                        e.wait_ge(ss.h, vv)
                        e.matmul(pctx[:], v_tm[:, nkt - 1, :], p_t[(nkt - 1) % 2][:],
                                 start=(nkt - 1 == 0), stop=True).then_inc(pe.h, 1)
                    @blk.scalar
                    def _(e):
                        for kt in range(nkt):
                            bb = kt % 2
                            e.wait_ge(pe.h, ST_m[kt])
                            if kt >= 2:
                                e.wait_ge(pe.h, PV_m[kt - 2])
                            e.activation(p_t[bb][:], pst[bb][:], AF.Exp, scale=0.125,
                                         bias=bnd[:, 0:1]).then_inc(act.h, 1)
                    @blk.vector
                    def _(e):
                        for kt in range(nkt - 4, nkt):
                            e.wait_ge(act.h, exp_m[kt])
                            j = kt - (nkt - 4)
                            e.tensor_mul(p_t[kt % 2][:], p_t[kt % 2][:],
                                         dm_mask(P, j)).then_inc(dve.h, 1)
                pe.v = m
                act.v = act0 + nkt
                dve.v = dve0 + 4
                P.last = (pe, m)

                steps = []
                steps.append(("dve", lambda e: e.reciprocal(invr[:], pctx[64:65, :])))
                for j in range(4):
                    steps.append(("pe", lambda e, j=j: e.transpose(
                        pcl[:, j:j + 1], invr[0:1, j * 128:(j + 1) * 128], P.c["c_id32"][0:1, 0:1])))
                    steps.append(("dve", lambda e, j=j: e.tensor_copy(invc[:, j:j + 1], pcl[:, j:j + 1])))
                steps.append(("act", lambda e: e.copy(ctxs[:], pctx[0:64, :])))
                for j in range(4):
                    steps.append(("pe", lambda e, j=j: e.matmul(
                        psh[:], ctxs[:, j * 128:(j + 1) * 128], P.sb["woT"][:],
                        start=True, stop=True)))
                    steps.append(("act", lambda e, qc=qc, j=j, h=h: e.activation(
                        s_buf[:, qc * 4 + j, h * 64:(h + 1) * 64], psh[:],
                        AF.Copy, scale=invc[:, j:j + 1])))
                chain(P, steps)


def _attn_pipe(P, kh, qh, vtm_h, bndcol, pst, pctx, p_t, qc):
    nc = P.nc
    pe, act, dve = P.s["pe"], P.s["act"], P.s["dve"]
    nkt = (qc + 1) * 4
    qsl = slice(qc * 512, (qc + 1) * 512)
    pe0, act0, dve0 = pe.v, act.v, dve.v
    ST_m = {}
    PV_m = {}
    m = pe0
    for kt in range(nkt):
        m += 1
        ST_m[kt] = m
        if kt >= 1:
            m += 1
            PV_m[kt - 1] = m
    m += 1
    PV_m[nkt - 1] = m
    exp_m = {kt: act0 + kt + 1 for kt in range(nkt)}
    msk_m = {kt: dve0 + (kt - (nkt - 4)) + 1 for kt in range(nkt - 4, nkt)}

    def p_ready(kt):
        if kt >= nkt - 4:
            return (dve, msk_m[kt])
        return (act, exp_m[kt])

    start_wait = P.last
    with nc.Block() as blk:
        @blk.tensor
        def _(e):
            if start_wait is not None:
                e.wait_ge(start_wait[0].h, start_wait[1])
            for kt in range(nkt):
                bb = kt % 2
                if kt >= 2:
                    e.wait_ge(act.h, exp_m[kt - 2])
                e.matmul(pst[bb][:], kh[:, kt * 128:(kt + 1) * 128],
                         qh[:, qsl], start=True, stop=True).then_inc(pe.h, 1)
                if kt >= 1:
                    ss, vv = p_ready(kt - 1)
                    e.wait_ge(ss.h, vv)
                    e.matmul(pctx[:], vtm_h[:, kt - 1, :], p_t[(kt - 1) % 2][:],
                             start=(kt - 1 == 0), stop=False).then_inc(pe.h, 1)
            ss, vv = p_ready(nkt - 1)
            e.wait_ge(ss.h, vv)
            e.matmul(pctx[:], vtm_h[:, nkt - 1, :], p_t[(nkt - 1) % 2][:],
                     start=(nkt - 1 == 0), stop=True).then_inc(pe.h, 1)
        @blk.scalar
        def _(e):
            for kt in range(nkt):
                bb = kt % 2
                e.wait_ge(pe.h, ST_m[kt])
                if kt >= 2:
                    e.wait_ge(pe.h, PV_m[kt - 2])
                e.activation(p_t[bb][:], pst[bb][:], AF.Exp, scale=0.125,
                             bias=bndcol).then_inc(act.h, 1)
        @blk.vector
        def _(e):
            for kt in range(nkt - 4, nkt):
                e.wait_ge(act.h, exp_m[kt])
                j = kt - (nkt - 4)
                e.tensor_mul(p_t[kt % 2][:], p_t[kt % 2][:],
                             dm_mask(P, j)).then_inc(dve.h, 1)
    pe.v = m
    act.v = act0 + nkt
    dve.v = dve0 + 4
    P.last = (pe, m)


def emit_l0_attn2(P, xn_fm, s_buf):
    nc = P.nc
    with ExitStack() as ctx:
        Qc2 = P.sbuf(ctx, "Qc2", [128, T], F32)
        Kc2 = P.sbuf(ctx, "Kc2", [128, T], F32)
        SQA = P.sbuf(ctx, "SQA", [128, T], F32)
        qhE = P.sbuf(ctx, "qhE", [128, T], BF16)
        qhO = P.sbuf(ctx, "qhO", [128, T], BF16)
        khE = P.sbuf(ctx, "khE", [128, T], BF16)
        khO = P.sbuf(ctx, "khO", [128, T], BF16)
        vtmE = P.sbuf(ctx, "vtmE", [128, NTILE, 65], BF16)
        vtmO = P.sbuf(ctx, "vtmO", [128, NTILE, 65], BF16)
        rr2q = P.sbuf(ctx, "rr2q", [2, T], F32)
        rr2k = P.sbuf(ctx, "rr2k", [2, T], F32)
        qmax = P.sbuf(ctx, "qmax", [1, 16], F32)
        mx = P.sbuf(ctx, "mx", [1, 8], F32)
        bnd2 = P.sbuf(ctx, "bnd2", [128, 2], F32)
        invrE = P.sbuf(ctx, "invrE", [1, 512], F32)
        invrO = P.sbuf(ctx, "invrO", [1, 512], F32)
        invc = P.sbuf(ctx, "invc", [128, 8], F32)
        ctxsE = P.sbuf(ctx, "ctxsE", [64, 512], BF16)
        ctxsO = P.sbuf(ctx, "ctxsO", [64, 512], BF16)
        p_t = [P.sbuf(ctx, "ptile", [128, 512], BF16) for _ in range(2)]
        pA = P.psum(ctx, "pA", [128, 512], F32)
        pB = P.psum(ctx, "pB", [128, 512], F32)
        pC = P.psum(ctx, "pC", [128, 512], F32)
        pD = P.psum(ctx, "pD", [128, 512], F32)
        pF = P.psum(ctx, "pF", [128, 72], F32)

        chain(P, [("dve", lambda e: (
            e.memset(vtmE[:], 0.0), e.memset(vtmE[:, :, 64:65], 1.0),
            e.memset(vtmO[:], 0.0), e.memset(vtmO[:, :, 64:65], 1.0))[-1])])

        for p in range(8):
            he, ho = 2 * p, 2 * p + 1
            steps = []
            for n in range(4):
                sl = slice(n * 512, (n + 1) * 512)
                pq = pA if n % 2 == 0 else pB
                pk = pC if n % 2 == 0 else pD
                steps.append(("pe", lambda e, sl=sl, pq=pq, pk=pk, p=p: (
                    e.matmul(pq[:], P.sb["weff_qbd"][:], xn_fm[:, p, sl], start=True, stop=True),
                    e.matmul(pk[:], P.sb["weff_kbd"][:], xn_fm[:, p, sl], start=True, stop=True))[-1]))
                steps.append(("act", lambda e, sl=sl, pq=pq, pk=pk: (
                    e.copy(Qc2[:, sl], pq[:]),
                    e.copy(Kc2[:, sl], pk[:]))[-1]))
            for half in range(2):
                t0 = half * 8
                def mmv(e, t0=t0, p=p):
                    for j in range(8):
                        tsl = slice((t0 + j) * 128, (t0 + j + 1) * 128)
                        e.matmul(pA[:, j * 64:(j + 1) * 64], xn_fm[0:64, p, tsl],
                                 P.sb["weff_kv"][0:64, 64:128], start=True, stop=True)
                        i1 = e.matmul(pB[:, j * 64:(j + 1) * 64], xn_fm[64:128, p, tsl],
                                      P.sb["weff_kv"][64:128, 64:128], start=True, stop=True)
                    return i1
                steps.append(("pe", mmv))
                steps.append(("act", lambda e, t0=t0: (
                    e.copy(vtmE[:, t0:t0 + 8, 0:64], pA[:].rearrange("p (a b) -> p a b", a=8)),
                    e.copy(vtmO[:, t0:t0 + 8, 0:64], pB[:].rearrange("p (a b) -> p a b", a=8)))[-1]))
            steps.append(("act", lambda e: e.activation(SQA[:], Qc2[:], AF.Square)))
            for n in range(4):
                sl = slice(n * 512, (n + 1) * 512)
                pX = pC if n % 2 == 0 else pD
                steps.append(("pe", lambda e, sl=sl, pX=pX: e.matmul(
                    pX[0:2, :], P.c["c_sel2"][:], SQA[:, sl], start=True, stop=True)))
                steps.append(("dve", lambda e, sl=sl, pX=pX: e.tensor_scalar_mul(
                    rr2q[:, sl], pX[0:2, :], 1.0 / 64.0)))
            steps.append(("act", lambda e: e.activation(SQA[:], Kc2[:], AF.Square)))
            for n in range(4):
                sl = slice(n * 512, (n + 1) * 512)
                pX = pC if n % 2 == 0 else pD
                steps.append(("pe", lambda e, sl=sl, pX=pX: e.matmul(
                    pX[0:2, :], P.c["c_sel2"][:], SQA[:, sl], start=True, stop=True)))
                steps.append(("dve", lambda e, sl=sl, pX=pX: e.tensor_scalar_mul(
                    rr2k[:, sl], pX[0:2, :], 1.0 / 64.0)))
            steps.append(("dve", lambda e: (
                e.tensor_scalar_add(rr2q[:], rr2q[:], RMS_EPS0),
                e.tensor_scalar_add(rr2k[:], rr2k[:], RMS_EPS0))[-1]))
            steps.append(("act", lambda e: (
                e.activation(rr2q[:], rr2q[:], AF.Ln),
                e.activation(rr2k[:], rr2k[:], AF.Ln))[-1]))
            steps.append(("act", lambda e: (
                e.activation(rr2q[:], rr2q[:], AF.Exp, scale=-0.5),
                e.activation(rr2k[:], rr2k[:], AF.Exp, scale=-0.5))[-1]))
            steps.append(("dve", lambda e: (
                e.tensor_scalar_mul(Qc2[:], Qc2[:], P.sb["qnw2"][:, 0:1]),
                e.tensor_scalar_mul(Kc2[:], Kc2[:], P.sb["knw2"][:, 0:1]))[-1]))
            for n in range(4):
                sl = slice(n * 512, (n + 1) * 512)
                steps.append(("pe", lambda e, sl=sl: (
                    e.matmul(pA[:], P.c["c_exp2"][:], rr2q[:, sl], start=True, stop=True),
                    e.matmul(pB[:], P.c["c_exp2"][:], rr2k[:, sl], start=True, stop=True))[-1]))
                steps.append(("dve", lambda e, sl=sl: (
                    e.tensor_mul(qhE[0:64, sl], Qc2[0:64, sl], pA[0:64, :]),
                    e.tensor_mul(qhO[64:128, sl], Qc2[64:128, sl], pA[64:128, :]),
                    e.tensor_mul(khE[0:64, sl], Kc2[0:64, sl], pB[0:64, :]),
                    e.tensor_mul(khO[64:128, sl], Kc2[64:128, sl], pB[64:128, :]))[-1]))
            for n in range(4):
                sl = slice(n * 512, (n + 1) * 512)
                steps.append(("pe", lambda e, sl=sl, p=p: (
                    e.matmul(pA[0:64, :], P.sb["wqrT"][0:64, :], qhE[0:64, sl], start=True, stop=True),
                    e.matmul(pB[0:64, :], P.sb["wqrT"][64:128, :], qhO[64:128, sl], start=True, stop=True),
                    e.matmul(pC[0:64, :], P.sb["wkrT"][0:64, :], xn_fm[0:64, p, sl], start=True, stop=True),
                    e.matmul(pD[0:64, :], P.sb["wkrT"][64:128, :], xn_fm[64:128, p, sl], start=True, stop=True))[-1]))
                steps.append(("act", lambda e, sl=sl: (
                    e.copy(qhE[64:128, sl], pA[0:64, :]),
                    e.copy(qhO[0:64, sl], pB[0:64, :]),
                    e.copy(khE[64:128, sl], pC[0:64, :]),
                    e.copy(khO[0:64, sl], pD[0:64, :]))[-1]))
            steps.append(("act", lambda e: (
                e.activation(SQA[:], qhE[:], AF.Square),
                e.activation(Qc2[:], qhO[:], AF.Square))[-1]))
            for n in range(4):
                sl = slice(n * 512, (n + 1) * 512)
                pX, pY = (pA, pB) if n % 2 == 0 else (pC, pD)
                steps.append(("pe", lambda e, sl=sl, pX=pX, pY=pY: (
                    e.matmul(pX[0:1, :], P.c["c_ones_c"][:], SQA[:, sl], start=True, stop=True),
                    e.matmul(pY[0:1, :], P.c["c_ones_c"][:], Qc2[:, sl], start=True, stop=True))[-1]))
                steps.append(("dve", lambda e, n=n, pX=pX, pY=pY: (
                    e.reduce_max(qmax[:, n:n + 1], pX[0:1, :], axis=AX),
                    e.reduce_max(qmax[:, 4 + n:5 + n], pY[0:1, :], axis=AX))[-1]))
            steps.append(("act", lambda e: (
                e.activation(SQA[:], khE[:], AF.Square),
                e.activation(Qc2[:], khO[:], AF.Square))[-1]))
            for n in range(4):
                sl = slice(n * 512, (n + 1) * 512)
                pX, pY = (pA, pB) if n % 2 == 0 else (pC, pD)
                steps.append(("pe", lambda e, sl=sl, pX=pX, pY=pY: (
                    e.matmul(pX[0:1, :], P.c["c_ones_c"][:], SQA[:, sl], start=True, stop=True),
                    e.matmul(pY[0:1, :], P.c["c_ones_c"][:], Qc2[:, sl], start=True, stop=True))[-1]))
                steps.append(("dve", lambda e, n=n, pX=pX, pY=pY: (
                    e.reduce_max(qmax[:, 8 + n:9 + n], pX[0:1, :], axis=AX),
                    e.reduce_max(qmax[:, 12 + n:13 + n], pY[0:1, :], axis=AX))[-1]))
            steps.append(("dve", lambda e: (
                e.reduce_max(mx[:, 0:1], qmax[:, 0:4], axis=AX),
                e.reduce_max(mx[:, 1:2], qmax[:, 4:8], axis=AX),
                e.reduce_max(mx[:, 2:3], qmax[:, 8:12], axis=AX),
                e.reduce_max(mx[:, 3:4], qmax[:, 12:16], axis=AX))[-1]))
            steps.append(("dve", lambda e: (
                e.tensor_mul(mx[:, 4:5], mx[:, 0:1], mx[:, 2:3]),
                e.tensor_mul(mx[:, 5:6], mx[:, 1:2], mx[:, 3:4]))[-1]))
            steps.append(("act", lambda e: e.activation(mx[:, 4:6], mx[:, 4:6], AF.Ln)))
            steps.append(("act", lambda e: e.activation(mx[:, 4:6], mx[:, 4:6], AF.Exp, scale=0.5)))
            steps.append(("act", lambda e: e.mul(mx[:, 4:6], mx[:, 4:6], -0.125)))
            steps.append(("pe", lambda e: e.matmul(pF[:, 0:2], P.c["c_ones_r"][:], mx[:, 4:6], start=True, stop=True)))
            steps.append(("dve", lambda e: e.tensor_copy(bnd2[:], pF[:, 0:2])))
            chain(P, steps)

            for qc in range(4):
                _attn_pipe(P, khE, qhE, vtmE, bnd2[:, 0:1], [pA, pB], pC[0:65, :], p_t, qc)
                _attn_pipe(P, khO, qhO, vtmO, bnd2[:, 1:2], [pA, pB], pD[0:65, :], p_t, qc)
                steps = []
                steps.append(("dve", lambda e: (
                    e.reciprocal(invrE[:], pC[64:65, :]),
                    e.reciprocal(invrO[:], pD[64:65, :]))[-1]))
                def trs(e):
                    for j in range(4):
                        e.transpose(pF[:, j:j + 1], invrE[0:1, j * 128:(j + 1) * 128],
                                    P.c["c_id32"][0:1, 0:1])
                    for j in range(4):
                        i1 = e.transpose(pF[:, 4 + j:5 + j], invrO[0:1, j * 128:(j + 1) * 128],
                                         P.c["c_id32"][0:1, 0:1])
                    return i1
                steps.append(("pe", trs))
                steps.append(("dve", lambda e: e.tensor_copy(invc[:], pF[:, 0:8])))
                steps.append(("dve", lambda e: (
                    e.tensor_copy(ctxsE[:], pC[0:64, :]),
                    e.tensor_copy(ctxsO[:], pD[0:64, :]))[-1]))
                def outmm(e):
                    for j in range(4):
                        e.matmul(pA[:, j * 64:(j + 1) * 64], ctxsE[:, j * 128:(j + 1) * 128],
                                 P.sb["woT"][:], start=True, stop=True)
                    for j in range(4):
                        i1 = e.matmul(pB[:, j * 64:(j + 1) * 64], ctxsO[:, j * 128:(j + 1) * 128],
                                      P.sb["woT"][:], start=True, stop=True)
                    return i1
                steps.append(("pe", outmm))
                def sout(e, qc=qc, he=he, ho=ho):
                    for j in range(4):
                        e.tensor_scalar_mul(s_buf[:, qc * 4 + j, he * 64:(he + 1) * 64],
                                            pA[:, j * 64:(j + 1) * 64], invc[:, j:j + 1])
                    for j in range(4):
                        i1 = e.tensor_scalar_mul(s_buf[:, qc * 4 + j, ho * 64:(ho + 1) * 64],
                                                 pB[:, j * 64:(j + 1) * 64], invc[:, 4 + j:5 + j])
                    return i1
                steps.append(("dve", sout))
                chain(P, steps)



def emit_l0_gating(P, xn_fm, combine):
    nc = P.nc
    with ExitStack() as ctx:
        sc = P.sbuf(ctx, "gsc", [128, 16], F32)
        sel = P.sbuf(ctx, "gsel", [128, 16], F32)
        top8 = P.sbuf(ctx, "gtop", [128, 8], F32)
        den = P.sbuf(ctx, "gden", [128, 1], F32)
        pg = P.psum(ctx, "pg", [128, 16], F32)
        for i in range(NTILE):
            steps = []
            def mms(e, i=i):
                for c in range(NCH):
                    i1 = e.matmul(pg[:], xn_fm[:, c, i * 128:(i + 1) * 128],
                                  P.sb["gwT"][:, c, :], start=(c == 0), stop=(c == 7))
                return i1
            steps.append(("pe", mms))
            steps.append(("act", lambda e: e.activation(sc[:], pg[:], AF.Sigmoid)))
            steps.append(("dve", lambda e: e.tensor_add(sel[:], sc[:], P.sb["l0bias"][:])))
            steps.append(("dve", lambda e: e.max(out=top8[:], in_=sel[:])))
            steps.append(("dve", lambda e: e.tensor_single_scalar(sel[:], sel[:], top8[:, 7:8], ALU.is_ge)))
            steps.append(("dve", lambda e: e.tensor_mul(sel[:], sel[:], sc[:])))
            steps.append(("dve", lambda e: e.reduce_sum(den[:], sel[:], axis=AX)))
            steps.append(("dve", lambda e: e.reciprocal(den[:], den[:])))
            steps.append(("dve", lambda e: e.tensor_scalar_mul(sel[:], sel[:], den[:, 0:1])))
            steps.append(("dve", lambda e, i=i: e.tensor_scalar_mul(combine[:, i, :], sel[:], 2.826)))
            chain(P, steps)


def emit_l0_ffn(P, xn_fm, y_buf, combine):
    nc = P.nc
    with ExitStack() as ctx:
        h_full = P.sbuf(ctx, "hfull", [128, NCH, T], BF16)
        wg_b = [P.sbuf(ctx, "wgb", [128, 8, 128], BF16) for _ in range(2)]
        wu_b = [P.sbuf(ctx, "wub", [128, 8, 128], BF16) for _ in range(2)]
        wd_b = [P.sbuf(ctx, "wdb", [128, 8, 1024], BF16) for _ in range(2)]
        ts = [P.sbuf(ctx, "tsilu", [128, 512], BF16) for _ in range(2)]
        pg = [P.psum(ctx, "pgm", [128, 512], F32) for _ in range(2)]
        pu = [P.psum(ctx, "pum", [128, 512], F32) for _ in range(2)]
        pd = [P.psum(ctx, "pdm", [128, 512], F32) for _ in range(2)]
        sd, pe, act, dve = P.s["sd"], P.s["pe"], P.s["act"], P.s["dve"]
        sg = P.s["sg"]

        experts = [("sh", 48, 49, 50, None)] + [
            ("e%d" % e, e, 16 + e, 32 + e, e) for e in range(16)]

        scr = P.w["scr_w0"]

        def wg_src(m, c):
            return scr[m].rearrange("(c p) i -> p c i", p=128)[:, :, c * 128:(c + 1) * 128]

        def wd_src(m):
            return scr[m].rearrange("(c p) dd -> p c dd", p=128)

        ld_tgt = {}

        def issue_wd(e, ei):
            md_ = experts[ei][3]
            e.dma_start(wd_b[ei % 2][:], wd_src(md_)).then_inc(sd.h, 16)
            sd.v += 16
            ld_tgt[("wd", ei)] = sd.v

        def issue_wgwu(e, ei, c):
            mg_, mu_ = experts[ei][1], experts[ei][2]
            e.dma_start(wg_b[c % 2][:], wg_src(mg_, c)).then_inc(sd.h, 16)
            e.dma_start(wu_b[c % 2][:], wg_src(mu_, c)).then_inc(sd.h, 16)
            sd.v += 32
            ld_tgt[("wgwu", ei, c)] = sd.v

        with nc.Block() as blk:
            @blk.sync
            def _(e):
                if "ws" in P.s:
                    e.wait_ge(P.s["ws"].h, 48)
                issue_wd(e, 0)
                issue_wgwu(e, 0, 0)

        for ei, (nm, mg, mu, md, eidx) in enumerate(experts):
            for c in range(NCH):
                pe0, act0, dve0 = pe.v, act.v, dve.v
                g_m = {}
                u_m = {}
                m = pe0
                for n in range(4):
                    m += 1; g_m[n] = m
                    m += 1; u_m[n] = m
                silu_m = {n: act0 + n + 1 for n in range(4)}
                h_m = {n: dve0 + n + 1 for n in range(4)}
                with nc.Block() as blk:
                    @blk.sync
                    def _(e, ei=ei, c=c):
                        if c < NCH - 1:
                            issue_wgwu(e, ei, c + 1)
                        elif ei + 1 < len(experts):
                            if "ws" in P.s:
                                e.wait_ge(P.s["ws"].h, 48 * (ei + 2))
                            issue_wd(e, ei + 1)
                            issue_wgwu(e, ei + 1, 0)
                    @blk.tensor
                    def _(e):
                        e.wait_ge(sd.h, ld_tgt[("wgwu", ei, c)])
                        for n in range(4):
                            bb = n % 2
                            if n >= 2:
                                e.wait_ge(act.h, silu_m[n - 2])
                            for kc in range(NCH):
                                i1 = e.matmul(pg[bb][:], wg_b[c % 2][:, kc, :],
                                              xn_fm[:, kc, n * 512:(n + 1) * 512],
                                              start=(kc == 0), stop=(kc == 7))
                            i1.then_inc(pe.h, 1)
                            if n >= 2:
                                e.wait_ge(dve.h, h_m[n - 2])
                            for kc in range(NCH):
                                i1 = e.matmul(pu[bb][:], wu_b[c % 2][:, kc, :],
                                              xn_fm[:, kc, n * 512:(n + 1) * 512],
                                              start=(kc == 0), stop=(kc == 7))
                            i1.then_inc(pe.h, 1)
                    @blk.scalar
                    def _(e):
                        for n in range(4):
                            bb = n % 2
                            e.wait_ge(pe.h, g_m[n])
                            if n >= 2:
                                e.wait_ge(dve.h, h_m[n - 2])
                            e.activation(ts[bb][:], pg[bb][:], AF.Silu).then_inc(act.h, 1)
                    @blk.vector
                    def _(e):
                        for n in range(4):
                            bb = n % 2
                            e.wait_ge(act.h, silu_m[n])
                            e.wait_ge(pe.h, u_m[n])
                            e.tensor_mul(h_full[:, c, n * 512:(n + 1) * 512],
                                         ts[bb][:], pu[bb][:]).then_inc(dve.h, 1)
                pe.v = m
                act.v = act0 + 4
                dve.v = dve0 + 4
            pe0, dve0 = pe.v, dve.v
            dn_m = {}
            m = pe0
            items = [(t, j2) for t in range(NTILE) for j2 in range(2)]
            for idx, (t, j2) in enumerate(items):
                m += 1
                dn_m[(t, j2)] = m
            ev_m = {it: dve0 + i + 1 for i, it in enumerate(items)}
            with nc.Block() as blk:
                @blk.tensor
                def _(e):
                    for idx, (t, j2) in enumerate(items):
                        bb = idx % 2
                        if idx == 0:
                            e.wait_ge(sd.h, ld_tgt[("wd", ei)])
                        if idx >= 2:
                            e.wait_ge(dve.h, ev_m[items[idx - 2]])
                        for kc in range(NCH):
                            i1 = e.matmul(pd[bb][:], h_full[:, kc, t * 128:(t + 1) * 128],
                                          wd_b[ei % 2][:, kc, j2 * 512:(j2 + 1) * 512],
                                          start=(kc == 0), stop=(kc == 7))
                        i1.then_inc(pe.h, 1)
                @blk.vector
                def _(e):
                    for idx, (t, j2) in enumerate(items):
                        bb = idx % 2
                        e.wait_ge(pe.h, dn_m[(t, j2)])
                        dst = y_buf[:, t, j2 * 512:(j2 + 1) * 512]
                        if eidx is None:
                            e.tensor_copy(dst, pd[bb][:]).then_inc(dve.h, 1)
                        else:
                            e.scalar_tensor_tensor(dst, pd[bb][:],
                                                   combine[:, t, eidx:eidx + 1], dst,
                                                   ALU.mult, ALU.add).then_inc(dve.h, 1)
            pe.v = m
            dve.v = dve0 + len(items)





def stage_l1_small(P):
    nc = P.nc
    P.sb["convq"] = P.sbuf(P.ctx, "convq", [128, 6, 4], F32)
    P.sb["convk"] = P.sbuf(P.ctx, "convk", [128, 6, 4], F32)
    P.sb["convv"] = P.sbuf(P.ctx, "convv", [128, 12, 4], F32)
    P.sb["dtb"] = P.sbuf(P.ctx, "dtb", [128, 6], F32)
    P.sb["negA"] = P.sbuf(P.ctx, "negA", [128, 6], F32)
    P.sb["nwb"] = P.sbuf(P.ctx, "nwb", [128, 256], F32)
    P.sb["l1bias"] = P.sbuf(P.ctx, "l1bias", [128, 8], F32)
    P.sb["ln1b"] = P.sbuf(P.ctx, "ln1b", [128, 1024], F32)
    P.sb["ln2b"] = P.sbuf(P.ctx, "ln2b", [128, 1024], F32)
    with ExitStack() as ctx:
        row = P.sbuf(ctx, "strow", [1, 1024], F32)
        row2 = P.sbuf(ctx, "strow2", [1, 6], F32)
        pb = P.psum(ctx, "pstb", [128, 512], F32)
        steps = []
        steps.append(("sync_dma", lambda e: (
            e.dma_start(P.sb["convq"][:], P.w["L1_attn_conv_q"].rearrange("(c p) j -> p c j", p=128)).then_inc(P.s["sd"].h, 16),
            e.dma_start(P.sb["convk"][:], P.w["L1_attn_conv_k"].rearrange("(c p) j -> p c j", p=128)).then_inc(P.s["sd"].h, 16),
            e.dma_start(P.sb["convv"][:], P.w["L1_attn_conv_v"].rearrange("(c p) j -> p c j", p=128)))[-1], 3))
        steps.append(("sync_dma", lambda e: e.dma_start(row2[:], P.w["L1_attn_dt_bias"].rearrange("(o e) -> o e", o=1))))
        steps.append(("pe", lambda e: e.matmul(pb[:, 0:6], P.c["c_ones_r"][:], row2[:], start=True, stop=True)))
        steps.append(("act", lambda e: e.copy(P.sb["dtb"][:], pb[:, 0:6])))
        steps.append(("sync_dma", lambda e: e.dma_start(row2[:], P.w["L1_attn_A_log"].rearrange("(o e) -> o e", o=1))))
        steps.append(("act", lambda e: e.activation(row2[:], row2[:], AF.Exp)))
        steps.append(("act", lambda e: e.mul(row2[:], row2[:], -1.0)))
        steps.append(("pe", lambda e: e.matmul(pb[:, 0:6], P.c["c_ones_r"][:], row2[:], start=True, stop=True)))
        steps.append(("act", lambda e: e.copy(P.sb["negA"][:], pb[:, 0:6])))
        steps.append(("sync_dma", lambda e: e.dma_start(row[:, 0:256], P.w["L1_attn_norm_w"].rearrange("(o e) -> o e", o=1))))
        steps.append(("pe", lambda e: e.matmul(pb[:, 0:256], P.c["c_ones_r"][:], row[:, 0:256], start=True, stop=True)))
        steps.append(("act", lambda e: e.copy(P.sb["nwb"][:], pb[:, 0:256])))
        steps.append(("sync_dma", lambda e: e.dma_start(row[:, 0:8], P.w["L1_ffn_expert_bias"].rearrange("(o e) -> o e", o=1))))
        steps.append(("pe", lambda e: e.matmul(pb[:, 0:8], P.c["c_ones_r"][:], row[:, 0:8], start=True, stop=True)))
        steps.append(("act", lambda e: e.copy(P.sb["l1bias"][:], pb[:, 0:8])))
        for wname, dst in (("L1_ffn_w_ln1", "ln1b"), ("L1_ffn_w_ln2", "ln2b")):
            steps.append(("sync_dma", lambda e, wname=wname: e.dma_start(row[:], P.w[wname].rearrange("(o e) -> o e", o=1))))
            for j in range(2):
                steps.append(("pe", lambda e, j=j: e.matmul(pb[:], P.c["c_ones_r"][:], row[:, j * 512:(j + 1) * 512], start=True, stop=True)))
                steps.append(("act", lambda e, dst=dst, j=j: e.copy(P.sb[dst][:, j * 512:(j + 1) * 512], pb[:])))
        chain(P, steps)


def emit_l1_attn_passA(P, xn_fm, scr_q, scr_k, scr_v, scr_g, beta_all, g_all):
    nc = P.nc
    GS = 512
    NG = T // GS
    with ExitStack() as ctx:
        wb = P.sbuf(ctx, "wbs", [128, 8, 6], BF16)
        wa = P.sbuf(ctx, "was", [128, 8, 6], BF16)
        t6 = P.sbuf(ctx, "t6", [128, 6], F32)
        t6b = P.sbuf(ctx, "t6b", [128, 6], F32)
        pba = P.psum(ctx, "pba", [128, 6], F32)
        pbb = P.psum(ctx, "pbb", [128, 6], F32)
        steps = [("pool_dma", lambda e: (
            e.dma_start(wb[:], P.w["L1_attn_Wb"].rearrange("(c p) h -> p c h", p=128)).then_inc(P.s["sg"].h, 16),
            e.dma_start(wa[:], P.w["L1_attn_Wa"].rearrange("(c p) h -> p c h", p=128)))[-1], 2)]
        for i in range(NTILE):
            def mmb(e, i=i):
                for c in range(NCH):
                    i1 = e.matmul(pba[:], xn_fm[:, c, i * 128:(i + 1) * 128], wb[:, c, :],
                                  start=(c == 0), stop=(c == 7))
                return i1
            steps.append(("pe", mmb))
            def mma(e, i=i):
                for c in range(NCH):
                    i1 = e.matmul(pbb[:], xn_fm[:, c, i * 128:(i + 1) * 128], wa[:, c, :],
                                  start=(c == 0), stop=(c == 7))
                return i1
            steps.append(("pe", mma))
            steps.append(("act", lambda e, i=i: e.activation(beta_all[:, i, :], pba[:], AF.Sigmoid)))
            steps.append(("dve", lambda e: e.tensor_add(t6[:], pbb[:], P.sb["dtb"][:])))
            steps.append(("act", lambda e: e.activation(t6b[:], t6[:], AF.Abs)))
            steps.append(("act", lambda e: e.activation(t6b[:], t6b[:], AF.Exp, scale=-1.0)))
            steps.append(("act", lambda e: e.activation(t6b[:], t6b[:], AF.Ln, bias=1.0)))
            steps.append(("dve", lambda e: e.tensor_scalar_max(t6[:], t6[:], 0.0)))
            steps.append(("dve", lambda e: e.tensor_add(t6[:], t6[:], t6b[:])))
            steps.append(("dve", lambda e, i=i: e.tensor_mul(g_all[:, i, :], t6[:], P.sb["negA"][:])))
        chain(P, steps)
    with ExitStack() as ctx:
        wg = P.sbuf(ctx, "wgs", [128, 8, 1536], BF16)
        gb = P.sbuf(ctx, "gbs", [128, 512], BF16)
        pgg = P.psum(ctx, "pgg", [128, 512], F32)
        steps = [("pool_dma", lambda e: e.dma_start(wg[:], P.w["L1_attn_Wg"].rearrange("(c p) h -> p c h", p=128)))]
        for i in range(NTILE):
            for n3 in range(3):
                def mmg(e, i=i, n3=n3):
                    for c in range(NCH):
                        i1 = e.matmul(pgg[:], xn_fm[:, c, i * 128:(i + 1) * 128],
                                      wg[:, c, n3 * 512:(n3 + 1) * 512], start=(c == 0), stop=(c == 7))
                    return i1
                steps.append(("pe", mmg))
                steps.append(("act", lambda e: e.activation(gb[:], pgg[:], AF.Silu)))
                steps.append(("sync_dma", lambda e, i=i, n3=n3: e.dma_start(
                    scr_g[i * 128:(i + 1) * 128, n3 * 512:(n3 + 1) * 512], gb[:])))
        chain(P, steps)
    for which, wname, cname, nmrow, scr, do_norm, qscale in (
            ("q", "L1_attn_Wq", "convq", 6, scr_q, True, 128 ** -0.5),
            ("k", "L1_attn_Wk", "convk", 6, scr_k, True, 1.0),
            ("v", "L1_attn_Wv", "convv", 12, scr_v, False, 1.0)):
        with ExitStack() as ctx:
            wqs = P.sbuf(ctx, "wqs", [128, 8, nmrow * 128], BF16)
            xb = P.sbuf(ctx, "xb", [128, nmrow, GS + 3], F32)
            cacc = P.sbuf(ctx, "cacc", [128, GS], F32)
            cout = P.sbuf(ctx, "cout", [128, nmrow, GS], BF16)
            sqv = P.sbuf(ctx, "sqv", [128, GS], F32)
            rr = P.sbuf(ctx, "rr", [1, GS], F32)
            pqv = P.psum(ctx, "pqv", [128, 512], F32)
            pnb = P.psum(ctx, "pnb", [128, 512], F32)
            pnr = P.psum(ctx, "pnr", [1, 512], F32)
            ptt = P.psum(ctx, "ptt", [128, 128], BF16)
            steps = [("pool_dma", lambda e, wname=wname: e.dma_start(
                wqs[:], P.w[wname].rearrange("(c p) h -> p c h", p=128)))]
            steps.append(("dve", lambda e: e.memset(xb[:, :, 0:3], 0.0)))
            for gdx in range(NG):
                if gdx > 0:
                    steps.append(("dve", lambda e: e.tensor_copy(xb[:, :, 0:3], xb[:, :, GS:GS + 3])))
                for m in range(nmrow):
                    def mmx(e, gdx=gdx, m=m):
                        for c in range(NCH):
                            i1 = e.matmul(pqv[:], wqs[:, c, m * 128:(m + 1) * 128],
                                          xn_fm[:, c, gdx * GS:(gdx + 1) * GS],
                                          start=(c == 0), stop=(c == 7))
                        return i1
                    steps.append(("pe", mmx))
                    steps.append(("act", lambda e, m=m: e.copy(xb[:, m, 3:3 + GS], pqv[:])))
                for m in range(nmrow):
                    steps.append(("dve", lambda e, m=m, cname=cname: e.tensor_scalar_mul(
                        cacc[:], xb[:, m, 0:GS], P.sb[cname][:, m, 0:1])))
                    for j in range(1, 4):
                        steps.append(("dve", lambda e, m=m, j=j, cname=cname: e.scalar_tensor_tensor(
                            cacc[:], xb[:, m, j:j + GS], P.sb[cname][:, m, j:j + 1], cacc[:],
                            ALU.mult, ALU.add)))
                    steps.append(("act", lambda e, m=m: e.activation(cout[:, m, :], cacc[:], AF.Silu)))
                    if do_norm:
                        steps.append(("act", lambda e, m=m: e.activation(sqv[:], cout[:, m, :], AF.Square)))
                        steps.append(("pe", lambda e: e.matmul(pnr[:], P.c["c_ones_c"][:], sqv[:], start=True, stop=True)))
                        steps.append(("dve", lambda e: e.tensor_scalar_add(rr[:], pnr[:], 1e-12)))
                        steps.append(("act", lambda e: e.activation(rr[:], rr[:], AF.Ln)))
                        steps.append(("act", lambda e: e.activation(rr[:], rr[:], AF.Exp, scale=-0.5)))
                        if qscale != 1.0:
                            steps.append(("act", lambda e, qscale=qscale: e.mul(rr[:], rr[:], qscale)))
                        steps.append(("pe", lambda e: e.matmul(pnb[:], P.c["c_ones_r"][:], rr[:], start=True, stop=True)))
                        steps.append(("dve", lambda e, m=m: e.tensor_mul(cout[:, m, :], cout[:, m, :], pnb[:])))
                if which in ("q", "k"):
                    steps.append(("sync_dma", lambda e, gdx=gdx, scr=scr: e.dma_start(
                        scr[:, :, gdx * GS:(gdx + 1) * GS], cout[:])))
                else:
                    for hh in range(6):
                        for tt4 in range(4):
                            for half in range(2):
                                steps.append(("pe", lambda e, hh=hh, tt4=tt4, half=half: e.transpose(
                                    ptt[:], cout[:, 2 * hh + half, tt4 * 128:(tt4 + 1) * 128],
                                    P.c["c_idbf"][:])))
                                steps.append(("act", lambda e, hh=hh, tt4=tt4, half=half: e.copy(
                                    sqv.bitcast(BF16)[:, half * 128:(half + 1) * 128] if False else cacc.bitcast(BF16)[:, half * 128:(half + 1) * 128], ptt[:])))
                            steps.append(("sync_dma", lambda e, hh=hh, gdx=gdx, tt4=tt4, scr=scr: e.dma_start(
                                scr[hh, gdx * GS + tt4 * 128: gdx * GS + (tt4 + 1) * 128, :],
                                cacc.bitcast(BF16)[:, 0:256])))
            chain(P, steps)


def emit_l1_attn_passA2(P, xn_fm, buf_a, scr_q, scr_k, scr_v, scr_g, beta_all, g_all):
    nc = P.nc
    GS = 512
    NG = T // GS
    flat32 = buf_a[:].rearrange("p t d -> p (t d)")
    flat16 = buf_a.bitcast(BF16)[:].rearrange("p t d -> p (t d)")

    def carve(base_words, nmrow):
        xb = flat32[:, base_words:base_words + nmrow * 515].rearrange(
            "p (a b) -> p a b", a=nmrow)
        cacc = flat32[:, base_words + nmrow * 515 + 14:
                      base_words + nmrow * 515 + 14 + nmrow * 512].rearrange(
            "p (a b) -> p a b", a=nmrow)
        cw = base_words + nmrow * 515 + 14 + nmrow * 512
        cout = flat16[:, 2 * cw:2 * cw + nmrow * 512].rearrange(
            "p (a b) -> p a b", a=nmrow)
        return xb, cacc, cout

    with ExitStack() as ctx:
        wb = P.sbuf(ctx, "wbs", [128, 8, 6], BF16)
        wa = P.sbuf(ctx, "was", [128, 8, 6], BF16)
        t96 = P.sbuf(ctx, "t96", [128, 96], F32)
        t96b = P.sbuf(ctx, "t96b", [128, 96], F32)
        wqs_q = P.sbuf(ctx, "wqsq", [128, 8, 768], BF16)
        wqs_k = P.sbuf(ctx, "wqsk", [128, 8, 768], BF16)
        rr_q = P.sbuf(ctx, "rrq", [1, 3072], F32)
        rr_k = P.sbuf(ctx, "rrk", [1, 3072], F32)
        pX = [P.psum(ctx, "pXq", [128, 512], F32) for _ in range(2)]
        pY = [P.psum(ctx, "pYk", [128, 512], F32) for _ in range(2)]
        pba = P.psum(ctx, "pba", [128, 96], F32)
        pbb = P.psum(ctx, "pbb", [128, 96], F32)

        dtb_bc = P.sb["dtb"][:].unsqueeze(1).to_broadcast([128, NTILE, 6])
        negA_bc = P.sb["negA"][:].unsqueeze(1).to_broadcast([128, NTILE, 6])

        sbg = [("pool_dma", lambda e: (
            e.dma_start(wb[:], P.w["L1_attn_Wb"].rearrange("(c p) h -> p c h", p=128)).then_inc(P.s["sg"].h, 16),
            e.dma_start(wa[:], P.w["L1_attn_Wa"].rearrange("(c p) h -> p c h", p=128)))[-1], 2)]

        def mmb(e):
            for i in range(NTILE):
                for c in range(NCH):
                    i1 = e.matmul(pba[:, i * 6:(i + 1) * 6],
                                  xn_fm[:, c, i * 128:(i + 1) * 128], wb[:, c, :],
                                  start=(c == 0), stop=(c == 7))
            return i1
        sbg.append(("pe", mmb))
        sbg.append(("act", lambda e: e.activation(
            beta_all[:].rearrange("p t h -> p (t h)"), pba[:], AF.Sigmoid)))

        def mma(e):
            for i in range(NTILE):
                for c in range(NCH):
                    i1 = e.matmul(pbb[:, i * 6:(i + 1) * 6],
                                  xn_fm[:, c, i * 128:(i + 1) * 128], wa[:, c, :],
                                  start=(c == 0), stop=(c == 7))
            return i1
        sbg.append(("pe", mma))
        sbg.append(("dve", lambda e: e.tensor_add(
            t96[:].rearrange("p (t h) -> p t h", t=NTILE),
            pbb[:].rearrange("p (t h) -> p t h", t=NTILE), dtb_bc)))
        sbg.append(("act", lambda e: e.activation(t96b[:], t96[:], AF.Abs)))
        sbg.append(("act", lambda e: e.activation(t96b[:], t96b[:], AF.Exp, scale=-1.0)))
        sbg.append(("act", lambda e: e.activation(t96b[:], t96b[:], AF.Ln, bias=1.0)))
        sbg.append(("dve", lambda e: e.tensor_scalar_max(t96[:], t96[:], 0.0)))
        sbg.append(("dve", lambda e: e.tensor_add(t96[:], t96[:], t96b[:])))
        sbg.append(("dve", lambda e: e.tensor_mul(
            g_all[:], t96[:].rearrange("p (t h) -> p t h", t=NTILE), negA_bc)))

        def qk_stream(wname, cname, wqs, rr, xb, cacc, cout, scr, qscale):
            st = [("pool_dma", lambda e: e.dma_start(
                wqs[:], P.w[wname].rearrange("(c p) h -> p c h", p=128)))]
            st.append(("dve", lambda e: e.memset(xb[:, :, 0:3], 0.0)))
            pp = pX if scr is scr_q else pY
            for gdx in range(NG):
                if gdx > 0:
                    st.append(("dve", lambda e: e.tensor_copy(xb[:, :, 0:3], xb[:, :, GS:GS + 3])))
                for m in range(6):
                    def mmx(e, gdx=gdx, m=m):
                        for c in range(NCH):
                            i1 = e.matmul(pp[m % 2][:], wqs[:, c, m * 128:(m + 1) * 128],
                                          xn_fm[:, c, gdx * GS:(gdx + 1) * GS],
                                          start=(c == 0), stop=(c == 7))
                        return i1
                    st.append(("pe", mmx))
                    st.append(("act", lambda e, m=m: e.copy(xb[:, m, 3:3 + GS], pp[m % 2][:])))
                for j in range(4):
                    def cstep(e, j=j):
                        for m in range(6):
                            if j == 0:
                                i1 = e.tensor_scalar_mul(cacc[:, m, :], xb[:, m, 0:GS],
                                                         P.sb[cname][:, m, 0:1])
                            else:
                                i1 = e.scalar_tensor_tensor(cacc[:, m, :], xb[:, m, j:j + GS],
                                                            P.sb[cname][:, m, j:j + 1],
                                                            cacc[:, m, :], ALU.mult, ALU.add)
                        return i1
                    st.append(("dve", cstep))
                st.append(("act", lambda e: e.activation(
                    cout[:].rearrange("p a b -> p (a b)"), cacc[:].rearrange("p a b -> p (a b)"), AF.Silu)))
                st.append(("act", lambda e: e.activation(
                    cacc[:].rearrange("p a b -> p (a b)"), cout[:].rearrange("p a b -> p (a b)"), AF.Square)))
                for m in range(6):
                    st.append(("pe", lambda e, m=m: e.matmul(
                        pp[m % 2][0:1, :], P.c["c_ones_c"][:], cacc[:, m, :], start=True, stop=True)))
                    st.append(("dve", lambda e, m=m: e.tensor_scalar_add(
                        rr[:, m * 512:(m + 1) * 512], pp[m % 2][0:1, :], 1e-12)))
                st.append(("act", lambda e: e.activation(rr[:], rr[:], AF.Ln)))
                st.append(("act", lambda e: e.activation(rr[:], rr[:], AF.Exp, scale=-0.5)))
                lhsb = P.c["c_qsc"] if qscale != 1.0 else P.c["c_ones_r"]
                for m in range(6):
                    st.append(("pe", lambda e, m=m, lhsb=lhsb: e.matmul(
                        pp[m % 2][:], lhsb[:], rr[:, m * 512:(m + 1) * 512], start=True, stop=True)))
                    st.append(("dve", lambda e, m=m: e.tensor_mul(
                        cout[:, m, :], cout[:, m, :], pp[m % 2][:])))
                st.append(("sync_dma", lambda e, gdx=gdx: e.dma_start(
                    scr[:, :, gdx * GS:(gdx + 1) * GS], cout[:])))
            return st

        xb_q, cacc_q, cout_q = carve(0, 6)
        xb_k, cacc_k, cout_k = carve(8192, 6)
        sq = qk_stream("L1_attn_Wq", "convq", wqs_q, rr_q, xb_q, cacc_q, cout_q, scr_q, 128 ** -0.5)
        sk = qk_stream("L1_attn_Wk", "convk", wqs_k, rr_k, xb_k, cacc_k, cout_k, scr_k, 1.0)
        chain_multi(P, [sbg, sq, sk])

    with ExitStack() as ctx:
        wqs_v = P.sbuf(ctx, "wqsv", [128, 8, 1536], BF16)
        wg = P.sbuf(ctx, "wgs", [128, 8, 1536], BF16)
        gb = [P.sbuf(ctx, "gbs", [128, 512], BF16) for _ in range(2)]
        vstage = P.sbuf(ctx, "vstage", [128, 1536], BF16)
        pX = [P.psum(ctx, "pXv", [128, 512], F32) for _ in range(2)]
        pY = [P.psum(ctx, "pYg", [128, 512], F32) for _ in range(2)]
        pVT = P.psum(ctx, "pVT", [128, 512], BF16)
        xb_v, cacc_v, cout_v = carve(0, 12)

        sv = [("pool_dma", lambda e: e.dma_start(
            wqs_v[:], P.w["L1_attn_Wv"].rearrange("(c p) h -> p c h", p=128)))]
        sv.append(("dve", lambda e: e.memset(xb_v[:, :, 0:3], 0.0)))
        for gdx in range(NG):
            if gdx > 0:
                sv.append(("dve", lambda e: e.tensor_copy(xb_v[:, :, 0:3], xb_v[:, :, GS:GS + 3])))
            for m in range(12):
                def mmx(e, gdx=gdx, m=m):
                    for c in range(NCH):
                        i1 = e.matmul(pX[m % 2][:], wqs_v[:, c, m * 128:(m + 1) * 128],
                                      xn_fm[:, c, gdx * GS:(gdx + 1) * GS],
                                      start=(c == 0), stop=(c == 7))
                    return i1
                sv.append(("pe", mmx))
                sv.append(("act", lambda e, m=m: e.copy(xb_v[:, m, 3:3 + GS], pX[m % 2][:])))
            for j in range(4):
                def cstep(e, j=j):
                    for m in range(12):
                        if j == 0:
                            i1 = e.tensor_scalar_mul(cacc_v[:, m, :], xb_v[:, m, 0:GS],
                                                     P.sb["convv"][:, m, 0:1])
                        else:
                            i1 = e.scalar_tensor_tensor(cacc_v[:, m, :], xb_v[:, m, j:j + GS],
                                                        P.sb["convv"][:, m, j:j + 1],
                                                        cacc_v[:, m, :], ALU.mult, ALU.add)
                    return i1
                sv.append(("dve", cstep))
            sv.append(("act", lambda e: e.activation(
                cout_v[:].rearrange("p a b -> p (a b)"), cacc_v[:].rearrange("p a b -> p (a b)"), AF.Silu)))
            for tt in range(4):
                tsl = slice(tt * 128, (tt + 1) * 128)
                for blk3 in range(3):
                    def trp(e, blk3=blk3, tsl=tsl):
                        for u in range(4):
                            r = blk3 * 4 + u
                            i1 = e.transpose(pVT[:, u * 128:(u + 1) * 128],
                                             cout_v[:, r, tsl], P.c["c_idbf"][:])
                        return i1
                    sv.append(("pe", trp))
                    sv.append(("act", lambda e, blk3=blk3: e.copy(
                        vstage[:, blk3 * 512:(blk3 + 1) * 512], pVT[:])))
                sv.append(("sync_dma", lambda e, gdx=gdx, tt=tt: e.dma_start(
                    scr_v[:, gdx * 512 + tt * 128:gdx * 512 + (tt + 1) * 128, :].transpose([1, 0, 2]),
                    vstage[:].rearrange("p (h d) -> p h d", h=6))))

        sg_ = [("pool_dma", lambda e: e.dma_start(
            wg[:], P.w["L1_attn_Wg"].rearrange("(c p) h -> p c h", p=128)))]
        for i in range(NTILE):
            for n3 in range(3):
                idx = i * 3 + n3
                def mmg(e, i=i, n3=n3, idx=idx):
                    for c in range(NCH):
                        i1 = e.matmul(pY[idx % 2][:], xn_fm[:, c, i * 128:(i + 1) * 128],
                                      wg[:, c, n3 * 512:(n3 + 1) * 512], start=(c == 0), stop=(c == 7))
                    return i1
                sg_.append(("pe", mmg))
                sg_.append(("act", lambda e, idx=idx: e.activation(gb[idx % 2][:], pY[idx % 2][:], AF.Silu)))
                sg_.append(("act_dma", lambda e, i=i, n3=n3, idx=idx: e.dma_start(
                    scr_g[i * 128:(i + 1) * 128, n3 * 512:(n3 + 1) * 512], gb[idx % 2][:])))
        chain_multi(P, [sv, sg_])


def emit_l1_attn_passB(P, s_buf, scr_q, scr_k, scr_v, scr_g, beta_all, g_all):
    nc = P.nc
    NH = 6
    W = NH * 128
    with ExitStack() as ctx:
        wo = P.sbuf(ctx, "wo1", [128, 12, 1024], BF16)
        kc_s = P.sbuf(ctx, "kcs", [128, NH, 128], BF16)
        qc_sb = P.sbuf(ctx, "qcsb", [128, NH, 128], BF16)
        vtm = P.sbuf(ctx, "vtm1", [128, NH, 256], BF16)
        gate_c = P.sbuf(ctx, "gatec", [128, 1536], BF16)
        Gtm = P.sbuf(ctx, "Gtm", [128, 6], F32)
        GhT = P.sbuf(ctx, "GhT", [6, 128], F32)
        bhT = P.sbuf(ctx, "bhT", [6, 128], F32)
        big = P.sbuf(ctx, "bigscr", [128, 2048], F32)
        tmp6 = big[0:6, 0:768]
        PkM = big[:, 0:768]
        bcol = big[:, 768:1536]
        Ed = P.sbuf(ctx, "Ed", [128, 768], F32)
        expEd = P.sbuf(ctx, "expEd", [128, 768], F32)
        expGb = P.sbuf(ctx, "expGb", [128, 768], BF16)
        BBs = P.sbuf(ctx, "BBs", [128, NH, 128], BF16)
        bbs = P.sbuf(ctx, "bbs", [128, NH, 128], BF16)
        B2f = P.sbuf(ctx, "B2f", [128, NH, 128], BF16)
        b2f = P.sbuf(ctx, "b2f", [128, NH, 128], BF16)
        Xp = P.sbuf(ctx, "Xp", [128, NH, 128], BF16)
        xp = P.sbuf(ctx, "xp", [128, NH, 128], BF16)
        kbar = P.sbuf(ctx, "kbar", [128, NH, 128], BF16)
        qbar = P.sbuf(ctx, "qbar", [128, NH, 128], BF16)
        W2t = P.sbuf(ctx, "W2t", [128, NH, 128], BF16)
        RHS0 = P.sbuf(ctx, "RHS0", [128, NH, 256], BF16)
        Us = P.sbuf(ctx, "Us", [128, NH, 256], BF16)
        ktm = P.sbuf(ctx, "ktm", [128, 128], BF16)
        k2tm = P.sbuf(ctx, "k2tm", [128, 128], BF16)
        o_s = big[:, 0:1536]
        o_bf = P.sbuf(ctx, "o_bf", [128, 1536], BF16)
        o_fm = P.sbuf(ctx, "o_fm", [128, 12, 128], BF16)
        oss = P.sbuf(ctx, "oss", [128, 8], F32)
        sqsc = P.sbuf(ctx, "sqsc", [128, 256], F32)
        cC = P.sbuf(ctx, "cC", [128, 6], F32)
        r16 = P.sbuf(ctx, "r16", [1, 6], F32)
        S_st = [P.sbuf(ctx, "Sst", [128, 256], F32) for _ in range(NH)]
        S_bf = P.sbuf(ctx, "Sbf", [128, NH, 256], BF16)
        pP1 = P.psum(ctx, "pP1", [128, 512], F32)
        pP2 = P.psum(ctx, "pP2", [128, 256], F32)
        pO = [P.psum(ctx, "pO", [128, 512], F32) for _ in range(3)]
        pU = P.psum(ctx, "pU", [128, 256], F32)
        pT = P.psum(ctx, "pT", [128, 128], BF16)
        pS = P.psum(ctx, "pSx", [128, 512], F32)

        def mm6(steps, dst_sb, lhs_fn, rhs_fn, cast=True):
            def mms(e):
                for h in range(NH):
                    tgt = pP1[:, h * 128:(h + 1) * 128] if h < 4 else pP2[:, (h - 4) * 128:(h - 3) * 128]
                    i1 = e.matmul(tgt, lhs_fn(h), rhs_fn(h), start=True, stop=True)
                return i1
            steps.append(("pe", mms))
            steps.append(("act", lambda e: e.copy(dst_sb[:, 0:512], pP1[:])))
            steps.append(("act", lambda e: e.copy(dst_sb[:, 512:768], pP2[:])))

        with nc.Block() as blk:
            @blk.gpsimd
            def _(e):
                e.dma_start(wo[:], P.w["L1_attn_Wo"].rearrange("(c p) h -> p c h", p=128)).then_inc(P.s["sg"].h, 16)
                P.s["sg"].v += 16
                e.wait_ge(P.s["sg"].h, P.s["sg"].v)
            @blk.vector
            def _(e):
                for h in range(NH):
                    e.memset(S_st[h][:], 0.0)
                e.memset(S_bf[:], 0.0)
        P.last = None

        striu_bc = P.c["c_striu"][:].unsqueeze(1).to_broadcast([128, NH, 128])
        triu_bc = P.c["c_triu"][:].unsqueeze(1).to_broadcast([128, NH, 128])
        idbf_bc = P.c["c_idbf"][:].unsqueeze(1).to_broadcast([128, NH, 128])
        nwb_bc = P.sb["nwb"][:].unsqueeze(1).to_broadcast([128, NH, 256])

        for ci in range(NTILE):
            csl = slice(ci * 128, (ci + 1) * 128)
            steps = []
            steps.append(("sync_dma", lambda e, ci=ci: (
                e.dma_start(kc_s[:], scr_k[:, :, ci * 128:(ci + 1) * 128]).then_inc(P.s["sd"].h, 16),
                e.dma_start(qc_sb[:], scr_q[:, :, ci * 128:(ci + 1) * 128]).then_inc(P.s["sd"].h, 16),
                e.dma_start(vtm[:], scr_v[:, ci * 128:(ci + 1) * 128, :].transpose([1, 0, 2])).then_inc(P.s["sd"].h, 16),
                e.dma_start(gate_c[:], scr_g[ci * 128:(ci + 1) * 128, :]))[-1], 4))
            steps.append(("pe", lambda e, ci=ci: e.matmul(pP1[:, 0:6], P.c["c_triu"][:], g_all[:, ci, :], start=True, stop=True)))
            steps.append(("dve", lambda e: e.tensor_copy(Gtm[:], pP1[:, 0:6])))
            steps.append(("pe", lambda e: e.transpose(pP1[0:6, 0:128], Gtm[:], P.c["c_id32"][:])))
            steps.append(("dve", lambda e: e.tensor_copy(GhT[:], pP1[0:6, 0:128])))
            steps.append(("pe", lambda e, ci=ci: e.transpose(pP1[0:6, 0:128], beta_all[:, ci, :], P.c["c_id32"][:])))
            steps.append(("dve", lambda e: e.tensor_copy(bhT[:], pP1[0:6, 0:128])))
            steps.append(("dve", lambda e: e.tensor_mul(tmp6.rearrange("p (a t) -> p a t", a=6), GhT[:].unsqueeze(1).to_broadcast([6, 6, 128]), P.c["c_hind"].rearrange("p (a t) -> p a t", a=6))))
            steps.append(("pe", lambda e: e.matmul(pP1[:], P.c["c_ones6"][:], tmp6[:, 0:512], start=True, stop=True)))
            steps.append(("pe", lambda e: e.matmul(pP2[:], P.c["c_ones6"][:], tmp6[:, 512:768], start=True, stop=True)))
            steps.append(("act", lambda e: e.activation(expGb[:, 0:512], pP1[:], AF.Exp)))
            steps.append(("act", lambda e: e.activation(expGb[:, 512:768], pP2[:], AF.Exp)))
            steps.append(("dve", lambda e: e.tensor_copy(Ed[:, 0:512], pP1[:])))
            steps.append(("dve", lambda e: e.tensor_copy(Ed[:, 512:768], pP2[:])))
            steps.append(("pe", lambda e: e.matmul(pP1[:], GhT[:], P.c["c_hind"][:, 0:512], start=True, stop=True)))
            steps.append(("pe", lambda e: e.matmul(pP2[:], GhT[:], P.c["c_hind"][:, 512:768], start=True, stop=True)))
            steps.append(("dve", lambda e: e.tensor_sub(Ed[:, 0:512], Ed[:, 0:512], pP1[:])))
            steps.append(("dve", lambda e: e.tensor_sub(Ed[:, 512:768], Ed[:, 512:768], pP2[:])))
            steps.append(("dve", lambda e: e.tensor_mul(
                Ed.rearrange("p (h t) -> p h t", h=NH)[:], Ed.rearrange("p (h t) -> p h t", h=NH)[:], striu_bc)))
            steps.append(("act", lambda e: e.activation(expEd[:], Ed[:], AF.Exp)))
            steps.append(("pe", lambda e: e.matmul(pP1[:], bhT[:], P.c["c_hind"][:, 0:512], start=True, stop=True)))
            steps.append(("pe", lambda e: e.matmul(pP2[:], bhT[:], P.c["c_hind"][:, 512:768], start=True, stop=True)))
            steps.append(("dve", lambda e: e.tensor_copy(bcol[:, 0:512], pP1[:])))
            steps.append(("dve", lambda e: e.tensor_copy(bcol[:, 512:768], pP2[:])))
            steps.append(("pe", lambda e: e.transpose(pP1[0:1, 0:6], GhT[:, 127:128], P.c["c_id32"][0:6, 0:6])))
            steps.append(("dve", lambda e: e.tensor_copy(r16[:], pP1[0:1, 0:6])))
            steps.append(("act", lambda e: e.activation(r16[:], r16[:], AF.Exp)))
            steps.append(("pe", lambda e: e.matmul(pP1[:, 0:6], P.c["c_ones_r"][:], r16[:], start=True, stop=True)))
            steps.append(("dve", lambda e: e.tensor_copy(cC[:], pP1[:, 0:6])))
            mm6(steps, PkM, lambda h: kc_s[:, h, :], lambda h: kc_s[:, h, :])
            steps.append(("dve", lambda e: e.tensor_mul(
                PkM.rearrange("p (h t) -> p h t", h=NH)[:], PkM.rearrange("p (h t) -> p h t", h=NH)[:], striu_bc)))
            steps.append(("dve", lambda e: e.tensor_mul(Ed[:], expEd[:], PkM[:])))
            steps.append(("dve", lambda e: e.tensor_mul(BBs.rearrange("p h t -> p (h t)"), Ed[:], bcol[:])))
            for h in range(NH):
                steps.append(("pe", lambda e, h=h: e.transpose(pT[:], BBs[:, h, :], P.c["c_idbf"][:])))
                steps.append(("act", lambda e, h=h: e.copy(bbs[:, h, :], pT[:])))
            if DEBUG is not None and ci == 0 and getattr(P, "dbg_seq", 1) == 0:
                steps.append(("sync_dma", lambda e: (
                    e.dma_start(DEBUG["bb"], BBs[:]).then_inc(P.s["sd"].h, 16),
                    e.dma_start(DEBUG["xp2"], bbs[:]))[-1], 2))
                steps.append(("sync_dma", lambda e: (
                    e.dma_start(DEBUG["pkm"], PkM),
                    )[-1], 1))
                steps.append(("sync_dma", lambda e: (
                    e.dma_start(DEBUG["bcol"], bcol),
                    )[-1], 1))
            steps.append(("dve", lambda e: e.tensor_sub(Xp[:], idbf_bc, BBs[:])))
            steps.append(("dve", lambda e: e.tensor_sub(xp[:], idbf_bc, bbs[:])))
            curB, curb = BBs, bbs
            for lvl in range(6):
                nB, nb = (B2f, b2f) if curB is BBs else (BBs, bbs)
                mm6(steps, nB[:].rearrange("p h t -> p (h t)"), lambda h, curb=curb: curb[:, h, :], lambda h, curB=curB: curB[:, h, :])
                if lvl < 5:
                    mm6(steps, nb[:].rearrange("p h t -> p (h t)"), lambda h, curB=curB: curB[:, h, :], lambda h, curb=curb: curb[:, h, :])
                def mmP(e, nB=nB):
                    for h in range(NH):
                        tgt = pP1[:, h * 128:(h + 1) * 128] if h < 4 else pP2[:, (h - 4) * 128:(h - 3) * 128]
                        i1 = e.matmul(tgt, xp[:, h, :], nB[:, h, :], start=True, stop=True)
                    return i1
                steps.append(("pe", mmP))
                steps.append(("dve", lambda e: e.tensor_add(Xp.rearrange("p h t -> p (h t)")[:, 0:512], Xp.rearrange("p h t -> p (h t)")[:, 0:512], pP1[:])))
                steps.append(("dve", lambda e: e.tensor_add(Xp.rearrange("p h t -> p (h t)")[:, 512:768], Xp.rearrange("p h t -> p (h t)")[:, 512:768], pP2[:])))
                if lvl < 5:
                    def mmp2(e, nB=nB):
                        for h in range(NH):
                            tgt = pP1[:, h * 128:(h + 1) * 128] if h < 4 else pP2[:, (h - 4) * 128:(h - 3) * 128]
                            i1 = e.matmul(tgt, nB[:, h, :], xp[:, h, :], start=True, stop=True)
                        return i1
                    steps.append(("pe", mmp2))
                    steps.append(("dve", lambda e: e.tensor_add(xp.rearrange("p h t -> p (h t)")[:, 0:512], xp.rearrange("p h t -> p (h t)")[:, 0:512], pP1[:])))
                    steps.append(("dve", lambda e: e.tensor_add(xp.rearrange("p h t -> p (h t)")[:, 512:768], xp.rearrange("p h t -> p (h t)")[:, 512:768], pP2[:])))
                curB, curb = nB, nb
            steps.append(("dve", lambda e: e.tensor_mul(kbar.rearrange("p h t -> p (h t)"), kc_s.rearrange("p h t -> p (h t)"), expGb[:])))
            steps.append(("dve", lambda e: e.tensor_mul(qbar.rearrange("p h t -> p (h t)"), qc_sb.rearrange("p h t -> p (h t)"), expGb[:])))
            for h in range(NH):
                steps.append(("pe", lambda e, h=h: e.matmul(pU[:], kbar[:, h, :], S_bf[:, h, :], start=True, stop=True)))
                steps.append(("dve", lambda e, h=h: e.tensor_sub(RHS0[:, h, :], vtm[:, h, :], pU[:])))
                steps.append(("pe", lambda e, h=h: e.matmul(pU[:], Xp[:, h, :], RHS0[:, h, :], start=True, stop=True)))
                steps.append(("dve", lambda e, h=h, ci=ci: e.tensor_scalar_mul(Us[:, h, :], pU[:], beta_all[:, ci, h:h + 1])))
            mm6(steps, PkM, lambda h: kc_s[:, h, :], lambda h: qc_sb[:, h, :])
            steps.append(("dve", lambda e: e.tensor_mul(
                PkM.rearrange("p (h t) -> p h t", h=NH)[:], PkM.rearrange("p (h t) -> p h t", h=NH)[:], triu_bc)))
            steps.append(("dve", lambda e: e.tensor_mul(W2t.rearrange("p h t -> p (h t)"), PkM[:], expEd[:])))
            def mmO(e):
                for h in range(NH):
                    tgt = pO[h // 2][:, (h % 2) * 256:(h % 2) * 256 + 256]
                    e.matmul(tgt, qbar[:, h, :], S_bf[:, h, :], start=True, stop=False)
                    i1 = e.matmul(tgt, W2t[:, h, :], Us[:, h, :], start=False, stop=True)
                return i1
            steps.append(("pe", mmO))
            steps.append(("dve", lambda e: e.memset(oss[:], 0.0)))
            for h in range(NH):
                steps.append(("act", lambda e, h=h: e.activation(
                    sqsc[:], pO[h // 2][:, (h % 2) * 256:(h % 2) * 256 + 256], AF.Square,
                    accum_out=oss[:, h:h + 1])))
            steps.append(("dve", lambda e: e.tensor_scalar_mul(oss[:, 0:6], oss[:, 0:6], 1.0 / 256.0)))
            steps.append(("dve", lambda e: e.tensor_scalar_add(oss[:, 0:6], oss[:, 0:6], 1e-5)))
            steps.append(("act", lambda e: e.activation(oss[:, 0:6], oss[:, 0:6], AF.Ln)))
            steps.append(("act", lambda e: e.activation(oss[:, 0:6], oss[:, 0:6], AF.Exp, scale=-0.5)))
            for h in range(NH):
                steps.append(("dve", lambda e, h=h: e.tensor_scalar_mul(
                    o_s[:, h * 256:(h + 1) * 256], pO[h // 2][:, (h % 2) * 256:(h % 2) * 256 + 256],
                    oss[:, h:h + 1])))
            steps.append(("dve", lambda e: e.tensor_mul(
                o_s.rearrange("p (h d) -> p h d", h=NH)[:], o_s.rearrange("p (h d) -> p h d", h=NH)[:], nwb_bc)))
            steps.append(("dve", lambda e: e.tensor_mul(o_s[:], o_s[:], gate_c[:])))
            steps.append(("dve", lambda e: e.tensor_copy(o_bf[:], o_s[:])))
            for kc2 in range(12):
                steps.append(("pe", lambda e, kc2=kc2: e.transpose(
                    pT[:], o_bf[:, kc2 * 128:(kc2 + 1) * 128], P.c["c_idbf"][:])))
                steps.append(("act", lambda e, kc2=kc2: e.copy(o_fm[:, kc2, :], pT[:])))
            for n2 in range(2):
                def mmS(e, n2=n2):
                    for kc2 in range(12):
                        i1 = e.matmul(pS[:], o_fm[:, kc2, :], wo[:, kc2, n2 * 512:(n2 + 1) * 512],
                                      start=(kc2 == 0), stop=(kc2 == 11))
                    return i1
                steps.append(("pe", mmS))
                steps.append(("act", lambda e, ci=ci, n2=n2: e.copy(s_buf[:, ci, n2 * 512:(n2 + 1) * 512], pS[:])))
            if DEBUG is not None and ci == 0 and getattr(P, "dbg_seq", 1) == 0:
                steps.append(("sync_dma", lambda e: (
                    e.dma_start(DEBUG["ed"], expEd[:]).then_inc(P.s["sd"].h, 16),
                    e.dma_start(DEBUG["xp"], Xp[:]).then_inc(P.s["sd"].h, 16),
                    e.dma_start(DEBUG["us"], Us[:]).then_inc(P.s["sd"].h, 16),
                    e.dma_start(DEBUG["os"], o_s))[-1], 4))
            for h in range(NH):
                steps.append(("pe", lambda e, h=h: e.transpose(pT[:], kc_s[:, h, :], P.c["c_idbf"][:])))
                steps.append(("act", lambda e: e.copy(ktm[:], pT[:])))
                steps.append(("dve", lambda e, h=h: e.tensor_scalar_mul(
                    k2tm[:], ktm[:], expEd[:, h * 128 + 127:h * 128 + 128])))
                steps.append(("pe", lambda e, h=h: e.matmul(pU[:], k2tm[:], Us[:, h, :], start=True, stop=True)))
                steps.append(("dve", lambda e, h=h: e.scalar_tensor_tensor(
                    S_st[h][:], S_st[h][:], cC[:, h:h + 1], pU[:], ALU.mult, ALU.add)))
                steps.append(("dve", lambda e, h=h: e.tensor_copy(S_bf[:, h, :], S_st[h][:])))
            chain(P, steps)



def emit_combine(P, x_ap, s_scr, y_buf, out_ap):
    nc = P.nc
    with ExitStack() as ctx:
        xt = [P.sbuf(ctx, "cbx", [128, D], F32) for _ in range(2)]
        st_ = [P.sbuf(ctx, "cbs", [128, D], F32) for _ in range(2)]
        gy = P.sbuf(ctx, "cbg", [128, D], F32)
        for i in range(NTILE):
            b = i % 2
            steps = []

            def load2(e, i=i, b=b):
                e.dma_start(xt[b][:], x_ap[i * 128:(i + 1) * 128, :]).then_inc(P.s["sd"].h, 16)
                return e.dma_start(st_[b][:], s_scr[:, i * 1024:(i + 1) * 1024])
            steps.append(("sync_dma", load2, 2))
            steps.append(("act", lambda e, i=i: e.activation(gy[:], y_buf[:, i, :], AF.Silu)))
            steps.append(("dve", lambda e, b=b: e.tensor_mul(gy[:], gy[:], st_[b][:])))
            steps.append(("dve", lambda e, b=b: e.tensor_add(gy[:], gy[:], xt[b][:])))
            steps.append(("act_dma", lambda e, i=i: e.dma_start(
                out_ap[i * 128:(i + 1) * 128, :], gy[:])))
            chain(P, steps)
    with nc.Block() as blk:
        @blk.vector
        def _(e):
            e.wait_ge(P.s["sq"].h, P.s["sq"].v)


def park_s(P, s_buf, s_scr):
    with P.nc.Block() as blk:
        @blk.sync
        def _(e):
            e.dma_start(s_scr, s_buf.rearrange("p t d -> p (t d)")).then_inc(P.s["sd"].h, 16)
            P.s["sd"].v += 16
            e.wait_ge(P.s["sd"].h, P.s["sd"].v)





def _emit_rms(P, R, xr_fm, wb_name):
    nc = P.nc
    with ExitStack() as ctx:
        sq = P.sbuf(ctx, "rsq", [128, D], F32)
        st = P.sbuf(ctx, "rst", [128, 4], F32)
        tb = P.sbuf(ctx, "rtb", [128, D], F32)
        xnb = P.sbuf(ctx, "rbf", [128, D], BF16)
        ptr = [P.psum(ctx, "rp", [128, 128], BF16) for _ in range(2)]
        for i in range(NTILE):
            steps = []
            steps.append(("act", lambda e, i=i: e.activation(sq[:], R[:, i, :], AF.Square, accum_out=st[:, 0:1])))
            steps.append(("dve", lambda e: e.tensor_scalar_mul(st[:, 1:2], st[:, 0:1], 1.0 / D)))
            steps.append(("dve", lambda e: e.tensor_scalar_add(st[:, 1:2], st[:, 1:2], 1e-5)))
            steps.append(("act", lambda e: e.activation(st[:, 2:3], st[:, 1:2], AF.Ln)))
            steps.append(("act", lambda e: e.activation(st[:, 2:3], st[:, 2:3], AF.Exp, scale=-0.5)))
            steps.append(("dve", lambda e, i=i: e.tensor_scalar_mul(tb[:], R[:, i, :], st[:, 2:3])))
            steps.append(("dve", lambda e, wb_name=wb_name: e.tensor_mul(xnb[:], tb[:], P.sb[wb_name][:])))
            for j in range(NCH):
                steps.append(("pe", lambda e, j=j: e.transpose(
                    ptr[j % 2][:], xnb[:, j * 128:(j + 1) * 128], P.c["c_idbf"][:])))
                steps.append(("dve", lambda e, i=i, j=j: e.tensor_copy(
                    xr_fm[:, j, i * 128:(i + 1) * 128], ptr[j % 2][:])))
            chain(P, steps)


def emit_l1_ffn_attn(P, xr_fm, R):
    nc = P.nc
    scr_q1 = P.w["scr_q1"]
    with ExitStack() as octx:
        kv_fm = P.sbuf(octx, "kvfm", [128, 4, T], BF16)
        krope = P.sbuf(octx, "krope", [32, T], BF16)
        ctx_full = P.sbuf(octx, "ctxf", [128, NCH, T], BF16)
        with P.nc.Block() as blk:
            @blk.gpsimd
            def _(e):
                e.memset(ctx_full[:], 0.0)
        P.last = None
        with ExitStack() as ctx:
            q_buf = P.sbuf(ctx, "qb1", [128, 4, T], BF16)
            wq = P.sbuf(ctx, "wq1", [128, 8, 512], BF16)
            ropeC = P.sbuf(ctx, "ropeC", [128, T], BF16)
            ropeS = P.sbuf(ctx, "ropeS", [128, T], BF16)
            t1 = P.sbuf(ctx, "rt1", [128, 512], F32)
            t2 = P.sbuf(ctx, "rt2", [128, 512], F32)
            pq = [P.psum(ctx, "pq1", [128, 512], F32) for _ in range(2)]
            psw = P.psum(ctx, "psw", [128, 512], F32)
            steps = [("sync_dma", lambda e: (
                e.dma_start(ropeC[:], P.w["c_ropeC"]).then_inc(P.s["sd"].h, 16),
                e.dma_start(ropeS[:], P.w["c_ropeS"]))[-1], 2)]
            for half in range(2):
                steps.append(("pool_dma", lambda e, half=half: e.dma_start(
                    wq[:], P.w["L1_ffn_Wq"].rearrange("(c p) n -> p c n", p=128)[:, :, half * 512:(half + 1) * 512])))
                for m in range(4):
                    for n in range(4):
                        def mmq(e, m=m, n=n):
                            for c in range(NCH):
                                i1 = e.matmul(pq[n % 2][:], wq[:, c, m * 128:(m + 1) * 128],
                                              xr_fm[:, c, n * 512:(n + 1) * 512], start=(c == 0), stop=(c == 7))
                            return i1
                        steps.append(("pe", mmq))
                        steps.append(("act", lambda e, m=m, n=n: e.copy(q_buf[:, m, n * 512:(n + 1) * 512], pq[n % 2][:])))
                for m in range(4):
                    for n in range(4):
                        sl = slice(n * 512, (n + 1) * 512)
                        steps.append(("pe", lambda e, m=m, sl=sl: e.matmul(
                            psw[:], P.c["c_sperm"][:], q_buf[:, m, sl], start=True, stop=True)))
                        steps.append(("dve", lambda e, m=m, sl=sl: e.tensor_mul(t1[:], q_buf[:, m, sl], ropeC[:, sl])))
                        steps.append(("dve", lambda e, sl=sl: e.tensor_mul(t2[:], psw[:], ropeS[:, sl])))
                        steps.append(("dve", lambda e, m=m, sl=sl: e.tensor_add(q_buf[:, m, sl], t1[:], t2[:])))
                steps.append(("sync_dma", lambda e, half=half: e.dma_start(
                    scr_q1[:, half * 4:(half + 1) * 4, :], q_buf[:])))
            chain(P, steps)
        with ExitStack() as ctx:
            wkva = P.sbuf(ctx, "wkva", [128, 8, 160], BF16)
            wkvb = P.sbuf(ctx, "wkvb", [128, 512], BF16)
            kvlat = P.sbuf(ctx, "kvlat", [128, T], BF16)
            krf = P.sbuf(ctx, "krf", [32, T], BF16)
            ropeC = P.sbuf(ctx, "ropeC2", [32, T], BF16)
            ropeS = P.sbuf(ctx, "ropeS2", [32, T], BF16)
            t1 = P.sbuf(ctx, "rt3", [32, 512], F32)
            t2 = P.sbuf(ctx, "rt4", [32, 512], F32)
            pq = [P.psum(ctx, "pq2", [128, 512], F32) for _ in range(2)]
            psw = P.psum(ctx, "psw2", [32, 512], F32)
            steps = [("pool_dma", lambda e: (
                e.dma_start(wkva[:], P.w["L1_ffn_Wkva"].rearrange("(c p) n -> p c n", p=128)).then_inc(P.s["sg"].h, 16),
                e.dma_start(wkvb[:], P.w["L1_ffn_Wkvb"]))[-1], 2)]
            steps.append(("sync_dma", lambda e: (
                e.dma_start(ropeC[:], P.w["c_ropeC"][32:64, :]).then_inc(P.s["sd"].h, 16),
                e.dma_start(ropeS[:], P.w["c_ropeS"][32:64, :]))[-1], 2))
            for n in range(4):
                sl = slice(n * 512, (n + 1) * 512)
                def mma(e, n=n, sl=sl):
                    for c in range(NCH):
                        i1 = e.matmul(pq[0][:], wkva[:, c, 0:128],
                                      xr_fm[:, c, sl], start=(c == 0), stop=(c == 7))
                    return i1
                steps.append(("pe", mma))
                steps.append(("act", lambda e, sl=sl: e.copy(kvlat[:, sl], pq[0][:])))
                def mmb(e, n=n, sl=sl):
                    for c in range(NCH):
                        i1 = e.matmul(pq[1][0:32, :], wkva[:, c, 128:160],
                                      xr_fm[:, c, sl], start=(c == 0), stop=(c == 7))
                    return i1
                steps.append(("pe", mmb))
                steps.append(("act", lambda e, sl=sl: e.copy(krf[:, sl], pq[1][0:32, :])))
                steps.append(("pe", lambda e, sl=sl: e.matmul(psw[:], P.c["c_sperm32"][:], krf[:, sl], start=True, stop=True)))
                steps.append(("dve", lambda e, sl=sl: e.tensor_mul(t1[:], krf[:, sl], ropeC[:, sl])))
                steps.append(("dve", lambda e, sl=sl: e.tensor_mul(t2[:], psw[:], ropeS[:, sl])))
                steps.append(("dve", lambda e, sl=sl: e.tensor_add(krope[:, sl], t1[:], t2[:])))
                for m4 in range(4):
                    steps.append(("pe", lambda e, m4=m4, sl=sl: e.matmul(
                        pq[0][:], wkvb[:, m4 * 128:(m4 + 1) * 128], kvlat[:, sl], start=True, stop=True)))
                    steps.append(("act", lambda e, m4=m4, sl=sl: e.copy(kv_fm[:, m4, sl], pq[0][:])))
            chain(P, steps)
        with ExitStack() as ctx:
            qp = P.sbuf(ctx, "qp3", [128, T], BF16)
            kpair = P.sbuf(ctx, "kp3", [128, T], BF16)
            vsrcO = P.sbuf(ctx, "vsrcO3", [32, T], BF16)
            tmp64 = P.sbuf(ctx, "tmp64_3", [64, T], BF16)
            v_tmE = P.sbuf(ctx, "vtmE3", [128, NTILE, 33], BF16)
            v_tmO = P.sbuf(ctx, "vtmO3", [128, NTILE, 33], BF16)
            sqh = P.sbuf(ctx, "sqh3", [128, 1024], F32)
            qmax2 = P.sbuf(ctx, "qmax2", [2, 8], F32)
            m2 = P.sbuf(ctx, "m2s", [2, 2], F32)
            row2 = P.sbuf(ctx, "row2s", [1, 2], F32)
            bnd2 = P.sbuf(ctx, "bnd2s", [128, 2], F32)
            invrE = P.sbuf(ctx, "invrE3", [1, 512], F32)
            invrO = P.sbuf(ctx, "invrO3", [1, 512], F32)
            ctx32E = P.sbuf(ctx, "ctx32E", [32, 512], BF16)
            ctx32O = P.sbuf(ctx, "ctx32O", [32, 512], BF16)
            p_tE = [P.sbuf(ctx, "ptE3", [128, 512], BF16) for _ in range(2)]
            p_tO = [P.sbuf(ctx, "ptO3", [128, 512], BF16) for _ in range(2)]
            pstE = [P.psum(ctx, "pstE3", [128, 512], F32) for _ in range(2)]
            pstO = [P.psum(ctx, "pstO3", [128, 512], F32) for _ in range(2)]
            pctxE = P.psum(ctx, "pctxE3", [33, 512], F32)
            pctxO = P.psum(ctx, "pctxO3", [33, 512], F32)
            pVT = P.psum(ctx, "pVT3", [128, 512], BF16)
            pSml = P.psum(ctx, "pSml3", [128, 8], F32)
            pe, act, dve = P.s["pe"], P.s["act"], P.s["dve"]
            chain(P, [("pool", lambda e: (
                e.memset(v_tmE[:], 0.0), e.memset(v_tmE[:, :, 32:33], 1.0),
                e.memset(v_tmO[:], 0.0), e.memset(v_tmO[:, :, 32:33], 1.0))[-1])])
            for j in range(8):
                he, ho = 2 * j, 2 * j + 1
                kb = (j % 2) * 64
                steps = []
                steps.append(("sync_dma", lambda e, j=j: e.dma_start(qp[:], scr_q1[:, j, :])))
                if j % 2 == 0:
                    def asm(e, j=j):
                        e.copy(kpair[0:32, :], kv_fm[0:32, j // 2, :])
                        e.copy(kpair[64:96, :], kv_fm[32:64, j // 2, :])
                        e.copy(kpair[32:64, :], krope[:])
                        e.copy(kpair[96:128, :], krope[:])
                        return e.copy(vsrcO[0:32, :], kv_fm[32:64, j // 2, :])
                else:
                    def asm(e, j=j):
                        e.copy(tmp64[0:64, :], kv_fm[64:128, j // 2, :])
                        e.copy(kpair[0:32, :], tmp64[0:32, :])
                        e.copy(kpair[64:96, :], tmp64[32:64, :])
                        e.copy(kpair[32:64, :], krope[:])
                        e.copy(kpair[96:128, :], krope[:])
                        return e.copy(vsrcO[0:32, :], tmp64[32:64, :])
                steps.append(("act", asm))
                for (src, vdst) in ((kpair, v_tmE), (vsrcO, v_tmO)):
                    def trv(e, src=src):
                        for u in range(16):
                            i1 = e.transpose(pVT[:, u * 32:(u + 1) * 32],
                                             src[0:32, u * 128:(u + 1) * 128],
                                             P.c["c_idbf"][0:32, 0:32])
                        return i1
                    steps.append(("pe", trv))
                    def vcp(e, vdst=vdst):
                        for u in range(16):
                            i1 = e.copy(vdst[:, u, 0:32], pVT[:, u * 32:(u + 1) * 32])
                        return i1
                    steps.append(("act", vcp))
                for ti, tile in ((0, qp), (1, kpair)):
                    for half in range(2):
                        hsl = slice(half * 1024, (half + 1) * 1024)
                        steps.append(("act", lambda e, tile=tile, hsl=hsl: e.activation(
                            sqh[:], tile[:, hsl], AF.Square)))
                        for n in range(2):
                            pX = pstE[n % 2]
                            steps.append(("pe", lambda e, n=n, pX=pX: e.matmul(
                                pX[0:2, :], P.c["c_sel2"][:], sqh[:, n * 512:(n + 1) * 512],
                                start=True, stop=True)))
                            steps.append(("dve", lambda e, ti=ti, half=half, n=n, pX=pX: e.reduce_max(
                                qmax2[:, ti * 4 + half * 2 + n:ti * 4 + half * 2 + n + 1],
                                pX[0:2, :], axis=AX)))
                steps.append(("dve", lambda e: (
                    e.reduce_max(m2[:, 0:1], qmax2[:, 0:4], axis=AX),
                    e.reduce_max(m2[:, 1:2], qmax2[:, 4:8], axis=AX))[-1]))
                steps.append(("dve", lambda e: e.tensor_mul(m2[:, 0:1], m2[:, 0:1], m2[:, 1:2])))
                steps.append(("act", lambda e: e.activation(m2[:, 0:1], m2[:, 0:1], AF.Ln)))
                steps.append(("act", lambda e: e.activation(m2[:, 0:1], m2[:, 0:1], AF.Exp, scale=0.5)))
                steps.append(("act", lambda e: e.mul(m2[:, 0:1], m2[:, 0:1], -0.125)))
                steps.append(("pe", lambda e: e.transpose(pSml[0:1, 0:2], m2[:, 0:1], P.c["c_id32"][0:2, 0:2])))
                steps.append(("dve", lambda e: e.tensor_copy(row2[:], pSml[0:1, 0:2])))
                steps.append(("pe", lambda e: e.matmul(pSml[:, 0:2], P.c["c_ones_r"][:], row2[:], start=True, stop=True)))
                steps.append(("dve", lambda e: e.tensor_copy(bnd2[:], pSml[:, 0:2])))
                chain(P, steps)
                for qc in range(4):
                    nkt = (qc + 1) * 4
                    qsl = slice(qc * 512, (qc + 1) * 512)
                    pe0, act0, dve0 = pe.v, act.v, dve.v
                    ST_m = {}
                    PV_m = {}
                    m = pe0
                    for kt in range(nkt):
                        m += 1
                        ST_m[kt] = m
                        if kt >= 1:
                            m += 1
                            PV_m[kt - 1] = m
                    m += 1
                    PV_m[nkt - 1] = m
                    exp_m = {kt: act0 + kt + 1 for kt in range(nkt)}
                    msk_m = {kt: dve0 + (kt - (nkt - 4)) + 1 for kt in range(nkt - 4, nkt)}

                    def p_ready(kt, nkt=nkt, msk_m=msk_m, exp_m=exp_m):
                        if kt >= nkt - 4:
                            return (dve, msk_m[kt])
                        return (act, exp_m[kt])

                    start_wait = P.last
                    with nc.Block() as blk:
                        @blk.tensor
                        def _(e):
                            if start_wait is not None:
                                e.wait_ge(start_wait[0].h, start_wait[1])
                            for kt in range(nkt):
                                bb = kt % 2
                                if kt >= 2:
                                    e.wait_ge(act.h, exp_m[kt - 2])
                                e.matmul(pstE[bb][:], kpair[0:64, kt * 128:(kt + 1) * 128],
                                         qp[0:64, qsl], start=True, stop=True)
                                e.matmul(pstO[bb][:], kpair[64:128, kt * 128:(kt + 1) * 128],
                                         qp[64:128, qsl], start=True, stop=True).then_inc(pe.h, 1)
                                if kt >= 1:
                                    ss, vv = p_ready(kt - 1)
                                    e.wait_ge(ss.h, vv)
                                    e.matmul(pctxE[:], v_tmE[:, kt - 1, :], p_tE[(kt - 1) % 2][:],
                                             start=(kt - 1 == 0), stop=False)
                                    e.matmul(pctxO[:], v_tmO[:, kt - 1, :], p_tO[(kt - 1) % 2][:],
                                             start=(kt - 1 == 0), stop=False).then_inc(pe.h, 1)
                            ss, vv = p_ready(nkt - 1)
                            e.wait_ge(ss.h, vv)
                            e.matmul(pctxE[:], v_tmE[:, nkt - 1, :], p_tE[(nkt - 1) % 2][:],
                                     start=(nkt - 1 == 0), stop=True)
                            e.matmul(pctxO[:], v_tmO[:, nkt - 1, :], p_tO[(nkt - 1) % 2][:],
                                     start=(nkt - 1 == 0), stop=True).then_inc(pe.h, 1)
                        @blk.scalar
                        def _(e):
                            for kt in range(nkt):
                                bb = kt % 2
                                e.wait_ge(pe.h, ST_m[kt])
                                if kt >= 2:
                                    e.wait_ge(pe.h, PV_m[kt - 2])
                                e.activation(p_tE[bb][:], pstE[bb][:], AF.Exp, scale=0.125,
                                             bias=bnd2[:, 0:1])
                                e.activation(p_tO[bb][:], pstO[bb][:], AF.Exp, scale=0.125,
                                             bias=bnd2[:, 1:2]).then_inc(act.h, 1)
                        @blk.vector
                        def _(e):
                            for kt in range(nkt - 4, nkt):
                                e.wait_ge(act.h, exp_m[kt])
                                jj = kt - (nkt - 4)
                                e.tensor_mul(p_tE[kt % 2][:], p_tE[kt % 2][:], dm_mask(P, jj))
                                e.tensor_mul(p_tO[kt % 2][:], p_tO[kt % 2][:],
                                             dm_mask(P, jj)).then_inc(dve.h, 1)
                    pe.v = m
                    act.v = act0 + nkt
                    dve.v = dve0 + 4
                    P.last = (pe, m)
                    steps = []
                    steps.append(("dve", lambda e: (
                        e.reciprocal(invrE[:], pctxE[32:33, :]),
                        e.reciprocal(invrO[:], pctxO[32:33, :]))[-1]))
                    steps.append(("act", lambda e: (
                        e.copy(ctx32E[:], pctxE[0:32, :]),
                        e.copy(ctx32O[:], pctxO[0:32, :]))[-1]))
                    steps.append(("pe", lambda e: (
                        e.matmul(pstE[0][0:32, :], P.c["c_ones_r"][0:1, 0:32], invrE[:], start=True, stop=True),
                        e.matmul(pstO[0][0:32, :], P.c["c_ones_r"][0:1, 0:32], invrO[:], start=True, stop=True))[-1]))
                    steps.append(("dve", lambda e, qsl=qsl, j=j: (
                        e.tensor_mul(ctx_full[0:32, j, qsl], ctx32E[:], pstE[0][0:32, :]),
                        e.tensor_mul(ctx_full[64:96, j, qsl], ctx32O[:], pstO[0][0:32, :]))[-1]))
                    chain(P, steps)
        with ExitStack() as ctx:
            wo = P.sbuf(ctx, "woeff", [128, 8, 1024], BF16)
            pS = P.psum(ctx, "pS4", [128, 512], F32)
            steps = [("pool_dma", lambda e: e.dma_start(
                wo[:], P.w["L1_ffn_Wo"].rearrange("(c p) n -> p c n", p=128)))]
            for t in range(NTILE):
                for n2 in range(2):
                    def mmS(e, t=t, n2=n2):
                        for kc in range(NCH):
                            i1 = e.matmul(pS[:], ctx_full[:, kc, t * 128:(t + 1) * 128],
                                          wo[:, kc, n2 * 512:(n2 + 1) * 512], start=(kc == 0), stop=(kc == 7))
                        return i1
                    steps.append(("pe", mmS))
                    steps.append(("dve", lambda e, t=t, n2=n2: e.tensor_add(
                        R[:, t, n2 * 512:(n2 + 1) * 512], R[:, t, n2 * 512:(n2 + 1) * 512], pS[:])))
            chain(P, steps)


def emit_l1_moe(P, xr_fm, R):
    nc = P.nc
    with ExitStack() as ctx:
        combine = P.sbuf(ctx, "comb1", [128, NTILE, 8], F32)
        with ExitStack() as c2:
            wgate = P.sbuf(c2, "wgate", [128, 8, 8], BF16)
            zrow = P.sbuf(c2, "zrow", [128, 8], F32)
            m1 = P.sbuf(c2, "m1", [128, 2], F32)
            top8 = P.sbuf(c2, "top81", [128, 8], F32)
            pgl = P.psum(c2, "pgl", [128, 8], F32)
            steps = [("pool_dma", lambda e: e.dma_start(wgate[:], P.w["L1_ffn_Wgate"].rearrange("(c p) n -> p c n", p=128)))]
            for i in range(NTILE):
                def mmg(e, i=i):
                    for c in range(NCH):
                        i1 = e.matmul(pgl[:], xr_fm[:, c, i * 128:(i + 1) * 128],
                                      wgate[:, c, :], start=(c == 0), stop=(c == 7))
                    return i1
                steps.append(("pe", mmg))
                steps.append(("dve", lambda e: e.tensor_add(zrow[:], pgl[:], P.sb["l1bias"][:])))
                steps.append(("dve", lambda e: e.reduce_max(m1[:, 0:1], zrow[:], axis=AX)))
                steps.append(("dve", lambda e: e.tensor_scalar_mul(m1[:, 0:1], m1[:, 0:1], -1.0)))
                steps.append(("act", lambda e: e.activation(zrow[:], zrow[:], AF.Exp, bias=m1[:, 0:1])))
                steps.append(("dve", lambda e: e.max(out=top8[:], in_=zrow[:])))
                steps.append(("dve", lambda e: e.tensor_single_scalar(top8[:], zrow[:], top8[:, 1:2], ALU.is_ge)))
                steps.append(("dve", lambda e: e.tensor_mul(zrow[:], zrow[:], top8[:])))
                steps.append(("dve", lambda e: e.reduce_sum(m1[:, 1:2], zrow[:], axis=AX)))
                steps.append(("dve", lambda e: e.reciprocal(m1[:, 1:2], m1[:, 1:2])))
                steps.append(("dve", lambda e, i=i: e.tensor_scalar_mul(combine[:, i, :], zrow[:], m1[:, 1:2])))
            chain(P, steps)
        h_full = P.sbuf(ctx, "hf1", [128, 4, T], BF16)
        wg_b = [P.sbuf(ctx, "wgb1", [128, 8, 512], BF16) for _ in range(2)]
        wu_b = [P.sbuf(ctx, "wub1", [128, 8, 512], BF16) for _ in range(2)]
        wd_b = [P.sbuf(ctx, "wdb1", [128, 4, 1024], BF16) for _ in range(2)]
        ts = [P.sbuf(ctx, "ts1", [128, 512], BF16) for _ in range(2)]
        pg = [P.psum(ctx, "pg1", [128, 512], F32) for _ in range(2)]
        pu = [P.psum(ctx, "pu1", [128, 512], F32) for _ in range(2)]
        pd = [P.psum(ctx, "pd1", [128, 512], F32) for _ in range(2)]
        sd, sg, pe, act, dve = P.s["sd"], P.s["sg"], P.s["pe"], P.s["act"], P.s["dve"]
        experts = [("sh", "L1_ffn_sWg", "L1_ffn_sWu", "L1_ffn_sWd", 0, None)] + [
            ("e%d" % e, "L1_ffn_Wg", "L1_ffn_Wu", "L1_ffn_Wd", e, e) for e in range(8)]
        for ei, (nm, ng, nu, nd, srcidx, eidx) in enumerate(experts):
            with nc.Block() as blk:
                @blk.gpsimd
                def _(e):
                    e.dma_start(wg_b[ei % 2][:], P.w[ng][srcidx].rearrange("(c p) i -> p c i", p=128)).then_inc(sg.h, 16)
                    e.dma_start(wu_b[ei % 2][:], P.w[nu][srcidx].rearrange("(c p) i -> p c i", p=128)).then_inc(sg.h, 16)
                    e.dma_start(wd_b[ei % 2][:], P.w[nd][srcidx].rearrange("(c p) i -> p c i", p=128)).then_inc(sg.h, 16)
                    sg.v += 48
                    e.wait_ge(sg.h, sg.v)
            pe0, act0, dve0 = pe.v, act.v, dve.v
            items = [(mm, n) for mm in range(4) for n in range(4)]
            g_m = {}
            u_m = {}
            mct = pe0
            for it in items:
                mct += 1
                g_m[it] = mct
                mct += 1
                u_m[it] = mct
            silu_m = {it: act0 + i + 1 for i, it in enumerate(items)}
            h_m = {it: dve0 + i + 1 for i, it in enumerate(items)}
            with nc.Block() as blk:
                @blk.tensor
                def _(e):
                    for idx, (mm, n) in enumerate(items):
                        bb = idx % 2
                        if idx >= 2:
                            e.wait_ge(act.h, silu_m[items[idx - 2]])
                        for kc in range(NCH):
                            i1 = e.matmul(pg[bb][:], wg_b[ei % 2][:, kc, mm * 128:(mm + 1) * 128],
                                          xr_fm[:, kc, n * 512:(n + 1) * 512], start=(kc == 0), stop=(kc == 7))
                        i1.then_inc(pe.h, 1)
                        if idx >= 2:
                            e.wait_ge(dve.h, h_m[items[idx - 2]])
                        for kc in range(NCH):
                            i1 = e.matmul(pu[bb][:], wu_b[ei % 2][:, kc, mm * 128:(mm + 1) * 128],
                                          xr_fm[:, kc, n * 512:(n + 1) * 512], start=(kc == 0), stop=(kc == 7))
                        i1.then_inc(pe.h, 1)
                @blk.scalar
                def _(e):
                    for idx, it in enumerate(items):
                        bb = idx % 2
                        e.wait_ge(pe.h, g_m[it])
                        if idx >= 2:
                            e.wait_ge(dve.h, h_m[items[idx - 2]])
                        e.activation(ts[bb][:], pg[bb][:], AF.Silu).then_inc(act.h, 1)
                @blk.vector
                def _(e):
                    for idx, (mm, n) in enumerate(items):
                        bb = idx % 2
                        e.wait_ge(act.h, silu_m[(mm, n)])
                        e.wait_ge(pe.h, u_m[(mm, n)])
                        e.tensor_mul(h_full[:, mm, n * 512:(n + 1) * 512],
                                     ts[bb][:], pu[bb][:]).then_inc(dve.h, 1)
            pe.v = mct
            act.v = act0 + 16
            dve.v = dve0 + 16
            pe0, dve0 = pe.v, dve.v
            items2 = [(t, j2) for t in range(NTILE) for j2 in range(2)]
            dn_m = {}
            mct = pe0
            for it in items2:
                mct += 1
                dn_m[it] = mct
            ev_m = {it: dve0 + i + 1 for i, it in enumerate(items2)}
            with nc.Block() as blk:
                @blk.tensor
                def _(e):
                    for idx, (t, j2) in enumerate(items2):
                        bb = idx % 2
                        if idx >= 2:
                            e.wait_ge(dve.h, ev_m[items2[idx - 2]])
                        for kc in range(4):
                            i1 = e.matmul(pd[bb][:], h_full[:, kc, t * 128:(t + 1) * 128],
                                          wd_b[ei % 2][:, kc, j2 * 512:(j2 + 1) * 512],
                                          start=(kc == 0), stop=(kc == 3))
                        i1.then_inc(pe.h, 1)
                @blk.vector
                def _(e):
                    for idx, (t, j2) in enumerate(items2):
                        bb = idx % 2
                        e.wait_ge(pe.h, dn_m[(t, j2)])
                        dst = R[:, t, j2 * 512:(j2 + 1) * 512]
                        if eidx is None:
                            e.tensor_add(dst, dst, pd[bb][:]).then_inc(dve.h, 1)
                        else:
                            e.scalar_tensor_tensor(dst, pd[bb][:], combine[:, t, eidx:eidx + 1],
                                                   dst, ALU.mult, ALU.add).then_inc(dve.h, 1)
            pe.v = mct
            dve.v = dve0 + len(items2)


def build_layer1():
    nc = bass.Bass("TRN2", target_bir_lowering=False)
    ctx = ExitStack()
    P = Prog(nc, ctx)
    names = (["c_ropeC", "c_ropeS"] + [n for n in WSHAPES if n.startswith("L1_")] + BASE_CONSTS)
    for n in names:
        shape, dt = WSHAPES[n]
        P.w[n] = nc.dram_tensor(n, list(shape), dt, kind="ExternalInput").ap()
    P.w["x1in"] = nc.dram_tensor("x1in", [2, T, D], F32, kind="ExternalInput").ap()
    out = nc.dram_tensor("out", [2, T, D], F32, kind="ExternalOutput").ap()
    scr_s = nc.dram_tensor("scr_s", [128, NTILE * D], F32, kind="Internal").ap()
    scr_xn = nc.dram_tensor("scr_xn", [T, D], F32, kind="Internal").ap()
    scr_q = nc.dram_tensor("scr_q", [128, 6, T], BF16, kind="Internal").ap()
    scr_k = nc.dram_tensor("scr_k", [128, 6, T], BF16, kind="Internal").ap()
    scr_v = nc.dram_tensor("scr_v", [6, T, 256], BF16, kind="Internal").ap()
    scr_g = nc.dram_tensor("scr_g", [T, 1536], BF16, kind="Internal").ap()
    P_scr_q1 = nc.dram_tensor("scr_q1", [128, NCH, T], BF16, kind="Internal").ap()

    P.w["scr_q1"] = P_scr_q1
    import os
    global DEBUG
    if os.environ.get("KDEBUG"):
        DEBUG = {"s": nc.dram_tensor("dbg_s", [2, 128, NTILE * D], F32, kind="ExternalOutput").ap(),
                 "y": nc.dram_tensor("dbg_y", [2, 128, NTILE, D], F32, kind="ExternalOutput").ap(),
                 "q": nc.dram_tensor("dbg_q", [128, 6, T], BF16, kind="ExternalOutput").ap(),
                 "k": nc.dram_tensor("dbg_k", [128, 6, T], BF16, kind="ExternalOutput").ap(),
                 "v": nc.dram_tensor("dbg_v", [6, T, 256], BF16, kind="ExternalOutput").ap(),
                 "bg": nc.dram_tensor("dbg_bg", [2, 128, NTILE, 6], F32, kind="ExternalOutput").ap(),
                 "ed": nc.dram_tensor("dbg_ed", [128, 768], F32, kind="ExternalOutput").ap(),
                 "xp": nc.dram_tensor("dbg_xp", [128, 6, 128], BF16, kind="ExternalOutput").ap(),
                 "us": nc.dram_tensor("dbg_us", [128, 6, 256], BF16, kind="ExternalOutput").ap(),
                 "os": nc.dram_tensor("dbg_os", [128, 1536], F32, kind="ExternalOutput").ap(),
                 "bb": nc.dram_tensor("dbg_bb", [128, 6, 128], BF16, kind="ExternalOutput").ap(),
                 "xp2": nc.dram_tensor("dbg_xp2", [128, 6, 128], BF16, kind="ExternalOutput").ap(),
                 "pkm": nc.dram_tensor("dbg_pkm", [128, 768], F32, kind="ExternalOutput").ap(),
                 "bcol": nc.dram_tensor("dbg_bcol", [128, 768], F32, kind="ExternalOutput").ap()}
    else:
        DEBUG = None
    with ctx:
        buf_a = P.sbuf(ctx, "BUFA", [128, NTILE, D], F32)
        xn_fm = P.sbuf(ctx, "XN", [128, NCH, T], BF16)
        load_consts(P, BASE_CONSTS)
        stage_l1_small(P)
        beta_all = P.sbuf(ctx, "betaA", [128, NTILE, 6], F32)
        g_all = P.sbuf(ctx, "gA", [128, NTILE, 6], F32)
        for seq in range(2):
            P.dbg_seq = seq
            emit_ln(P, P.w["x1in"][seq], xn_fm, xn32_scr=scr_xn)
            barrier(P)
            emit_l1_attn_passA(P, xn_fm, scr_q, scr_k, scr_v, scr_g, beta_all, g_all)
            barrier(P)
            if DEBUG is not None and seq == 0:
                with nc.Block() as blk:
                    @blk.sync
                    def _(e):
                        e.dma_start(DEBUG["q"], scr_q).then_inc(P.s["sd"].h, 16)
                        e.dma_start(DEBUG["k"], scr_k).then_inc(P.s["sd"].h, 16)
                        e.dma_start(DEBUG["v"], scr_v).then_inc(P.s["sd"].h, 16)
                        e.dma_start(DEBUG["bg"][0], beta_all[:]).then_inc(P.s["sd"].h, 16)
                        e.dma_start(DEBUG["bg"][1], g_all[:]).then_inc(P.s["sd"].h, 16)
                        P.s["sd"].v += 80
                        e.wait_ge(P.s["sd"].h, P.s["sd"].v)
            emit_l1_attn_passB(P, buf_a, scr_q, scr_k, scr_v, scr_g, beta_all, g_all)
            barrier(P)
            park_s(P, buf_a, scr_s)
            with nc.Block() as blk:
                @blk.sync
                def _(e):
                    e.dma_start(buf_a[:],
                                scr_xn.rearrange("(t p) d -> p t d", p=128)).then_inc(P.s["sd"].h, 16)
                    P.s["sd"].v += 16
                    e.wait_ge(P.s["sd"].h, P.s["sd"].v)
            _emit_rms(P, buf_a, xn_fm, "ln1b")
            barrier(P)
            emit_l1_ffn_attn(P, xn_fm, buf_a)
            barrier(P)
            _emit_rms(P, buf_a, xn_fm, "ln2b")
            barrier(P)
            emit_l1_moe(P, xn_fm, buf_a)
            barrier(P)
            if DEBUG is not None:
                with nc.Block() as blk:
                    @blk.sync
                    def _(e):
                        e.dma_start(DEBUG["s"][seq], scr_s).then_inc(P.s["sd"].h, 16)
                        e.dma_start(DEBUG["y"][seq], buf_a[:]).then_inc(P.s["sd"].h, 16)
                        P.s["sd"].v += 32
                        e.wait_ge(P.s["sd"].h, P.s["sd"].v)
            emit_combine(P, P.w["x1in"][seq], scr_s, buf_a, out[seq])
            barrier(P)
        P.check()
    return nc



def build_layer0():
    nc = bass.Bass("TRN2", target_bir_lowering=False)
    ctx = ExitStack()
    P = Prog(nc, ctx)
    names = ["x0"] + [n for n in WSHAPES if n.startswith("L0_")] + BASE_CONSTS
    for n in names:
        shape, dt = WSHAPES[n]
        P.w[n] = nc.dram_tensor(n, list(shape), dt, kind="ExternalInput").ap()
    x1_out = nc.dram_tensor("x1", [2, T, D], F32, kind="ExternalOutput").ap()
    global DEBUG
    import os
    if os.environ.get("KDEBUG"):
        DEBUG = {"srow": nc.dram_tensor("dbg_srow", [2, 16, 1, 16], F32, kind="ExternalOutput").ap(),
                 "q": nc.dram_tensor("dbg_q", [2, 128, T], BF16, kind="ExternalOutput").ap(),
                 "k": nc.dram_tensor("dbg_k", [2, 128, T], BF16, kind="ExternalOutput").ap()}
    scr_s = nc.dram_tensor("scr_s", [128, NTILE * D], F32, kind="Internal").ap()

    with ctx:
        buf_a = P.sbuf(ctx, "BUFA", [128, NTILE, D], F32)
        xn_fm = P.sbuf(ctx, "XN", [128, NCH, T], BF16)
        load_consts(P, BASE_CONSTS)
        stage_l0_small(P)
        stage_l0_big(P)
        combine = P.sbuf(ctx, "comb", [128, NTILE, 16], F32)
        for seq in range(2):
            DBGSEQ[0] = seq
            emit_ln(P, P.w["x0"][seq], xn_fm)
            barrier(P)
            emit_l0_attn(P, xn_fm, buf_a)
            barrier(P)
            park_s(P, buf_a, scr_s)
            emit_l0_gating(P, xn_fm, combine)
            barrier(P)
            emit_l0_ffn(P, xn_fm, buf_a, combine)
            barrier(P)
            emit_combine(P, P.w["x0"][seq], scr_s, buf_a, x1_out[seq])
            barrier(P)
        P.check()
    return nc



def build_full():
    global DEBUG
    DEBUG = None
    nc = bass.Bass("TRN2", target_bir_lowering=False)
    ctx = ExitStack()
    P = Prog(nc, ctx)
    names = (["x0", "c_ropeC", "c_ropeS"] + BASE_CONSTS +
             [n for n in WSHAPES if n.startswith("L0_") or n.startswith("L1_")])
    for n in names:
        shape, dt = WSHAPES[n]
        P.w[n] = nc.dram_tensor(n, list(shape), dt, kind="ExternalInput").ap()
    out = nc.dram_tensor("out", [2, T, D], F32, kind="ExternalOutput").ap()
    x1scr = nc.dram_tensor("x1scr", [2, T, D], F32, kind="Internal").ap()
    scr_s = nc.dram_tensor("scr_s", [128, NTILE * D], F32, kind="Internal").ap()
    scr_xn = nc.dram_tensor("scr_xn", [T, D], F32, kind="Internal").ap()
    scr_q = nc.dram_tensor("scr_q", [128, 6, T], BF16, kind="Internal").ap()
    scr_k = nc.dram_tensor("scr_k", [128, 6, T], BF16, kind="Internal").ap()
    scr_v = nc.dram_tensor("scr_v", [6, T, 256], BF16, kind="Internal").ap()
    scr_g = nc.dram_tensor("scr_g", [T, 1536], BF16, kind="Internal").ap()
    P.w["scr_q1"] = nc.dram_tensor("scr_q1", [128, NCH, T], BF16, kind="Internal").ap()
    with ctx:
        buf_a = P.sbuf(ctx, "BUFA", [128, NTILE, D], F32)
        xn_fm = P.sbuf(ctx, "XN", [128, NCH, T], BF16)
        P.sb["joinT"] = P.sbuf(ctx, "joinT", [1, 8], F32)
        stage_l0_cast(P)
        load_consts(P, BASE_CONSTS)
        stage_l0_small(P)
        stage_l1_small(P)
        stage_l0_big(P)
        combine0 = P.sbuf(ctx, "comb", [128, NTILE, 16], F32)
        beta_all = P.sbuf(ctx, "betaA", [128, NTILE, 6], F32)
        g_all = P.sbuf(ctx, "gA", [128, NTILE, 6], F32)
        for seq in range(2):
            emit_ln(P, P.w["x0"][seq], xn_fm)
            emit_l0_attn2(P, xn_fm, buf_a)
            barrier(P)
            park_s(P, buf_a, scr_s)
            emit_l0_gating(P, xn_fm, combine0)
            barrier(P)
            emit_l0_ffn(P, xn_fm, buf_a, combine0)
            barrier(P)
            emit_combine(P, P.w["x0"][seq], scr_s, buf_a, x1scr[seq])
            barrier(P)
        P.check()
        full_barrier(P)
        fresh_sems(P)
        for seq in range(2):
            emit_ln(P, x1scr[seq], xn_fm, xn32_scr=scr_xn)
            emit_l1_attn_passA2(P, xn_fm, buf_a, scr_q, scr_k, scr_v, scr_g,
                                beta_all, g_all)
            barrier(P)
            emit_l1_attn_passB(P, buf_a, scr_q, scr_k, scr_v, scr_g, beta_all, g_all)
            barrier(P)
            park_s(P, buf_a, scr_s)
            with nc.Block() as blk:
                @blk.sync
                def _(e):
                    e.dma_start(buf_a[:],
                                scr_xn.rearrange("(t p) d -> p t d", p=128)).then_inc(P.s["sd"].h, 16)
                    P.s["sd"].v += 16
                    e.wait_ge(P.s["sd"].h, P.s["sd"].v)
            _emit_rms(P, buf_a, xn_fm, "ln1b")
            barrier(P)
            emit_l1_ffn_attn(P, xn_fm, buf_a)
            barrier(P)
            _emit_rms(P, buf_a, xn_fm, "ln2b")
            barrier(P)
            emit_l1_moe(P, xn_fm, buf_a)
            barrier(P)
            emit_combine(P, x1scr[seq], scr_s, buf_a, out[seq])
            barrier(P)
        P.check()
    return nc



_HOST_C = None
_PROGS = {}

def _consts():
    global _HOST_C
    if _HOST_C is None:
        import ml_dtypes
        _HOST_C = host_constants()
        for n, dt in CONST_DTYPES.items():
            if dt == BF16:
                _HOST_C[n] = _HOST_C[n].astype(ml_dtypes.bfloat16)
    return _HOST_C


def kernel(**inputs):
    consts = _consts()
    if "full" not in _PROGS:
        _PROGS["full"] = build_full()
    x = np.ascontiguousarray(np.asarray(inputs["L0_attn_x"], dtype=np.float32))
    w0T = np.empty((51, 1024, 1024), dtype=np.float32)
    w0T[0:16] = np.asarray(inputs["L0_ffn_w_gate"], dtype=np.float32).transpose(0, 2, 1)
    w0T[16:32] = np.asarray(inputs["L0_ffn_w_up"], dtype=np.float32).transpose(0, 2, 1)
    w0T[32:48] = np.asarray(inputs["L0_ffn_w_down"], dtype=np.float32).transpose(0, 2, 1)
    w0T[48] = np.asarray(inputs["L0_ffn_shared_gate_w"], dtype=np.float32).T
    w0T[49] = np.asarray(inputs["L0_ffn_shared_up_w"], dtype=np.float32).T
    w0T[50] = np.asarray(inputs["L0_ffn_shared_down_w"], dtype=np.float32).T
    maps = []
    for i in range(8):
        m = {"x0": x[2 * i:2 * i + 2], "L0_w0T": w0T}
        for n in inputs:
            if n in WSHAPES:
                m[n] = inputs[n]
        for n in BASE_CONSTS + ["c_ropeC", "c_ropeS"]:
            m[n] = consts[n]
        maps.append(m)
    res = run_bass_kernel_spmd(_PROGS["full"], maps, core_ids=list(range(8)))
    out = np.concatenate([np.asarray(res.results[i]["out"], dtype=np.float32)
                          for i in range(8)], axis=0)
    return out
